# version 11
# speedup vs baseline: 1.5008x; 1.0040x over previous
import numpy as np
import concourse.bass as bass
import concourse.mybir as mybir
from concourse.bass_utils import run_bass_kernel_spmd

F32 = mybir.dt.float32
BF16 = mybir.dt.bfloat16
AF = mybir.ActivationFunctionType
ALU = mybir.AluOpType
AX = mybir.AxisListType

B, T, D = 4, 4096, 1024
NCORES = 8
TLOC = 2048
NT = TLOC // 128
LN_EPS = 1e-5

RH = 4
RDK, RDV = 256, 512
RC = 128
RNC = TLOC // RC
GAMMAS = [1.0 - 2.0 ** (-5.0 - h) for h in range(RH)]

NE, NS = 16, 16
ES = NE * NS
FH = 256
NORM_EPS = 1e-12

AH, ADH = 16, 64
ME = 8
CAP = float((B * T) // ME)

_BUILD_UPTO = 99


def _host_consts(core):
    import ml_dtypes
    half = core % 2
    c = {}
    inv_freq = 1.0 / (10000.0 ** (np.arange(0, RDK, 2, dtype=np.float64) / RDK))
    tpos = half * TLOC + np.arange(TLOC, dtype=np.float64)
    fr = inv_freq[:, None] * tpos[None, :]
    c["rope_cos"] = np.cos(fr).astype(np.float32)
    c["rope_sin"] = np.sin(fr).astype(np.float32)
    c["rope_cos_q"] = (np.cos(fr) * (RDK ** -0.5)).astype(np.float32)
    c["rope_sin_q"] = (np.sin(fr) * (RDK ** -0.5)).astype(np.float32)
    maskT = np.zeros((RH, RC, RC), dtype=np.float32)
    cdec = np.zeros((RH, 128, RC), dtype=np.float32)
    kdec = np.zeros((RH, RC, 1), dtype=np.float32)
    for h, g in enumerate(GAMMAS):
        i = np.arange(RC, dtype=np.float64)
        dif = i[None, :] - i[:, None]
        maskT[h] = np.where(dif >= 0, g ** dif, 0.0).astype(np.float32)
        cdec[h] = np.broadcast_to((g ** (i + 1.0)).astype(np.float32), (128, RC))
        kdec[h] = (g ** (RC - 1.0 - i)).astype(np.float32)[:, None]
    c["ret_maskT"] = maskT
    c["ret_cdec"] = cdec
    c["ret_kdec"] = kdec
    c["ret_flag"] = np.full((128, 1), float(half), dtype=np.float32)
    c["ident"] = np.eye(128, dtype=np.float32).astype(ml_dtypes.bfloat16)
    c["ident_f32"] = np.eye(128, dtype=np.float32)
    c["ones_f32"] = np.ones((128, 128), dtype=np.float32)
    c["ones_bf16"] = np.ones((128, 128), dtype=np.float32).astype(ml_dtypes.bfloat16)
    c["m_rank"] = np.broadcast_to(np.arange(ME, 0, -1, dtype=np.float32), (128, ME)).copy()
    c["m_prefix"] = (np.arange(NCORES) < core).astype(np.float32)[:, None]
    return c


def _prep_inputs(inp):
    g = {}
    g["WqT"] = np.ascontiguousarray(np.asarray(inp["L0_attn_Wq"]).T)
    g["WkT"] = np.ascontiguousarray(np.asarray(inp["L0_attn_Wk"]).T)
    g["WvT"] = np.ascontiguousarray(np.asarray(inp["L0_attn_Wv"]).T)
    g["WgT"] = np.ascontiguousarray(np.asarray(inp["L0_attn_Wg"]).T)
    g["WoT"] = np.ascontiguousarray(np.asarray(inp["L0_attn_Wo"]).T)
    g["gnw_b"] = np.broadcast_to(np.asarray(inp["L0_attn_gnorm_w"]), (128, RDV)).copy()
    g["f_gamma_b"] = np.broadcast_to(np.asarray(inp["L0_ffn_ln_gamma"]), (128, D)).copy()
    g["f_beta_b"] = np.broadcast_to(np.asarray(inp["L0_ffn_ln_beta"]), (128, D)).copy()
    g["mu2d"] = np.ascontiguousarray(np.asarray(inp["L0_ffn_mu"]).reshape(D, ES))
    g["scale_b"] = np.broadcast_to(np.asarray(inp["L0_ffn_scale"]).reshape(1, 1), (128, 1)).copy()
    g["w1f"] = np.ascontiguousarray(np.asarray(inp["L0_ffn_w1"]).reshape(NE * D, FH))
    g["b1T"] = np.ascontiguousarray(np.asarray(inp["L0_ffn_b1"]))
    g["w2f"] = np.ascontiguousarray(np.asarray(inp["L0_ffn_w2"]).reshape(NE * FH, D))
    g["b2"] = np.ascontiguousarray(np.asarray(inp["L0_ffn_b2"]))
    g["a_gamma_b"] = np.broadcast_to(np.asarray(inp["L1_attn_ln_gamma"]), (128, D)).copy()
    g["a_beta_b"] = np.broadcast_to(np.asarray(inp["L1_attn_ln_beta"]), (128, D)).copy()
    g["WqkvT"] = np.ascontiguousarray(np.asarray(inp["L1_attn_w_qkv"]).T)
    g["WoutT"] = np.ascontiguousarray(np.asarray(inp["L1_attn_w_out"]).T)
    g["b_out"] = np.asarray(inp["L1_attn_b_out"]).reshape(1, D)
    g["mWg"] = np.ascontiguousarray(np.asarray(inp["L1_ffn_Wg"]))
    _wg = np.asarray(inp["L1_ffn_Wg"], dtype=np.float32)
    g["wg_bc"] = np.ascontiguousarray(
        np.broadcast_to(_wg.T[:, None, :], (ME, 128, D))).reshape(ME * 128, D)
    g["mWe"] = np.ascontiguousarray(np.asarray(inp["L1_ffn_We"]))
    g["m_be"] = np.asarray(inp["L1_ffn_be"]).reshape(1, D)
    x = np.asarray(inp["L0_attn_x"], dtype=np.float32).reshape(B * T, D)
    shards = [np.ascontiguousarray(x[c * TLOC:(c + 1) * TLOC]) for c in range(NCORES)]
    return g, shards


ENGS = ("tensor", "vector", "scalar", "gpsimd", "sync")


class Builder:
    def __init__(self, nc):
        self.nc = nc
        self._allocs = {}
        self.q = {e: [] for e in ENGS}
        self.sem_cm = {}
        self.sem = {}
        self.cnt = {e: 0 for e in ENGS}
        self.wm = {e: {} for e in ENGS}
        for e in ENGS:
            cm = nc.semaphore("sem_" + e)
            self.sem_cm[e] = cm
            self.sem[e] = cm.__enter__()
        self.NDMA = 4
        for d in ["dma_out"] + [f"dma_{e}{i}" for e in ("sync", "gpsimd")
                                for i in range(4)]:
            cm = nc.semaphore(d)
            self.sem_cm[d] = cm
            self.sem[d] = cm.__enter__()
        self.dmacnt = {f"dma_{e}{i}": 0 for e in ("sync", "gpsimd") for i in range(4)}
        self.dmarr = {"sync": 0, "gpsimd": 0}
        self.outcnt = 0
        cm = nc.semaphore("sem_cc")
        self.sem_cm["cc"] = cm
        self.sem["cc"] = cm.__enter__()
        self.cccnt = 0
        self.guards = {}

    def guard(self, name):
        return self.guards.get(name, ("vector", 0))

    def set_guard(self, name, tok):
        self.guards[name] = tok

    def _merge_emit(self, eng):
        if eng in ("vector", "gpsimd"):
            return lambda en: en.engine_nop()
        if eng in ("tensor", "scalar"):
            return lambda en: en.drain()
        return lambda en: en.nop()

    def merge(self, eng, deps):
        return self.step(eng, deps, self._merge_emit(eng))

    def barrier(self):
        toks = [self.merge(e, []) for e in ENGS]
        return self.merge("vector", toks)

    def sb(self, name, shape, dt=F32, side="right"):
        cm = self.nc.sbuf_tensor(name, shape, dt, side=side)
        h = cm.__enter__()
        self._alloc_idx = getattr(self, "_alloc_idx", 0) + 1
        self._allocs[name] = (cm, self._alloc_idx)
        return h

    def ps(self, name, shape, dt=F32, side="right"):
        cm = self.nc.psum_tensor(name, shape, dt, side=side)
        h = cm.__enter__()
        self._alloc_idx = getattr(self, "_alloc_idx", 0) + 1
        self._allocs[name] = (cm, self._alloc_idx)
        return h

    def free(self, *names):
        items = sorted([(self._allocs[n][1], n) for n in names], reverse=True)
        for _, n in items:
            self._allocs.pop(n)[0].__exit__(None, None, None)

    def _waits(self, eng, deps):
        wm = self.wm[eng]
        todo = []
        for key, val in deps:
            if val <= 0:
                continue
            if wm.get(key, 0) >= val:
                continue
            wm[key] = val
            todo.append((self.sem[key], val))
        return todo

    def step(self, eng, deps, emit, name=""):
        todo = self._waits(eng, deps)
        self.cnt[eng] += 1
        tok = (eng, self.cnt[eng])

        def run(e, todo=todo, emit=emit, tok=tok, eng=eng):
            for h, v in todo:
                e.wait_ge(h, v)
            insts = emit(e)
            last = insts[-1] if isinstance(insts, (list, tuple)) else insts
            last.then_inc(self.sem[eng], 1)
        self.q[eng].append(run)
        return tok

    def dma(self, eng, deps, pairs, name=""):
        i = self.dmarr[eng]
        self.dmarr[eng] = (i + 1) % 4
        key = f"dma_{eng}{i}"
        todo = self._waits(eng, deps + [(key, self.dmacnt[key])])
        n = len(pairs)
        self.dmacnt[key] += 16 * n
        tok = (key, self.dmacnt[key])

        def run(e, todo=todo, pairs=pairs, key=key):
            for h, v in todo:
                e.wait_ge(h, v)
            for dst, src in pairs:
                e.dma_start(out=dst, in_=src).then_inc(self.sem[key], 16)
        self.q[eng].append(run)
        return tok

    def coll(self, deps, kind, op, groups, ins, outs):
        todo = self._waits("gpsimd", deps + [("cc", self.cccnt)])
        self.cccnt += 1
        tok = ("cc", self.cccnt)

        def run(e, todo=todo):
            for h, v in todo:
                e.wait_ge(h, v)
            e.collective_compute(kind, op, replica_groups=groups, ins=ins,
                                 outs=outs).then_inc(self.sem["cc"], 1)
        self.q["gpsimd"].append(run)
        return tok

    def out_dma(self, deps, pairs):
        todo = self._waits("sync", deps + [(f"dma_sync{i}", self.dmacnt[f"dma_sync{i}"])
                                           for i in range(4)])
        n = len(pairs)
        self.outcnt += 16 * n
        tok = ("dma_out", self.outcnt)
        final = self.outcnt

        def run(e, todo=todo, pairs=pairs, final=final):
            for h, v in todo:
                e.wait_ge(h, v)
            for dst, src in pairs:
                e.dma_start(out=dst, in_=src).then_inc(self.sem["dma_out"], 16)
            e.wait_ge(self.sem["dma_out"], final)
        self.q["sync"].append(run)
        return tok

    def finalize(self, block):
        for e in ENGS:
            fns = self.q[e]
            if not fns:
                continue

            def run_all(eng, fns=fns):
                for f in fns:
                    f(eng)
            getattr(block, e)(run_all)


def build_nc(upto=99):
    nc = bass.Bass("TRN2", target_bir_lowering=False, dynamic_dma_scratch_size=16384)
    bld = Builder(nc)
    p = bld

    def din(name, shape, dt=F32):
        return nc.dram_tensor(name, shape, dt, kind="ExternalInput").ap()

    x_in = din("x_in", [TLOC, D])
    WqT, WkT = din("WqT", [D, D]), din("WkT", [D, D])
    WvT, WgT = din("WvT", [D, 2 * D]), din("WgT", [D, 2 * D])
    WoT = din("WoT", [2 * D, D])
    gnw_b = din("gnw_b", [128, RDV])
    f_gamma_b, f_beta_b = din("f_gamma_b", [128, D]), din("f_beta_b", [128, D])
    mu2d = din("mu2d", [D, ES])
    scale_b = din("scale_b", [128, 1])
    w1f = din("w1f", [NE * D, FH])
    b1_in = din("b1T", [NE, FH])
    w2f = din("w2f", [NE * FH, D])
    b2_in = din("b2", [NE, D])
    a_gamma_b, a_beta_b = din("a_gamma_b", [128, D]), din("a_beta_b", [128, D])
    WqkvT = din("WqkvT", [D, 3 * D])
    WoutT = din("WoutT", [D, D])
    b_out_in = din("b_out", [1, D])
    mWg_in = din("mWg", [D, ME])
    wg_bc_in = din("wg_bc", [ME * 128, D])
    mWe_in = din("mWe", [D, D])
    m_be_in = din("m_be", [1, D])
    rope_cos, rope_sin = din("rope_cos", [128, TLOC]), din("rope_sin", [128, TLOC])
    rope_cos_q, rope_sin_q = din("rope_cos_q", [128, TLOC]), din("rope_sin_q", [128, TLOC])
    ret_maskT = din("ret_maskT", [RH, RC, RC])
    ret_cdec = din("ret_cdec", [RH, 128, RC])
    ret_kdec = din("ret_kdec", [RH, RC, 1])
    ret_flag = din("ret_flag", [128, 1])
    ident_in = din("ident", [128, 128], BF16)
    identf_in = din("ident_f32", [128, 128])
    ones_f32_in = din("ones_f32", [128, 128])
    ones_bf16_in = din("ones_bf16", [128, 128], BF16)
    m_rank_in = din("m_rank", [128, ME])
    m_prefix_in = din("m_prefix", [NCORES, 1])

    x_out = nc.dram_tensor("x_out", [TLOC, D], F32, kind="ExternalOutput").ap()

    ffn_snd = nc.dram_tensor("ffn_snd", [ES, D + 8], F32)
    ffn_rcv = nc.dram_tensor("ffn_rcv", [2 * ES, D + 8], F32)
    st_snd = nc.dram_tensor("st_snd", [RDK, RDV], F32)
    st_rcv = nc.dram_tensor("st_rcv", [2 * RDK, RDV], F32)
    st_snd1 = nc.dram_tensor("st_snd1", [RDK, RDV], F32)
    st_rcv1 = nc.dram_tensor("st_rcv1", [2 * RDK, RDV], F32)
    cnt_snd = nc.dram_tensor("cnt_snd", [1, ME], F32)
    cnt_rcv = nc.dram_tensor("cnt_rcv", [NCORES, ME], F32)
    VW = 66
    k_snd = [nc.dram_tensor(f"k_snd{i}", [512, TLOC], BF16) for i in range(2)]
    k_rcv = [nc.dram_tensor(f"k_rcv{i}", [1024, TLOC], BF16) for i in range(2)]
    v_snd = [nc.dram_tensor(f"v_snd{i}", [TLOC, 8 * VW], BF16) for i in range(2)]
    v_rcv = [nc.dram_tensor(f"v_rcv{i}", [2 * TLOC, 8 * VW], BF16) for i in range(2)]
    kn_snd = [nc.dram_tensor(f"kn_snd{i}", [1, 8], F32) for i in range(2)]
    kn_rcv = [nc.dram_tensor(f"kn_rcv{i}", [2, 8], F32) for i in range(2)]

    PAIRS = [[0, 1], [2, 3], [4, 5], [6, 7]]
    ALL8 = [list(range(NCORES))]

    x = p.sb("x", [128, NT * D], side="left")
    ident = p.sb("ident_s", [128, 128], BF16, side="left")
    identf = p.sb("identf_s", [128, 128], side="left")
    ones16 = p.sb("ones16_s", [128, 128], BF16, side="left")
    stats = p.sb("stats", [128, 96], side="left")
    bnb = p.sb("bnb", [128, 24], side="left")
    sq2 = p.sb("sq2", [128, D], side="left")
    xnT = p.sb("xnT", [128, 8 * TLOC], BF16, side="left")
    xn = p.sb("xn", [128, NT * D], side="left")

    tok_x = p.dma("sync", [], [(x[:, i * D:(i + 1) * D], x_in[i * 128:(i + 1) * 128, :])
                               for i in range(NT)])
    tok_const = p.dma("sync", [], [(ident[:], ident_in), (identf[:], identf_in),
                                   (ones16[:], ones_bf16_in)])

    def ln_to(dst, src, dep, gamma=None, beta=None, dst16=None, name="ln",
              stats_only=False):
        dep = list(dep) if isinstance(dep, list) else [dep]
        stok = []
        for i in range(NT):
            a = src[:, i * D:(i + 1) * D]
            deps0 = dep + [p.guard("stats")]
            if i >= 2:
                deps0.append(stok[i - 2])
            t0 = p.step("vector", deps0, lambda v, a=a, i=i: [
                v.bn_stats(bnb[:, 12 * (i % 2):12 * (i % 2) + 6], a[:, 0:512]),
                v.bn_stats(bnb[:, 12 * (i % 2) + 6:12 * (i % 2) + 12], a[:, 512:1024]),
            ][-1])
            t1 = p.step("vector", [t0], lambda v, i=i: v.bn_aggr(
                stats[:, 2 * i:2 * i + 2], bnb[:, 12 * (i % 2):12 * (i % 2) + 12]))
            stok.append(t1)
        t2 = p.step("vector", stok, lambda v: [
            v.tensor_scalar_add(stats[:, 32 + i:32 + i + 1],
                                stats[:, 2 * i + 1:2 * i + 2], LN_EPS)
            for i in range(NT)][-1])
        t3 = p.step("scalar", [t2], lambda s: s.activation(
            stats[:, 32:32 + NT], stats[:, 32:32 + NT], AF.Sqrt))
        t4 = p.step("vector", [t3], lambda v: v.reciprocal(
            stats[:, 32:32 + NT], stats[:, 32:32 + NT]))
        if stats_only:
            p.set_guard("stats", t4)
            return t4
        wa = [p.step("vector", [t4] + dep, lambda v, i=i: v.tensor_scalar(
            dst[:, i * D:(i + 1) * D], src[:, i * D:(i + 1) * D],
            stats[:, 2 * i:2 * i + 1], stats[:, 32 + i:32 + i + 1],
            ALU.subtract, ALU.mult)) for i in range(NT)]
        if gamma is not None:
            wa = [p.step("vector", [wa[i]], lambda v, i=i: v.tensor_mul(
                dst[:, i * D:(i + 1) * D], dst[:, i * D:(i + 1) * D], gamma[:]))
                for i in range(NT)]
            wa = [p.step("vector", [wa[i]], lambda v, i=i: v.tensor_add(
                dst[:, i * D:(i + 1) * D], dst[:, i * D:(i + 1) * D], beta[:]))
                for i in range(NT)]
        if dst16 is not None:
            wa = [p.step("vector", [wa[i]], lambda v, i=i: v.tensor_copy(
                dst16[:, i * D:(i + 1) * D], dst[:, i * D:(i + 1) * D]))
                for i in range(NT)]
        fin = p.step("vector", wa, lambda v: v.engine_nop())
        p.set_guard("stats", fin)
        return fin

    pTa = p.ps("pTa", [128, 128], BF16)
    pTb = p.ps("pTb", [128, 128], BF16)
    pTf_cell = [None]

    def transposes(blocks, dep, extra_dep=None, scale_col=None):
        ev = [p.guard("pT"), p.guard("pT")]
        last = None
        deps0 = (list(dep) if isinstance(dep, list) else [dep])
        if extra_dep is not None:
            deps0.append(extra_dep)
        for b, (srcap, dstap) in enumerate(blocks):
            fp = (srcap.dtype == F32)
            pts = (pTf_cell[0], pTf_cell[0]) if fp else (pTa, pTb)
            pr, pc = srcap.shape[0], srcap.shape[1]
            pt = pts[b % 2][0:pc, 0:pr]
            tdeps = deps0 + [ev[b % 2]] + ([ev[1 - b % 2]] if fp else [])
            idm = identf if fp else ident
            tt = p.step("tensor", tdeps, lambda t, s=srcap, pt=pt, pr=pr, idm=idm:
                        t.transpose(pt, s, idm[0:pr, 0:pr]))

            if scale_col is None and b % 2 == 1:
                ev[b % 2] = p.step("scalar", [tt], lambda s, pt=pt, d=dstap:
                                   s.activation(d, pt, AF.Copy))
            else:
                def em(v, pt=pt, d=dstap, b=b):
                    if scale_col is None:
                        return v.tensor_copy(d, pt)
                    return v.tensor_scalar(d, pt, scale_col, None, ALU.mult)
                ev[b % 2] = p.step("vector", [tt], em)
            last = ev[b % 2]
        fin = p.step("vector", [e for e in ev if e], lambda v: v.engine_nop()) \
            if len(blocks) > 1 else last
        p.set_guard("pT", fin)
        return fin

    def _finish(tok, src):
        p.out_dma([tok], [(x_out[i * 128:(i + 1) * 128, :], src[:, i * D:(i + 1) * D])
                          for i in range(NT)])
        blk_cm = nc.Block()
        block = blk_cm.__enter__()
        bld.finalize(block)
        blk_cm.__exit__(None, None, None)
        return nc

    tok_xn = ln_to(xn, x, tok_x)

    ew2 = p.sb("ew2", [128, NT * ES], BF16)
    rsum2 = p.sb("rsum2", [128, NT])
    slp = [p.sb("slp0", [128, D + 8]), p.sb("slp1", [128, D + 8])]
    fg = p.sb("fg", [128, D]); fb = p.sb("fb", [128, D]); sclt = p.sb("sclt", [128, 8])
    tok_fgb = p.dma("sync", [], [(fg[:], f_gamma_b), (fb[:], f_beta_b),
                                 (sclt[:, 0:1], scale_b)])
    tok_sabs = p.step("scalar", [tok_fgb], lambda s: s.activation(
        sclt[:, 0:1], sclt[:, 0:1], AF.Abs))
    nscl = p.sb("nscl", [128, 1])
    tok_nscl = p.step("vector", [tok_sabs], lambda v: v.tensor_scalar(
        nscl[:], sclt[:, 0:1], -1.02, -1e-3, ALU.mult, ALU.add))

    mu_u = p.sb("mu_u", [128, 8 * ES], BF16)
    mu_s = p.sb("mu_s", [128, 4 * ES])
    musq16 = p.sb("musq16", [128, 4 * ES], BF16)
    p_mr = p.ps("p_mr", [1, ES])
    tok_half = []
    tok_mu_parts = []
    for half in range(2):
        tok_mu = p.dma("sync", [tok_half[-1]] if tok_half else [], [
            (mu_s[:, j * ES:(j + 1) * ES], mu2d[(half * 4 + j) * 128:(half * 4 + j) * 128 + 128, :])
            for j in range(4)])
        tok_msq = p.step("scalar", [tok_mu] + ([tok_half[-1]] if tok_half else []),
                         lambda s: [
            s.activation(musq16[:, j * ES:(j + 1) * ES], mu_s[:, j * ES:(j + 1) * ES],
                         AF.Square) for j in range(4)][-1])
        tok_mr = p.step("tensor", [tok_msq, tok_const], lambda t, half=half: [
            t.matmul(p_mr[:], ones16[:, 0:1], musq16[:, j * ES:(j + 1) * ES],
                     start=(half == 0 and j == 0), stop=(half == 1 and j == 3))
            for j in range(4)][-1])
        tok_half.append(p.step("vector", [tok_mr], lambda v: v.engine_nop()))
        tok_mu_parts.append((tok_mu, tok_mr))
    mu_rn = p.sb("mu_rn", [1, ES])
    mu_rn16 = p.sb("mu_rn16", [1, ES], BF16)
    mu_rb = p.sb("mu_rb", [128, ES])
    p_mb = p.ps("p_mb", [128, ES])
    tok_mrn = p.step("vector", [tok_half[-1]], lambda v: v.tensor_copy(mu_rn[:], p_mr[:]))
    tok_mrn = p.step("scalar", [tok_mrn], lambda s: s.activation(mu_rn[:], mu_rn[:], AF.Sqrt))
    tok_mrn = p.step("vector", [tok_mrn], lambda v: v.tensor_scalar_max(mu_rn[:], mu_rn[:], NORM_EPS))
    tok_mrn = p.step("vector", [tok_mrn], lambda v: v.reciprocal(mu_rn[:], mu_rn[:]))
    tok_mrn16 = p.step("vector", [tok_mrn], lambda v: v.tensor_copy(mu_rn16[:], mu_rn[:]))
    tok_mb = p.step("tensor", [tok_mrn16], lambda t: t.matmul(
        p_mb[:], ones16[0:1, :], mu_rn16[:], start=True, stop=True))
    tok_mrb = p.step("vector", [tok_mb], lambda v: v.tensor_copy(mu_rb[:], p_mb[:]))
    tok_muh = tok_mrb
    for half in range(2):
        tok_mu2 = p.dma("sync", [tok_muh], [
            (mu_s[:, j * ES:(j + 1) * ES], mu2d[(half * 4 + j) * 128:(half * 4 + j) * 128 + 128, :])
            for j in range(4)])
        tok_muh = p.step("vector", [tok_mu2, tok_mrb], lambda v, half=half: [
            v.tensor_mul(mu_u[:, (half * 4 + j) * ES:(half * 4 + j) * ES + ES],
                         mu_s[:, j * ES:(j + 1) * ES], mu_rb[:]) for j in range(4)][-1])
    tok_muu = tok_muh
    p.free("mu_s", "musq16", "p_mr", "mu_rn", "mu_rn16", "mu_rb", "p_mb")

    xn16t = p.sb("xn16t", [128, D], BF16)
    xnf_t = p.sb("xnf_t", [128, D])
    u16 = p.sb("u16", [128, D], BF16)
    uT = p.sb("uT", [128, D], BF16)
    xnf16 = p.sb("xnf16", [128, D], BF16)
    ewt = p.sb("ewt", [128, 2 * ES], BF16)
    mxc = p.sb("mxc", [128, 4])
    p_sl = [p.ps("p_sl0", [128, 1024]), p.ps("p_sl1", [128, 1024])]
    p_gs = p.ps("p_gs", [128, 2])
    p_lg0 = p.ps("p_lg0", [128, ES])
    p_lg = [p_lg0, p_lg0]

    ln2_stats = ln_to(None, xn, [tok_xn], stats_only=True)

    xn16w = [None, None]; xnTw = [None, None]
    xfw = [None, None]; u16w = [None, None]; uTw = [None, None]
    lgw = [None, None]; eww = [None, None]; xf16w = [None, None]
    slmm_last = [None]
    ew2_toks = []
    for i in range(NT):
        pa = i % 2
        d_a = [tok_xn] + [t_ for t_ in (xnTw[0], xnTw[1]) if t_]
        xn16w[pa] = p.step("vector", d_a, lambda v, i=i: v.tensor_copy(
            xn16t[:, 0:D], xn[:, i * D:(i + 1) * D]))
        blksT = [(xn16t[:, dc * 128: dc * 128 + 128],
                  xnT[:, dc * TLOC + i * 128: dc * TLOC + i * 128 + 128])
                 for dc in range(8)]
        xnTw[pa] = transposes(blksT, [xn16w[pa], tok_const])
        d_b = [ln2_stats, tok_fgb] + ([xf16w[1 - pa]] if xf16w[1 - pa] else []) + \
              ([u16w[1 - pa]] if u16w[1 - pa] else [])
        t_f = p.step("vector", d_b, lambda v, i=i: v.tensor_scalar(
            xnf_t[:, 0:D], xn[:, i * D:(i + 1) * D],
            stats[:, 2 * i:2 * i + 1], stats[:, 32 + i:32 + i + 1],
            ALU.subtract, ALU.mult))
        t_f = p.step("vector", [t_f], lambda v: v.tensor_mul(
            xnf_t[:, 0:D], xnf_t[:, 0:D], fg[:]))
        t_f = p.step("vector", [t_f], lambda v: v.tensor_add(
            xnf_t[:, 0:D], xnf_t[:, 0:D], fb[:]))
        t_n = p.step("scalar", [t_f], lambda s, i=i: s.activation(
            sq2[:, 0:D], xnf_t[:, 0:D], AF.Square,
            accum_out=stats[:, 48 + i:48 + i + 1]))
        t_n = p.step("scalar", [t_n], lambda s, i=i: s.activation(
            stats[:, 48 + i:48 + i + 1], stats[:, 48 + i:48 + i + 1], AF.Sqrt))
        t_n = p.step("vector", [t_n], lambda v, i=i: v.tensor_scalar_max(
            stats[:, 48 + i:48 + i + 1], stats[:, 48 + i:48 + i + 1], NORM_EPS))
        t_n = p.step("vector", [t_n], lambda v, i=i: v.reciprocal(
            stats[:, 48 + i:48 + i + 1], stats[:, 48 + i:48 + i + 1]))
        t_n = p.step("vector", [t_n, tok_sabs], lambda v, i=i: v.tensor_scalar(
            stats[:, 48 + i:48 + i + 1], stats[:, 48 + i:48 + i + 1],
            sclt[:, 0:1], None, ALU.mult))
        d_u = [t_n] + [t_ for t_ in (uTw[0], uTw[1]) if t_]
        u16w[pa] = p.step("vector", d_u, lambda v, i=i: v.tensor_scalar(
            u16[:, 0:D], xnf_t[:, 0:D],
            stats[:, 48 + i:48 + i + 1], None, ALU.mult))
        blksU = [(u16[:, dc * 128: dc * 128 + 128],
                  uT[:, dc * 128: dc * 128 + 128])
                 for dc in range(8)]
        d_e = [u16w[pa]] + [t_ for t_ in (lgw[0], lgw[1]) if t_]
        uTw[pa] = transposes(blksU, d_e)
        d_fm = [uTw[pa], tok_muu] + [t for t in eww if t]
        lgw[pa] = p.step("tensor", d_fm, lambda t, pa=pa: [
            t.matmul(p_lg[pa][:], uT[:, dc * 128: dc * 128 + 128],
                     mu_u[:, dc * ES:(dc + 1) * ES], start=(dc == 0), stop=(dc == 7))
            for dc in range(8)][-1])
        d_g = [lgw[pa], tok_nscl] + ([slmm_last[0]] if slmm_last[0] and i >= 2 else [])
        eww[pa] = p.step("scalar", d_g, lambda s, i=i, pa=pa: s.activation(
            ewt[:, pa * ES:(pa + 1) * ES], p_lg[pa][:], AF.Exp, bias=nscl[:, 0:1],
            accum_out=stats[:, 64 + i:64 + i + 1]))
        t_m = p.step("vector", [eww[pa]], lambda v, i=i, pa=pa: v.reduce_max(
            mxc[:, pa:pa + 1], ewt[:, pa * ES:(pa + 1) * ES], axis=AX.X))
        t_m2 = p.step("vector", [t_m], lambda v, i=i, pa=pa: v.reciprocal(
            mxc[:, 2 + pa: 3 + pa], mxc[:, pa:pa + 1]))
        tok_e2 = p.step("vector", [t_m2], lambda v, i=i, pa=pa: v.tensor_scalar(
            ew2[:, i * ES:(i + 1) * ES], ewt[:, pa * ES:(pa + 1) * ES],
            mxc[:, 2 + pa:3 + pa], None, ALU.mult))
        ew2_toks.append(tok_e2)
        t_rs = p.step("vector", [eww[pa]], lambda v, i=i: v.reciprocal(
            stats[:, 80 + i:80 + i + 1], stats[:, 64 + i:64 + i + 1]))
        t_rs = p.step("vector", [t_rs, t_m], lambda v, i=i, pa=pa: v.tensor_scalar(
            rsum2[:, i:i + 1], mxc[:, pa:pa + 1], stats[:, 80 + i:80 + i + 1],
            None, ALU.mult))
        ew2_toks.append(t_rs)
        d_j = [t_f] + ([slmm_last[0]] if slmm_last[0] and i >= 2 else [])
        xf16w[pa] = p.step("vector", d_j + [t_ for t_ in (slmm_last[0],) if t_],
                           lambda v: v.tensor_copy(xnf16[:, 0:D], xnf_t[:, 0:D]))
        slmm_last[0] = p.step("tensor", [xf16w[pa], eww[pa], tok_const],
                              lambda t, i=i, pa=pa: ([
            t.matmul(p_sl[ec][:, nn * 512:(nn + 1) * 512],
                     ewt[:, pa * ES + ec * 128: pa * ES + ec * 128 + 128],
                     xnf16[:, nn * 512: nn * 512 + 512],
                     start=(i == 0), stop=(i == NT - 1))
            for ec in range(2) for nn in range(2)] + [
            t.matmul(p_gs[:, ec:ec + 1],
                     ewt[:, pa * ES + ec * 128: pa * ES + ec * 128 + 128],
                     ones16[:, 0:1], start=(i == 0 and ec == 0),
                     stop=(i == NT - 1 and ec == 1))
            for ec in range(2)])[-1])
    tok_ew2 = p.step("vector", ew2_toks[-4:], lambda v: v.engine_nop())
    tok_slpz = p.step("vector", [], lambda v: [
        v.memset(slp[ec][:, D + 1:D + 8], 0.0) for ec in range(2)][-1])
    sl_ev = [p.step("vector", [slmm_last[0], tok_slpz], lambda v, ec=ec: [
        v.tensor_copy(slp[ec][:, 0:512], p_sl[ec][:, 0:512]),
        v.tensor_copy(slp[ec][:, 512:1024], p_sl[ec][:, 512:1024]),
        v.tensor_copy(slp[ec][:, D:D + 1], p_gs[:, ec:ec + 1]),
    ][-1]) for ec in range(2)]
    tok_xnT = p.step("vector", [xnTw[0], xnTw[1]], lambda v: v.engine_nop())

    tok_ffs = p.dma("gpsimd", sl_ev, [(ffn_snd.ap()[0:128, :], slp[0][:]),
                                      (ffn_snd.ap()[128:256, :], slp[1][:])])
    tok_ffc = p.coll([tok_ffs], "AllGather", ALU.bypass, PAIRS,
                     [ffn_snd.ap().opt()], [ffn_rcv.ap().opt()])
    p.free("xn16t", "xnf_t", "u16", "uT", "xnf16", "ewt", "mxc",
           "p_lg0", "mu_u", "fg", "fb", "nscl", "sclt")
    p.free("slp0", "slp1")
    p.free("p_sl0", "p_sl1", "p_gs")

    if upto < 4:
        return _finish(tok_ffc, xn)

    bar_p4 = p.barrier()
    slotT = p.sb("slotT", [128, 8 * ES], BF16)
    sli16 = [p.sb("sli16_0", [128, D], BF16), p.sb("sli16_1", [128, D], BF16)]
    sli = [p.sb("sli0", [128, D + 8]), p.sb("sli1", [128, D + 8])]
    slr = [p.sb("slr00", [128, D + 8]), p.sb("slr01", [128, D + 8]),
           p.sb("slr10", [128, D + 8]), p.sb("slr11", [128, D + 8])]
    tok_ffr = p.dma("gpsimd", [tok_ffc, tok_ew2, tok_xnT, bar_p4], [
        (slr[0][:], ffn_rcv.ap()[0:128, :]), (slr[1][:], ffn_rcv.ap()[128:256, :]),
        (slr[2][:], ffn_rcv.ap()[256:384, :]), (slr[3][:], ffn_rcv.ap()[384:512, :])])
    tok_sli = [p.step("vector", [tok_ffr], lambda v, ec=ec: v.tensor_add(
        sli[ec][:], slr[ec][:], slr[2 + ec][:])) for ec in range(2)]
    tok_gsr = [p.step("vector", [tok_sli[ec], p.guard("stats")], lambda v, ec=ec:
        v.reciprocal(stats[:, 90 + ec:91 + ec], sli[ec][:, D:D + 1])) for ec in range(2)]
    tok_sli16 = [p.step("vector", [tok_gsr[ec]], lambda v, ec=ec: v.tensor_scalar(
        sli16[ec][:], sli[ec][:, 0:D], stats[:, 90 + ec:91 + ec], None, ALU.mult))
        for ec in range(2)]
    p.free("slr00", "slr01", "slr10", "slr11", "sli0", "sli1")
    blks_s = []
    for ec in range(2):
        for dc in range(8):
            blks_s.append((sli16[ec][:, dc * 128:(dc + 1) * 128],
                           slotT[:, dc * ES + ec * 128: dc * ES + ec * 128 + 128]))
    tok_slotT = transposes(blks_s, tok_sli16[0], extra_dep=tok_sli16[1])
    p.free("sli16_0", "sli16_1")
    bar_slr = p.barrier()

    so16 = [p.sb("so16_0", [128, D], BF16), p.sb("so16_1", [128, D], BF16)]
    so_stage = [p.sb("so_st0", [16, D], BF16), p.sb("so_st1", [16, D], BF16)]
    b1s = p.sb("b1s", [128, 2 * NE])
    tok_b1 = p.dma("sync", [bar_slr], [(b1s[:, e * 2 + hc: e * 2 + hc + 1],
                                 b1_in[e:e + 1, hc * 128:(hc + 1) * 128])
                                for e in range(NE) for hc in range(2)])
    b2s16 = p.sb("b2s16", [33, D], BF16)
    w1b = [p.sb("w1b0", [128, 8 * FH], BF16), p.sb("w1b1", [128, 8 * FH], BF16)]
    w2b = [p.sb("w2b0", [128, 2 * D], BF16), p.sb("w2b1", [128, 2 * D], BF16)]
    hT = [p.sb("hT0", [128, 32], BF16), p.sb("hT1", [128, 32], BF16)]
    p_h = [p.ps("p_h0", [128, 32]), p.ps("p_h1", [128, 32])]
    p_so = [p.ps("p_so0", [16, 1024]), p.ps("p_so1", [16, 1024])]
    w1_tok = [None, None]; w2_tok = [None, None]
    hmm_tok = [None, None]; hact_tok = [None, None]
    somm_tok = [None, None]; soev_tok = [None, None]
    for e in range(NE):
        pa = e % 2
        d1 = [bar_slr] + ([hmm_tok[pa]] if hmm_tok[pa] else [])
        w1_tok[pa] = p.dma("gpsimd", d1, [
            (w1b[pa][:, dc * FH:(dc + 1) * FH],
             w1f[e * D + dc * 128: e * D + dc * 128 + 128, :]) for dc in range(8)])
        d2 = [bar_slr] + ([somm_tok[pa]] if somm_tok[pa] else [])
        w2_tok[pa] = p.dma("gpsimd", d2, [
            (w2b[pa][:, hc * D:(hc + 1) * D],
             w2f[e * FH + hc * 128: e * FH + hc * 128 + 128, :]) for hc in range(2)] + [
            (b2s16[32 * pa:32 * pa + 1, :], b2_in[e:e + 1, :])])
        hdeps = [w1_tok[pa], tok_slotT] + ([hact_tok[pa]] if hact_tok[pa] else [])
        hmm_tok[pa] = p.step("tensor", hdeps, lambda t, e=e, pa=pa: [
            t.matmul(p_h[pa][:, hc * 16:(hc + 1) * 16],
                     w1b[pa][:, dc * FH + hc * 128: dc * FH + hc * 128 + 128],
                     slotT[:, dc * ES + e * 16: dc * ES + e * 16 + 16],
                     start=(dc == 0), stop=(dc == 7))
            for hc in range(2) for dc in range(8)][-1])
        hact_deps = [hmm_tok[pa], tok_b1] + ([somm_tok[pa]] if somm_tok[pa] else [])
        hact_tok[pa] = p.step("scalar", hact_deps, lambda s, e=e, pa=pa: [
            s.activation(hT[pa][:, hc * 16:(hc + 1) * 16], p_h[pa][:, hc * 16:(hc + 1) * 16],
                         AF.Gelu, bias=b1s[:, e * 2 + hc: e * 2 + hc + 1])
            for hc in range(2)][-1])
        sdeps = [hact_tok[pa], w2_tok[pa]] + ([soev_tok[pa]] if soev_tok[pa] else [])
        somm_tok[pa] = p.step("tensor", sdeps, lambda t, e=e, pa=pa: ([
            t.matmul(p_so[pa][:, n2 * 512:(n2 + 1) * 512],
                     hT[pa][:, hc * 16:(hc + 1) * 16],
                     w2b[pa][:, hc * D + n2 * 512: hc * D + n2 * 512 + 512],
                     start=(hc == 0), stop=False)
            for n2 in range(2) for hc in range(2)] + [
            t.matmul(p_so[pa][:, n2 * 512:(n2 + 1) * 512],
                     ones16[32 * pa:32 * pa + 1, 0:16],
                     b2s16[32 * pa:32 * pa + 1, n2 * 512: n2 * 512 + 512],
                     start=False, stop=True)
            for n2 in range(2)])[-1])
        sodeps = [somm_tok[pa]] + ([soev_tok[pa]] if soev_tok[pa] else [])
        so_cp = p.step("vector", sodeps, lambda v, pa=pa:
            v.tensor_copy(so_stage[pa][:], p_so[pa][:, 0:1024]))
        soev_tok[pa] = p.dma("gpsimd", [so_cp], [
            (so16[e // 8][(e % 8) * 16:(e % 8) * 16 + 16, :], so_stage[pa][:])])
    tok_so = p.step("vector", [t for t in soev_tok if t], lambda v: v.engine_nop())
    p.free("w1b0", "w1b1", "w2b0", "w2b1", "hT0", "hT1", "p_h0", "p_h1",
           "p_so0", "p_so1", "b2s16", "b1s", "so_st0", "so_st1")

    ew2T = [p.sb("ew2T0", [128, TLOC], BF16), p.sb("ew2T1", [128, TLOC], BF16)]
    p_cb = [p.ps("p_cb0", [128, 1024]), p.ps("p_cb1", [128, 1024])]
    blks_e = []
    for ec in range(2):
        for i in range(NT):
            blks_e.append((ew2[:, i * ES + ec * 128: i * ES + ec * 128 + 128],
                           ew2T[ec][:, i * 128:(i + 1) * 128]))
    tok_ew2T = transposes(blks_e, [tok_ew2, bar_p4, tok_so])
    cb_mm = [None, None]; cb_ev = [None, None]
    for i in range(NT):
        pa = i % 2
        deps = [tok_ew2T, tok_so] + ([cb_ev[pa]] if cb_ev[pa] else [])
        cb_mm[pa] = p.step("tensor", deps, lambda t, i=i, pa=pa: [
            t.matmul(p_cb[pa][:, n2 * 512:(n2 + 1) * 512],
                     ew2T[ec][:, i * 128:(i + 1) * 128],
                     so16[ec][:, n2 * 512: n2 * 512 + 512],
                     start=(ec == 0), stop=(ec == 1))
            for n2 in range(2) for ec in range(2)][-1])
        cb_ev[pa] = p.step("vector", [cb_mm[pa], tok_ew2], lambda v, i=i, pa=pa: [
            v.scalar_tensor_tensor(x[:, i * D + n2 * 512: i * D + n2 * 512 + 512],
                                   p_cb[pa][:, n2 * 512:(n2 + 1) * 512],
                                   rsum2[:, i:i + 1],
                                   x[:, i * D + n2 * 512: i * D + n2 * 512 + 512],
                                   ALU.mult, ALU.add)
            for n2 in range(2)][-1])
    tok_ffn0 = p.step("vector", [t for t in cb_ev if t], lambda v: v.engine_nop())
    p.free("p_cb0", "p_cb1", "ew2T0", "ew2T1", "so16_0", "so16_1", "slotT",
           "ew2", "rsum2")
    p.free("xn")

    if upto < 5:
        return _finish(tok_ffn0, x)

    bar0 = p.barrier()
    p.set_guard("pT", bar0)

    rcq = p.sb("rcq", [128, TLOC], BF16); rsq = p.sb("rsq", [128, TLOC], BF16)
    rck = p.sb("rck", [128, TLOC], BF16); rsk = p.sb("rsk", [128, TLOC], BF16)
    maskT_s = p.sb("maskT_s", [128, RH * RC])
    cdec_s = p.sb("cdec_s", [128, RH * RC])
    kdec_s = p.sb("kdec_s", [128, RH])
    flag_s = p.sb("flag_s", [128, 1])
    gnw_s = p.sb("gnw_s", [128, RDV])
    tstage = p.sb("tstage", [128, TLOC])
    tok_tb = bar0
    for tin, tout in ((rope_cos_q, rcq), (rope_sin_q, rsq), (rope_cos, rck), (rope_sin, rsk)):
        tok_ld = p.dma("sync", [tok_tb, bar0], [(tstage[:], tin)])
        tok_tb = p.step("vector", [tok_ld], lambda v, tout=tout: v.tensor_copy(tout[:], tstage[:]))
    tok_rc = p.dma("sync", [bar0], [
        (gnw_s[:], gnw_b), (flag_s[:], ret_flag)] + [
        (maskT_s[:, h * RC:(h + 1) * RC], ret_maskT[h]) for h in range(RH)] + [
        (cdec_s[:, h * RC:(h + 1) * RC], ret_cdec[h]) for h in range(RH)] + [
        (kdec_s[:, h:h + 1], ret_kdec[h]) for h in range(RH)])
    tok_rc = p.step("vector", [tok_rc, tok_tb], lambda v: v.engine_nop())
    p.free("tstage")

    NSC = 4
    SCH = TLOC // NSC
    CPS = SCH // RC
    tok_l0 = tok_ffn0

    p_pj = p.ps("p_pj", [128, 1024])
    p_vps = p_pj
    rtmp = p.sb("rtmp", [128, 4 * SCH])
    rt2 = p.sb("rt2", [128, 4 * SCH])
    qd16 = p.sb("qd16", [128, 2 * 256], BF16)
    smid16 = p.sb("smid16", [128, 2 * 512], BF16)
    q16 = p.sb("q16", [128, 2 * SCH], BF16)

    def proj_rope1(dst16, wslc, tcol0, cosT, sinT, deps, slot=0):
        deps = list(deps)
        evg = p.guard("proj_ev")
        rg = p.guard(f"projS{slot}")
        off = slot * 2 * SCH
        ev = []
        for oc in range(2):
            mm = p.step("tensor", deps + [evg], lambda t, oc=oc, wslc=wslc,
                        tcol0=tcol0: [
                t.matmul(p_pj[:, oc * 512: oc * 512 + 512],
                         wslc[:, dc * 256 + oc * 128: dc * 256 + oc * 128 + 128],
                         xnT[:, dc * TLOC + tcol0: dc * TLOC + tcol0 + SCH],
                         start=(dc == 0), stop=(dc == 7)) for dc in range(8)][-1])
            ev.append(p.step("vector", [mm, rg] + deps, lambda v, oc=oc, off=off:
                v.tensor_copy(rtmp[:, off + oc * SCH: off + (oc + 1) * SCH],
                              p_pj[:, oc * 512: oc * 512 + 512])))
        tok_raw = p.step("vector", ev, lambda v: v.engine_nop())
        p.set_guard("proj_ev", tok_raw)
        cs = cosT[:, tcol0:tcol0 + SCH]
        sn = sinT[:, tcol0:tcol0 + SCH]
        a0 = rtmp[:, off:off + SCH]; a1 = rtmp[:, off + SCH:off + 2 * SCH]
        b0 = rt2[:, off:off + SCH]; b1 = rt2[:, off + SCH:off + 2 * SCH]
        tA = p.step("vector", [tok_raw], lambda v, a0=a0, b0=b0, cs=cs:
                    v.tensor_mul(b0, a0, cs))
        tB = p.step("gpsimd", [tok_raw], lambda g_, a1=a1, b1=b1, sn=sn:
                    g_.tensor_mul(b1, a1, sn))
        r0 = p.step("vector", [tA, tB] + deps, lambda v, b0=b0, b1=b1,
                    dst16=dst16: v.tensor_sub(dst16[:, 0:SCH], b0, b1))
        tC = p.step("vector", [r0], lambda v, a1=a1, b0=b0, cs=cs:
                    v.tensor_mul(b0, a1, cs))
        tD = p.step("gpsimd", [r0], lambda g_, a0=a0, b1=b1, sn=sn:
                    g_.tensor_mul(b1, a0, sn))
        r1 = p.step("vector", [tC, tD] + deps, lambda v, b0=b0, b1=b1,
                    dst16=dst16: v.tensor_add(dst16[:, SCH:2 * SCH], b0, b1))
        p.set_guard(f"projS{slot}", r1)
        return r1

    st_bounce = [st_snd, st_snd1]
    st_rbounce = [st_rcv, st_rcv1]
    wq_handles = {}
    o_buf = {}
    tok_h_done = {}
    for h in range(RH):
        pair_i = h % 2
        if True:
            wqb = p.sb(f"wqb{h}", [128, 8 * 256], BF16)
            wq_handles[h] = wqb
            o_h = p.sb(f"o_h{h}", [128, RNC * 512])
            o_buf[h] = o_h
            wkb = p.sb(f"wkb{h}", [128, 8 * 256], BF16)
            wvb = p.sb(f"wvb{h}", [128, 8 * 512], BF16)
            k16 = p.sb(f"k16_{h}", [128, 2 * SCH], BF16)
            v16 = p.sb(f"v16_{h}", [128, 4 * 512], BF16)
            ktm = p.sb(f"ktm_{h}", [128, 2 * 256], BF16)
            S_f = p.sb(f"S_f{h}", [128, 2 * 512])
            S16 = p.sb(f"S16_{h}", [128, 2 * 512], BF16)
            at16 = p.sb(f"at16_{h}", [128, 2 * 128], BF16)
            p_at = p.ps(f"p_at{h}", [128, 128])
            p_o = p.ps(f"p_o{h}", [128, 512])
            p_ds = p.ps(f"p_ds{h}", [128, 1024])
            tok_wq = p.dma("gpsimd", [bar0, p.guard("wq")], [
                (wqb[:, dc * 256:(dc + 1) * 256],
                 WqT[dc * 128:(dc + 1) * 128, h * 256: h * 256 + 256]) for dc in range(8)])
            tok_wk = p.dma("gpsimd", [bar0, p.guard("wq")], [
                (wkb[:, dc * 256:(dc + 1) * 256],
                 WkT[dc * 128:(dc + 1) * 128, h * 256: h * 256 + 256]) for dc in range(8)])
            tok_wv = p.dma("gpsimd", [bar0, p.guard("wq")], [
                (wvb[:, dc * 512:(dc + 1) * 512],
                 WvT[dc * 128:(dc + 1) * 128, h * 512: h * 512 + 512]) for dc in range(8)])
            tok_z = p.step("vector", [tok_l0, tok_rc, bar0, p.guard("xadd")],
                           lambda v, S_f=S_f: v.memset(S_f[:], 0.0))
            tok_z16 = p.step("scalar", [tok_z], lambda s, S16=S16: s.memzero(S16[:]))

            s16_last = [tok_z16]
            supd_last = [tok_z]
            oev_last = [tok_z]
            atev_last = [tok_z]
            ds_last = [tok_z16]
            qdw = [None, None]
            atw = [None, None]
            scT = [tok_z16]; scV = [tok_z]
            oev_toks = []
            for sc in range(NSC):
                tcol0 = sc * SCH
                pdeps = [tok_rc, tok_xnT, scT[0], scV[0], p.guard("xadd")]
                tq = proj_rope1(q16, wqb, tcol0, rcq, rsq,
                                pdeps + [tok_wq, p.guard("q16")], slot=0)
                tk = proj_rope1(k16, wkb, tcol0, rck, rsk,
                                pdeps + [tok_wk], slot=1)
                vev = [None, None]
                for tt_ in range(4):
                    pa_ = tt_ % 2
                    vdeps = pdeps + [tok_wv, p.guard("proj_ev")] + \
                            ([vev[pa_]] if vev[pa_] else [])
                    vmm = p.step("tensor", vdeps, lambda t, tt_=tt_, pa_=pa_,
                                 wvb=wvb, tcol0=tcol0: [
                        t.matmul(p_vps[:, pa_ * 512: pa_ * 512 + 512],
                                 xnT[:, dc * TLOC + tcol0 + tt_ * 128:
                                     dc * TLOC + tcol0 + tt_ * 128 + 128],
                                 wvb[:, dc * 512:(dc + 1) * 512],
                                 start=(dc == 0), stop=(dc == 7)) for dc in range(8)][-1])
                    vev[pa_] = p.step("vector", [vmm] + pdeps, lambda v, tt_=tt_, pa_=pa_,
                                      v16=v16: v.tensor_copy(
                        v16[:, tt_ * 512:(tt_ + 1) * 512],
                        p_vps[:, pa_ * 512: pa_ * 512 + 512]))
                tv = p.step("vector", [e for e in vev if e], lambda v: v.engine_nop())
                p.set_guard("proj_ev", tv)
                for cc in range(CPS):
                    c = sc * CPS + cc
                    pa = c % 2
                    kblks = [(k16[:, blk * SCH + cc * 128: blk * SCH + cc * 128 + 128],
                              ktm[:, pa * 256 + blk * 128: pa * 256 + blk * 128 + 128])
                             for blk in range(2)]
                    tok_ktm = transposes(kblks, [tk, ds_last[0]],
                                         scale_col=kdec_s[:, h:h + 1])
                    qdeps = [tq, tok_rc] + ([qdw[pa]] if qdw[pa] else [])
                    tok_qd = p.step("vector", qdeps, lambda v, pa=pa, cc=cc, h=h: [
                        v.tensor_mul(qd16[:, pa * 256 + blk * 128: pa * 256 + blk * 128 + 128],
                                     q16[:, blk * SCH + cc * 128: blk * SCH + cc * 128 + 128],
                                     cdec_s[:, h * RC:(h + 1) * RC]) for blk in range(2)][-1])
                    amm = p.step("tensor", [tk, tq, atev_last[0]],
                                 lambda t, cc=cc, p_at=p_at, k16=k16: [
                        t.matmul(p_at[:],
                                 k16[:, blk * SCH + cc * 128: blk * SCH + cc * 128 + 128],
                                 q16[:, blk * SCH + cc * 128: blk * SCH + cc * 128 + 128],
                                 start=(blk == 0), stop=(blk == 1)) for blk in range(2)][-1])
                    aev = p.step("vector", [amm, tok_rc] + ([atw[pa]] if atw[pa] else []),
                                 lambda v, pa=pa, h=h, p_at=p_at, at16=at16: v.tensor_mul(
                        at16[:, pa * 128:(pa + 1) * 128], p_at[:],
                        maskT_s[:, h * RC:(h + 1) * RC]))
                    atev_last[0] = aev
                    omm = p.step("tensor", [aev, tok_qd, tv, oev_last[0]] + s16_last,
                                 lambda t, pa=pa, cc=cc, p_o=p_o, at16=at16, S16=S16, v16=v16: ([
                        t.matmul(p_o[:], at16[:, pa * 128:(pa + 1) * 128],
                                 v16[:, cc * 512:(cc + 1) * 512], start=True, stop=False)] + [
                        t.matmul(p_o[:], qd16[:, pa * 256 + blk * 128: pa * 256 + blk * 128 + 128],
                                 S16[:, blk * 512:(blk + 1) * 512],
                                 start=False, stop=(blk == 1)) for blk in range(2)])[-1])
                    qdw[pa] = omm
                    atw[pa] = omm
                    oev = p.step("vector", [omm], lambda v, c=c, p_o=p_o, o_h=o_h:
                        v.tensor_copy(o_h[:, c * 512:(c + 1) * 512], p_o[:]))
                    oev_last[0] = oev
                    oev_toks.append(oev)
                    dmm = p.step("tensor", [tok_ktm, tv] + supd_last,
                                 lambda t, pa=pa, cc=cc, p_ds=p_ds, ktm=ktm, v16=v16: [
                        t.matmul(p_ds[:, blk * 512:(blk + 1) * 512],
                                 ktm[:, pa * 256 + blk * 128: pa * 256 + blk * 128 + 128],
                                 v16[:, cc * 512:(cc + 1) * 512],
                                 start=True, stop=True) for blk in range(2)][-1])
                    ds_last[0] = dmm
                    gC = GAMMAS[h] ** RC
                    supd0 = p.step("vector", [dmm] + s16_last,
                                   lambda v, gC=gC, S_f=S_f, p_ds=p_ds:
                        v.scalar_tensor_tensor(S_f[:, 0:512], S_f[:, 0:512], gC,
                                               p_ds[:, 0:512], ALU.mult, ALU.add))
                    supd1 = p.step("vector", [dmm] + s16_last,
                                   lambda v, gC=gC, S_f=S_f, p_ds=p_ds:
                        v.scalar_tensor_tensor(S_f[:, 512:1024], S_f[:, 512:1024],
                                               gC, p_ds[:, 512:1024],
                                               ALU.mult, ALU.add))
                    supd_last = [supd0, supd1]
                    s16a = p.step("scalar", [supd0, omm],
                                  lambda s, S16=S16, S_f=S_f:
                        s.copy(S16[:, 0:512], S_f[:, 0:512]))
                    s16b = p.step("gpsimd", [supd1, omm],
                                  lambda g_, S16=S16, S_f=S_f:
                        g_.tensor_copy(S16[:, 512:1024], S_f[:, 512:1024]))
                    s16_last = [s16a, s16b]
                scT[0] = p.merge("tensor", [])
                scV[0] = p.step("vector", [], lambda v: v.engine_nop())
            tok_scan = p.step("vector", oev_toks + s16_last + supd_last,
                              lambda v: v.engine_nop())
            p.set_guard("q16", tok_scan)
            tok_h_done[h] = tok_scan
            sb_ = st_bounce[pair_i]
            tok_sp = p.dma("gpsimd", [tok_scan, p.guard(f"stb{pair_i}")], [
                (sb_.ap()[blk * 128: blk * 128 + 128, :],
                 S_f[:, blk * 512:(blk + 1) * 512]) for blk in range(2)])
            tok_h_done[h] = tok_sp
            p.free(f"wkb{h}", f"wvb{h}", f"k16_{h}", f"v16_{h}", f"ktm_{h}",
                   f"S_f{h}", f"S16_{h}", f"at16_{h}",
                   f"p_at{h}", f"p_o{h}", f"p_ds{h}")
            p.set_guard("wq", tok_scan)

        rb_ = st_rbounce[pair_i]
        tok_scoll = p.coll([tok_h_done[h]], "AllGather", ALU.bypass,
                           PAIRS, [st_bounce[pair_i].ap().opt()], [rb_.ap().opt()])
        tok_sm = p.dma("gpsimd", [tok_scoll, p.guard("smid")], [
            (smid16[:, sl * 512:(sl + 1) * 512], rb_.ap()[sl * 128:(sl + 1) * 128, :])
            for sl in range(2)])
        tok_sm16 = p.step("vector", [tok_sm, tok_rc], lambda v: [
            v.tensor_scalar(smid16[:, sl * 512:(sl + 1) * 512],
                            smid16[:, sl * 512:(sl + 1) * 512],
                            flag_s[:, 0:1], None, ALU.mult) for sl in range(2)][-1])
        p.set_guard(f"stb{pair_i}", tok_scoll)

        if True:
            o_h = o_buf[h]
            wgb = p.sb(f"wgb{h}", [128, 8 * 512], BF16)
            wob = p.sb(f"wob{h}", [128, 4 * 1024], BF16)
            tok_wg = p.dma("gpsimd", [bar0, p.guard("wgb"), tok_scan], [
                (wgb[:, dc * 512:(dc + 1) * 512],
                 WgT[dc * 128:(dc + 1) * 128, h * 512: h * 512 + 512]) for dc in range(8)])
            tok_wo = p.dma("gpsimd", [bar0, p.guard("wob"), tok_scan], [
                (wob[:, ic * 1024:(ic + 1) * 1024],
                 WoT[h * 512 + ic * 128: h * 512 + ic * 128 + 128, :]) for ic in range(4)])
            sg16 = p.sb(f"sg16_{h}", [128, 2 * 512], BF16)
            og16 = p.sb(f"og16_{h}", [128, 2 * 512], BF16)
            ogT = p.sb(f"ogT_{h}", [128, 2 * 512], BF16)
            t_on = p.sb(f"t_on_{h}", [128, 2 * 512])
            vst = p.sb(f"vst_{h}", [128, RNC])
            p_c = p.ps(f"p_c_{h}", [128, 512])
            p_g = p.ps(f"p_g_{h}", [128, 512])
            p_w = p.ps(f"p_w_{h}", [128, 1024])
            cadd_last = [tok_sm16]
            tsil_last = [tok_sm16]
            xadd_last = [tok_sm16, p.guard("xadd")]
            qdw2 = [None, None]
            sgw = [None, None]
            ogw = [None, None]
            ogTw = [None, None]
            tonw = [None, None]
            sqw = [None, None]
            sq_toks = []
            epiT = [tok_sm16]; epiV = [tok_sm16]
            smoff = 0
            for sc in range(NSC):
                tcol0 = sc * SCH
                tq = proj_rope1(q16, wq_handles[h], tcol0, rcq, rsq,
                                [tok_scan, epiT[0], epiV[0], p.guard("q16")],
                                slot=sc % 2)
                p.set_guard("q16", tq)
                for cc in range(CPS):
                    c = sc * CPS + cc
                    pa = c % 2
                    gCj = GAMMAS[h] ** (RC * c)
                    qdeps = [tq, tok_rc] + ([qdw2[pa]] if qdw2[pa] else [])
                    tok_qdg = p.step("vector", qdeps,
                                     lambda v, pa=pa, cc=cc, h=h, gCj=gCj: [
                        v.scalar_tensor_tensor(
                            qd16[:, pa * 256 + blk * 128: pa * 256 + blk * 128 + 128],
                            q16[:, blk * SCH + cc * 128: blk * SCH + cc * 128 + 128],
                            gCj, cdec_s[:, h * RC:(h + 1) * RC],
                            ALU.mult, ALU.mult) for blk in range(2)][-1])
                    cmm = p.step("tensor", [tok_qdg, tok_sm16, cadd_last[0]],
                                 lambda t, pa=pa, p_c=p_c, smoff=smoff: [
                        t.matmul(p_c[:],
                                 qd16[:, pa * 256 + blk * 128: pa * 256 + blk * 128 + 128],
                                 smid16[:, smoff + blk * 512: smoff + blk * 512 + 512],
                                 start=(blk == 0), stop=(blk == 1)) for blk in range(2)][-1])
                    qdw2[pa] = cmm
                    cadd = p.step("vector", [cmm], lambda v, c=c, p_c=p_c, o_h=o_h:
                        v.tensor_add(o_h[:, c * 512:(c + 1) * 512],
                                     o_h[:, c * 512:(c + 1) * 512], p_c[:]))
                    cadd_last[0] = cadd
                    svdeps = [cadd] + ([sqw[pa]] if sqw[pa] else [])
                    sqw[pa] = p.step("scalar", svdeps,
                                     lambda s, c=c, pa=pa, o_h=o_h, vst=vst:
                        s.activation(sq2[:, (pa % 2) * 512:(pa % 2) * 512 + 512],
                                     o_h[:, c * 512:(c + 1) * 512],
                                     AF.Square, accum_out=vst[:, c:c + 1]))
                    sq_toks.append(sqw[pa])
                epiT[0] = p.merge("tensor", [])
                epiV[0] = p.step("vector", [], lambda v: v.engine_nop())
            tvb = p.step("vector", sq_toks, lambda v, vst=vst: v.tensor_scalar(
                vst[:], vst[:], 1.0 / RDV, LN_EPS, ALU.mult, ALU.add))
            tvb = p.step("scalar", [tvb], lambda s, vst=vst: s.activation(
                vst[:], vst[:], AF.Sqrt))
            tvb = p.step("vector", [tvb], lambda v, vst=vst: v.reciprocal(
                vst[:], vst[:]))
            for c in range(RNC):
                pa = c % 2
                gmm = p.step("tensor", [tok_wg, tok_xnT, tsil_last[0]],
                             lambda t, c=c, p_g=p_g, wgb=wgb: [
                    t.matmul(p_g[:],
                             xnT[:, dc * TLOC + c * 128: dc * TLOC + c * 128 + 128],
                             wgb[:, dc * 512:(dc + 1) * 512],
                             start=(dc == 0), stop=(dc == 7)) for dc in range(8)][-1])
                tsil = p.step("scalar", [gmm] + ([sgw[pa]] if sgw[pa] else []),
                              lambda s, pa=pa, p_g=p_g, sg16=sg16: s.activation(
                    sg16[:, pa * 512:(pa + 1) * 512], p_g[:], AF.Silu))
                tsil_last[0] = tsil
                tdeps = [tvb, tok_rc] + ([tonw[pa]] if tonw[pa] else [])
                ton = p.step("vector", tdeps, lambda v, c=c, pa=pa, o_h=o_h,
                             vst=vst, t_on=t_on: v.scalar_tensor_tensor(
                    t_on[:, pa * 512:(pa + 1) * 512],
                    o_h[:, c * 512:(c + 1) * 512],
                    vst[:, c:c + 1], gnw_s[:], ALU.mult, ALU.mult))
                ogdeps = [ton, tsil] + ([ogw[pa]] if ogw[pa] else [])
                tog = p.step("gpsimd", ogdeps, lambda g_, pa=pa, og16=og16,
                             t_on=t_on, sg16=sg16: g_.tensor_mul(
                    og16[:, pa * 512:(pa + 1) * 512],
                    t_on[:, pa * 512:(pa + 1) * 512],
                    sg16[:, pa * 512:(pa + 1) * 512]))
                sgw[pa] = tog
                tonw[pa] = tog
                oblks = [(og16[:, pa * 512 + ic * 128: pa * 512 + ic * 128 + 128],
                          ogT[:, pa * 512 + ic * 128: pa * 512 + ic * 128 + 128])
                         for ic in range(4)]
                tok_ogT = transposes(oblks, [tog] + ([ogTw[pa]] if ogTw[pa] else []))
                ogw[pa] = tok_ogT
                wmm = p.step("tensor", [tok_ogT, tok_wo] + xadd_last,
                             lambda t, pa=pa, p_w=p_w, ogT=ogT, wob=wob: [
                    t.matmul(p_w[:, n2 * 512:(n2 + 1) * 512],
                             ogT[:, pa * 512 + ic * 128: pa * 512 + ic * 128 + 128],
                             wob[:, ic * 1024 + n2 * 512: ic * 1024 + n2 * 512 + 512],
                             start=(ic == 0), stop=(ic == 3))
                    for n2 in range(2) for ic in range(4)][-1])
                ogTw[pa] = wmm
                xadd = p.step("vector", [wmm], lambda v, c=c, p_w=p_w: [
                    v.tensor_add(x[:, c * D + n2 * 512: c * D + n2 * 512 + 512],
                                 x[:, c * D + n2 * 512: c * D + n2 * 512 + 512],
                                 p_w[:, n2 * 512:(n2 + 1) * 512]) for n2 in range(2)][-1])
                xadd_last = [xadd]
            tok_epi_h = p.step("vector", [xadd_last[0], cadd_last[0]],
                               lambda v: v.engine_nop())
            p.set_guard("xadd", tok_epi_h)
            p.set_guard("smid", tok_epi_h)
            p.set_guard("wgb", tok_epi_h); p.set_guard("wob", tok_epi_h)
            p.set_guard("q16", tok_epi_h)
            p.set_guard("wq", tok_epi_h)
            p.free(f"wqb{h}", f"wgb{h}", f"wob{h}", f"sg16_{h}", f"og16_{h}",
                   f"ogT_{h}", f"t_on_{h}", f"vst_{h}",
                   f"p_c_{h}", f"p_g_{h}", f"p_w_{h}", f"o_h{h}")

    tok_l0_done = p.step("vector", [p.guard("xadd")], lambda v: v.engine_nop())
    p.free("rcq", "rsq", "rck", "rsk", "maskT_s", "cdec_s", "kdec_s", "flag_s",
           "gnw_s", "q16", "qd16", "smid16",
           "p_pj", "rtmp", "rt2")
    bar1 = p.barrier()

    if upto < 6:
        return _finish(tok_l0_done, x)

    xnT3 = xnT
    ones32s = p.sb("ones32s", [128, 128])
    logm = p.sb("logm", [128, NT * ME])
    oneh = p.sb("oneh", [128, NT * ME])
    mscale = p.sb("mscale", [128, NT])
    ag = p.sb("ag", [128, D]); ab = p.sb("ab", [128, D])
    mrank_s = p.sb("mrank_s", [128, ME])
    mpfx_s = p.sb("mpfx_s", [NCORES, 1])
    nmx = p.sb("nmx", [128, NT])
    gate = p.sb("gate", [128, NT])
    rts = p.sb("rts", [128, 2 * ME])
    ohT = p.sb("ohT", [8, TLOC], BF16)
    scn = p.sb("scn", [8, TLOC // 2])
    scnc = p.sb("scnc", [8, 2])
    scnT = p.sb("scnT", [128, NT * ME])
    cnts = p.sb("cnts", [NCORES, ME])
    pfxT = p.sb("pfxT", [1, 8])
    pfxb = p.sb("pfxb", [128, ME])
    xn2 = p.sb("xn2", [128, NT * D])
    tok_l1c = p.dma("sync", [bar1], [(ag[:], a_gamma_b), (ab[:], a_beta_b),
                                     (ones32s[:], ones_f32_in),
                                     (mrank_s[:], m_rank_in), (mpfx_s[:], m_prefix_in)])
    tok_xn2 = ln_to(xn2, x, [tok_l0_done, bar1])
    ln3_stats = ln_to(None, xn2, [tok_xn2], stats_only=True)

    x3t = p.sb("x3t", [128, D])
    x3t16 = p.sb("x3t16", [128, D], BF16)
    x3_w = [None, None]
    x3T_w = [None, None]
    for i in range(NT):
        pa = i % 2
        d_3 = [ln3_stats, tok_l1c] + ([x3T_w[0]] if x3T_w[0] else [])
        t3 = p.step("vector", d_3, lambda v, i=i: v.tensor_scalar(
            x3t[:, 0:D], xn2[:, i * D:(i + 1) * D],
            stats[:, 2 * i:2 * i + 1], stats[:, 32 + i:32 + i + 1],
            ALU.subtract, ALU.mult))
        t3 = p.step("vector", [t3], lambda v: v.tensor_mul(x3t[:, 0:D], x3t[:, 0:D], ag[:]))
        t3 = p.step("vector", [t3], lambda v: v.tensor_add(x3t[:, 0:D], x3t[:, 0:D], ab[:]))
        x3_w[pa] = p.step("vector", [t3] + ([x3T_w[0]] if x3T_w[0] else []),
                          lambda v: v.tensor_copy(x3t16[:, 0:D], x3t[:, 0:D]))
        blks3 = [(x3t16[:, dc * 128: dc * 128 + 128],
                  xnT3[:, dc * TLOC + i * 128: dc * TLOC + i * 128 + 128])
                 for dc in range(8)]
        x3T_w[0] = transposes(blks3, [x3_w[pa]])
    tok_xnT3 = p.merge("vector", [x3T_w[0]])

    wgr = p.sb("wgr", [128, 2 * D])
    ttd = p.sb("ttd", [128, 2 * D])
    ttr_last = {}
    wge_tok = [None, None]
    tr_w = [None, None]
    for e in range(ME):
        sl = e % 2
        wdeps = [tok_l1c] + ([ttr_last.get(e - 2)] if ttr_last.get(e - 2) else [])
        wge_tok[sl] = p.dma("sync", wdeps,
                            [(wgr[:, sl * D:(sl + 1) * D],
                              wg_bc_in[e * 128:(e + 1) * 128, :])])
        t_ = None
        for i in range(NT):
            sl2 = i % 2
            tdeps = ([wge_tok[sl], tok_xn2] if i == 0 else [t_]) + \
                    ([tr_w[sl2]] if tr_w[sl2] else [])
            t_ = p.step("vector", tdeps, lambda v, i=i, sl=sl, sl2=sl2:
                v.tensor_mul(ttd[:, sl2 * D:(sl2 + 1) * D],
                             xn2[:, i * D:(i + 1) * D],
                             wgr[:, sl * D:(sl + 1) * D]))
            tr_w[sl2] = p.step("scalar", [t_], lambda s, i=i, e=e, sl2=sl2:
                s.activation(ttd[:, sl2 * D:(sl2 + 1) * D],
                             ttd[:, sl2 * D:(sl2 + 1) * D], AF.Copy,
                             accum_out=logm[:, i * ME + e: i * ME + e + 1]))
        ttr_last[e] = p.merge("scalar", [tr_w[0], tr_w[1]])
    tok_logm = ttr_last[ME - 1]
    p.free("ttd", "wgr", "x3t16", "x3t")
    bar_rt = p.barrier()
    pTf_cell[0] = p.ps("pTf", [128, 128])

    tok_gate = [bar_rt]
    oh_toks = []
    for i in range(NT):
        t1 = p.step("vector", [tok_logm] + ([tok_gate[0]] if tok_gate[0] else []),
                    lambda v, i=i: v.reduce_max(nmx[:, i:i + 1],
                                                logm[:, i * ME:(i + 1) * ME],
                                                axis=AX.X, negate=True))
        t2 = p.step("scalar", [t1], lambda s, i=i: s.activation(
            rts[:, 0:ME], logm[:, i * ME:(i + 1) * ME], AF.Exp,
            bias=nmx[:, i:i + 1], accum_out=gate[:, i:i + 1]))
        t3 = p.step("vector", [t1], lambda v, i=i: v.tensor_scalar(
            oneh[:, i * ME:(i + 1) * ME], logm[:, i * ME:(i + 1) * ME],
            nmx[:, i:i + 1], None, ALU.add))
        t4 = p.step("vector", [t3], lambda v, i=i: v.tensor_scalar(
            oneh[:, i * ME:(i + 1) * ME], oneh[:, i * ME:(i + 1) * ME],
            0.0, None, ALU.is_ge))
        t5 = p.step("vector", [t4, tok_l1c, t2], lambda v, i=i: v.tensor_mul(
            rts[:, ME:2 * ME], oneh[:, i * ME:(i + 1) * ME], mrank_s[:]))
        t6 = p.step("vector", [t5], lambda v, i=i: v.reduce_max(
            nmx[:, i:i + 1], rts[:, ME:2 * ME], axis=AX.X, negate=True))
        t7 = p.step("vector", [t6], lambda v, i=i: v.tensor_scalar(
            rts[:, ME:2 * ME], rts[:, ME:2 * ME], nmx[:, i:i + 1], None, ALU.add))
        t8 = p.step("vector", [t7], lambda v, i=i: v.tensor_scalar(
            oneh[:, i * ME:(i + 1) * ME], rts[:, ME:2 * ME], 0.0, None, ALU.is_ge))
        oh_toks.append(t8)
        tok_gate[0] = t8
    tok_oneh = p.merge("vector", oh_toks[-2:])
    tok_grec = p.step("vector", [tok_oneh], lambda v: v.reciprocal(gate[:], gate[:]))

    tok_oh16 = p.step("vector", [tok_grec, bar_rt], lambda v: v.engine_nop())
    blksO = [(oneh[:, i * ME:(i + 1) * ME], ohT[0:8, i * 128:(i + 1) * 128])
             for i in range(NT)]
    tok_ohT = transposes(blksO, tok_oh16)
    HT = TLOC // 2
    tok_scnA = p.step("vector", [tok_ohT], lambda v: v.tensor_tensor_scan(
        scn[:], ohT[0:8, 0:HT], ohT[0:8, 0:HT], 0.0, ALU.add, ALU.bypass))
    tok_carry = p.step("vector", [tok_scnA], lambda v: v.tensor_copy(
        scnc[:, 0:1], scn[:, HT - 1:HT]))
    blksS1 = [(scn[0:8, i * 128:(i + 1) * 128], scnT[:, i * ME:(i + 1) * ME])
              for i in range(NT // 2)]
    tok_scnT1 = transposes(blksS1, [tok_carry])
    tok_scnB = p.step("vector", [tok_scnT1], lambda v: v.tensor_tensor_scan(
        scn[:], ohT[0:8, HT:TLOC], ohT[0:8, HT:TLOC], scnc[:, 0:1],
        ALU.add, ALU.bypass))
    blksS2 = [(scn[0:8, i * 128:(i + 1) * 128], scnT[:, (NT // 2 + i) * ME:(NT // 2 + i + 1) * ME])
              for i in range(NT // 2)]
    tok_scnT2 = transposes(blksS2, [tok_scnB])
    tok_scn = tok_scnB
    tok_cs = p.dma("gpsimd", [tok_scn], [(cnt_snd.ap()[0:1, 0:ME], scn[:, HT - 1:HT])])
    tok_cc8 = p.coll([tok_cs], "AllGather", ALU.bypass, ALL8,
                     [cnt_snd.ap().opt()], [cnt_rcv.ap().opt()])
    tok_cr = p.dma("gpsimd", [tok_cc8], [(cnts[:], cnt_rcv.ap()[:, :])])
    p_pfx = p.ps("p_pfx", [1, 8])
    tok_pfx = p.step("tensor", [tok_cr, tok_l1c], lambda t: t.matmul(
        p_pfx[:], mpfx_s[0:NCORES, :], cnts[:, 0:ME], start=True, stop=True))
    tok_pfxe = p.step("vector", [tok_pfx], lambda v: v.tensor_copy(
        pfxT[0:1, 0:8], p_pfx[:]))
    p_pfb = p.ps("p_pfb", [128, 8])
    tok_pfb = p.step("tensor", [tok_pfxe], lambda t: t.matmul(
        p_pfb[:], ones32s[0:1, :], pfxT[0:1, 0:8], start=True, stop=True))
    tok_pfbe = p.step("vector", [tok_pfb], lambda v: v.tensor_copy(pfxb[:], p_pfb[:]))
    tok_scnT = p.merge("vector", [tok_scnT2])
    sel_ev = [None]
    for i in range(NT):
        t1 = p.step("vector", [tok_scnT, tok_oneh] + ([sel_ev[0]] if sel_ev[0] else []),
                    lambda v, i=i: v.tensor_mul(rts[:, 0:ME], oneh[:, i * ME:(i + 1) * ME],
                                                scnT[:, i * ME:(i + 1) * ME]))
        t2 = p.step("vector", [t1], lambda v, i=i: v.reduce_sum(
            mscale[:, i:i + 1], rts[:, 0:ME], axis=AX.X))
        t3 = p.step("vector", [t2, tok_pfbe], lambda v, i=i: v.tensor_mul(
            rts[:, 0:ME], oneh[:, i * ME:(i + 1) * ME], pfxb[:]))
        t3 = p.step("vector", [t3], lambda v, i=i: v.reduce_sum(
            rts[:, ME:ME + 1], rts[:, 0:ME], axis=AX.X))
        t4 = p.step("vector", [t3], lambda v, i=i: v.scalar_tensor_tensor(
            mscale[:, i:i + 1], mscale[:, i:i + 1], -1.0, rts[:, ME:ME + 1],
            ALU.add, ALU.add))
        t5 = p.step("vector", [t4], lambda v, i=i: v.tensor_scalar(
            mscale[:, i:i + 1], mscale[:, i:i + 1], CAP, None, ALU.is_lt))
        t6 = p.step("vector", [t5, tok_grec], lambda v, i=i: v.tensor_mul(
            mscale[:, i:i + 1], mscale[:, i:i + 1], gate[:, i:i + 1]))
        sel_ev[0] = t6
    tok_mscale = p.merge("vector", [sel_ev[0]])

    if upto < 7:
        return _finish(tok_mscale, xn2)

    web = p.sb("web", [128, 8 * D], BF16)
    be16 = p.sb("be16", [1, D], BF16)
    tok_we = p.dma("gpsimd", [bar1, bar_rt], [
        (web[:, dc * D:(dc + 1) * D], mWe_in[dc * 128:(dc + 1) * 128, :])
        for dc in range(8)] + [(be16[:], m_be_in)])
    p.free("p_pfx", "p_pfb", "pTf")
    uT3 = p.sb("uT3", [128, D], BF16)
    x2m = p.sb("x2m", [128, D], BF16)
    p_y = [p.ps("p_y0", [128, 1024]), p.ps("p_y1", [128, 1024])]
    y_mm = [None, None]; y_ev = [None, None]
    cast_w = [None, None]; tr_w = [None, None]
    for i in range(NT):
        pa = i % 2
        cdeps = [tok_xn2, bar_rt] + ([tr_w[pa]] if tr_w[pa] else [])
        cast_w[pa] = p.step("vector", cdeps + [t_ for t_ in (tr_w[0], tr_w[1]) if t_],
                            lambda v, i=i: v.tensor_copy(
            x2m[:, 0:D], xn2[:, i * D:(i + 1) * D]))
        blksM = [(x2m[:, dc * 128: dc * 128 + 128],
                  uT3[:, dc * 128: dc * 128 + 128])
                 for dc in range(8)]
        tr_w[pa] = transposes(blksM, [cast_w[pa]] + [t_ for t_ in (y_mm[0], y_mm[1]) if t_])
        deps = [tok_we, tr_w[pa], tok_mscale] + ([y_ev[pa]] if y_ev[pa] else [])
        y_mm[pa] = p.step("tensor", deps, lambda t, i=i, pa=pa: ([
            t.matmul(p_y[pa][:, n2 * 512:(n2 + 1) * 512],
                     uT3[:, dc * 128: dc * 128 + 128],
                     web[:, dc * D + n2 * 512: dc * D + n2 * 512 + 512],
                     start=(dc == 0), stop=False)
            for n2 in range(2) for dc in range(8)] + [
            t.matmul(p_y[pa][:, n2 * 512:(n2 + 1) * 512],
                     ones16[0:1, 0:128], be16[0:1, n2 * 512: n2 * 512 + 512],
                     start=False, stop=True)
            for n2 in range(2)])[-1])
        y_ev[pa] = p.step("vector", [y_mm[pa], tok_mscale], lambda v, i=i, pa=pa: [
            v.scalar_tensor_tensor(x[:, i * D + n2 * 512: i * D + n2 * 512 + 512],
                                   p_y[pa][:, n2 * 512:(n2 + 1) * 512],
                                   mscale[:, i:i + 1],
                                   x[:, i * D + n2 * 512: i * D + n2 * 512 + 512],
                                   ALU.mult, ALU.add) for n2 in range(2)][-1])
    tok_moe = p.merge("vector", [y_ev[0], y_ev[1]])
    p.free("web", "be16", "p_y0", "p_y1", "uT3", "x2m", "xn2")

    if upto < 8:
        return _finish(tok_moe, x)

    p.free("logm", "oneh", "mscale", "ag", "ab",
           "mrank_s", "mpfx_s",
           "nmx", "gate", "rts", "ohT", "scn", "scnc", "scnT", "cnts", "pfxT", "pfxb")
    p.free("pTa", "pTb")
    bar_a = p.barrier()

    ps_big = p.ps("ps_big", [128, 2048])
    ps_nrm = p.ps("ps_nrm", [1, 512])
    ps_q = p.ps("ps_q", [64, 512])
    ps_av = p.ps("ps_av", [65, 512])
    ps_rdb = p.ps("ps_rdb", [64, 512])

    ISC = 1.0 / (ADH ** 0.5)
    NKB = T // 128
    xadd_last2 = [tok_moe]
    for hh in range(2):
        nrow = p.sb(f"nrow_{hh}", [1, 640])
        kbs = p.sb(f"kbs_{hh}", [1, 8])
        wq2 = p.sb(f"wq2_{hh}", [128, 8 * 512], BF16)
        wk2 = p.sb(f"wk2_{hh}", [128, 8 * 512], BF16)
        wv2 = p.sb(f"wv2_{hh}", [128, 8 * 512], BF16)
        kst = p.sb(f"kst_{hh}", [128, 4 * 512], BF16)
        vst2 = p.sb(f"vst2_{hh}", [128, 4 * 8 * VW], BF16)
        sqs = p.sb(f"sqs_{hh}", [128, 4 * 512], BF16)
        p_pr = ps_big
        p_n = ps_nrm
        tok_w2l = p.dma("gpsimd", [bar_a, p.guard("wq2")], [
            (wq2[:, dc * 512:(dc + 1) * 512],
             WqkvT[dc * 128:(dc + 1) * 128, hh * 512: hh * 512 + 512])
            for dc in range(8)] + [
            (wk2[:, dc * 512:(dc + 1) * 512],
             WqkvT[dc * 128:(dc + 1) * 128, D + hh * 512: D + hh * 512 + 512])
            for dc in range(8)] + [
            (wv2[:, dc * 512:(dc + 1) * 512],
             WqkvT[dc * 128:(dc + 1) * 128, 2 * D + hh * 512: 2 * D + hh * 512 + 512])
            for dc in range(8)])

        k_dma = [None, None, None, None]
        sq_w = [None, None, None, None]
        nred = [None]
        for pp in range(4):
            for tg in range(4):
                pa = (pp * 4 + tg) % 4
                deps = [tok_w2l, tok_xnT3] + ([k_dma[pa]] if k_dma[pa] else [])
                kmm = p.step("tensor", deps, lambda t, pp=pp, tg=tg, pa=pa, wk2=wk2: [
                    t.matmul(p_pr[:, pa * 512: pa * 512 + 512],
                             wk2[:, dc * 512 + pp * 128: dc * 512 + pp * 128 + 128],
                             xnT3[:, dc * TLOC + tg * 512: dc * TLOC + tg * 512 + 512],
                             start=(dc == 0), stop=(dc == 7)) for dc in range(8)][-1])
                kev = p.step("vector", [kmm], lambda v, pa=pa, kst=kst: v.tensor_copy(
                    kst[:, pa * 512: pa * 512 + 512], p_pr[:, pa * 512: pa * 512 + 512]))
                sdeps = [kev] + ([sq_w[pa]] if sq_w[pa] else [])
                tsq = p.step("scalar", sdeps, lambda s, pa=pa, kst=kst, sqs=sqs:
                    s.activation(sqs[:, pa * 512: pa * 512 + 512],
                                 kst[:, pa * 512: pa * 512 + 512], AF.Square))
                last_nm = None
                for hd in range(2):
                    idx = 512 + (pp * 2 + hd) * 4 + tg
                    nmdeps = [tsq] + ([nred[0]] if nred[0] else [])
                    last_nm = p.step("tensor", nmdeps,
                                     lambda t, pa=pa, hd=hd, sqs=sqs, p_n=p_n: t.matmul(
                        p_n[0:1, 0:512],
                        ones16[64 * hd:64 * hd + 64, 0:1],
                        sqs[64 * hd:64 * hd + 64, pa * 512: pa * 512 + 512],
                        start=True, stop=True))
                    ncp = p.step("vector", [last_nm], lambda v, nrow=nrow, p_n=p_n:
                        v.tensor_copy(nrow[0:1, 0:512], p_n[0:1, 0:512]))
                    nred[0] = p.step("vector", [ncp], lambda v, idx=idx, nrow=nrow:
                        v.reduce_max(nrow[0:1, idx:idx + 1],
                                     nrow[0:1, 0:512], axis=AX.X))
                sq_w[pa] = last_nm
                k_dma[pa] = p.dma("gpsimd", [kev], [
                    (k_snd[hh].ap()[pp * 128:(pp + 1) * 128, tg * 512:(tg + 1) * 512],
                     kst[:, pa * 512: pa * 512 + 512])])
        kb_tok = [nred[0]]
        for h8 in range(8):
            kb_tok[0] = p.step("vector", [kb_tok[0]],
                               lambda v, h8=h8, nrow=nrow: v.reduce_max(
                nrow[0:1, 576 + h8:577 + h8],
                nrow[0:1, 512 + h8 * 4: 512 + h8 * 4 + 4], axis=AX.X))

        v_dma = [None, None, None, None]
        for ti in range(NT):
            pa = ti % 4
            deps = [tok_w2l, tok_xnT3] + ([v_dma[pa]] if v_dma[pa] else [])
            vmm = p.step("tensor", deps, lambda t, ti=ti, pa=pa, wv2=wv2: [
                t.matmul(p_pr[:, pa * 512: pa * 512 + 512],
                         xnT3[:, dc * TLOC + ti * 128: dc * TLOC + ti * 128 + 128],
                         wv2[:, dc * 512:(dc + 1) * 512],
                         start=(dc == 0), stop=(dc == 7)) for dc in range(8)][-1])
            vz = p.step("vector", [vmm] + ([v_dma[pa]] if v_dma[pa] else []),
                        lambda v, pa=pa, vst2=vst2: v.memset(
                vst2[:, pa * 8 * VW:(pa + 1) * 8 * VW], 0.0))
            vev = p.step("vector", [vz], lambda v, pa=pa, vst2=vst2: [
                v.tensor_copy(
                    vst2[:, pa * 8 * VW + h8 * VW: pa * 8 * VW + h8 * VW + 64],
                    p_pr[:, pa * 512 + h8 * 64: pa * 512 + h8 * 64 + 64])
                for h8 in range(8)][-1])
            vone = p.step("vector", [vev], lambda v, pa=pa, vst2=vst2: [
                v.memset(vst2[:, pa * 8 * VW + h8 * VW + 64:
                              pa * 8 * VW + h8 * VW + 65], 1.0)
                for h8 in range(8)][-1])
            v_dma[pa] = p.dma("gpsimd", [vone], [
                (v_snd[hh].ap()[ti * 128:(ti + 1) * 128, :],
                 vst2[:, pa * 8 * VW:(pa + 1) * 8 * VW])])

        tok_kns = p.dma("gpsimd", [kb_tok[0]], [
            (kn_snd[hh].ap()[0:1, 0:8], nrow[0:1, 576:584])])
        tok_knc = p.coll([tok_kns], "AllGather", ALU.bypass, PAIRS,
                         [kn_snd[hh].ap().opt()], [kn_rcv[hh].ap().opt()])
        tok_kc = p.coll([d for d in k_dma if d] + [tok_knc],
                        "AllGather", ALU.bypass, PAIRS,
                        [k_snd[hh].ap().opt()], [k_rcv[hh].ap().opt()])
        tok_vc = p.coll([d for d in v_dma if d] + [tok_kc],
                        "AllGather", ALU.bypass, PAIRS,
                        [v_snd[hh].ap().opt()], [v_rcv[hh].ap().opt()])

        p.free(f"wk2_{hh}", f"wv2_{hh}", f"kst_{hh}", f"vst2_{hh}", f"sqs_{hh}")
        bar_kv = p.barrier()

        knr = p.sb(f"knr_{hh}", [1, 16])
        tok_knl = p.dma("gpsimd", [tok_knc, bar_kv], [
            (knr[0:1, 0:8], kn_rcv[hh].ap()[0:1, :]),
            (knr[0:1, 8:16], kn_rcv[hh].ap()[1:2, :])])
        tkb = p.step("vector", [tok_knl], lambda v, knr=knr, kbs=kbs: v.tensor_max(
            kbs[0:1, 0:8], knr[0:1, 0:8], knr[0:1, 8:16]))
        tkb = p.step("scalar", [tkb], lambda s, kbs=kbs: s.activation(
            kbs[0:1, 0:8], kbs[0:1, 0:8], AF.Sqrt))

        vf = p.sb(f"vf_{hh}", [128, NKB * 8 * VW], BF16)
        tok_vf = p.dma("sync", [tok_vc, bar_kv], [
            (vf[:, kb * 8 * VW:(kb + 1) * 8 * VW],
             v_rcv[hh].ap()[kb * 128:(kb + 1) * 128, :])
            for kb in range(NKB)])
        o_half = p.sb(f"o_half_{hh}", [128, 4 * TLOC], BF16)
        qt = [p.sb(f"qt0_{hh}", [65, TLOC], BF16), p.sb(f"qt1_{hh}", [65, TLOC], BF16)]
        kfh = [p.sb(f"kfh0_{hh}", [65, T], BF16), p.sb(f"kfh1_{hh}", [65, T], BF16)]
        nq = p.sb(f"nq_{hh}", [1, TLOC])
        nq16 = p.sb(f"nq16_{hh}", [1, TLOC], BF16)
        ex16 = p.sb(f"ex16_{hh}", [128, 2048], BF16)
        qsq16 = p.sb(f"qsq16_{hh}", [64, 512], BF16)
        rdr = p.sb(f"rdr_{hh}", [1, 512])
        rdb_s = p.sb(f"rdb_s_{hh}", [64, 512])
        kb2 = p.sb(f"kb2_{hh}", [1, 8])
        p_q = ps_q
        p_qn = ps_nrm
        p_s = ps_big
        p_av = ps_av
        p_rdb = ps_rdb
        tkb2 = p.step("vector", [tkb], lambda v, kbs=kbs, kb2=kb2: v.tensor_mul(
            kb2[0:1, :], kbs[0:1, :], kbs[0:1, :]))
        tkb2 = p.step("vector", [tkb2], lambda v, kb2=kb2: v.tensor_scalar_mul(
            kb2[0:1, :], kb2[0:1, :], 1.0201))
        qev_g = [None]
        exp_w = [None, None]
        av_w = [None, None]
        av_prev = [None]
        on_prev = [None]
        rd_w = [None]
        nq_w = [None]
        qn_mm = [None]
        qrow_tok = {}
        qevl = {}
        kfo_tok = {}
        tail_prev = [None]

        def emit_kf(h8, dep_done):
            hpa = h8 % 2
            pp, hd = h8 // 2, h8 % 2
            kdeps = [tok_kc, bar_kv] + ([dep_done] if dep_done else [])
            tok_kfh = p.dma("sync", kdeps, [
                (kfh[hpa][0:64, r * TLOC:(r + 1) * TLOC],
                 k_rcv[hh].ap()[r * 512 + pp * 128 + hd * 64:
                                r * 512 + pp * 128 + hd * 64 + 64, :])
                for r in range(2)])
            kfo_tok[hpa] = p.step("vector", [tok_kfh], lambda v, hpa=hpa, kfh=kfh:
                                  v.memset(kfh[hpa][64:65, :], 1.0))

        def make_build(h8, dep_done):
            hpa = h8 % 2
            thunks = []
            qev_cell = {}

            def mk_qmm(tg):
                def th():
                    qdeps = [tok_w2l, tok_xnT3] + ([dep_done] if dep_done else []) \
                            + ([qev_g[0]] if qev_g[0] else [])
                    qmm = p.step("tensor", qdeps, lambda t, h8=h8, tg=tg, wq2=wq2,
                                 p_q=p_q, xnT3=xnT3: [
                        t.matmul(p_q[:],
                                 wq2[:, dc * 512 + h8 * 64: dc * 512 + h8 * 64 + 64],
                                 xnT3[:, dc * TLOC + tg * 512:
                                      dc * TLOC + tg * 512 + 512],
                                 start=(dc == 0), stop=(dc == 7))
                        for dc in range(8)][-1])
                    qev = p.step("vector", [qmm], lambda v, hpa=hpa, tg=tg, qt=qt,
                                 p_q=p_q: v.tensor_copy(
                        qt[hpa][0:64, tg * 512:(tg + 1) * 512], p_q[:]))
                    qev_g[0] = qev
                    qev_cell[tg] = qev
                    qev_cell[('sq', tg)] = p.step("vector",
                        [qev] + ([qn_mm[0]] if qn_mm[0] else []),
                        lambda v, hpa=hpa, tg=tg, qt=qt, qsq16=qsq16: v.tensor_mul(
                            qsq16[:], qt[hpa][0:64, tg * 512:(tg + 1) * 512],
                            qt[hpa][0:64, tg * 512:(tg + 1) * 512]))
                return th

            def mk_qn(tg):
                def th():
                    deps = [qev_cell[('sq', tg)]] + ([nq_w[0]] if nq_w[0] else [])
                    qn_mm[0] = p.step("tensor", deps, lambda t, qsq16=qsq16,
                                      p_qn=p_qn: t.matmul(
                        p_qn[:], ones16[0:64, 0:1], qsq16[:],
                        start=True, stop=True))
                    nwd = [qn_mm[0]] + ([tail_prev[0]] if tail_prev[0] else [])
                    nq_w[0] = p.step("vector", nwd, lambda v, tg=tg, nq=nq,
                                     p_qn=p_qn: v.tensor_copy(
                        nq[0:1, tg * 512:(tg + 1) * 512], p_qn[0:1, :]))
                return th

            for tg in range(4):
                thunks.append(mk_qmm(tg))
                thunks.append(mk_qn(tg))

            def tail():
                t0 = p.step("gpsimd", [nq_w[0], tkb2], lambda g, h8=h8, nq=nq,
                            kb2=kb2: g.tensor_scalar(
                    nq[0:1, 0:TLOC], nq[0:1, 0:TLOC], kb2[0:1, h8:h8 + 1], None,
                    ALU.mult))
                t1s = p.step("scalar", [t0], lambda s, nq=nq: s.activation(
                    nq[0:1, 0:TLOC], nq[0:1, 0:TLOC], AF.Sqrt))
                t2s = p.step("gpsimd", [t1s], lambda g, nq=nq, nq16=nq16:
                    g.tensor_scalar(nq16[0:1, :], nq[0:1, 0:TLOC], -1.0, -0.08,
                                    ALU.mult, ALU.add))
                qrow_tok[hpa] = p.dma("gpsimd", [t2s] +
                                      ([dep_done] if dep_done else []),
                                      [(qt[hpa][64:65, :], nq16[0:1, :])])
                qevl[hpa] = qev_cell[3]
                tail_prev[0] = qrow_tok[hpa]
            return thunks, tail

        emit_kf(0, None)
        th0, tl0 = make_build(0, None)
        for t in th0:
            t()
        tl0()
        flash_done = [None]
        for h8 in range(8):
            hpa = h8 % 2
            pp, hd = h8 // 2, h8 % 2
            prev_done = flash_done[0]
            nb = make_build(h8 + 1, prev_done) if h8 < 7 else None
            pend = list(nb[0]) if nb else []

            def emit_avs(pbx, etok, h8=h8):
                basex = (pbx % 2) * 1024
                tokav = None
                for i in range(2):
                    kb = 2 * pbx + i
                    avdeps = [etok] + ([tok_vf] if kb == 0 else []) + \
                             ([on_prev[0]] if kb == 0 and on_prev[0] else [])
                    tokav = p.step("tensor", avdeps, lambda t, kb=kb,
                                   col=basex + i * 512, h8=h8, vf=vf, ex16=ex16,
                                   p_av=p_av: t.matmul(
                        p_av[:],
                        vf[:, kb * 8 * VW + h8 * VW: kb * 8 * VW + h8 * VW + 65],
                        ex16[:, col:col + 512],
                        start=(kb == 0), stop=(kb == NKB - 1)))
                av_w[pbx % 2] = tokav
                av_prev[0] = tokav

            for qq in range(4):
                prev = None
                for pb in range(NKB // 2):
                    ps_ = pb % 2
                    base = ps_ * 1024
                    kb0 = 2 * pb
                    sdeps = [kfo_tok[hpa], qrow_tok[hpa], qevl[hpa]] + \
                            ([exp_w[ps_]] if exp_w[ps_] else [])
                    smm = p.step("tensor", sdeps, lambda t, qq=qq, kb0=kb0,
                                 base=base, hpa=hpa, kfh=kfh, qt=qt, p_s=p_s: [
                        t.matmul(p_s[:, base + i * 512: base + i * 512 + 512],
                                 kfh[hpa][0:65, (kb0 + i) * 128:(kb0 + i + 1) * 128],
                                 qt[hpa][0:65, qq * 512:(qq + 1) * 512],
                                 start=True, stop=True) for i in range(2)][-1])
                    edeps = [smm] + ([av_w[ps_]] if av_w[ps_] else [])
                    etok = p.step("scalar", edeps, lambda s, base=base, p_s=p_s,
                                  ex16=ex16: s.activation(
                        ex16[:, base:base + 1024], p_s[:, base:base + 1024],
                        AF.Exp, scale=ISC))
                    exp_w[ps_] = etok
                    if prev is not None:
                        emit_avs(*prev)
                    prev = (pb, etok)
                    if pend and qq == 1 and pb % 2 == 0:
                        pend.pop(0)()
                emit_avs(*prev)
                if nb and qq == 0:
                    emit_kf(h8 + 1, prev_done)
                if nb and qq == 2:
                    nb[1]()
                t1 = p.step("vector", [av_prev[0]] + ([rd_w[0]] if rd_w[0] else []),
                            lambda v, p_av=p_av, rdr=rdr: v.reciprocal(
                    rdr[0:1, :], p_av[64:65, :]))
                t2 = p.step("tensor", [t1], lambda t, rdr=rdr, p_rdb=p_rdb: t.matmul(
                    p_rdb[:], ones32s[0:1, 0:64], rdr[0:1, :], start=True, stop=True))
                t3 = p.step("vector", [t2], lambda v, p_rdb=p_rdb, rdb_s=rdb_s:
                    v.tensor_copy(rdb_s[:], p_rdb[:]))
                on_prev[0] = p.step("vector", [t3], lambda v, pp=pp, hd=hd, qq=qq,
                                    p_av=p_av, rdb_s=rdb_s, o_half=o_half:
                    v.tensor_mul(
                        o_half[64 * hd:64 * hd + 64,
                               pp * TLOC + qq * 512: pp * TLOC + qq * 512 + 512],
                        p_av[0:64, :], rdb_s[:]))
                rd_w[0] = on_prev[0]
            flash_done[0] = on_prev[0]
        tok_oh2 = p.merge("vector", [on_prev[0]])

        p.free(f"qt0_{hh}", f"qt1_{hh}", f"kfh0_{hh}", f"kfh1_{hh}",
               f"nq_{hh}", f"nq16_{hh}", f"ex16_{hh}", f"qsq16_{hh}",
               f"rdr_{hh}", f"rdb_s_{hh}", f"kb2_{hh}")
        bar_wo = p.barrier()
        wo2 = p.sb(f"wo2_{hh}", [128, 4 * D], BF16)
        bo16 = p.sb(f"bo16_{hh}", [1, D], BF16)
        tok_wo2 = p.dma("gpsimd", [bar_wo], [
            (wo2[:, pp * D:(pp + 1) * D],
             WoutT[hh * 512 + pp * 128: hh * 512 + pp * 128 + 128, :])
            for pp in range(4)] + [(bo16[:], b_out_in)])
        wo_mm = [None, None]; wo_ev = [None, None]
        for i in range(NT):
            pa = i % 2
            deps = [tok_oh2, tok_wo2, exp_w[0], exp_w[1]] + \
                   ([wo_ev[pa]] if wo_ev[pa] else [])
            wo_mm[pa] = p.step("tensor", deps, lambda t, i=i, pa=pa, hh=hh,
                               o_half=o_half, wo2=wo2, bo16=bo16: ([
                t.matmul(ps_big[:, pa * 1024 + n2 * 512: pa * 1024 + n2 * 512 + 512],
                         o_half[:, pp * TLOC + i * 128: pp * TLOC + i * 128 + 128],
                         wo2[:, pp * D + n2 * 512: pp * D + n2 * 512 + 512],
                         start=(pp == 0), stop=(pp == 3 and hh == 1))
                for n2 in range(2) for pp in range(4)] + ([
                t.matmul(ps_big[:, pa * 1024 + n2 * 512: pa * 1024 + n2 * 512 + 512],
                         ones16[0:1, 0:128], bo16[0:1, n2 * 512: n2 * 512 + 512],
                         start=False, stop=True)
                for n2 in range(2)] if hh == 0 else []))[-1])
            wo_ev[pa] = p.step("vector", [wo_mm[pa], xadd_last2[0]],
                               lambda v, i=i, pa=pa: [
                v.tensor_add(x[:, i * D + n2 * 512: i * D + n2 * 512 + 512],
                             x[:, i * D + n2 * 512: i * D + n2 * 512 + 512],
                             ps_big[:, pa * 1024 + n2 * 512: pa * 1024 + n2 * 512 + 512])
                for n2 in range(2)][-1])
        xadd_last2[0] = p.merge("vector", [wo_ev[0], wo_ev[1]])
        p.set_guard("wq2", p.barrier())
        p.free(f"wo2_{hh}", f"bo16_{hh}",
               f"o_half_{hh}", f"vf_{hh}", f"knr_{hh}", f"wq2_{hh}",
               f"nrow_{hh}", f"kbs_{hh}")

    return _finish(xadd_last2[0], x)


def kernel(_upto=99, **inputs):
    g, shards = _prep_inputs(inputs)
    nc = build_nc(_upto)
    in_maps = []
    for c in range(NCORES):
        m = dict(g)
        m["x_in"] = shards[c]
        m.update(_host_consts(c))
        in_maps.append(m)
    res = run_bass_kernel_spmd(nc, in_maps, core_ids=list(range(NCORES)))
    out = np.concatenate([res.results[c]["x_out"] for c in range(NCORES)], axis=0)
    return out.reshape(B, T, D)


# version 12
# speedup vs baseline: 1.5319x; 1.0207x over previous
import numpy as np
import concourse.bass as bass
import concourse.mybir as mybir
from concourse.bass_utils import run_bass_kernel_spmd

F32 = mybir.dt.float32
BF16 = mybir.dt.bfloat16
AF = mybir.ActivationFunctionType
ALU = mybir.AluOpType
AX = mybir.AxisListType

B, T, D = 4, 4096, 1024
NCORES = 8
TLOC = 2048
NT = TLOC // 128
LN_EPS = 1e-5

RH = 4
RDK, RDV = 256, 512
RC = 128
RNC = TLOC // RC
GAMMAS = [1.0 - 2.0 ** (-5.0 - h) for h in range(RH)]

NE, NS = 16, 16
ES = NE * NS
FH = 256
NORM_EPS = 1e-12

AH, ADH = 16, 64
ME = 8
CAP = float((B * T) // ME)

_BUILD_UPTO = 99


def _host_consts(core):
    import ml_dtypes
    half = core % 2
    c = {}
    inv_freq = 1.0 / (10000.0 ** (np.arange(0, RDK, 2, dtype=np.float64) / RDK))
    tpos = half * TLOC + np.arange(TLOC, dtype=np.float64)
    fr = inv_freq[:, None] * tpos[None, :]
    c["rope_cos"] = np.cos(fr).astype(np.float32)
    c["rope_sin"] = np.sin(fr).astype(np.float32)
    c["rope_cos_q"] = (np.cos(fr) * (RDK ** -0.5)).astype(np.float32)
    c["rope_sin_q"] = (np.sin(fr) * (RDK ** -0.5)).astype(np.float32)
    maskT = np.zeros((RH, RC, RC), dtype=np.float32)
    cdec = np.zeros((RH, 128, RC), dtype=np.float32)
    kdec = np.zeros((RH, RC, 1), dtype=np.float32)
    for h, g in enumerate(GAMMAS):
        i = np.arange(RC, dtype=np.float64)
        dif = i[None, :] - i[:, None]
        maskT[h] = np.where(dif >= 0, g ** dif, 0.0).astype(np.float32)
        cdec[h] = np.broadcast_to((g ** (i + 1.0)).astype(np.float32), (128, RC))
        kdec[h] = (g ** (RC - 1.0 - i)).astype(np.float32)[:, None]
    c["ret_maskT"] = maskT
    c["ret_cdec"] = cdec
    c["ret_kdec"] = kdec
    c["ret_flag"] = np.full((128, 1), float(half), dtype=np.float32)
    c["ident"] = np.eye(128, dtype=np.float32).astype(ml_dtypes.bfloat16)
    c["ident_f32"] = np.eye(128, dtype=np.float32)
    c["ones_f32"] = np.ones((128, 128), dtype=np.float32)
    c["ones_bf16"] = np.ones((128, 128), dtype=np.float32).astype(ml_dtypes.bfloat16)
    c["m_rank"] = np.broadcast_to(np.arange(ME, 0, -1, dtype=np.float32), (128, ME)).copy()
    c["m_prefix"] = (np.arange(NCORES) < core).astype(np.float32)[:, None]
    return c


def _prep_inputs(inp):
    g = {}
    g["WqT"] = np.ascontiguousarray(np.asarray(inp["L0_attn_Wq"]).T)
    g["WkT"] = np.ascontiguousarray(np.asarray(inp["L0_attn_Wk"]).T)
    g["WvT"] = np.ascontiguousarray(np.asarray(inp["L0_attn_Wv"]).T)
    g["WgT"] = np.ascontiguousarray(np.asarray(inp["L0_attn_Wg"]).T)
    g["WoT"] = np.ascontiguousarray(np.asarray(inp["L0_attn_Wo"]).T)
    g["gnw_b"] = np.broadcast_to(np.asarray(inp["L0_attn_gnorm_w"]), (128, RDV)).copy()
    g["f_gamma_b"] = np.broadcast_to(np.asarray(inp["L0_ffn_ln_gamma"]), (128, D)).copy()
    g["f_beta_b"] = np.broadcast_to(np.asarray(inp["L0_ffn_ln_beta"]), (128, D)).copy()
    g["mu2d"] = np.ascontiguousarray(np.asarray(inp["L0_ffn_mu"]).reshape(D, ES))
    g["scale_b"] = np.broadcast_to(np.asarray(inp["L0_ffn_scale"]).reshape(1, 1), (128, 1)).copy()
    g["w1f"] = np.ascontiguousarray(np.asarray(inp["L0_ffn_w1"]).reshape(NE * D, FH))
    g["b1T"] = np.ascontiguousarray(np.asarray(inp["L0_ffn_b1"]))
    g["w2f"] = np.ascontiguousarray(np.asarray(inp["L0_ffn_w2"]).reshape(NE * FH, D))
    g["b2"] = np.ascontiguousarray(np.asarray(inp["L0_ffn_b2"]))
    g["a_gamma_b"] = np.broadcast_to(np.asarray(inp["L1_attn_ln_gamma"]), (128, D)).copy()
    g["a_beta_b"] = np.broadcast_to(np.asarray(inp["L1_attn_ln_beta"]), (128, D)).copy()
    g["WqkvT"] = np.ascontiguousarray(np.asarray(inp["L1_attn_w_qkv"]).T)
    g["WoutT"] = np.ascontiguousarray(np.asarray(inp["L1_attn_w_out"]).T)
    g["b_out"] = np.asarray(inp["L1_attn_b_out"]).reshape(1, D)
    g["mWg"] = np.ascontiguousarray(np.asarray(inp["L1_ffn_Wg"]))
    _wg = np.asarray(inp["L1_ffn_Wg"], dtype=np.float32)
    g["wg_bc"] = np.ascontiguousarray(
        np.broadcast_to(_wg.T[:, None, :], (ME, 128, D))).reshape(ME * 128, D)
    g["mWe"] = np.ascontiguousarray(np.asarray(inp["L1_ffn_We"]))
    g["m_be"] = np.asarray(inp["L1_ffn_be"]).reshape(1, D)
    x = np.asarray(inp["L0_attn_x"], dtype=np.float32).reshape(B * T, D)
    shards = [np.ascontiguousarray(x[c * TLOC:(c + 1) * TLOC]) for c in range(NCORES)]
    return g, shards


ENGS = ("tensor", "vector", "scalar", "gpsimd", "sync")


class Builder:
    def __init__(self, nc):
        self.nc = nc
        self._allocs = {}
        self.q = {e: [] for e in ENGS}
        self.sem_cm = {}
        self.sem = {}
        self.cnt = {e: 0 for e in ENGS}
        self.wm = {e: {} for e in ENGS}
        for e in ENGS:
            cm = nc.semaphore("sem_" + e)
            self.sem_cm[e] = cm
            self.sem[e] = cm.__enter__()
        self.NDMA = 4
        for d in ["dma_out"] + [f"dma_{e}{i}" for e in ("sync", "gpsimd")
                                for i in range(4)]:
            cm = nc.semaphore(d)
            self.sem_cm[d] = cm
            self.sem[d] = cm.__enter__()
        self.dmacnt = {f"dma_{e}{i}": 0 for e in ("sync", "gpsimd") for i in range(4)}
        self.dmarr = {"sync": 0, "gpsimd": 0}
        self.outcnt = 0
        cm = nc.semaphore("sem_cc")
        self.sem_cm["cc"] = cm
        self.sem["cc"] = cm.__enter__()
        self.cccnt = 0
        self.guards = {}

    def guard(self, name):
        return self.guards.get(name, ("vector", 0))

    def set_guard(self, name, tok):
        self.guards[name] = tok

    def _merge_emit(self, eng):
        if eng in ("vector", "gpsimd"):
            return lambda en: en.engine_nop()
        if eng in ("tensor", "scalar"):
            return lambda en: en.drain()
        return lambda en: en.nop()

    def merge(self, eng, deps):
        return self.step(eng, deps, self._merge_emit(eng))

    def barrier(self):
        toks = [self.merge(e, []) for e in ENGS]
        return self.merge("vector", toks)

    def sb(self, name, shape, dt=F32, side="right"):
        cm = self.nc.sbuf_tensor(name, shape, dt, side=side)
        h = cm.__enter__()
        self._alloc_idx = getattr(self, "_alloc_idx", 0) + 1
        self._allocs[name] = (cm, self._alloc_idx)
        return h

    def ps(self, name, shape, dt=F32, side="right"):
        cm = self.nc.psum_tensor(name, shape, dt, side=side)
        h = cm.__enter__()
        self._alloc_idx = getattr(self, "_alloc_idx", 0) + 1
        self._allocs[name] = (cm, self._alloc_idx)
        return h

    def free(self, *names):
        items = sorted([(self._allocs[n][1], n) for n in names], reverse=True)
        for _, n in items:
            self._allocs.pop(n)[0].__exit__(None, None, None)

    def _waits(self, eng, deps):
        wm = self.wm[eng]
        todo = []
        for key, val in deps:
            if val <= 0:
                continue
            if wm.get(key, 0) >= val:
                continue
            wm[key] = val
            todo.append((self.sem[key], val))
        return todo

    def step(self, eng, deps, emit, name=""):
        todo = self._waits(eng, deps)
        self.cnt[eng] += 1
        tok = (eng, self.cnt[eng])

        def run(e, todo=todo, emit=emit, tok=tok, eng=eng):
            for h, v in todo:
                e.wait_ge(h, v)
            insts = emit(e)
            last = insts[-1] if isinstance(insts, (list, tuple)) else insts
            last.then_inc(self.sem[eng], 1)
        self.q[eng].append(run)
        return tok

    def dma(self, eng, deps, pairs, name=""):
        i = self.dmarr[eng]
        self.dmarr[eng] = (i + 1) % 4
        key = f"dma_{eng}{i}"
        todo = self._waits(eng, deps + [(key, self.dmacnt[key])])
        n = len(pairs)
        self.dmacnt[key] += 16 * n
        tok = (key, self.dmacnt[key])

        def run(e, todo=todo, pairs=pairs, key=key):
            for h, v in todo:
                e.wait_ge(h, v)
            for dst, src in pairs:
                e.dma_start(out=dst, in_=src).then_inc(self.sem[key], 16)
        self.q[eng].append(run)
        return tok

    def coll(self, deps, kind, op, groups, ins, outs):
        todo = self._waits("gpsimd", deps + [("cc", self.cccnt)])
        self.cccnt += 1
        tok = ("cc", self.cccnt)

        def run(e, todo=todo):
            for h, v in todo:
                e.wait_ge(h, v)
            e.collective_compute(kind, op, replica_groups=groups, ins=ins,
                                 outs=outs).then_inc(self.sem["cc"], 1)
        self.q["gpsimd"].append(run)
        return tok

    def out_dma(self, deps, pairs, wait=True):
        todo = self._waits("sync", deps + [(f"dma_sync{i}", self.dmacnt[f"dma_sync{i}"])
                                           for i in range(4)])
        n = len(pairs)
        self.outcnt += 16 * n
        tok = ("dma_out", self.outcnt)
        final = self.outcnt

        def run(e, todo=todo, pairs=pairs, final=final, wait=wait):
            for h, v in todo:
                e.wait_ge(h, v)
            for dst, src in pairs:
                e.dma_start(out=dst, in_=src).then_inc(self.sem["dma_out"], 16)
            if wait:
                e.wait_ge(self.sem["dma_out"], final)
        self.q["sync"].append(run)
        return tok

    def out_wait(self):
        final = self.outcnt

        def run(e, final=final):
            e.wait_ge(self.sem["dma_out"], final)
        self.q["sync"].append(run)

    def finalize(self, block):
        for e in ENGS:
            fns = self.q[e]
            if not fns:
                continue

            def run_all(eng, fns=fns):
                for f in fns:
                    f(eng)
            getattr(block, e)(run_all)


def build_nc(upto=99):
    nc = bass.Bass("TRN2", target_bir_lowering=False, dynamic_dma_scratch_size=16384)
    bld = Builder(nc)
    p = bld

    def din(name, shape, dt=F32):
        return nc.dram_tensor(name, shape, dt, kind="ExternalInput").ap()

    x_in = din("x_in", [TLOC, D])
    WqT, WkT = din("WqT", [D, D]), din("WkT", [D, D])
    WvT, WgT = din("WvT", [D, 2 * D]), din("WgT", [D, 2 * D])
    WoT = din("WoT", [2 * D, D])
    gnw_b = din("gnw_b", [128, RDV])
    f_gamma_b, f_beta_b = din("f_gamma_b", [128, D]), din("f_beta_b", [128, D])
    mu2d = din("mu2d", [D, ES])
    scale_b = din("scale_b", [128, 1])
    w1f = din("w1f", [NE * D, FH])
    b1_in = din("b1T", [NE, FH])
    w2f = din("w2f", [NE * FH, D])
    b2_in = din("b2", [NE, D])
    a_gamma_b, a_beta_b = din("a_gamma_b", [128, D]), din("a_beta_b", [128, D])
    WqkvT = din("WqkvT", [D, 3 * D])
    WoutT = din("WoutT", [D, D])
    b_out_in = din("b_out", [1, D])
    mWg_in = din("mWg", [D, ME])
    wg_bc_in = din("wg_bc", [ME * 128, D])
    mWe_in = din("mWe", [D, D])
    m_be_in = din("m_be", [1, D])
    rope_cos, rope_sin = din("rope_cos", [128, TLOC]), din("rope_sin", [128, TLOC])
    rope_cos_q, rope_sin_q = din("rope_cos_q", [128, TLOC]), din("rope_sin_q", [128, TLOC])
    ret_maskT = din("ret_maskT", [RH, RC, RC])
    ret_cdec = din("ret_cdec", [RH, 128, RC])
    ret_kdec = din("ret_kdec", [RH, RC, 1])
    ret_flag = din("ret_flag", [128, 1])
    ident_in = din("ident", [128, 128], BF16)
    identf_in = din("ident_f32", [128, 128])
    ones_f32_in = din("ones_f32", [128, 128])
    ones_bf16_in = din("ones_bf16", [128, 128], BF16)
    m_rank_in = din("m_rank", [128, ME])
    m_prefix_in = din("m_prefix", [NCORES, 1])

    x_out = nc.dram_tensor("x_out", [TLOC, D], F32, kind="ExternalOutput").ap()

    ffn_snd = nc.dram_tensor("ffn_snd", [ES, D + 8], F32)
    ffn_rcv = nc.dram_tensor("ffn_rcv", [2 * ES, D + 8], F32)
    st_snd = nc.dram_tensor("st_snd", [RDK, RDV], F32)
    st_rcv = nc.dram_tensor("st_rcv", [2 * RDK, RDV], F32)
    st_snd1 = nc.dram_tensor("st_snd1", [RDK, RDV], F32)
    st_rcv1 = nc.dram_tensor("st_rcv1", [2 * RDK, RDV], F32)
    cnt_snd = nc.dram_tensor("cnt_snd", [1, ME], F32)
    cnt_rcv = nc.dram_tensor("cnt_rcv", [NCORES, ME], F32)
    VW = 66
    k_snd = [nc.dram_tensor(f"k_snd{i}", [512, TLOC], BF16) for i in range(2)]
    k_rcv = [nc.dram_tensor(f"k_rcv{i}", [1024, TLOC], BF16) for i in range(2)]
    v_snd = [nc.dram_tensor(f"v_snd{i}", [TLOC, 8 * VW], BF16) for i in range(2)]
    v_rcv = [nc.dram_tensor(f"v_rcv{i}", [2 * TLOC, 8 * VW], BF16) for i in range(2)]
    kn_snd = [nc.dram_tensor(f"kn_snd{i}", [1, 8], F32) for i in range(2)]
    kn_rcv = [nc.dram_tensor(f"kn_rcv{i}", [2, 8], F32) for i in range(2)]

    PAIRS = [[0, 1], [2, 3], [4, 5], [6, 7]]
    ALL8 = [list(range(NCORES))]

    x = p.sb("x", [128, NT * D], side="left")
    ident = p.sb("ident_s", [128, 128], BF16, side="left")
    identf = p.sb("identf_s", [128, 128], side="left")
    ones16 = p.sb("ones16_s", [128, 128], BF16, side="left")
    stats = p.sb("stats", [128, 96], side="left")
    bnb = p.sb("bnb", [128, 24], side="left")
    sq2 = p.sb("sq2", [128, D], side="left")
    xnT = p.sb("xnT", [128, 8 * TLOC], BF16, side="left")
    xn = p.sb("xn", [128, NT * D], side="left")

    tok_x = p.dma("sync", [], [(x[:, i * D:(i + 1) * D], x_in[i * 128:(i + 1) * 128, :])
                               for i in range(NT)])
    tok_const = p.dma("sync", [], [(ident[:], ident_in), (identf[:], identf_in),
                                   (ones16[:], ones_bf16_in)])

    def ln_to(dst, src, dep, gamma=None, beta=None, dst16=None, name="ln",
              stats_only=False):
        dep = list(dep) if isinstance(dep, list) else [dep]
        stok = []
        for i in range(NT):
            a = src[:, i * D:(i + 1) * D]
            deps0 = dep + [p.guard("stats")]
            if i >= 2:
                deps0.append(stok[i - 2])
            t0 = p.step("vector", deps0, lambda v, a=a, i=i: [
                v.bn_stats(bnb[:, 12 * (i % 2):12 * (i % 2) + 6], a[:, 0:512]),
                v.bn_stats(bnb[:, 12 * (i % 2) + 6:12 * (i % 2) + 12], a[:, 512:1024]),
            ][-1])
            t1 = p.step("vector", [t0], lambda v, i=i: v.bn_aggr(
                stats[:, 2 * i:2 * i + 2], bnb[:, 12 * (i % 2):12 * (i % 2) + 12]))
            stok.append(t1)
        t2 = p.step("vector", stok, lambda v: [
            v.tensor_scalar_add(stats[:, 32 + i:32 + i + 1],
                                stats[:, 2 * i + 1:2 * i + 2], LN_EPS)
            for i in range(NT)][-1])
        t3 = p.step("scalar", [t2], lambda s: s.activation(
            stats[:, 32:32 + NT], stats[:, 32:32 + NT], AF.Sqrt))
        t4 = p.step("vector", [t3], lambda v: v.reciprocal(
            stats[:, 32:32 + NT], stats[:, 32:32 + NT]))
        if stats_only:
            p.set_guard("stats", t4)
            return t4
        wa = [p.step("vector", [t4] + dep, lambda v, i=i: v.tensor_scalar(
            dst[:, i * D:(i + 1) * D], src[:, i * D:(i + 1) * D],
            stats[:, 2 * i:2 * i + 1], stats[:, 32 + i:32 + i + 1],
            ALU.subtract, ALU.mult)) for i in range(NT)]
        if gamma is not None:
            wa = [p.step("vector", [wa[i]], lambda v, i=i: v.tensor_mul(
                dst[:, i * D:(i + 1) * D], dst[:, i * D:(i + 1) * D], gamma[:]))
                for i in range(NT)]
            wa = [p.step("vector", [wa[i]], lambda v, i=i: v.tensor_add(
                dst[:, i * D:(i + 1) * D], dst[:, i * D:(i + 1) * D], beta[:]))
                for i in range(NT)]
        if dst16 is not None:
            wa = [p.step("vector", [wa[i]], lambda v, i=i: v.tensor_copy(
                dst16[:, i * D:(i + 1) * D], dst[:, i * D:(i + 1) * D]))
                for i in range(NT)]
        fin = p.step("vector", wa, lambda v: v.engine_nop())
        p.set_guard("stats", fin)
        return fin

    pTa = p.ps("pTa", [128, 128], BF16)
    pTb = p.ps("pTb", [128, 128], BF16)
    pTf_cell = [None]

    def transposes(blocks, dep, extra_dep=None, scale_col=None):
        ev = [p.guard("pT"), p.guard("pT")]
        last = None
        deps0 = (list(dep) if isinstance(dep, list) else [dep])
        if extra_dep is not None:
            deps0.append(extra_dep)
        for b, (srcap, dstap) in enumerate(blocks):
            fp = (srcap.dtype == F32)
            pts = (pTf_cell[0], pTf_cell[0]) if fp else (pTa, pTb)
            pr, pc = srcap.shape[0], srcap.shape[1]
            pt = pts[b % 2][0:pc, 0:pr]
            tdeps = deps0 + [ev[b % 2]] + ([ev[1 - b % 2]] if fp else [])
            idm = identf if fp else ident
            tt = p.step("tensor", tdeps, lambda t, s=srcap, pt=pt, pr=pr, idm=idm:
                        t.transpose(pt, s, idm[0:pr, 0:pr]))

            if scale_col is None and b % 2 == 1:
                ev[b % 2] = p.step("scalar", [tt], lambda s, pt=pt, d=dstap:
                                   s.activation(d, pt, AF.Copy))
            else:
                def em(v, pt=pt, d=dstap, b=b):
                    if scale_col is None:
                        return v.tensor_copy(d, pt)
                    return v.tensor_scalar(d, pt, scale_col, None, ALU.mult)
                ev[b % 2] = p.step("vector", [tt], em)
            last = ev[b % 2]
        fin = p.step("vector", [e for e in ev if e], lambda v: v.engine_nop()) \
            if len(blocks) > 1 else last
        p.set_guard("pT", fin)
        return fin

    def _finish(tok, src):
        if tok is not None:
            p.out_dma([tok], [(x_out[i * 128:(i + 1) * 128, :],
                               src[:, i * D:(i + 1) * D]) for i in range(NT)])
        else:
            p.out_wait()
        blk_cm = nc.Block()
        block = blk_cm.__enter__()
        bld.finalize(block)
        blk_cm.__exit__(None, None, None)
        return nc

    tok_xn = ln_to(xn, x, tok_x)

    ew2 = p.sb("ew2", [128, NT * ES], BF16)
    rsum2 = p.sb("rsum2", [128, NT])
    slp = [p.sb("slp0", [128, D + 8]), p.sb("slp1", [128, D + 8])]
    fg = p.sb("fg", [128, D]); fb = p.sb("fb", [128, D]); sclt = p.sb("sclt", [128, 8])
    tok_fgb = p.dma("sync", [], [(fg[:], f_gamma_b), (fb[:], f_beta_b),
                                 (sclt[:, 0:1], scale_b)])
    tok_sabs = p.step("scalar", [tok_fgb], lambda s: s.activation(
        sclt[:, 0:1], sclt[:, 0:1], AF.Abs))
    nscl = p.sb("nscl", [128, 1])
    tok_nscl = p.step("vector", [tok_sabs], lambda v: v.tensor_scalar(
        nscl[:], sclt[:, 0:1], -1.02, -1e-3, ALU.mult, ALU.add))

    mu_u = p.sb("mu_u", [128, 8 * ES], BF16)
    mu_s = p.sb("mu_s", [128, 4 * ES])
    musq16 = p.sb("musq16", [128, 4 * ES], BF16)
    p_mr = p.ps("p_mr", [1, ES])
    tok_half = []
    tok_mu_parts = []
    for half in range(2):
        tok_mu = p.dma("sync", [tok_half[-1]] if tok_half else [], [
            (mu_s[:, j * ES:(j + 1) * ES], mu2d[(half * 4 + j) * 128:(half * 4 + j) * 128 + 128, :])
            for j in range(4)])
        tok_msq = p.step("scalar", [tok_mu] + ([tok_half[-1]] if tok_half else []),
                         lambda s: [
            s.activation(musq16[:, j * ES:(j + 1) * ES], mu_s[:, j * ES:(j + 1) * ES],
                         AF.Square) for j in range(4)][-1])
        tok_mr = p.step("tensor", [tok_msq, tok_const], lambda t, half=half: [
            t.matmul(p_mr[:], ones16[:, 0:1], musq16[:, j * ES:(j + 1) * ES],
                     start=(half == 0 and j == 0), stop=(half == 1 and j == 3))
            for j in range(4)][-1])
        tok_half.append(p.step("vector", [tok_mr], lambda v: v.engine_nop()))
        tok_mu_parts.append((tok_mu, tok_mr))
    mu_rn = p.sb("mu_rn", [1, ES])
    mu_rn16 = p.sb("mu_rn16", [1, ES], BF16)
    mu_rb = p.sb("mu_rb", [128, ES])
    p_mb = p.ps("p_mb", [128, ES])
    tok_mrn = p.step("vector", [tok_half[-1]], lambda v: v.tensor_copy(mu_rn[:], p_mr[:]))
    tok_mrn = p.step("scalar", [tok_mrn], lambda s: s.activation(mu_rn[:], mu_rn[:], AF.Sqrt))
    tok_mrn = p.step("vector", [tok_mrn], lambda v: v.tensor_scalar_max(mu_rn[:], mu_rn[:], NORM_EPS))
    tok_mrn = p.step("vector", [tok_mrn], lambda v: v.reciprocal(mu_rn[:], mu_rn[:]))
    tok_mrn16 = p.step("vector", [tok_mrn], lambda v: v.tensor_copy(mu_rn16[:], mu_rn[:]))
    tok_mb = p.step("tensor", [tok_mrn16], lambda t: t.matmul(
        p_mb[:], ones16[0:1, :], mu_rn16[:], start=True, stop=True))
    tok_mrb = p.step("vector", [tok_mb], lambda v: v.tensor_copy(mu_rb[:], p_mb[:]))
    tok_muh = tok_mrb
    for half in range(2):
        tok_mu2 = p.dma("sync", [tok_muh], [
            (mu_s[:, j * ES:(j + 1) * ES], mu2d[(half * 4 + j) * 128:(half * 4 + j) * 128 + 128, :])
            for j in range(4)])
        tok_muh = p.step("vector", [tok_mu2, tok_mrb], lambda v, half=half: [
            v.tensor_mul(mu_u[:, (half * 4 + j) * ES:(half * 4 + j) * ES + ES],
                         mu_s[:, j * ES:(j + 1) * ES], mu_rb[:]) for j in range(4)][-1])
    tok_muu = tok_muh
    p.free("mu_s", "musq16", "p_mr", "mu_rn", "mu_rn16", "mu_rb", "p_mb")

    xn16t = p.sb("xn16t", [128, D], BF16)
    xnf_t = p.sb("xnf_t", [128, D])
    u16 = p.sb("u16", [128, D], BF16)
    uT = p.sb("uT", [128, D], BF16)
    xnf16 = p.sb("xnf16", [128, D], BF16)
    ewt = p.sb("ewt", [128, 2 * ES], BF16)
    mxc = p.sb("mxc", [128, 4])
    p_sl = [p.ps("p_sl0", [128, 1024]), p.ps("p_sl1", [128, 1024])]
    p_gs = p.ps("p_gs", [128, 2])
    p_lg0 = p.ps("p_lg0", [128, ES])
    p_lg = [p_lg0, p_lg0]

    ln2_stats = ln_to(None, xn, [tok_xn], stats_only=True)

    xn16w = [None, None]; xnTw = [None, None]
    xfw = [None, None]; u16w = [None, None]; uTw = [None, None]
    lgw = [None, None]; eww = [None, None]; xf16w = [None, None]
    slmm_last = [None]
    ew2_toks = []
    for i in range(NT):
        pa = i % 2
        d_a = [tok_xn] + [t_ for t_ in (xnTw[0], xnTw[1]) if t_]
        xn16w[pa] = p.step("vector", d_a, lambda v, i=i: v.tensor_copy(
            xn16t[:, 0:D], xn[:, i * D:(i + 1) * D]))
        blksT = [(xn16t[:, dc * 128: dc * 128 + 128],
                  xnT[:, dc * TLOC + i * 128: dc * TLOC + i * 128 + 128])
                 for dc in range(8)]
        xnTw[pa] = transposes(blksT, [xn16w[pa], tok_const])
        d_b = [ln2_stats, tok_fgb] + ([xf16w[1 - pa]] if xf16w[1 - pa] else []) + \
              ([u16w[1 - pa]] if u16w[1 - pa] else [])
        t_f = p.step("vector", d_b, lambda v, i=i: v.tensor_scalar(
            xnf_t[:, 0:D], xn[:, i * D:(i + 1) * D],
            stats[:, 2 * i:2 * i + 1], stats[:, 32 + i:32 + i + 1],
            ALU.subtract, ALU.mult))
        t_f = p.step("vector", [t_f], lambda v: v.tensor_mul(
            xnf_t[:, 0:D], xnf_t[:, 0:D], fg[:]))
        t_f = p.step("vector", [t_f], lambda v: v.tensor_add(
            xnf_t[:, 0:D], xnf_t[:, 0:D], fb[:]))
        t_n = p.step("scalar", [t_f], lambda s, i=i: s.activation(
            sq2[:, 0:D], xnf_t[:, 0:D], AF.Square,
            accum_out=stats[:, 48 + i:48 + i + 1]))
        t_n = p.step("scalar", [t_n], lambda s, i=i: s.activation(
            stats[:, 48 + i:48 + i + 1], stats[:, 48 + i:48 + i + 1], AF.Sqrt))
        t_n = p.step("vector", [t_n], lambda v, i=i: v.tensor_scalar_max(
            stats[:, 48 + i:48 + i + 1], stats[:, 48 + i:48 + i + 1], NORM_EPS))
        t_n = p.step("vector", [t_n], lambda v, i=i: v.reciprocal(
            stats[:, 48 + i:48 + i + 1], stats[:, 48 + i:48 + i + 1]))
        t_n = p.step("vector", [t_n, tok_sabs], lambda v, i=i: v.tensor_scalar(
            stats[:, 48 + i:48 + i + 1], stats[:, 48 + i:48 + i + 1],
            sclt[:, 0:1], None, ALU.mult))
        d_u = [t_n] + [t_ for t_ in (uTw[0], uTw[1]) if t_]
        u16w[pa] = p.step("vector", d_u, lambda v, i=i: v.tensor_scalar(
            u16[:, 0:D], xnf_t[:, 0:D],
            stats[:, 48 + i:48 + i + 1], None, ALU.mult))
        blksU = [(u16[:, dc * 128: dc * 128 + 128],
                  uT[:, dc * 128: dc * 128 + 128])
                 for dc in range(8)]
        d_e = [u16w[pa]] + [t_ for t_ in (lgw[0], lgw[1]) if t_]
        uTw[pa] = transposes(blksU, d_e)
        d_fm = [uTw[pa], tok_muu] + [t for t in eww if t]
        lgw[pa] = p.step("tensor", d_fm, lambda t, pa=pa: [
            t.matmul(p_lg[pa][:], uT[:, dc * 128: dc * 128 + 128],
                     mu_u[:, dc * ES:(dc + 1) * ES], start=(dc == 0), stop=(dc == 7))
            for dc in range(8)][-1])
        d_g = [lgw[pa], tok_nscl] + ([slmm_last[0]] if slmm_last[0] and i >= 2 else [])
        eww[pa] = p.step("scalar", d_g, lambda s, i=i, pa=pa: s.activation(
            ewt[:, pa * ES:(pa + 1) * ES], p_lg[pa][:], AF.Exp, bias=nscl[:, 0:1],
            accum_out=stats[:, 64 + i:64 + i + 1]))
        t_m = p.step("vector", [eww[pa]], lambda v, i=i, pa=pa: v.reduce_max(
            mxc[:, pa:pa + 1], ewt[:, pa * ES:(pa + 1) * ES], axis=AX.X))
        t_m2 = p.step("vector", [t_m], lambda v, i=i, pa=pa: v.reciprocal(
            mxc[:, 2 + pa: 3 + pa], mxc[:, pa:pa + 1]))
        tok_e2 = p.step("vector", [t_m2], lambda v, i=i, pa=pa: v.tensor_scalar(
            ew2[:, i * ES:(i + 1) * ES], ewt[:, pa * ES:(pa + 1) * ES],
            mxc[:, 2 + pa:3 + pa], None, ALU.mult))
        ew2_toks.append(tok_e2)
        t_rs = p.step("vector", [eww[pa]], lambda v, i=i: v.reciprocal(
            stats[:, 80 + i:80 + i + 1], stats[:, 64 + i:64 + i + 1]))
        t_rs = p.step("vector", [t_rs, t_m], lambda v, i=i, pa=pa: v.tensor_scalar(
            rsum2[:, i:i + 1], mxc[:, pa:pa + 1], stats[:, 80 + i:80 + i + 1],
            None, ALU.mult))
        ew2_toks.append(t_rs)
        d_j = [t_f] + ([slmm_last[0]] if slmm_last[0] and i >= 2 else [])
        xf16w[pa] = p.step("vector", d_j + [t_ for t_ in (slmm_last[0],) if t_],
                           lambda v: v.tensor_copy(xnf16[:, 0:D], xnf_t[:, 0:D]))
        slmm_last[0] = p.step("tensor", [xf16w[pa], eww[pa], tok_const],
                              lambda t, i=i, pa=pa: ([
            t.matmul(p_sl[ec][:, nn * 512:(nn + 1) * 512],
                     ewt[:, pa * ES + ec * 128: pa * ES + ec * 128 + 128],
                     xnf16[:, nn * 512: nn * 512 + 512],
                     start=(i == 0), stop=(i == NT - 1))
            for ec in range(2) for nn in range(2)] + [
            t.matmul(p_gs[:, ec:ec + 1],
                     ewt[:, pa * ES + ec * 128: pa * ES + ec * 128 + 128],
                     ones16[:, 0:1], start=(i == 0 and ec == 0),
                     stop=(i == NT - 1 and ec == 1))
            for ec in range(2)])[-1])
    tok_ew2 = p.step("vector", ew2_toks[-4:], lambda v: v.engine_nop())
    tok_slpz = p.step("vector", [], lambda v: [
        v.memset(slp[ec][:, D + 1:D + 8], 0.0) for ec in range(2)][-1])
    sl_ev = [p.step("vector", [slmm_last[0], tok_slpz], lambda v, ec=ec: [
        v.tensor_copy(slp[ec][:, 0:512], p_sl[ec][:, 0:512]),
        v.tensor_copy(slp[ec][:, 512:1024], p_sl[ec][:, 512:1024]),
        v.tensor_copy(slp[ec][:, D:D + 1], p_gs[:, ec:ec + 1]),
    ][-1]) for ec in range(2)]
    tok_xnT = p.step("vector", [xnTw[0], xnTw[1]], lambda v: v.engine_nop())

    tok_ffs = p.dma("gpsimd", sl_ev, [(ffn_snd.ap()[0:128, :], slp[0][:]),
                                      (ffn_snd.ap()[128:256, :], slp[1][:])])
    tok_ffc = p.coll([tok_ffs], "AllGather", ALU.bypass, PAIRS,
                     [ffn_snd.ap().opt()], [ffn_rcv.ap().opt()])
    p.free("xn16t", "xnf_t", "u16", "uT", "xnf16", "ewt", "mxc",
           "p_lg0", "mu_u", "fg", "fb", "nscl", "sclt")
    p.free("slp0", "slp1")
    p.free("p_sl0", "p_sl1", "p_gs")

    if upto < 4:
        return _finish(tok_ffc, xn)

    bar_p4 = p.barrier()
    slotT = p.sb("slotT", [128, 8 * ES], BF16)
    sli16 = [p.sb("sli16_0", [128, D], BF16), p.sb("sli16_1", [128, D], BF16)]
    sli = [p.sb("sli0", [128, D + 8]), p.sb("sli1", [128, D + 8])]
    slr = [p.sb("slr00", [128, D + 8]), p.sb("slr01", [128, D + 8]),
           p.sb("slr10", [128, D + 8]), p.sb("slr11", [128, D + 8])]
    tok_ffr = p.dma("gpsimd", [tok_ffc, tok_ew2, tok_xnT, bar_p4], [
        (slr[0][:], ffn_rcv.ap()[0:128, :]), (slr[1][:], ffn_rcv.ap()[128:256, :]),
        (slr[2][:], ffn_rcv.ap()[256:384, :]), (slr[3][:], ffn_rcv.ap()[384:512, :])])
    tok_sli = [p.step("vector", [tok_ffr], lambda v, ec=ec: v.tensor_add(
        sli[ec][:], slr[ec][:], slr[2 + ec][:])) for ec in range(2)]
    tok_gsr = [p.step("vector", [tok_sli[ec], p.guard("stats")], lambda v, ec=ec:
        v.reciprocal(stats[:, 90 + ec:91 + ec], sli[ec][:, D:D + 1])) for ec in range(2)]
    tok_sli16 = [p.step("vector", [tok_gsr[ec]], lambda v, ec=ec: v.tensor_scalar(
        sli16[ec][:], sli[ec][:, 0:D], stats[:, 90 + ec:91 + ec], None, ALU.mult))
        for ec in range(2)]
    p.free("slr00", "slr01", "slr10", "slr11", "sli0", "sli1")
    blks_s = []
    for ec in range(2):
        for dc in range(8):
            blks_s.append((sli16[ec][:, dc * 128:(dc + 1) * 128],
                           slotT[:, dc * ES + ec * 128: dc * ES + ec * 128 + 128]))
    tok_slotT = transposes(blks_s, tok_sli16[0], extra_dep=tok_sli16[1])
    p.free("sli16_0", "sli16_1")
    bar_slr = p.barrier()

    so16 = [p.sb("so16_0", [128, D], BF16), p.sb("so16_1", [128, D], BF16)]
    so_stage = [p.sb("so_st0", [16, D], BF16), p.sb("so_st1", [16, D], BF16)]
    b1s = p.sb("b1s", [128, 2 * NE])
    tok_b1 = p.dma("sync", [bar_slr], [(b1s[:, e * 2 + hc: e * 2 + hc + 1],
                                 b1_in[e:e + 1, hc * 128:(hc + 1) * 128])
                                for e in range(NE) for hc in range(2)])
    b2s16 = p.sb("b2s16", [33, D], BF16)
    w1b = [p.sb("w1b0", [128, 8 * FH], BF16), p.sb("w1b1", [128, 8 * FH], BF16)]
    w2b = [p.sb("w2b0", [128, 2 * D], BF16), p.sb("w2b1", [128, 2 * D], BF16)]
    hT = [p.sb("hT0", [128, 32], BF16), p.sb("hT1", [128, 32], BF16)]
    p_h = [p.ps("p_h0", [128, 32]), p.ps("p_h1", [128, 32])]
    p_so = [p.ps("p_so0", [16, 1024]), p.ps("p_so1", [16, 1024])]
    w1_tok = [None, None]; w2_tok = [None, None]
    hmm_tok = [None, None]; hact_tok = [None, None]
    somm_tok = [None, None]; soev_tok = [None, None]
    for e in range(NE):
        pa = e % 2
        d1 = [bar_slr] + ([hmm_tok[pa]] if hmm_tok[pa] else [])
        w1_tok[pa] = p.dma("gpsimd", d1, [
            (w1b[pa][:, dc * FH:(dc + 1) * FH],
             w1f[e * D + dc * 128: e * D + dc * 128 + 128, :]) for dc in range(8)])
        d2 = [bar_slr] + ([somm_tok[pa]] if somm_tok[pa] else [])
        w2_tok[pa] = p.dma("gpsimd", d2, [
            (w2b[pa][:, hc * D:(hc + 1) * D],
             w2f[e * FH + hc * 128: e * FH + hc * 128 + 128, :]) for hc in range(2)] + [
            (b2s16[32 * pa:32 * pa + 1, :], b2_in[e:e + 1, :])])
        hdeps = [w1_tok[pa], tok_slotT] + ([hact_tok[pa]] if hact_tok[pa] else [])
        hmm_tok[pa] = p.step("tensor", hdeps, lambda t, e=e, pa=pa: [
            t.matmul(p_h[pa][:, hc * 16:(hc + 1) * 16],
                     w1b[pa][:, dc * FH + hc * 128: dc * FH + hc * 128 + 128],
                     slotT[:, dc * ES + e * 16: dc * ES + e * 16 + 16],
                     start=(dc == 0), stop=(dc == 7))
            for hc in range(2) for dc in range(8)][-1])
        hact_deps = [hmm_tok[pa], tok_b1] + ([somm_tok[pa]] if somm_tok[pa] else [])
        hact_tok[pa] = p.step("scalar", hact_deps, lambda s, e=e, pa=pa: [
            s.activation(hT[pa][:, hc * 16:(hc + 1) * 16], p_h[pa][:, hc * 16:(hc + 1) * 16],
                         AF.Gelu, bias=b1s[:, e * 2 + hc: e * 2 + hc + 1])
            for hc in range(2)][-1])
        sdeps = [hact_tok[pa], w2_tok[pa]] + ([soev_tok[pa]] if soev_tok[pa] else [])
        somm_tok[pa] = p.step("tensor", sdeps, lambda t, e=e, pa=pa: ([
            t.matmul(p_so[pa][:, n2 * 512:(n2 + 1) * 512],
                     hT[pa][:, hc * 16:(hc + 1) * 16],
                     w2b[pa][:, hc * D + n2 * 512: hc * D + n2 * 512 + 512],
                     start=(hc == 0), stop=False)
            for n2 in range(2) for hc in range(2)] + [
            t.matmul(p_so[pa][:, n2 * 512:(n2 + 1) * 512],
                     ones16[32 * pa:32 * pa + 1, 0:16],
                     b2s16[32 * pa:32 * pa + 1, n2 * 512: n2 * 512 + 512],
                     start=False, stop=True)
            for n2 in range(2)])[-1])
        sodeps = [somm_tok[pa]] + ([soev_tok[pa]] if soev_tok[pa] else [])
        so_cp = p.step("vector", sodeps, lambda v, pa=pa:
            v.tensor_copy(so_stage[pa][:], p_so[pa][:, 0:1024]))
        soev_tok[pa] = p.dma("gpsimd", [so_cp], [
            (so16[e // 8][(e % 8) * 16:(e % 8) * 16 + 16, :], so_stage[pa][:])])
    tok_so = p.step("vector", [t for t in soev_tok if t], lambda v: v.engine_nop())
    p.free("w1b0", "w1b1", "w2b0", "w2b1", "hT0", "hT1", "p_h0", "p_h1",
           "p_so0", "p_so1", "b2s16", "b1s", "so_st0", "so_st1")

    ew2T = [p.sb("ew2T0", [128, TLOC], BF16), p.sb("ew2T1", [128, TLOC], BF16)]
    p_cb = [p.ps("p_cb0", [128, 1024]), p.ps("p_cb1", [128, 1024])]
    blks_e = []
    for ec in range(2):
        for i in range(NT):
            blks_e.append((ew2[:, i * ES + ec * 128: i * ES + ec * 128 + 128],
                           ew2T[ec][:, i * 128:(i + 1) * 128]))
    tok_ew2T = transposes(blks_e, [tok_ew2, bar_p4, tok_so])
    cb_mm = [None, None]; cb_ev = [None, None]
    for i in range(NT):
        pa = i % 2
        deps = [tok_ew2T, tok_so] + ([cb_ev[pa]] if cb_ev[pa] else [])
        cb_mm[pa] = p.step("tensor", deps, lambda t, i=i, pa=pa: [
            t.matmul(p_cb[pa][:, n2 * 512:(n2 + 1) * 512],
                     ew2T[ec][:, i * 128:(i + 1) * 128],
                     so16[ec][:, n2 * 512: n2 * 512 + 512],
                     start=(ec == 0), stop=(ec == 1))
            for n2 in range(2) for ec in range(2)][-1])
        cb_ev[pa] = p.step("vector", [cb_mm[pa], tok_ew2], lambda v, i=i, pa=pa: [
            v.scalar_tensor_tensor(x[:, i * D + n2 * 512: i * D + n2 * 512 + 512],
                                   p_cb[pa][:, n2 * 512:(n2 + 1) * 512],
                                   rsum2[:, i:i + 1],
                                   x[:, i * D + n2 * 512: i * D + n2 * 512 + 512],
                                   ALU.mult, ALU.add)
            for n2 in range(2)][-1])
    tok_ffn0 = p.step("vector", [t for t in cb_ev if t], lambda v: v.engine_nop())
    p.free("p_cb0", "p_cb1", "ew2T0", "ew2T1", "so16_0", "so16_1", "slotT",
           "ew2", "rsum2")
    p.free("xn")

    if upto < 5:
        return _finish(tok_ffn0, x)

    bar0 = p.barrier()
    p.set_guard("pT", bar0)

    rcq = p.sb("rcq", [128, TLOC], BF16); rsq = p.sb("rsq", [128, TLOC], BF16)
    rck = p.sb("rck", [128, TLOC], BF16); rsk = p.sb("rsk", [128, TLOC], BF16)
    maskT_s = p.sb("maskT_s", [128, RH * RC])
    cdec_s = p.sb("cdec_s", [128, RH * RC])
    kdec_s = p.sb("kdec_s", [128, RH])
    flag_s = p.sb("flag_s", [128, 1])
    gnw_s = p.sb("gnw_s", [128, RDV])
    tstage = p.sb("tstage", [128, TLOC])
    tok_tb = bar0
    for tin, tout in ((rope_cos_q, rcq), (rope_sin_q, rsq), (rope_cos, rck), (rope_sin, rsk)):
        tok_ld = p.dma("sync", [tok_tb, bar0], [(tstage[:], tin)])
        tok_tb = p.step("vector", [tok_ld], lambda v, tout=tout: v.tensor_copy(tout[:], tstage[:]))
    tok_rc = p.dma("sync", [bar0], [
        (gnw_s[:], gnw_b), (flag_s[:], ret_flag)] + [
        (maskT_s[:, h * RC:(h + 1) * RC], ret_maskT[h]) for h in range(RH)] + [
        (cdec_s[:, h * RC:(h + 1) * RC], ret_cdec[h]) for h in range(RH)] + [
        (kdec_s[:, h:h + 1], ret_kdec[h]) for h in range(RH)])
    tok_rc = p.step("vector", [tok_rc, tok_tb], lambda v: v.engine_nop())
    p.free("tstage")

    NSC = 4
    SCH = TLOC // NSC
    CPS = SCH // RC
    tok_l0 = tok_ffn0

    p_pj = p.ps("p_pj", [128, 1024])
    p_vps = p_pj
    rtmp = p.sb("rtmp", [128, 4 * SCH])
    rt2 = p.sb("rt2", [128, 4 * SCH])
    qd16 = p.sb("qd16", [128, 2 * 256], BF16)
    smid16 = p.sb("smid16", [128, 2 * 512], BF16)
    q16 = p.sb("q16", [128, 2 * SCH], BF16)

    def proj_rope1(dst16, wslc, tcol0, cosT, sinT, deps, slot=0):
        deps = list(deps)
        evg = p.guard("proj_ev")
        rg = p.guard(f"projS{slot}")
        off = slot * 2 * SCH
        ev = []
        for oc in range(2):
            mm = p.step("tensor", deps + [evg], lambda t, oc=oc, wslc=wslc,
                        tcol0=tcol0: [
                t.matmul(p_pj[:, oc * 512: oc * 512 + 512],
                         wslc[:, dc * 256 + oc * 128: dc * 256 + oc * 128 + 128],
                         xnT[:, dc * TLOC + tcol0: dc * TLOC + tcol0 + SCH],
                         start=(dc == 0), stop=(dc == 7)) for dc in range(8)][-1])
            ev.append(p.step("vector", [mm, rg] + deps, lambda v, oc=oc, off=off:
                v.tensor_copy(rtmp[:, off + oc * SCH: off + (oc + 1) * SCH],
                              p_pj[:, oc * 512: oc * 512 + 512])))
        tok_raw = p.step("vector", ev, lambda v: v.engine_nop())
        p.set_guard("proj_ev", tok_raw)
        cs = cosT[:, tcol0:tcol0 + SCH]
        sn = sinT[:, tcol0:tcol0 + SCH]
        a0 = rtmp[:, off:off + SCH]; a1 = rtmp[:, off + SCH:off + 2 * SCH]
        b0 = rt2[:, off:off + SCH]; b1 = rt2[:, off + SCH:off + 2 * SCH]
        tA = p.step("vector", [tok_raw], lambda v, a0=a0, b0=b0, cs=cs:
                    v.tensor_mul(b0, a0, cs))
        tB = p.step("gpsimd", [tok_raw], lambda g_, a1=a1, b1=b1, sn=sn:
                    g_.tensor_mul(b1, a1, sn))
        r0 = p.step("vector", [tA, tB] + deps, lambda v, b0=b0, b1=b1,
                    dst16=dst16: v.tensor_sub(dst16[:, 0:SCH], b0, b1))
        tC = p.step("vector", [r0], lambda v, a1=a1, b0=b0, cs=cs:
                    v.tensor_mul(b0, a1, cs))
        tD = p.step("gpsimd", [r0], lambda g_, a0=a0, b1=b1, sn=sn:
                    g_.tensor_mul(b1, a0, sn))
        r1 = p.step("vector", [tC, tD] + deps, lambda v, b0=b0, b1=b1,
                    dst16=dst16: v.tensor_add(dst16[:, SCH:2 * SCH], b0, b1))
        p.set_guard(f"projS{slot}", r1)
        return r1

    st_bounce = [st_snd, st_snd1]
    st_rbounce = [st_rcv, st_rcv1]
    wq_handles = {}
    o_buf = {}
    tok_h_done = {}
    for h in range(RH):
        pair_i = h % 2
        if True:
            wqb = p.sb(f"wqb{h}", [128, 8 * 256], BF16)
            wq_handles[h] = wqb
            o_h = p.sb(f"o_h{h}", [128, RNC * 512])
            o_buf[h] = o_h
            wkb = p.sb(f"wkb{h}", [128, 8 * 256], BF16)
            wvb = p.sb(f"wvb{h}", [128, 8 * 512], BF16)
            k16 = p.sb(f"k16_{h}", [128, 2 * SCH], BF16)
            v16 = p.sb(f"v16_{h}", [128, 4 * 512], BF16)
            ktm = p.sb(f"ktm_{h}", [128, 2 * 256], BF16)
            S_f = p.sb(f"S_f{h}", [128, 2 * 512])
            S16 = p.sb(f"S16_{h}", [128, 2 * 512], BF16)
            at16 = p.sb(f"at16_{h}", [128, 2 * 128], BF16)
            p_at = p.ps(f"p_at{h}", [128, 128])
            p_o = p.ps(f"p_o{h}", [128, 512])
            p_ds = p.ps(f"p_ds{h}", [128, 1024])
            tok_wq = p.dma("gpsimd", [bar0, p.guard("wq")], [
                (wqb[:, dc * 256:(dc + 1) * 256],
                 WqT[dc * 128:(dc + 1) * 128, h * 256: h * 256 + 256]) for dc in range(8)])
            tok_wk = p.dma("gpsimd", [bar0, p.guard("wq")], [
                (wkb[:, dc * 256:(dc + 1) * 256],
                 WkT[dc * 128:(dc + 1) * 128, h * 256: h * 256 + 256]) for dc in range(8)])
            tok_wv = p.dma("gpsimd", [bar0, p.guard("wq")], [
                (wvb[:, dc * 512:(dc + 1) * 512],
                 WvT[dc * 128:(dc + 1) * 128, h * 512: h * 512 + 512]) for dc in range(8)])
            tok_z = p.step("vector", [tok_l0, tok_rc, bar0, p.guard("xadd")],
                           lambda v, S_f=S_f: v.memset(S_f[:], 0.0))
            tok_z16 = p.step("scalar", [tok_z], lambda s, S16=S16: s.memzero(S16[:]))

            s16_last = [tok_z16]
            supd_last = [tok_z]
            oev_last = [tok_z]
            atev_last = [tok_z]
            ds_last = [tok_z16]
            qdw = [None, None]
            atw = [None, None]
            scT = [tok_z16]; scV = [tok_z]
            oev_toks = []
            for sc in range(NSC):
                tcol0 = sc * SCH
                pdeps = [tok_rc, tok_xnT, scT[0], scV[0], p.guard("xadd")]
                tq = proj_rope1(q16, wqb, tcol0, rcq, rsq,
                                pdeps + [tok_wq, p.guard("q16")], slot=0)
                tk = proj_rope1(k16, wkb, tcol0, rck, rsk,
                                pdeps + [tok_wk], slot=1)
                vev = [None, None]
                for tt_ in range(4):
                    pa_ = tt_ % 2
                    vdeps = pdeps + [tok_wv, p.guard("proj_ev")] + \
                            ([vev[pa_]] if vev[pa_] else [])
                    vmm = p.step("tensor", vdeps, lambda t, tt_=tt_, pa_=pa_,
                                 wvb=wvb, tcol0=tcol0: [
                        t.matmul(p_vps[:, pa_ * 512: pa_ * 512 + 512],
                                 xnT[:, dc * TLOC + tcol0 + tt_ * 128:
                                     dc * TLOC + tcol0 + tt_ * 128 + 128],
                                 wvb[:, dc * 512:(dc + 1) * 512],
                                 start=(dc == 0), stop=(dc == 7)) for dc in range(8)][-1])
                    vev[pa_] = p.step("vector", [vmm] + pdeps, lambda v, tt_=tt_, pa_=pa_,
                                      v16=v16: v.tensor_copy(
                        v16[:, tt_ * 512:(tt_ + 1) * 512],
                        p_vps[:, pa_ * 512: pa_ * 512 + 512]))
                tv = p.step("vector", [e for e in vev if e], lambda v: v.engine_nop())
                p.set_guard("proj_ev", tv)
                for cc in range(CPS):
                    c = sc * CPS + cc
                    pa = c % 2
                    kblks = [(k16[:, blk * SCH + cc * 128: blk * SCH + cc * 128 + 128],
                              ktm[:, pa * 256 + blk * 128: pa * 256 + blk * 128 + 128])
                             for blk in range(2)]
                    tok_ktm = transposes(kblks, [tk, ds_last[0]],
                                         scale_col=kdec_s[:, h:h + 1])
                    qdeps = [tq, tok_rc] + ([qdw[pa]] if qdw[pa] else [])
                    tok_qd = p.step("vector", qdeps, lambda v, pa=pa, cc=cc, h=h: [
                        v.tensor_mul(qd16[:, pa * 256 + blk * 128: pa * 256 + blk * 128 + 128],
                                     q16[:, blk * SCH + cc * 128: blk * SCH + cc * 128 + 128],
                                     cdec_s[:, h * RC:(h + 1) * RC]) for blk in range(2)][-1])
                    amm = p.step("tensor", [tk, tq, atev_last[0]],
                                 lambda t, cc=cc, p_at=p_at, k16=k16: [
                        t.matmul(p_at[:],
                                 k16[:, blk * SCH + cc * 128: blk * SCH + cc * 128 + 128],
                                 q16[:, blk * SCH + cc * 128: blk * SCH + cc * 128 + 128],
                                 start=(blk == 0), stop=(blk == 1)) for blk in range(2)][-1])
                    aev = p.step("vector", [amm, tok_rc] + ([atw[pa]] if atw[pa] else []),
                                 lambda v, pa=pa, h=h, p_at=p_at, at16=at16: v.tensor_mul(
                        at16[:, pa * 128:(pa + 1) * 128], p_at[:],
                        maskT_s[:, h * RC:(h + 1) * RC]))
                    atev_last[0] = aev
                    omm = p.step("tensor", [aev, tok_qd, tv, oev_last[0]] + s16_last,
                                 lambda t, pa=pa, cc=cc, p_o=p_o, at16=at16, S16=S16, v16=v16: ([
                        t.matmul(p_o[:], at16[:, pa * 128:(pa + 1) * 128],
                                 v16[:, cc * 512:(cc + 1) * 512], start=True, stop=False)] + [
                        t.matmul(p_o[:], qd16[:, pa * 256 + blk * 128: pa * 256 + blk * 128 + 128],
                                 S16[:, blk * 512:(blk + 1) * 512],
                                 start=False, stop=(blk == 1)) for blk in range(2)])[-1])
                    qdw[pa] = omm
                    atw[pa] = omm
                    oev = p.step("vector", [omm], lambda v, c=c, p_o=p_o, o_h=o_h:
                        v.tensor_copy(o_h[:, c * 512:(c + 1) * 512], p_o[:]))
                    oev_last[0] = oev
                    oev_toks.append(oev)
                    dmm = p.step("tensor", [tok_ktm, tv] + supd_last,
                                 lambda t, pa=pa, cc=cc, p_ds=p_ds, ktm=ktm, v16=v16: [
                        t.matmul(p_ds[:, blk * 512:(blk + 1) * 512],
                                 ktm[:, pa * 256 + blk * 128: pa * 256 + blk * 128 + 128],
                                 v16[:, cc * 512:(cc + 1) * 512],
                                 start=True, stop=True) for blk in range(2)][-1])
                    ds_last[0] = dmm
                    gC = GAMMAS[h] ** RC
                    supd0 = p.step("vector", [dmm] + s16_last,
                                   lambda v, gC=gC, S_f=S_f, p_ds=p_ds:
                        v.scalar_tensor_tensor(S_f[:, 0:512], S_f[:, 0:512], gC,
                                               p_ds[:, 0:512], ALU.mult, ALU.add))
                    supd1 = p.step("vector", [dmm] + s16_last,
                                   lambda v, gC=gC, S_f=S_f, p_ds=p_ds:
                        v.scalar_tensor_tensor(S_f[:, 512:1024], S_f[:, 512:1024],
                                               gC, p_ds[:, 512:1024],
                                               ALU.mult, ALU.add))
                    supd_last = [supd0, supd1]
                    s16a = p.step("scalar", [supd0, omm],
                                  lambda s, S16=S16, S_f=S_f:
                        s.copy(S16[:, 0:512], S_f[:, 0:512]))
                    s16b = p.step("gpsimd", [supd1, omm],
                                  lambda g_, S16=S16, S_f=S_f:
                        g_.tensor_copy(S16[:, 512:1024], S_f[:, 512:1024]))
                    s16_last = [s16a, s16b]
                scT[0] = p.merge("tensor", [])
                scV[0] = p.step("vector", [], lambda v: v.engine_nop())
            tok_scan = p.step("vector", oev_toks + s16_last + supd_last,
                              lambda v: v.engine_nop())
            p.set_guard("q16", tok_scan)
            tok_h_done[h] = tok_scan
            sb_ = st_bounce[pair_i]
            tok_sp = p.dma("gpsimd", [tok_scan, p.guard(f"stb{pair_i}")], [
                (sb_.ap()[blk * 128: blk * 128 + 128, :],
                 S_f[:, blk * 512:(blk + 1) * 512]) for blk in range(2)])
            tok_h_done[h] = tok_sp
            p.free(f"wkb{h}", f"wvb{h}", f"k16_{h}", f"v16_{h}", f"ktm_{h}",
                   f"S_f{h}", f"S16_{h}", f"at16_{h}",
                   f"p_at{h}", f"p_o{h}", f"p_ds{h}")
            p.set_guard("wq", tok_scan)

        rb_ = st_rbounce[pair_i]
        tok_scoll = p.coll([tok_h_done[h]], "AllGather", ALU.bypass,
                           PAIRS, [st_bounce[pair_i].ap().opt()], [rb_.ap().opt()])
        tok_sm = p.dma("gpsimd", [tok_scoll, p.guard("smid")], [
            (smid16[:, sl * 512:(sl + 1) * 512], rb_.ap()[sl * 128:(sl + 1) * 128, :])
            for sl in range(2)])
        tok_sm16 = p.step("vector", [tok_sm, tok_rc], lambda v: [
            v.tensor_scalar(smid16[:, sl * 512:(sl + 1) * 512],
                            smid16[:, sl * 512:(sl + 1) * 512],
                            flag_s[:, 0:1], None, ALU.mult) for sl in range(2)][-1])
        p.set_guard(f"stb{pair_i}", tok_scoll)

        if True:
            o_h = o_buf[h]
            wgb = p.sb(f"wgb{h}", [128, 8 * 512], BF16)
            wob = p.sb(f"wob{h}", [128, 4 * 1024], BF16)
            tok_wg = p.dma("gpsimd", [bar0, p.guard("wgb"), tok_scan], [
                (wgb[:, dc * 512:(dc + 1) * 512],
                 WgT[dc * 128:(dc + 1) * 128, h * 512: h * 512 + 512]) for dc in range(8)])
            tok_wo = p.dma("gpsimd", [bar0, p.guard("wob"), tok_scan], [
                (wob[:, ic * 1024:(ic + 1) * 1024],
                 WoT[h * 512 + ic * 128: h * 512 + ic * 128 + 128, :]) for ic in range(4)])
            sg16 = p.sb(f"sg16_{h}", [128, 2 * 512], BF16)
            og16 = p.sb(f"og16_{h}", [128, 2 * 512], BF16)
            ogT = p.sb(f"ogT_{h}", [128, 2 * 512], BF16)
            t_on = p.sb(f"t_on_{h}", [128, 2 * 512])
            vst = p.sb(f"vst_{h}", [128, RNC])
            p_c = p.ps(f"p_c_{h}", [128, 512])
            p_g = p.ps(f"p_g_{h}", [128, 512])
            p_w = p.ps(f"p_w_{h}", [128, 1024])
            cadd_last = [tok_sm16]
            tsil_last = [tok_sm16]
            xadd_last = [tok_sm16, p.guard("xadd")]
            qdw2 = [None, None]
            sgw = [None, None]
            ogw = [None, None]
            ogTw = [None, None]
            tonw = [None, None]
            sqw = [None, None]
            sq_toks = []
            epiT = [tok_sm16]; epiV = [tok_sm16]
            smoff = 0
            for sc in range(NSC):
                tcol0 = sc * SCH
                tq = proj_rope1(q16, wq_handles[h], tcol0, rcq, rsq,
                                [tok_scan, epiT[0], epiV[0], p.guard("q16")],
                                slot=sc % 2)
                p.set_guard("q16", tq)
                for cc in range(CPS):
                    c = sc * CPS + cc
                    pa = c % 2
                    gCj = GAMMAS[h] ** (RC * c)
                    qdeps = [tq, tok_rc] + ([qdw2[pa]] if qdw2[pa] else [])
                    tok_qdg = p.step("vector", qdeps,
                                     lambda v, pa=pa, cc=cc, h=h, gCj=gCj: [
                        v.scalar_tensor_tensor(
                            qd16[:, pa * 256 + blk * 128: pa * 256 + blk * 128 + 128],
                            q16[:, blk * SCH + cc * 128: blk * SCH + cc * 128 + 128],
                            gCj, cdec_s[:, h * RC:(h + 1) * RC],
                            ALU.mult, ALU.mult) for blk in range(2)][-1])
                    cmm = p.step("tensor", [tok_qdg, tok_sm16, cadd_last[0]],
                                 lambda t, pa=pa, p_c=p_c, smoff=smoff: [
                        t.matmul(p_c[:],
                                 qd16[:, pa * 256 + blk * 128: pa * 256 + blk * 128 + 128],
                                 smid16[:, smoff + blk * 512: smoff + blk * 512 + 512],
                                 start=(blk == 0), stop=(blk == 1)) for blk in range(2)][-1])
                    qdw2[pa] = cmm
                    cadd = p.step("vector", [cmm], lambda v, c=c, p_c=p_c, o_h=o_h:
                        v.tensor_add(o_h[:, c * 512:(c + 1) * 512],
                                     o_h[:, c * 512:(c + 1) * 512], p_c[:]))
                    cadd_last[0] = cadd
                    svdeps = [cadd] + ([sqw[pa]] if sqw[pa] else [])
                    sqw[pa] = p.step("scalar", svdeps,
                                     lambda s, c=c, pa=pa, o_h=o_h, vst=vst:
                        s.activation(sq2[:, (pa % 2) * 512:(pa % 2) * 512 + 512],
                                     o_h[:, c * 512:(c + 1) * 512],
                                     AF.Square, accum_out=vst[:, c:c + 1]))
                    sq_toks.append(sqw[pa])
                epiT[0] = p.merge("tensor", [])
                epiV[0] = p.step("vector", [], lambda v: v.engine_nop())
            tvb = p.step("vector", sq_toks, lambda v, vst=vst: v.tensor_scalar(
                vst[:], vst[:], 1.0 / RDV, LN_EPS, ALU.mult, ALU.add))
            tvb = p.step("scalar", [tvb], lambda s, vst=vst: s.activation(
                vst[:], vst[:], AF.Sqrt))
            tvb = p.step("vector", [tvb], lambda v, vst=vst: v.reciprocal(
                vst[:], vst[:]))
            for c in range(RNC):
                pa = c % 2
                gmm = p.step("tensor", [tok_wg, tok_xnT, tsil_last[0]],
                             lambda t, c=c, p_g=p_g, wgb=wgb: [
                    t.matmul(p_g[:],
                             xnT[:, dc * TLOC + c * 128: dc * TLOC + c * 128 + 128],
                             wgb[:, dc * 512:(dc + 1) * 512],
                             start=(dc == 0), stop=(dc == 7)) for dc in range(8)][-1])
                tsil = p.step("scalar", [gmm] + ([sgw[pa]] if sgw[pa] else []),
                              lambda s, pa=pa, p_g=p_g, sg16=sg16: s.activation(
                    sg16[:, pa * 512:(pa + 1) * 512], p_g[:], AF.Silu))
                tsil_last[0] = tsil
                tdeps = [tvb, tok_rc] + ([tonw[pa]] if tonw[pa] else [])
                ton = p.step("vector", tdeps, lambda v, c=c, pa=pa, o_h=o_h,
                             vst=vst, t_on=t_on: v.scalar_tensor_tensor(
                    t_on[:, pa * 512:(pa + 1) * 512],
                    o_h[:, c * 512:(c + 1) * 512],
                    vst[:, c:c + 1], gnw_s[:], ALU.mult, ALU.mult))
                ogdeps = [ton, tsil] + ([ogw[pa]] if ogw[pa] else [])
                tog = p.step("gpsimd", ogdeps, lambda g_, pa=pa, og16=og16,
                             t_on=t_on, sg16=sg16: g_.tensor_mul(
                    og16[:, pa * 512:(pa + 1) * 512],
                    t_on[:, pa * 512:(pa + 1) * 512],
                    sg16[:, pa * 512:(pa + 1) * 512]))
                sgw[pa] = tog
                tonw[pa] = tog
                oblks = [(og16[:, pa * 512 + ic * 128: pa * 512 + ic * 128 + 128],
                          ogT[:, pa * 512 + ic * 128: pa * 512 + ic * 128 + 128])
                         for ic in range(4)]
                tok_ogT = transposes(oblks, [tog] + ([ogTw[pa]] if ogTw[pa] else []))
                ogw[pa] = tok_ogT
                wmm = p.step("tensor", [tok_ogT, tok_wo] + xadd_last,
                             lambda t, pa=pa, p_w=p_w, ogT=ogT, wob=wob: [
                    t.matmul(p_w[:, n2 * 512:(n2 + 1) * 512],
                             ogT[:, pa * 512 + ic * 128: pa * 512 + ic * 128 + 128],
                             wob[:, ic * 1024 + n2 * 512: ic * 1024 + n2 * 512 + 512],
                             start=(ic == 0), stop=(ic == 3))
                    for n2 in range(2) for ic in range(4)][-1])
                ogTw[pa] = wmm
                xadd = p.step("vector", [wmm], lambda v, c=c, p_w=p_w: [
                    v.tensor_add(x[:, c * D + n2 * 512: c * D + n2 * 512 + 512],
                                 x[:, c * D + n2 * 512: c * D + n2 * 512 + 512],
                                 p_w[:, n2 * 512:(n2 + 1) * 512]) for n2 in range(2)][-1])
                xadd_last = [xadd]
            tok_epi_h = p.step("vector", [xadd_last[0], cadd_last[0]],
                               lambda v: v.engine_nop())
            p.set_guard("xadd", tok_epi_h)
            p.set_guard("smid", tok_epi_h)
            p.set_guard("wgb", tok_epi_h); p.set_guard("wob", tok_epi_h)
            p.set_guard("q16", tok_epi_h)
            p.set_guard("wq", tok_epi_h)
            p.free(f"wqb{h}", f"wgb{h}", f"wob{h}", f"sg16_{h}", f"og16_{h}",
                   f"ogT_{h}", f"t_on_{h}", f"vst_{h}",
                   f"p_c_{h}", f"p_g_{h}", f"p_w_{h}", f"o_h{h}")

    tok_l0_done = p.step("vector", [p.guard("xadd")], lambda v: v.engine_nop())
    p.free("rcq", "rsq", "rck", "rsk", "maskT_s", "cdec_s", "kdec_s", "flag_s",
           "gnw_s", "q16", "qd16", "smid16",
           "p_pj", "rtmp", "rt2")
    bar1 = p.barrier()

    if upto < 6:
        return _finish(tok_l0_done, x)

    xnT3 = xnT
    ones32s = p.sb("ones32s", [128, 128])
    logm = p.sb("logm", [128, NT * ME])
    oneh = p.sb("oneh", [128, NT * ME])
    mscale = p.sb("mscale", [128, NT])
    ag = p.sb("ag", [128, D]); ab = p.sb("ab", [128, D])
    mrank_s = p.sb("mrank_s", [128, ME])
    mpfx_s = p.sb("mpfx_s", [NCORES, 1])
    nmx = p.sb("nmx", [128, NT])
    gate = p.sb("gate", [128, NT])
    rts = p.sb("rts", [128, 2 * ME])
    ohT = p.sb("ohT", [8, TLOC], BF16)
    scn = p.sb("scn", [8, TLOC // 2])
    scnc = p.sb("scnc", [8, 2])
    scnT = p.sb("scnT", [128, NT * ME])
    cnts = p.sb("cnts", [NCORES, ME])
    pfxT = p.sb("pfxT", [1, 8])
    pfxb = p.sb("pfxb", [128, ME])
    xn2 = p.sb("xn2", [128, NT * D])
    tok_l1c = p.dma("sync", [bar1], [(ag[:], a_gamma_b), (ab[:], a_beta_b),
                                     (ones32s[:], ones_f32_in),
                                     (mrank_s[:], m_rank_in), (mpfx_s[:], m_prefix_in)])
    tok_xn2 = ln_to(xn2, x, [tok_l0_done, bar1])
    ln3_stats = ln_to(None, xn2, [tok_xn2], stats_only=True)

    x3t = p.sb("x3t", [128, D])
    x3t16 = p.sb("x3t16", [128, D], BF16)
    x3_w = [None, None]
    x3T_w = [None, None]
    for i in range(NT):
        pa = i % 2
        d_3 = [ln3_stats, tok_l1c] + ([x3T_w[0]] if x3T_w[0] else [])
        t3 = p.step("vector", d_3, lambda v, i=i: v.tensor_scalar(
            x3t[:, 0:D], xn2[:, i * D:(i + 1) * D],
            stats[:, 2 * i:2 * i + 1], stats[:, 32 + i:32 + i + 1],
            ALU.subtract, ALU.mult))
        t3 = p.step("vector", [t3], lambda v: v.tensor_mul(x3t[:, 0:D], x3t[:, 0:D], ag[:]))
        t3 = p.step("vector", [t3], lambda v: v.tensor_add(x3t[:, 0:D], x3t[:, 0:D], ab[:]))
        x3_w[pa] = p.step("vector", [t3] + ([x3T_w[0]] if x3T_w[0] else []),
                          lambda v: v.tensor_copy(x3t16[:, 0:D], x3t[:, 0:D]))
        blks3 = [(x3t16[:, dc * 128: dc * 128 + 128],
                  xnT3[:, dc * TLOC + i * 128: dc * TLOC + i * 128 + 128])
                 for dc in range(8)]
        x3T_w[0] = transposes(blks3, [x3_w[pa]])
    tok_xnT3 = p.merge("vector", [x3T_w[0]])

    wgr = p.sb("wgr", [128, 2 * D])
    ttd = p.sb("ttd", [128, 2 * D])
    ttr_last = {}
    wge_tok = [None, None]
    tr_w = [None, None]
    for e in range(ME):
        sl = e % 2
        wdeps = [tok_l1c] + ([ttr_last.get(e - 2)] if ttr_last.get(e - 2) else [])
        wge_tok[sl] = p.dma("sync", wdeps,
                            [(wgr[:, sl * D:(sl + 1) * D],
                              wg_bc_in[e * 128:(e + 1) * 128, :])])
        t_ = None
        for i in range(NT):
            sl2 = i % 2
            tdeps = ([wge_tok[sl], tok_xn2] if i == 0 else [t_]) + \
                    ([tr_w[sl2]] if tr_w[sl2] else [])
            t_ = p.step("vector", tdeps, lambda v, i=i, sl=sl, sl2=sl2:
                v.tensor_mul(ttd[:, sl2 * D:(sl2 + 1) * D],
                             xn2[:, i * D:(i + 1) * D],
                             wgr[:, sl * D:(sl + 1) * D]))
            tr_w[sl2] = p.step("scalar", [t_], lambda s, i=i, e=e, sl2=sl2:
                s.activation(ttd[:, sl2 * D:(sl2 + 1) * D],
                             ttd[:, sl2 * D:(sl2 + 1) * D], AF.Copy,
                             accum_out=logm[:, i * ME + e: i * ME + e + 1]))
        ttr_last[e] = p.merge("scalar", [tr_w[0], tr_w[1]])
    tok_logm = ttr_last[ME - 1]
    p.free("ttd", "wgr", "x3t16", "x3t")
    bar_rt = p.barrier()
    pTf_cell[0] = p.ps("pTf", [128, 128])

    tok_gate = [bar_rt]
    oh_toks = []
    for i in range(NT):
        t1 = p.step("vector", [tok_logm] + ([tok_gate[0]] if tok_gate[0] else []),
                    lambda v, i=i: v.reduce_max(nmx[:, i:i + 1],
                                                logm[:, i * ME:(i + 1) * ME],
                                                axis=AX.X, negate=True))
        t2 = p.step("scalar", [t1], lambda s, i=i: s.activation(
            rts[:, 0:ME], logm[:, i * ME:(i + 1) * ME], AF.Exp,
            bias=nmx[:, i:i + 1], accum_out=gate[:, i:i + 1]))
        t3 = p.step("vector", [t1], lambda v, i=i: v.tensor_scalar(
            oneh[:, i * ME:(i + 1) * ME], logm[:, i * ME:(i + 1) * ME],
            nmx[:, i:i + 1], None, ALU.add))
        t4 = p.step("vector", [t3], lambda v, i=i: v.tensor_scalar(
            oneh[:, i * ME:(i + 1) * ME], oneh[:, i * ME:(i + 1) * ME],
            0.0, None, ALU.is_ge))
        t5 = p.step("vector", [t4, tok_l1c, t2], lambda v, i=i: v.tensor_mul(
            rts[:, ME:2 * ME], oneh[:, i * ME:(i + 1) * ME], mrank_s[:]))
        t6 = p.step("vector", [t5], lambda v, i=i: v.reduce_max(
            nmx[:, i:i + 1], rts[:, ME:2 * ME], axis=AX.X, negate=True))
        t7 = p.step("vector", [t6], lambda v, i=i: v.tensor_scalar(
            rts[:, ME:2 * ME], rts[:, ME:2 * ME], nmx[:, i:i + 1], None, ALU.add))
        t8 = p.step("vector", [t7], lambda v, i=i: v.tensor_scalar(
            oneh[:, i * ME:(i + 1) * ME], rts[:, ME:2 * ME], 0.0, None, ALU.is_ge))
        oh_toks.append(t8)
        tok_gate[0] = t8
    tok_oneh = p.merge("vector", oh_toks[-2:])
    tok_grec = p.step("vector", [tok_oneh], lambda v: v.reciprocal(gate[:], gate[:]))

    tok_oh16 = p.step("vector", [tok_grec, bar_rt], lambda v: v.engine_nop())
    blksO = [(oneh[:, i * ME:(i + 1) * ME], ohT[0:8, i * 128:(i + 1) * 128])
             for i in range(NT)]
    tok_ohT = transposes(blksO, tok_oh16)
    HT = TLOC // 2
    tok_scnA = p.step("vector", [tok_ohT], lambda v: v.tensor_tensor_scan(
        scn[:], ohT[0:8, 0:HT], ohT[0:8, 0:HT], 0.0, ALU.add, ALU.bypass))
    tok_carry = p.step("vector", [tok_scnA], lambda v: v.tensor_copy(
        scnc[:, 0:1], scn[:, HT - 1:HT]))
    blksS1 = [(scn[0:8, i * 128:(i + 1) * 128], scnT[:, i * ME:(i + 1) * ME])
              for i in range(NT // 2)]
    tok_scnT1 = transposes(blksS1, [tok_carry])
    tok_scnB = p.step("vector", [tok_scnT1], lambda v: v.tensor_tensor_scan(
        scn[:], ohT[0:8, HT:TLOC], ohT[0:8, HT:TLOC], scnc[:, 0:1],
        ALU.add, ALU.bypass))
    blksS2 = [(scn[0:8, i * 128:(i + 1) * 128], scnT[:, (NT // 2 + i) * ME:(NT // 2 + i + 1) * ME])
              for i in range(NT // 2)]
    tok_scnT2 = transposes(blksS2, [tok_scnB])
    tok_scn = tok_scnB
    tok_cs = p.dma("gpsimd", [tok_scn], [(cnt_snd.ap()[0:1, 0:ME], scn[:, HT - 1:HT])])
    tok_cc8 = p.coll([tok_cs], "AllGather", ALU.bypass, ALL8,
                     [cnt_snd.ap().opt()], [cnt_rcv.ap().opt()])
    tok_cr = p.dma("gpsimd", [tok_cc8], [(cnts[:], cnt_rcv.ap()[:, :])])
    p_pfx = p.ps("p_pfx", [1, 8])
    tok_pfx = p.step("tensor", [tok_cr, tok_l1c], lambda t: t.matmul(
        p_pfx[:], mpfx_s[0:NCORES, :], cnts[:, 0:ME], start=True, stop=True))
    tok_pfxe = p.step("vector", [tok_pfx], lambda v: v.tensor_copy(
        pfxT[0:1, 0:8], p_pfx[:]))
    p_pfb = p.ps("p_pfb", [128, 8])
    tok_pfb = p.step("tensor", [tok_pfxe], lambda t: t.matmul(
        p_pfb[:], ones32s[0:1, :], pfxT[0:1, 0:8], start=True, stop=True))
    tok_pfbe = p.step("vector", [tok_pfb], lambda v: v.tensor_copy(pfxb[:], p_pfb[:]))
    tok_scnT = p.merge("vector", [tok_scnT2])
    sel_ev = [None]
    for i in range(NT):
        t1 = p.step("vector", [tok_scnT, tok_oneh] + ([sel_ev[0]] if sel_ev[0] else []),
                    lambda v, i=i: v.tensor_mul(rts[:, 0:ME], oneh[:, i * ME:(i + 1) * ME],
                                                scnT[:, i * ME:(i + 1) * ME]))
        t2 = p.step("vector", [t1], lambda v, i=i: v.reduce_sum(
            mscale[:, i:i + 1], rts[:, 0:ME], axis=AX.X))
        t3 = p.step("vector", [t2, tok_pfbe], lambda v, i=i: v.tensor_mul(
            rts[:, 0:ME], oneh[:, i * ME:(i + 1) * ME], pfxb[:]))
        t3 = p.step("vector", [t3], lambda v, i=i: v.reduce_sum(
            rts[:, ME:ME + 1], rts[:, 0:ME], axis=AX.X))
        t4 = p.step("vector", [t3], lambda v, i=i: v.scalar_tensor_tensor(
            mscale[:, i:i + 1], mscale[:, i:i + 1], -1.0, rts[:, ME:ME + 1],
            ALU.add, ALU.add))
        t5 = p.step("vector", [t4], lambda v, i=i: v.tensor_scalar(
            mscale[:, i:i + 1], mscale[:, i:i + 1], CAP, None, ALU.is_lt))
        t6 = p.step("vector", [t5, tok_grec], lambda v, i=i: v.tensor_mul(
            mscale[:, i:i + 1], mscale[:, i:i + 1], gate[:, i:i + 1]))
        sel_ev[0] = t6
    tok_mscale = p.merge("vector", [sel_ev[0]])

    if upto < 7:
        return _finish(tok_mscale, xn2)

    web = p.sb("web", [128, 8 * D], BF16)
    be16 = p.sb("be16", [1, D], BF16)
    tok_we = p.dma("gpsimd", [bar1, bar_rt], [
        (web[:, dc * D:(dc + 1) * D], mWe_in[dc * 128:(dc + 1) * 128, :])
        for dc in range(8)] + [(be16[:], m_be_in)])
    p.free("p_pfx", "p_pfb", "pTf")
    uT3 = p.sb("uT3", [128, D], BF16)
    x2m = p.sb("x2m", [128, D], BF16)
    p_y = [p.ps("p_y0", [128, 1024]), p.ps("p_y1", [128, 1024])]
    y_mm = [None, None]; y_ev = [None, None]
    cast_w = [None, None]; tr_w = [None, None]
    for i in range(NT):
        pa = i % 2
        cdeps = [tok_xn2, bar_rt] + ([tr_w[pa]] if tr_w[pa] else [])
        cast_w[pa] = p.step("vector", cdeps + [t_ for t_ in (tr_w[0], tr_w[1]) if t_],
                            lambda v, i=i: v.tensor_copy(
            x2m[:, 0:D], xn2[:, i * D:(i + 1) * D]))
        blksM = [(x2m[:, dc * 128: dc * 128 + 128],
                  uT3[:, dc * 128: dc * 128 + 128])
                 for dc in range(8)]
        tr_w[pa] = transposes(blksM, [cast_w[pa]] + [t_ for t_ in (y_mm[0], y_mm[1]) if t_])
        deps = [tok_we, tr_w[pa], tok_mscale] + ([y_ev[pa]] if y_ev[pa] else [])
        y_mm[pa] = p.step("tensor", deps, lambda t, i=i, pa=pa: ([
            t.matmul(p_y[pa][:, n2 * 512:(n2 + 1) * 512],
                     uT3[:, dc * 128: dc * 128 + 128],
                     web[:, dc * D + n2 * 512: dc * D + n2 * 512 + 512],
                     start=(dc == 0), stop=False)
            for n2 in range(2) for dc in range(8)] + [
            t.matmul(p_y[pa][:, n2 * 512:(n2 + 1) * 512],
                     ones16[0:1, 0:128], be16[0:1, n2 * 512: n2 * 512 + 512],
                     start=False, stop=True)
            for n2 in range(2)])[-1])
        y_ev[pa] = p.step("vector", [y_mm[pa], tok_mscale], lambda v, i=i, pa=pa: [
            v.scalar_tensor_tensor(x[:, i * D + n2 * 512: i * D + n2 * 512 + 512],
                                   p_y[pa][:, n2 * 512:(n2 + 1) * 512],
                                   mscale[:, i:i + 1],
                                   x[:, i * D + n2 * 512: i * D + n2 * 512 + 512],
                                   ALU.mult, ALU.add) for n2 in range(2)][-1])
    tok_moe = p.merge("vector", [y_ev[0], y_ev[1]])
    p.free("web", "be16", "p_y0", "p_y1", "uT3", "x2m", "xn2")

    if upto < 8:
        return _finish(tok_moe, x)

    p.free("logm", "oneh", "mscale", "ag", "ab",
           "mrank_s", "mpfx_s",
           "nmx", "gate", "rts", "ohT", "scn", "scnc", "scnT", "cnts", "pfxT", "pfxb")
    p.free("pTa", "pTb")
    bar_a = p.barrier()

    ps_big = p.ps("ps_big", [128, 2048])
    ps_nrm = p.ps("ps_nrm", [1, 512])
    ps_q = p.ps("ps_q", [64, 512])
    ps_av = p.ps("ps_av", [65, 512])
    ps_rdb = p.ps("ps_rdb", [64, 512])

    ISC = 1.0 / (ADH ** 0.5)
    NKB = T // 128
    xadd_last2 = [tok_moe]

    kbs_b = [p.sb("kbs_0", [1, 8]), p.sb("kbs_1", [1, 8])]
    coll = {}
    proj_evs = []
    k_dma = [None, None, None, None]
    sq_w = [None, None, None, None]
    v_dma = [None, None, None, None]
    for hh in range(2):
        nrow = p.sb(f"nrow_{hh}", [1, 640])
        wk2 = p.sb(f"wk2_{hh}", [128, 8 * 512], BF16)
        wv2 = p.sb(f"wv2_{hh}", [128, 8 * 512], BF16)
        kst = p.sb(f"kst_{hh}", [128, 4 * 512], BF16)
        vst2 = p.sb(f"vst2_{hh}", [128, 4 * 8 * VW], BF16)
        sqs = p.sb(f"sqs_{hh}", [128, 4 * 512], BF16)
        p_pr = ps_big
        p_n = ps_nrm
        tok_w2l = p.dma("gpsimd", [bar_a, p.guard("projbufs")], [
            (wk2[:, dc * 512:(dc + 1) * 512],
             WqkvT[dc * 128:(dc + 1) * 128, D + hh * 512: D + hh * 512 + 512])
            for dc in range(8)] + [
            (wv2[:, dc * 512:(dc + 1) * 512],
             WqkvT[dc * 128:(dc + 1) * 128, 2 * D + hh * 512: 2 * D + hh * 512 + 512])
            for dc in range(8)])

        nred = [None]
        for pp in range(4):
            for tg in range(4):
                pa = (pp * 4 + tg) % 4
                deps = [tok_w2l, tok_xnT3] + ([k_dma[pa]] if k_dma[pa] else [])
                kmm = p.step("tensor", deps, lambda t, pp=pp, tg=tg, pa=pa, wk2=wk2: [
                    t.matmul(p_pr[:, pa * 512: pa * 512 + 512],
                             wk2[:, dc * 512 + pp * 128: dc * 512 + pp * 128 + 128],
                             xnT3[:, dc * TLOC + tg * 512: dc * TLOC + tg * 512 + 512],
                             start=(dc == 0), stop=(dc == 7)) for dc in range(8)][-1])
                kev = p.step("vector", [kmm], lambda v, pa=pa, kst=kst: v.tensor_copy(
                    kst[:, pa * 512: pa * 512 + 512], p_pr[:, pa * 512: pa * 512 + 512]))
                sdeps = [kev] + ([sq_w[pa]] if sq_w[pa] else [])
                tsq = p.step("scalar", sdeps, lambda s, pa=pa, kst=kst, sqs=sqs:
                    s.activation(sqs[:, pa * 512: pa * 512 + 512],
                                 kst[:, pa * 512: pa * 512 + 512], AF.Square))
                last_nm = None
                for hd in range(2):
                    idx = 512 + (pp * 2 + hd) * 4 + tg
                    nmdeps = [tsq] + ([nred[0]] if nred[0] else [])
                    last_nm = p.step("tensor", nmdeps,
                                     lambda t, pa=pa, hd=hd, sqs=sqs, p_n=p_n: t.matmul(
                        p_n[0:1, 0:512],
                        ones16[64 * hd:64 * hd + 64, 0:1],
                        sqs[64 * hd:64 * hd + 64, pa * 512: pa * 512 + 512],
                        start=True, stop=True))
                    ncp = p.step("vector", [last_nm], lambda v, nrow=nrow, p_n=p_n:
                        v.tensor_copy(nrow[0:1, 0:512], p_n[0:1, 0:512]))
                    nred[0] = p.step("vector", [ncp], lambda v, idx=idx, nrow=nrow:
                        v.reduce_max(nrow[0:1, idx:idx + 1],
                                     nrow[0:1, 0:512], axis=AX.X))
                sq_w[pa] = last_nm
                k_dma[pa] = p.dma("gpsimd", [kev], [
                    (k_snd[hh].ap()[pp * 128:(pp + 1) * 128, tg * 512:(tg + 1) * 512],
                     kst[:, pa * 512: pa * 512 + 512])])
        kb_tok = [nred[0]]
        for h8 in range(8):
            kb_tok[0] = p.step("vector", [kb_tok[0]],
                               lambda v, h8=h8, nrow=nrow: v.reduce_max(
                nrow[0:1, 576 + h8:577 + h8],
                nrow[0:1, 512 + h8 * 4: 512 + h8 * 4 + 4], axis=AX.X))

        for ti in range(NT):
            pa = ti % 4
            deps = [tok_w2l, tok_xnT3] + ([v_dma[pa]] if v_dma[pa] else [])
            vmm = p.step("tensor", deps, lambda t, ti=ti, pa=pa, wv2=wv2: [
                t.matmul(p_pr[:, pa * 512: pa * 512 + 512],
                         xnT3[:, dc * TLOC + ti * 128: dc * TLOC + ti * 128 + 128],
                         wv2[:, dc * 512:(dc + 1) * 512],
                         start=(dc == 0), stop=(dc == 7)) for dc in range(8)][-1])
            vz = p.step("vector", [vmm] + ([v_dma[pa]] if v_dma[pa] else []),
                        lambda v, pa=pa, vst2=vst2: v.memset(
                vst2[:, pa * 8 * VW:(pa + 1) * 8 * VW], 0.0))
            vev = p.step("vector", [vz], lambda v, pa=pa, vst2=vst2: [
                v.tensor_copy(
                    vst2[:, pa * 8 * VW + h8 * VW: pa * 8 * VW + h8 * VW + 64],
                    p_pr[:, pa * 512 + h8 * 64: pa * 512 + h8 * 64 + 64])
                for h8 in range(8)][-1])
            vone = p.step("vector", [vev], lambda v, pa=pa, vst2=vst2: [
                v.memset(vst2[:, pa * 8 * VW + h8 * VW + 64:
                              pa * 8 * VW + h8 * VW + 65], 1.0)
                for h8 in range(8)][-1])
            v_dma[pa] = p.dma("gpsimd", [vone], [
                (v_snd[hh].ap()[ti * 128:(ti + 1) * 128, :],
                 vst2[:, pa * 8 * VW:(pa + 1) * 8 * VW])])

        tok_kns = p.dma("gpsimd", [kb_tok[0]], [
            (kn_snd[hh].ap()[0:1, 0:8], nrow[0:1, 576:584])])
        tok_knc = p.coll([tok_kns] + ([coll[0][2]] if hh == 1 else []),
                         "AllGather", ALU.bypass, PAIRS,
                         [kn_snd[hh].ap().opt()], [kn_rcv[hh].ap().opt()])
        tok_kc = p.coll([d for d in k_dma if d] + [tok_knc],
                        "AllGather", ALU.bypass, PAIRS,
                        [k_snd[hh].ap().opt()], [k_rcv[hh].ap().opt()])
        tok_vc = p.coll([d for d in v_dma if d] + [tok_kc],
                        "AllGather", ALU.bypass, PAIRS,
                        [v_snd[hh].ap().opt()], [v_rcv[hh].ap().opt()])
        coll[hh] = (tok_knc, tok_kc, tok_vc)

        pdone = p.merge("vector", [d for d in k_dma if d] +
                        [d for d in v_dma if d] +
                        [d for d in sq_w if d] + [nred[0], kb_tok[0]])
        proj_evs.append(pdone)
        p.set_guard("projbufs", pdone)
        p.free(f"wk2_{hh}", f"wv2_{hh}", f"kst_{hh}", f"vst2_{hh}", f"sqs_{hh}",
               f"nrow_{hh}")
    psum_ok = p.merge("vector", proj_evs)
    p.set_guard("wq2", psum_ok)

    for hh in range(2):
        kbs = kbs_b[hh]
        wq2 = p.sb(f"wq2_{hh}", [128, 8 * 512], BF16)
        tok_w2l = p.dma("gpsimd", [p.guard("wq2")], [
            (wq2[:, dc * 512:(dc + 1) * 512],
             WqkvT[dc * 128:(dc + 1) * 128, hh * 512: hh * 512 + 512])
            for dc in range(8)])
        bar_kv = p.guard("wq2")
        tok_kc = coll[hh][1]

        knr = p.sb(f"knr_{hh}", [1, 16])
        tok_knl = p.dma("gpsimd", [coll[hh][0], bar_kv], [
            (knr[0:1, 0:8], kn_rcv[hh].ap()[0:1, :]),
            (knr[0:1, 8:16], kn_rcv[hh].ap()[1:2, :])])
        tkb = p.step("vector", [tok_knl], lambda v, knr=knr, kbs=kbs: v.tensor_max(
            kbs[0:1, 0:8], knr[0:1, 0:8], knr[0:1, 8:16]))
        tkb = p.step("scalar", [tkb], lambda s, kbs=kbs: s.activation(
            kbs[0:1, 0:8], kbs[0:1, 0:8], AF.Sqrt))

        vf = p.sb(f"vf_{hh}", [128, NKB * 8 * VW], BF16)
        tok_vf = p.dma("sync", [coll[hh][2], bar_kv], [
            (vf[:, kb * 8 * VW:(kb + 1) * 8 * VW],
             v_rcv[hh].ap()[kb * 128:(kb + 1) * 128, :])
            for kb in range(NKB)])
        o_half = p.sb(f"o_half_{hh}", [128, 4 * TLOC], BF16)
        qt = [p.sb(f"qt0_{hh}", [65, TLOC], BF16), p.sb(f"qt1_{hh}", [65, TLOC], BF16)]
        kfh = [p.sb(f"kfh0_{hh}", [65, T], BF16), p.sb(f"kfh1_{hh}", [65, T], BF16)]
        nq = p.sb(f"nq_{hh}", [1, TLOC])
        nq16 = p.sb(f"nq16_{hh}", [1, TLOC], BF16)
        ex16 = p.sb(f"ex16_{hh}", [128, 2048], BF16)
        qsq16 = p.sb(f"qsq16_{hh}", [64, 512], BF16)
        rdr = p.sb(f"rdr_{hh}", [1, 512])
        rdb_s = p.sb(f"rdb_s_{hh}", [64, 512])
        kb2 = p.sb(f"kb2_{hh}", [1, 8])
        p_q = ps_q
        p_qn = ps_nrm
        p_s = ps_big
        p_av = ps_av
        p_rdb = ps_rdb
        tkb2 = p.step("vector", [tkb], lambda v, kbs=kbs, kb2=kb2: v.tensor_mul(
            kb2[0:1, :], kbs[0:1, :], kbs[0:1, :]))
        tkb2 = p.step("vector", [tkb2], lambda v, kb2=kb2: v.tensor_scalar_mul(
            kb2[0:1, :], kb2[0:1, :], 1.0201))
        seed = p.guard("wq2")
        qev_g = [None]
        exp_w = [seed, seed]
        av_w = [None, None]
        av_prev = [None]
        on_prev = [None]
        rd_w = [None]
        nq_w = [seed]
        qn_mm = [None]
        qrow_tok = {}
        qevl = {}
        kfo_tok = {}
        tail_prev = [None]

        def emit_kf(h8, dep_done):
            hpa = h8 % 2
            pp, hd = h8 // 2, h8 % 2
            kdeps = [tok_kc, bar_kv] + ([dep_done] if dep_done else [])
            tok_kfh = p.dma("sync", kdeps, [
                (kfh[hpa][0:64, r * TLOC:(r + 1) * TLOC],
                 k_rcv[hh].ap()[r * 512 + pp * 128 + hd * 64:
                                r * 512 + pp * 128 + hd * 64 + 64, :])
                for r in range(2)])
            kfo_tok[hpa] = p.step("vector", [tok_kfh], lambda v, hpa=hpa, kfh=kfh:
                                  v.memset(kfh[hpa][64:65, :], 1.0))

        def make_build(h8, dep_done):
            hpa = h8 % 2
            thunks = []
            qev_cell = {}

            def mk_qmm(tg):
                def th():
                    qdeps = [tok_w2l, tok_xnT3] + ([dep_done] if dep_done else []) \
                            + ([qev_g[0]] if qev_g[0] else [])
                    qmm = p.step("tensor", qdeps, lambda t, h8=h8, tg=tg, wq2=wq2,
                                 p_q=p_q, xnT3=xnT3: [
                        t.matmul(p_q[:],
                                 wq2[:, dc * 512 + h8 * 64: dc * 512 + h8 * 64 + 64],
                                 xnT3[:, dc * TLOC + tg * 512:
                                      dc * TLOC + tg * 512 + 512],
                                 start=(dc == 0), stop=(dc == 7))
                        for dc in range(8)][-1])
                    qev = p.step("vector", [qmm], lambda v, hpa=hpa, tg=tg, qt=qt,
                                 p_q=p_q: v.tensor_copy(
                        qt[hpa][0:64, tg * 512:(tg + 1) * 512], p_q[:]))
                    qev_g[0] = qev
                    qev_cell[tg] = qev
                    qev_cell[('sq', tg)] = p.step("vector",
                        [qev] + ([qn_mm[0]] if qn_mm[0] else []),
                        lambda v, hpa=hpa, tg=tg, qt=qt, qsq16=qsq16: v.tensor_mul(
                            qsq16[:], qt[hpa][0:64, tg * 512:(tg + 1) * 512],
                            qt[hpa][0:64, tg * 512:(tg + 1) * 512]))
                return th

            def mk_qn(tg):
                def th():
                    deps = [qev_cell[('sq', tg)]] + ([nq_w[0]] if nq_w[0] else [])
                    qn_mm[0] = p.step("tensor", deps, lambda t, qsq16=qsq16,
                                      p_qn=p_qn: t.matmul(
                        p_qn[:], ones16[0:64, 0:1], qsq16[:],
                        start=True, stop=True))
                    nwd = [qn_mm[0]] + ([tail_prev[0]] if tail_prev[0] else [])
                    nq_w[0] = p.step("vector", nwd, lambda v, tg=tg, nq=nq,
                                     p_qn=p_qn: v.tensor_copy(
                        nq[0:1, tg * 512:(tg + 1) * 512], p_qn[0:1, :]))
                return th

            for tg in range(4):
                thunks.append(mk_qmm(tg))
                thunks.append(mk_qn(tg))

            def tail():
                t0 = p.step("gpsimd", [nq_w[0], tkb2], lambda g, h8=h8, nq=nq,
                            kb2=kb2: g.tensor_scalar(
                    nq[0:1, 0:TLOC], nq[0:1, 0:TLOC], kb2[0:1, h8:h8 + 1], None,
                    ALU.mult))
                t1s = p.step("scalar", [t0], lambda s, nq=nq: s.activation(
                    nq[0:1, 0:TLOC], nq[0:1, 0:TLOC], AF.Sqrt))
                t2s = p.step("gpsimd", [t1s], lambda g, nq=nq, nq16=nq16:
                    g.tensor_scalar(nq16[0:1, :], nq[0:1, 0:TLOC], -1.0, -0.08,
                                    ALU.mult, ALU.add))
                qrow_tok[hpa] = p.dma("gpsimd", [t2s] +
                                      ([dep_done] if dep_done else []),
                                      [(qt[hpa][64:65, :], nq16[0:1, :])])
                qevl[hpa] = qev_cell[3]
                tail_prev[0] = qrow_tok[hpa]
            return thunks, tail

        emit_kf(0, None)
        th0, tl0 = make_build(0, None)
        for t in th0:
            t()
        tl0()
        flash_done = [None]
        for h8 in range(8):
            hpa = h8 % 2
            pp, hd = h8 // 2, h8 % 2
            prev_done = flash_done[0]
            nb = make_build(h8 + 1, prev_done) if h8 < 7 else None
            pend = list(nb[0]) if nb else []

            def emit_avs(pbx, etok, h8=h8):
                basex = (pbx % 2) * 1024
                tokav = None
                for i in range(2):
                    kb = 2 * pbx + i
                    avdeps = [etok] + ([tok_vf] if kb == 0 else []) + \
                             ([on_prev[0]] if kb == 0 and on_prev[0] else [])
                    tokav = p.step("tensor", avdeps, lambda t, kb=kb,
                                   col=basex + i * 512, h8=h8, vf=vf, ex16=ex16,
                                   p_av=p_av: t.matmul(
                        p_av[:],
                        vf[:, kb * 8 * VW + h8 * VW: kb * 8 * VW + h8 * VW + 65],
                        ex16[:, col:col + 512],
                        start=(kb == 0), stop=(kb == NKB - 1)))
                av_w[pbx % 2] = tokav
                av_prev[0] = tokav

            for qq in range(4):
                prev = None
                for pb in range(NKB // 2):
                    ps_ = pb % 2
                    base = ps_ * 1024
                    kb0 = 2 * pb
                    sdeps = [kfo_tok[hpa], qrow_tok[hpa], qevl[hpa]] + \
                            ([exp_w[ps_]] if exp_w[ps_] else [])
                    smm = p.step("tensor", sdeps, lambda t, qq=qq, kb0=kb0,
                                 base=base, hpa=hpa, kfh=kfh, qt=qt, p_s=p_s: [
                        t.matmul(p_s[:, base + i * 512: base + i * 512 + 512],
                                 kfh[hpa][0:65, (kb0 + i) * 128:(kb0 + i + 1) * 128],
                                 qt[hpa][0:65, qq * 512:(qq + 1) * 512],
                                 start=True, stop=True) for i in range(2)][-1])
                    edeps = [smm] + ([av_w[ps_]] if av_w[ps_] else [])
                    etok = p.step("scalar", edeps, lambda s, base=base, p_s=p_s,
                                  ex16=ex16: s.activation(
                        ex16[:, base:base + 1024], p_s[:, base:base + 1024],
                        AF.Exp, scale=ISC))
                    exp_w[ps_] = etok
                    if prev is not None:
                        emit_avs(*prev)
                    prev = (pb, etok)
                    if pend and qq == 1 and pb % 2 == 0:
                        pend.pop(0)()
                emit_avs(*prev)
                if nb and qq == 0:
                    emit_kf(h8 + 1, prev_done)
                if nb and qq == 2:
                    nb[1]()
                t1 = p.step("vector", [av_prev[0]] + ([rd_w[0]] if rd_w[0] else []),
                            lambda v, p_av=p_av, rdr=rdr: v.reciprocal(
                    rdr[0:1, :], p_av[64:65, :]))
                t2 = p.step("tensor", [t1], lambda t, rdr=rdr, p_rdb=p_rdb: t.matmul(
                    p_rdb[:], ones32s[0:1, 0:64], rdr[0:1, :], start=True, stop=True))
                t3 = p.step("vector", [t2], lambda v, p_rdb=p_rdb, rdb_s=rdb_s:
                    v.tensor_copy(rdb_s[:], p_rdb[:]))
                on_prev[0] = p.step("vector", [t3], lambda v, pp=pp, hd=hd, qq=qq,
                                    p_av=p_av, rdb_s=rdb_s, o_half=o_half:
                    v.tensor_mul(
                        o_half[64 * hd:64 * hd + 64,
                               pp * TLOC + qq * 512: pp * TLOC + qq * 512 + 512],
                        p_av[0:64, :], rdb_s[:]))
                rd_w[0] = on_prev[0]
            flash_done[0] = on_prev[0]
        tok_oh2 = p.merge("vector", [on_prev[0]])

        p.free(f"qt0_{hh}", f"qt1_{hh}", f"kfh0_{hh}", f"kfh1_{hh}",
               f"nq_{hh}", f"nq16_{hh}", f"ex16_{hh}", f"qsq16_{hh}",
               f"rdr_{hh}", f"rdb_s_{hh}", f"kb2_{hh}")
        bar_wo = p.barrier()
        wo2 = p.sb(f"wo2_{hh}", [128, 4 * D], BF16)
        bo16 = p.sb(f"bo16_{hh}", [1, D], BF16)
        tok_wo2 = p.dma("gpsimd", [bar_wo], [
            (wo2[:, pp * D:(pp + 1) * D],
             WoutT[hh * 512 + pp * 128: hh * 512 + pp * 128 + 128, :])
            for pp in range(4)] + [(bo16[:], b_out_in)])
        wo_mm = [None, None]; wo_ev = [None, None]
        for i in range(NT):
            pa = i % 2
            deps = [tok_oh2, tok_wo2, exp_w[0], exp_w[1]] + \
                   ([wo_ev[pa]] if wo_ev[pa] else [])
            wo_mm[pa] = p.step("tensor", deps, lambda t, i=i, pa=pa, hh=hh,
                               o_half=o_half, wo2=wo2, bo16=bo16: ([
                t.matmul(ps_big[:, pa * 1024 + n2 * 512: pa * 1024 + n2 * 512 + 512],
                         o_half[:, pp * TLOC + i * 128: pp * TLOC + i * 128 + 128],
                         wo2[:, pp * D + n2 * 512: pp * D + n2 * 512 + 512],
                         start=(pp == 0), stop=(pp == 3 and hh == 1))
                for n2 in range(2) for pp in range(4)] + ([
                t.matmul(ps_big[:, pa * 1024 + n2 * 512: pa * 1024 + n2 * 512 + 512],
                         ones16[0:1, 0:128], bo16[0:1, n2 * 512: n2 * 512 + 512],
                         start=False, stop=True)
                for n2 in range(2)] if hh == 0 else []))[-1])
            wo_ev[pa] = p.step("vector", [wo_mm[pa], xadd_last2[0]],
                               lambda v, i=i, pa=pa: [
                v.tensor_add(x[:, i * D + n2 * 512: i * D + n2 * 512 + 512],
                             x[:, i * D + n2 * 512: i * D + n2 * 512 + 512],
                             ps_big[:, pa * 1024 + n2 * 512: pa * 1024 + n2 * 512 + 512])
                for n2 in range(2)][-1])
            if hh == 1:
                p.out_dma([wo_ev[pa]],
                          [(x_out[i * 128:(i + 1) * 128, :], x[:, i * D:(i + 1) * D])],
                          wait=False)
        xadd_last2[0] = p.merge("vector", [wo_ev[0], wo_ev[1]])
        p.set_guard("wq2", xadd_last2[0])
        p.free(f"wo2_{hh}", f"bo16_{hh}",
               f"o_half_{hh}", f"vf_{hh}", f"knr_{hh}", f"wq2_{hh}")
    p.free("kbs_1", "kbs_0")

    return _finish(None, x)


def kernel(_upto=99, **inputs):
    g, shards = _prep_inputs(inputs)
    nc = build_nc(_upto)
    in_maps = []
    for c in range(NCORES):
        m = dict(g)
        m["x_in"] = shards[c]
        m.update(_host_consts(c))
        in_maps.append(m)
    res = run_bass_kernel_spmd(nc, in_maps, core_ids=list(range(NCORES)))
    out = np.concatenate([res.results[c]["x_out"] for c in range(NCORES)], axis=0)
    return out.reshape(B, T, D)


# version 14
# speedup vs baseline: 1.5677x; 1.0233x over previous
import numpy as np
import concourse.bass as bass
import concourse.mybir as mybir
from concourse.bass_utils import run_bass_kernel_spmd

F32 = mybir.dt.float32
BF16 = mybir.dt.bfloat16
AF = mybir.ActivationFunctionType
ALU = mybir.AluOpType
AX = mybir.AxisListType

B, T, D = 4, 4096, 1024
NCORES = 8
TLOC = 2048
NT = TLOC // 128
LN_EPS = 1e-5

RH = 4
RDK, RDV = 256, 512
RC = 128
RNC = TLOC // RC
GAMMAS = [1.0 - 2.0 ** (-5.0 - h) for h in range(RH)]

NE, NS = 16, 16
ES = NE * NS
FH = 256
NORM_EPS = 1e-12

AH, ADH = 16, 64
ME = 8
CAP = float((B * T) // ME)

_BUILD_UPTO = 99


def _host_consts(core):
    import ml_dtypes
    half = core % 2
    c = {}
    inv_freq = 1.0 / (10000.0 ** (np.arange(0, RDK, 2, dtype=np.float64) / RDK))
    tpos = half * TLOC + np.arange(TLOC, dtype=np.float64)
    fr = inv_freq[:, None] * tpos[None, :]
    c["rope_cos"] = np.cos(fr).astype(np.float32)
    c["rope_sin"] = np.sin(fr).astype(np.float32)
    c["rope_cos_q"] = (np.cos(fr) * (RDK ** -0.5)).astype(np.float32)
    c["rope_sin_q"] = (np.sin(fr) * (RDK ** -0.5)).astype(np.float32)
    maskT = np.zeros((RH, RC, RC), dtype=np.float32)
    cdec = np.zeros((RH, 128, RC), dtype=np.float32)
    kdec = np.zeros((RH, RC, 1), dtype=np.float32)
    for h, g in enumerate(GAMMAS):
        i = np.arange(RC, dtype=np.float64)
        dif = i[None, :] - i[:, None]
        maskT[h] = np.where(dif >= 0, g ** dif, 0.0).astype(np.float32)
        cdec[h] = np.broadcast_to((g ** (i + 1.0)).astype(np.float32), (128, RC))
        kdec[h] = (g ** (RC - 1.0 - i)).astype(np.float32)[:, None]
    c["ret_maskT"] = maskT
    c["ret_cdec"] = cdec
    c["ret_kdec"] = kdec
    c["ret_flag"] = np.full((128, 1), float(half), dtype=np.float32)
    c["ident"] = np.eye(128, dtype=np.float32).astype(ml_dtypes.bfloat16)
    c["ident_f32"] = np.eye(128, dtype=np.float32)
    c["ones_f32"] = np.ones((128, 128), dtype=np.float32)
    c["ones_bf16"] = np.ones((128, 128), dtype=np.float32).astype(ml_dtypes.bfloat16)
    c["m_rank"] = np.broadcast_to(np.arange(ME, 0, -1, dtype=np.float32), (128, ME)).copy()
    c["m_prefix"] = (np.arange(NCORES) < core).astype(np.float32)[:, None]
    return c


def _prep_inputs(inp):
    g = {}
    g["WqT"] = np.ascontiguousarray(np.asarray(inp["L0_attn_Wq"]).T)
    g["WkT"] = np.ascontiguousarray(np.asarray(inp["L0_attn_Wk"]).T)
    g["WvT"] = np.ascontiguousarray(np.asarray(inp["L0_attn_Wv"]).T)
    g["WgT"] = np.ascontiguousarray(np.asarray(inp["L0_attn_Wg"]).T)
    g["WoT"] = np.ascontiguousarray(np.asarray(inp["L0_attn_Wo"]).T)
    g["gnw_b"] = np.broadcast_to(np.asarray(inp["L0_attn_gnorm_w"]), (128, RDV)).copy()
    g["f_gamma_b"] = np.broadcast_to(np.asarray(inp["L0_ffn_ln_gamma"]), (128, D)).copy()
    g["f_beta_b"] = np.broadcast_to(np.asarray(inp["L0_ffn_ln_beta"]), (128, D)).copy()
    g["mu2d"] = np.ascontiguousarray(np.asarray(inp["L0_ffn_mu"]).reshape(D, ES))
    g["scale_b"] = np.broadcast_to(np.asarray(inp["L0_ffn_scale"]).reshape(1, 1), (128, 1)).copy()
    g["w1f"] = np.ascontiguousarray(np.asarray(inp["L0_ffn_w1"]).reshape(NE * D, FH))
    g["b1T"] = np.ascontiguousarray(np.asarray(inp["L0_ffn_b1"]))
    g["w2f"] = np.ascontiguousarray(np.asarray(inp["L0_ffn_w2"]).reshape(NE * FH, D))
    g["b2"] = np.ascontiguousarray(np.asarray(inp["L0_ffn_b2"]))
    g["a_gamma_b"] = np.broadcast_to(np.asarray(inp["L1_attn_ln_gamma"]), (128, D)).copy()
    g["a_beta_b"] = np.broadcast_to(np.asarray(inp["L1_attn_ln_beta"]), (128, D)).copy()
    g["WqkvT"] = np.ascontiguousarray(np.asarray(inp["L1_attn_w_qkv"]).T)
    g["WoutT"] = np.ascontiguousarray(np.asarray(inp["L1_attn_w_out"]).T)
    g["b_out"] = np.asarray(inp["L1_attn_b_out"]).reshape(1, D)
    g["mWg"] = np.ascontiguousarray(np.asarray(inp["L1_ffn_Wg"]))
    _wg = np.asarray(inp["L1_ffn_Wg"], dtype=np.float32)
    g["wg_bc"] = np.ascontiguousarray(
        np.broadcast_to(_wg.T[:, None, :], (ME, 128, D))).reshape(ME * 128, D)
    g["mWe"] = np.ascontiguousarray(np.asarray(inp["L1_ffn_We"]))
    g["m_be"] = np.asarray(inp["L1_ffn_be"]).reshape(1, D)
    x = np.asarray(inp["L0_attn_x"], dtype=np.float32).reshape(B * T, D)
    shards = [np.ascontiguousarray(x[c * TLOC:(c + 1) * TLOC]) for c in range(NCORES)]
    return g, shards


ENGS = ("tensor", "vector", "scalar", "gpsimd", "sync")


class Builder:
    def __init__(self, nc):
        self.nc = nc
        self._allocs = {}
        self.q = {e: [] for e in ENGS}
        self.sem_cm = {}
        self.sem = {}
        self.cnt = {e: 0 for e in ENGS}
        self.wm = {e: {} for e in ENGS}
        for e in ENGS:
            cm = nc.semaphore("sem_" + e)
            self.sem_cm[e] = cm
            self.sem[e] = cm.__enter__()
        self.NDMA = 4
        for d in ["dma_out"] + [f"dma_{e}{i}" for e in ("sync", "gpsimd")
                                for i in range(4)]:
            cm = nc.semaphore(d)
            self.sem_cm[d] = cm
            self.sem[d] = cm.__enter__()
        self.dmacnt = {f"dma_{e}{i}": 0 for e in ("sync", "gpsimd") for i in range(4)}
        self.dmarr = {"sync": 0, "gpsimd": 0}
        self.outcnt = 0
        cm = nc.semaphore("sem_cc")
        self.sem_cm["cc"] = cm
        self.sem["cc"] = cm.__enter__()
        self.cccnt = 0
        self.guards = {}

    def guard(self, name):
        return self.guards.get(name, ("vector", 0))

    def set_guard(self, name, tok):
        self.guards[name] = tok

    def _merge_emit(self, eng):
        if eng in ("vector", "gpsimd"):
            return lambda en: en.engine_nop()
        if eng in ("tensor", "scalar"):
            return lambda en: en.drain()
        return lambda en: en.nop()

    def merge(self, eng, deps):
        return self.step(eng, deps, self._merge_emit(eng))

    def barrier(self):
        toks = [self.merge(e, []) for e in ENGS]
        return self.merge("vector", toks)

    def sb(self, name, shape, dt=F32, side="right"):
        cm = self.nc.sbuf_tensor(name, shape, dt, side=side)
        h = cm.__enter__()
        self._alloc_idx = getattr(self, "_alloc_idx", 0) + 1
        self._allocs[name] = (cm, self._alloc_idx)
        return h

    def ps(self, name, shape, dt=F32, side="right"):
        cm = self.nc.psum_tensor(name, shape, dt, side=side)
        h = cm.__enter__()
        self._alloc_idx = getattr(self, "_alloc_idx", 0) + 1
        self._allocs[name] = (cm, self._alloc_idx)
        return h

    def free(self, *names):
        items = sorted([(self._allocs[n][1], n) for n in names], reverse=True)
        for _, n in items:
            self._allocs.pop(n)[0].__exit__(None, None, None)

    def _waits(self, eng, deps):
        wm = self.wm[eng]
        todo = []
        for key, val in deps:
            if val <= 0:
                continue
            if wm.get(key, 0) >= val:
                continue
            wm[key] = val
            todo.append((self.sem[key], val))
        return todo

    def step(self, eng, deps, emit, name=""):
        todo = self._waits(eng, deps)
        self.cnt[eng] += 1
        tok = (eng, self.cnt[eng])

        def run(e, todo=todo, emit=emit, tok=tok, eng=eng):
            for h, v in todo:
                e.wait_ge(h, v)
            insts = emit(e)
            last = insts[-1] if isinstance(insts, (list, tuple)) else insts
            last.then_inc(self.sem[eng], 1)
        self.q[eng].append(run)
        return tok

    def dma(self, eng, deps, pairs, name=""):
        i = self.dmarr[eng]
        self.dmarr[eng] = (i + 1) % 4
        key = f"dma_{eng}{i}"
        todo = self._waits(eng, deps + [(key, self.dmacnt[key])])
        n = len(pairs)
        self.dmacnt[key] += 16 * n
        tok = (key, self.dmacnt[key])

        def run(e, todo=todo, pairs=pairs, key=key):
            for h, v in todo:
                e.wait_ge(h, v)
            for dst, src in pairs:
                e.dma_start(out=dst, in_=src).then_inc(self.sem[key], 16)
        self.q[eng].append(run)
        return tok

    def coll(self, deps, kind, op, groups, ins, outs):
        todo = self._waits("gpsimd", deps + [("cc", self.cccnt)])
        self.cccnt += 1
        tok = ("cc", self.cccnt)

        def run(e, todo=todo):
            for h, v in todo:
                e.wait_ge(h, v)
            e.collective_compute(kind, op, replica_groups=groups, ins=ins,
                                 outs=outs).then_inc(self.sem["cc"], 1)
        self.q["gpsimd"].append(run)
        return tok

    def out_dma(self, deps, pairs, wait=True):
        todo = self._waits("sync", deps + [(f"dma_sync{i}", self.dmacnt[f"dma_sync{i}"])
                                           for i in range(4)])
        n = len(pairs)
        self.outcnt += 16 * n
        tok = ("dma_out", self.outcnt)
        final = self.outcnt

        def run(e, todo=todo, pairs=pairs, final=final, wait=wait):
            for h, v in todo:
                e.wait_ge(h, v)
            for dst, src in pairs:
                e.dma_start(out=dst, in_=src).then_inc(self.sem["dma_out"], 16)
            if wait:
                e.wait_ge(self.sem["dma_out"], final)
        self.q["sync"].append(run)
        return tok

    def out_wait(self):
        final = self.outcnt

        def run(e, final=final):
            e.wait_ge(self.sem["dma_out"], final)
        self.q["sync"].append(run)

    def finalize(self, block):
        for e in ENGS:
            fns = self.q[e]
            if not fns:
                continue

            def run_all(eng, fns=fns):
                for f in fns:
                    f(eng)
            getattr(block, e)(run_all)


def build_nc(upto=99):
    nc = bass.Bass("TRN2", target_bir_lowering=False, dynamic_dma_scratch_size=16384)
    bld = Builder(nc)
    p = bld

    def din(name, shape, dt=F32):
        return nc.dram_tensor(name, shape, dt, kind="ExternalInput").ap()

    x_in = din("x_in", [TLOC, D])
    WqT, WkT = din("WqT", [D, D]), din("WkT", [D, D])
    WvT, WgT = din("WvT", [D, 2 * D]), din("WgT", [D, 2 * D])
    WoT = din("WoT", [2 * D, D])
    gnw_b = din("gnw_b", [128, RDV])
    f_gamma_b, f_beta_b = din("f_gamma_b", [128, D]), din("f_beta_b", [128, D])
    mu2d = din("mu2d", [D, ES])
    scale_b = din("scale_b", [128, 1])
    w1f = din("w1f", [NE * D, FH])
    b1_in = din("b1T", [NE, FH])
    w2f = din("w2f", [NE * FH, D])
    b2_in = din("b2", [NE, D])
    a_gamma_b, a_beta_b = din("a_gamma_b", [128, D]), din("a_beta_b", [128, D])
    WqkvT = din("WqkvT", [D, 3 * D])
    WoutT = din("WoutT", [D, D])
    b_out_in = din("b_out", [1, D])
    mWg_in = din("mWg", [D, ME])
    wg_bc_in = din("wg_bc", [ME * 128, D])
    mWe_in = din("mWe", [D, D])
    m_be_in = din("m_be", [1, D])
    rope_cos, rope_sin = din("rope_cos", [128, TLOC]), din("rope_sin", [128, TLOC])
    rope_cos_q, rope_sin_q = din("rope_cos_q", [128, TLOC]), din("rope_sin_q", [128, TLOC])
    ret_maskT = din("ret_maskT", [RH, RC, RC])
    ret_cdec = din("ret_cdec", [RH, 128, RC])
    ret_kdec = din("ret_kdec", [RH, RC, 1])
    ret_flag = din("ret_flag", [128, 1])
    ident_in = din("ident", [128, 128], BF16)
    identf_in = din("ident_f32", [128, 128])
    ones_f32_in = din("ones_f32", [128, 128])
    ones_bf16_in = din("ones_bf16", [128, 128], BF16)
    m_rank_in = din("m_rank", [128, ME])
    m_prefix_in = din("m_prefix", [NCORES, 1])

    x_out = nc.dram_tensor("x_out", [TLOC, D], F32, kind="ExternalOutput").ap()

    ffn_snd = nc.dram_tensor("ffn_snd", [ES, D + 8], F32)
    ffn_rcv = nc.dram_tensor("ffn_rcv", [2 * ES, D + 8], F32)
    st_snd = nc.dram_tensor("st_snd", [RDK, RDV], F32)
    st_rcv = nc.dram_tensor("st_rcv", [2 * RDK, RDV], F32)
    st_snd1 = nc.dram_tensor("st_snd1", [RDK, RDV], F32)
    st_rcv1 = nc.dram_tensor("st_rcv1", [2 * RDK, RDV], F32)
    cnt_snd = nc.dram_tensor("cnt_snd", [1, ME], F32)
    cnt_rcv = nc.dram_tensor("cnt_rcv", [NCORES, ME], F32)
    VW = 66
    k_snd = [nc.dram_tensor(f"k_snd{i}", [512, TLOC], BF16) for i in range(2)]
    k_rcv = [nc.dram_tensor(f"k_rcv{i}", [1024, TLOC], BF16) for i in range(2)]
    v_snd = [nc.dram_tensor(f"v_snd{i}", [TLOC, 8 * VW], BF16) for i in range(2)]
    v_rcv = [nc.dram_tensor(f"v_rcv{i}", [2 * TLOC, 8 * VW], BF16) for i in range(2)]
    kn_snd = [nc.dram_tensor(f"kn_snd{i}", [1, 8], F32) for i in range(2)]
    kn_rcv = [nc.dram_tensor(f"kn_rcv{i}", [2, 8], F32) for i in range(2)]

    PAIRS = [[0, 1], [2, 3], [4, 5], [6, 7]]
    ALL8 = [list(range(NCORES))]

    x = p.sb("x", [128, NT * D], side="left")
    ident = p.sb("ident_s", [128, 128], BF16, side="left")
    identf = p.sb("identf_s", [128, 128], side="left")
    ones16 = p.sb("ones16_s", [128, 128], BF16, side="left")
    stats = p.sb("stats", [128, 96], side="left")
    bnb = p.sb("bnb", [128, 24], side="left")
    sq2 = p.sb("sq2", [128, D], side="left")
    xnT = p.sb("xnT", [128, 8 * TLOC], BF16, side="left")
    xn = p.sb("xn", [128, NT * D], side="left")

    tok_x = p.dma("sync", [], [(x[:, i * D:(i + 1) * D], x_in[i * 128:(i + 1) * 128, :])
                               for i in range(NT)])
    tok_const = p.dma("sync", [], [(ident[:], ident_in), (identf[:], identf_in),
                                   (ones16[:], ones_bf16_in)])

    def ln_to(dst, src, dep, gamma=None, beta=None, dst16=None, name="ln",
              stats_only=False):
        dep = list(dep) if isinstance(dep, list) else [dep]
        stok = []
        for i in range(NT):
            a = src[:, i * D:(i + 1) * D]
            deps0 = dep + [p.guard("stats")]
            if i >= 2:
                deps0.append(stok[i - 2])
            t0 = p.step("vector", deps0, lambda v, a=a, i=i: [
                v.bn_stats(bnb[:, 12 * (i % 2):12 * (i % 2) + 6], a[:, 0:512]),
                v.bn_stats(bnb[:, 12 * (i % 2) + 6:12 * (i % 2) + 12], a[:, 512:1024]),
            ][-1])
            t1 = p.step("vector", [t0], lambda v, i=i: v.bn_aggr(
                stats[:, 2 * i:2 * i + 2], bnb[:, 12 * (i % 2):12 * (i % 2) + 12]))
            stok.append(t1)
        t2 = p.step("vector", stok, lambda v: [
            v.tensor_scalar_add(stats[:, 32 + i:32 + i + 1],
                                stats[:, 2 * i + 1:2 * i + 2], LN_EPS)
            for i in range(NT)][-1])
        t3 = p.step("scalar", [t2], lambda s: s.activation(
            stats[:, 32:32 + NT], stats[:, 32:32 + NT], AF.Sqrt))
        t4 = p.step("vector", [t3], lambda v: v.reciprocal(
            stats[:, 32:32 + NT], stats[:, 32:32 + NT]))
        if stats_only:
            p.set_guard("stats", t4)
            return t4
        wa = [p.step("vector", [t4] + dep, lambda v, i=i: v.tensor_scalar(
            dst[:, i * D:(i + 1) * D], src[:, i * D:(i + 1) * D],
            stats[:, 2 * i:2 * i + 1], stats[:, 32 + i:32 + i + 1],
            ALU.subtract, ALU.mult)) for i in range(NT)]
        if gamma is not None:
            wa = [p.step("vector", [wa[i]], lambda v, i=i: v.tensor_mul(
                dst[:, i * D:(i + 1) * D], dst[:, i * D:(i + 1) * D], gamma[:]))
                for i in range(NT)]
            wa = [p.step("vector", [wa[i]], lambda v, i=i: v.tensor_add(
                dst[:, i * D:(i + 1) * D], dst[:, i * D:(i + 1) * D], beta[:]))
                for i in range(NT)]
        if dst16 is not None:
            wa = [p.step("vector", [wa[i]], lambda v, i=i: v.tensor_copy(
                dst16[:, i * D:(i + 1) * D], dst[:, i * D:(i + 1) * D]))
                for i in range(NT)]
        fin = p.step("vector", wa, lambda v: v.engine_nop())
        p.set_guard("stats", fin)
        return fin

    pTa = p.ps("pTa", [128, 128], BF16)
    pTb = p.ps("pTb", [128, 128], BF16)
    pTf_cell = [None]

    def transposes(blocks, dep, extra_dep=None, scale_col=None):
        ev = [p.guard("pT"), p.guard("pT")]
        last = None
        deps0 = (list(dep) if isinstance(dep, list) else [dep])
        if extra_dep is not None:
            deps0.append(extra_dep)
        for b, (srcap, dstap) in enumerate(blocks):
            fp = (srcap.dtype == F32)
            pts = (pTf_cell[0], pTf_cell[0]) if fp else (pTa, pTb)
            pr, pc = srcap.shape[0], srcap.shape[1]
            pt = pts[b % 2][0:pc, 0:pr]
            tdeps = deps0 + [ev[b % 2]] + ([ev[1 - b % 2]] if fp else [])
            idm = identf if fp else ident
            tt = p.step("tensor", tdeps, lambda t, s=srcap, pt=pt, pr=pr, idm=idm:
                        t.transpose(pt, s, idm[0:pr, 0:pr]))

            if scale_col is None and b % 2 == 1:
                ev[b % 2] = p.step("scalar", [tt], lambda s, pt=pt, d=dstap:
                                   s.activation(d, pt, AF.Copy))
            else:
                def em(v, pt=pt, d=dstap, b=b):
                    if scale_col is None:
                        return v.tensor_copy(d, pt)
                    return v.tensor_scalar(d, pt, scale_col, None, ALU.mult)
                ev[b % 2] = p.step("vector", [tt], em)
            last = ev[b % 2]
        fin = p.step("vector", [e for e in ev if e], lambda v: v.engine_nop()) \
            if len(blocks) > 1 else last
        p.set_guard("pT", fin)
        return fin

    def _finish(tok, src):
        if tok is not None:
            p.out_dma([tok], [(x_out[i * 128:(i + 1) * 128, :],
                               src[:, i * D:(i + 1) * D]) for i in range(NT)])
        else:
            p.out_wait()
        blk_cm = nc.Block()
        block = blk_cm.__enter__()
        bld.finalize(block)
        blk_cm.__exit__(None, None, None)
        return nc

    tok_xn = ln_to(xn, x, tok_x)

    ew2 = p.sb("ew2", [128, NT * ES], BF16)
    rsum2 = p.sb("rsum2", [128, NT])
    slp = [p.sb("slp0", [128, D + 8]), p.sb("slp1", [128, D + 8])]
    fg = p.sb("fg", [128, D]); fb = p.sb("fb", [128, D]); sclt = p.sb("sclt", [128, 8])
    tok_fgb = p.dma("sync", [], [(fg[:], f_gamma_b), (fb[:], f_beta_b),
                                 (sclt[:, 0:1], scale_b)])
    tok_sabs = p.step("scalar", [tok_fgb], lambda s: s.activation(
        sclt[:, 0:1], sclt[:, 0:1], AF.Abs))
    nscl = p.sb("nscl", [128, 1])
    tok_nscl = p.step("vector", [tok_sabs], lambda v: v.tensor_scalar(
        nscl[:], sclt[:, 0:1], -1.02, -1e-3, ALU.mult, ALU.add))

    mu_u = p.sb("mu_u", [128, 8 * ES], BF16)
    mu_s = p.sb("mu_s", [128, 4 * ES])
    musq16 = p.sb("musq16", [128, 4 * ES], BF16)
    p_mr = p.ps("p_mr", [1, ES])
    tok_half = []
    tok_mu_parts = []
    for half in range(2):
        tok_mu = p.dma("sync", [tok_half[-1]] if tok_half else [], [
            (mu_s[:, j * ES:(j + 1) * ES], mu2d[(half * 4 + j) * 128:(half * 4 + j) * 128 + 128, :])
            for j in range(4)])
        tok_msq = p.step("scalar", [tok_mu] + ([tok_half[-1]] if tok_half else []),
                         lambda s: [
            s.activation(musq16[:, j * ES:(j + 1) * ES], mu_s[:, j * ES:(j + 1) * ES],
                         AF.Square) for j in range(4)][-1])
        tok_mr = p.step("tensor", [tok_msq, tok_const], lambda t, half=half: [
            t.matmul(p_mr[:], ones16[:, 0:1], musq16[:, j * ES:(j + 1) * ES],
                     start=(half == 0 and j == 0), stop=(half == 1 and j == 3))
            for j in range(4)][-1])
        tok_half.append(p.step("vector", [tok_mr], lambda v: v.engine_nop()))
        tok_mu_parts.append((tok_mu, tok_mr))
    mu_rn = p.sb("mu_rn", [1, ES])
    mu_rn16 = p.sb("mu_rn16", [1, ES], BF16)
    mu_rb = p.sb("mu_rb", [128, ES])
    p_mb = p.ps("p_mb", [128, ES])
    tok_mrn = p.step("vector", [tok_half[-1]], lambda v: v.tensor_copy(mu_rn[:], p_mr[:]))
    tok_mrn = p.step("scalar", [tok_mrn], lambda s: s.activation(mu_rn[:], mu_rn[:], AF.Sqrt))
    tok_mrn = p.step("vector", [tok_mrn], lambda v: v.tensor_scalar_max(mu_rn[:], mu_rn[:], NORM_EPS))
    tok_mrn = p.step("vector", [tok_mrn], lambda v: v.reciprocal(mu_rn[:], mu_rn[:]))
    tok_mrn16 = p.step("vector", [tok_mrn], lambda v: v.tensor_copy(mu_rn16[:], mu_rn[:]))
    tok_mb = p.step("tensor", [tok_mrn16], lambda t: t.matmul(
        p_mb[:], ones16[0:1, :], mu_rn16[:], start=True, stop=True))
    tok_mrb = p.step("vector", [tok_mb], lambda v: v.tensor_copy(mu_rb[:], p_mb[:]))
    tok_muh = tok_mrb
    for half in range(2):
        tok_mu2 = p.dma("sync", [tok_muh], [
            (mu_s[:, j * ES:(j + 1) * ES], mu2d[(half * 4 + j) * 128:(half * 4 + j) * 128 + 128, :])
            for j in range(4)])
        tok_muh = p.step("vector", [tok_mu2, tok_mrb], lambda v, half=half: [
            v.tensor_mul(mu_u[:, (half * 4 + j) * ES:(half * 4 + j) * ES + ES],
                         mu_s[:, j * ES:(j + 1) * ES], mu_rb[:]) for j in range(4)][-1])
    tok_muu = tok_muh
    p.free("mu_s", "musq16", "p_mr", "mu_rn", "mu_rn16", "mu_rb", "p_mb")

    xn16t = p.sb("xn16t", [128, D], BF16)
    xnf_t = p.sb("xnf_t", [128, D])
    u16 = p.sb("u16", [128, D], BF16)
    uT = p.sb("uT", [128, D], BF16)
    xnf16 = p.sb("xnf16", [128, D], BF16)
    ewt = p.sb("ewt", [128, 2 * ES], BF16)
    mxc = p.sb("mxc", [128, 4])
    p_sl = [p.ps("p_sl0", [128, 1024]), p.ps("p_sl1", [128, 1024])]
    p_gs = p.ps("p_gs", [128, 2])
    p_lg0 = p.ps("p_lg0", [128, ES])
    p_lg = [p_lg0, p_lg0]

    ln2_stats = ln_to(None, xn, [tok_xn], stats_only=True)

    xn16w = [None, None]; xnTw = [None, None]
    xfw = [None, None]; u16w = [None, None]; uTw = [None, None]
    lgw = [None, None]; eww = [None, None]; xf16w = [None, None]
    slmm_last = [None]
    ew2_toks = []
    for i in range(NT):
        pa = i % 2
        d_a = [tok_xn] + [t_ for t_ in (xnTw[0], xnTw[1]) if t_]
        xn16w[pa] = p.step("vector", d_a, lambda v, i=i: v.tensor_copy(
            xn16t[:, 0:D], xn[:, i * D:(i + 1) * D]))
        blksT = [(xn16t[:, dc * 128: dc * 128 + 128],
                  xnT[:, dc * TLOC + i * 128: dc * TLOC + i * 128 + 128])
                 for dc in range(8)]
        xnTw[pa] = transposes(blksT, [xn16w[pa], tok_const])
        d_b = [ln2_stats, tok_fgb] + ([xf16w[1 - pa]] if xf16w[1 - pa] else []) + \
              ([u16w[1 - pa]] if u16w[1 - pa] else [])
        t_f = p.step("vector", d_b, lambda v, i=i: v.tensor_scalar(
            xnf_t[:, 0:D], xn[:, i * D:(i + 1) * D],
            stats[:, 2 * i:2 * i + 1], stats[:, 32 + i:32 + i + 1],
            ALU.subtract, ALU.mult))
        t_f = p.step("vector", [t_f], lambda v: v.tensor_mul(
            xnf_t[:, 0:D], xnf_t[:, 0:D], fg[:]))
        t_f = p.step("vector", [t_f], lambda v: v.tensor_add(
            xnf_t[:, 0:D], xnf_t[:, 0:D], fb[:]))
        t_n = p.step("scalar", [t_f], lambda s, i=i: s.activation(
            sq2[:, 0:D], xnf_t[:, 0:D], AF.Square,
            accum_out=stats[:, 48 + i:48 + i + 1]))
        t_n = p.step("scalar", [t_n], lambda s, i=i: s.activation(
            stats[:, 48 + i:48 + i + 1], stats[:, 48 + i:48 + i + 1], AF.Sqrt))
        t_n = p.step("vector", [t_n], lambda v, i=i: v.tensor_scalar_max(
            stats[:, 48 + i:48 + i + 1], stats[:, 48 + i:48 + i + 1], NORM_EPS))
        t_n = p.step("vector", [t_n], lambda v, i=i: v.reciprocal(
            stats[:, 48 + i:48 + i + 1], stats[:, 48 + i:48 + i + 1]))
        t_n = p.step("vector", [t_n, tok_sabs], lambda v, i=i: v.tensor_scalar(
            stats[:, 48 + i:48 + i + 1], stats[:, 48 + i:48 + i + 1],
            sclt[:, 0:1], None, ALU.mult))
        d_u = [t_n] + [t_ for t_ in (uTw[0], uTw[1]) if t_]
        u16w[pa] = p.step("vector", d_u, lambda v, i=i: v.tensor_scalar(
            u16[:, 0:D], xnf_t[:, 0:D],
            stats[:, 48 + i:48 + i + 1], None, ALU.mult))
        blksU = [(u16[:, dc * 128: dc * 128 + 128],
                  uT[:, dc * 128: dc * 128 + 128])
                 for dc in range(8)]
        d_e = [u16w[pa]] + [t_ for t_ in (lgw[0], lgw[1]) if t_]
        uTw[pa] = transposes(blksU, d_e)
        d_fm = [uTw[pa], tok_muu] + [t for t in eww if t]
        lgw[pa] = p.step("tensor", d_fm, lambda t, pa=pa: [
            t.matmul(p_lg[pa][:], uT[:, dc * 128: dc * 128 + 128],
                     mu_u[:, dc * ES:(dc + 1) * ES], start=(dc == 0), stop=(dc == 7))
            for dc in range(8)][-1])
        d_g = [lgw[pa], tok_nscl] + ([slmm_last[0]] if slmm_last[0] and i >= 2 else [])
        eww[pa] = p.step("scalar", d_g, lambda s, i=i, pa=pa: s.activation(
            ewt[:, pa * ES:(pa + 1) * ES], p_lg[pa][:], AF.Exp, bias=nscl[:, 0:1],
            accum_out=stats[:, 64 + i:64 + i + 1]))
        t_m = p.step("vector", [eww[pa]], lambda v, i=i, pa=pa: v.reduce_max(
            mxc[:, pa:pa + 1], ewt[:, pa * ES:(pa + 1) * ES], axis=AX.X))
        t_m2 = p.step("vector", [t_m], lambda v, i=i, pa=pa: v.reciprocal(
            mxc[:, 2 + pa: 3 + pa], mxc[:, pa:pa + 1]))
        tok_e2 = p.step("vector", [t_m2], lambda v, i=i, pa=pa: v.tensor_scalar(
            ew2[:, i * ES:(i + 1) * ES], ewt[:, pa * ES:(pa + 1) * ES],
            mxc[:, 2 + pa:3 + pa], None, ALU.mult))
        ew2_toks.append(tok_e2)
        t_rs = p.step("vector", [eww[pa]], lambda v, i=i: v.reciprocal(
            stats[:, 80 + i:80 + i + 1], stats[:, 64 + i:64 + i + 1]))
        t_rs = p.step("vector", [t_rs, t_m], lambda v, i=i, pa=pa: v.tensor_scalar(
            rsum2[:, i:i + 1], mxc[:, pa:pa + 1], stats[:, 80 + i:80 + i + 1],
            None, ALU.mult))
        ew2_toks.append(t_rs)
        d_j = [t_f] + ([slmm_last[0]] if slmm_last[0] and i >= 2 else [])
        xf16w[pa] = p.step("vector", d_j + [t_ for t_ in (slmm_last[0],) if t_],
                           lambda v: v.tensor_copy(xnf16[:, 0:D], xnf_t[:, 0:D]))
        slmm_last[0] = p.step("tensor", [xf16w[pa], eww[pa], tok_const],
                              lambda t, i=i, pa=pa: ([
            t.matmul(p_sl[ec][:, nn * 512:(nn + 1) * 512],
                     ewt[:, pa * ES + ec * 128: pa * ES + ec * 128 + 128],
                     xnf16[:, nn * 512: nn * 512 + 512],
                     start=(i == 0), stop=(i == NT - 1))
            for ec in range(2) for nn in range(2)] + [
            t.matmul(p_gs[:, ec:ec + 1],
                     ewt[:, pa * ES + ec * 128: pa * ES + ec * 128 + 128],
                     ones16[:, 0:1], start=(i == 0 and ec == 0),
                     stop=(i == NT - 1 and ec == 1))
            for ec in range(2)])[-1])
    tok_ew2 = p.step("vector", ew2_toks[-4:], lambda v: v.engine_nop())
    tok_slpz = p.step("vector", [], lambda v: [
        v.memset(slp[ec][:, D + 1:D + 8], 0.0) for ec in range(2)][-1])
    sl_ev = [p.step("vector", [slmm_last[0], tok_slpz], lambda v, ec=ec: [
        v.tensor_copy(slp[ec][:, 0:512], p_sl[ec][:, 0:512]),
        v.tensor_copy(slp[ec][:, 512:1024], p_sl[ec][:, 512:1024]),
        v.tensor_copy(slp[ec][:, D:D + 1], p_gs[:, ec:ec + 1]),
    ][-1]) for ec in range(2)]
    tok_xnT = p.step("vector", [xnTw[0], xnTw[1]], lambda v: v.engine_nop())

    tok_ffs = p.dma("gpsimd", sl_ev, [(ffn_snd.ap()[0:128, :], slp[0][:]),
                                      (ffn_snd.ap()[128:256, :], slp[1][:])])
    tok_ffc = p.coll([tok_ffs], "AllGather", ALU.bypass, PAIRS,
                     [ffn_snd.ap().opt()], [ffn_rcv.ap().opt()])
    p.free("xn16t", "xnf_t", "u16", "uT", "xnf16", "ewt", "mxc",
           "p_lg0", "mu_u", "fg", "fb", "nscl", "sclt")
    p.free("slp0", "slp1")
    p.free("p_sl0", "p_sl1", "p_gs")

    if upto < 4:
        return _finish(tok_ffc, xn)

    bar_p4 = p.barrier()
    slotT = p.sb("slotT", [128, 8 * ES], BF16)
    sli16 = [p.sb("sli16_0", [128, D], BF16), p.sb("sli16_1", [128, D], BF16)]
    sli = [p.sb("sli0", [128, D + 8]), p.sb("sli1", [128, D + 8])]
    slr = [p.sb("slr00", [128, D + 8]), p.sb("slr01", [128, D + 8]),
           p.sb("slr10", [128, D + 8]), p.sb("slr11", [128, D + 8])]
    tok_ffr = p.dma("gpsimd", [tok_ffc, tok_ew2, tok_xnT, bar_p4], [
        (slr[0][:], ffn_rcv.ap()[0:128, :]), (slr[1][:], ffn_rcv.ap()[128:256, :]),
        (slr[2][:], ffn_rcv.ap()[256:384, :]), (slr[3][:], ffn_rcv.ap()[384:512, :])])
    tok_sli = [p.step("vector", [tok_ffr], lambda v, ec=ec: v.tensor_add(
        sli[ec][:], slr[ec][:], slr[2 + ec][:])) for ec in range(2)]
    tok_gsr = [p.step("vector", [tok_sli[ec], p.guard("stats")], lambda v, ec=ec:
        v.reciprocal(stats[:, 90 + ec:91 + ec], sli[ec][:, D:D + 1])) for ec in range(2)]
    tok_sli16 = [p.step("vector", [tok_gsr[ec]], lambda v, ec=ec: v.tensor_scalar(
        sli16[ec][:], sli[ec][:, 0:D], stats[:, 90 + ec:91 + ec], None, ALU.mult))
        for ec in range(2)]
    p.free("slr00", "slr01", "slr10", "slr11", "sli0", "sli1")
    blks_s = []
    for ec in range(2):
        for dc in range(8):
            blks_s.append((sli16[ec][:, dc * 128:(dc + 1) * 128],
                           slotT[:, dc * ES + ec * 128: dc * ES + ec * 128 + 128]))
    tok_slotT = transposes(blks_s, tok_sli16[0], extra_dep=tok_sli16[1])
    p.free("sli16_0", "sli16_1")
    bar_slr = p.barrier()

    so16 = [p.sb("so16_0", [128, D], BF16), p.sb("so16_1", [128, D], BF16)]
    so_stage = [p.sb("so_st0", [16, D], BF16), p.sb("so_st1", [16, D], BF16)]
    b1s = p.sb("b1s", [128, 2 * NE])
    tok_b1 = p.dma("sync", [bar_slr], [(b1s[:, e * 2 + hc: e * 2 + hc + 1],
                                 b1_in[e:e + 1, hc * 128:(hc + 1) * 128])
                                for e in range(NE) for hc in range(2)])
    b2s16 = p.sb("b2s16", [33, D], BF16)
    w1b = [p.sb("w1b0", [128, 8 * FH], BF16), p.sb("w1b1", [128, 8 * FH], BF16)]
    w2b = [p.sb("w2b0", [128, 2 * D], BF16), p.sb("w2b1", [128, 2 * D], BF16)]
    hT = [p.sb("hT0", [128, 32], BF16), p.sb("hT1", [128, 32], BF16)]
    p_h = [p.ps("p_h0", [128, 32]), p.ps("p_h1", [128, 32])]
    p_so = [p.ps("p_so0", [16, 1024]), p.ps("p_so1", [16, 1024])]
    w1_tok = [None, None]; w2_tok = [None, None]
    hmm_tok = [None, None]; hact_tok = [None, None]
    somm_tok = [None, None]; soev_tok = [None, None]
    for e in range(NE):
        pa = e % 2
        d1 = [bar_slr] + ([hmm_tok[pa]] if hmm_tok[pa] else [])
        w1_tok[pa] = p.dma("gpsimd", d1, [
            (w1b[pa][:, dc * FH:(dc + 1) * FH],
             w1f[e * D + dc * 128: e * D + dc * 128 + 128, :]) for dc in range(8)])
        d2 = [bar_slr] + ([somm_tok[pa]] if somm_tok[pa] else [])
        w2_tok[pa] = p.dma("gpsimd", d2, [
            (w2b[pa][:, hc * D:(hc + 1) * D],
             w2f[e * FH + hc * 128: e * FH + hc * 128 + 128, :]) for hc in range(2)] + [
            (b2s16[32 * pa:32 * pa + 1, :], b2_in[e:e + 1, :])])
        hdeps = [w1_tok[pa], tok_slotT] + ([hact_tok[pa]] if hact_tok[pa] else [])
        hmm_tok[pa] = p.step("tensor", hdeps, lambda t, e=e, pa=pa: [
            t.matmul(p_h[pa][:, hc * 16:(hc + 1) * 16],
                     w1b[pa][:, dc * FH + hc * 128: dc * FH + hc * 128 + 128],
                     slotT[:, dc * ES + e * 16: dc * ES + e * 16 + 16],
                     start=(dc == 0), stop=(dc == 7))
            for hc in range(2) for dc in range(8)][-1])
        hact_deps = [hmm_tok[pa], tok_b1] + ([somm_tok[pa]] if somm_tok[pa] else [])
        hact_tok[pa] = p.step("scalar", hact_deps, lambda s, e=e, pa=pa: [
            s.activation(hT[pa][:, hc * 16:(hc + 1) * 16], p_h[pa][:, hc * 16:(hc + 1) * 16],
                         AF.Gelu, bias=b1s[:, e * 2 + hc: e * 2 + hc + 1])
            for hc in range(2)][-1])
        sdeps = [hact_tok[pa], w2_tok[pa]] + ([soev_tok[pa]] if soev_tok[pa] else [])
        somm_tok[pa] = p.step("tensor", sdeps, lambda t, e=e, pa=pa: ([
            t.matmul(p_so[pa][:, n2 * 512:(n2 + 1) * 512],
                     hT[pa][:, hc * 16:(hc + 1) * 16],
                     w2b[pa][:, hc * D + n2 * 512: hc * D + n2 * 512 + 512],
                     start=(hc == 0), stop=False)
            for n2 in range(2) for hc in range(2)] + [
            t.matmul(p_so[pa][:, n2 * 512:(n2 + 1) * 512],
                     ones16[32 * pa:32 * pa + 1, 0:16],
                     b2s16[32 * pa:32 * pa + 1, n2 * 512: n2 * 512 + 512],
                     start=False, stop=True)
            for n2 in range(2)])[-1])
        sodeps = [somm_tok[pa]] + ([soev_tok[pa]] if soev_tok[pa] else [])
        so_cp = p.step("vector", sodeps, lambda v, pa=pa:
            v.tensor_copy(so_stage[pa][:], p_so[pa][:, 0:1024]))
        soev_tok[pa] = p.dma("gpsimd", [so_cp], [
            (so16[e // 8][(e % 8) * 16:(e % 8) * 16 + 16, :], so_stage[pa][:])])
    tok_so = p.step("vector", [t for t in soev_tok if t], lambda v: v.engine_nop())
    p.free("w1b0", "w1b1", "w2b0", "w2b1", "hT0", "hT1", "p_h0", "p_h1",
           "p_so0", "p_so1", "b2s16", "b1s", "so_st0", "so_st1")

    ew2T = [p.sb("ew2T0", [128, TLOC], BF16), p.sb("ew2T1", [128, TLOC], BF16)]
    p_cb = [p.ps("p_cb0", [128, 1024]), p.ps("p_cb1", [128, 1024])]
    blks_e = []
    for ec in range(2):
        for i in range(NT):
            blks_e.append((ew2[:, i * ES + ec * 128: i * ES + ec * 128 + 128],
                           ew2T[ec][:, i * 128:(i + 1) * 128]))
    tok_ew2T = transposes(blks_e, [tok_ew2, bar_p4, tok_so])
    cb_mm = [None, None]; cb_ev = [None, None]
    for i in range(NT):
        pa = i % 2
        deps = [tok_ew2T, tok_so] + ([cb_ev[pa]] if cb_ev[pa] else [])
        cb_mm[pa] = p.step("tensor", deps, lambda t, i=i, pa=pa: [
            t.matmul(p_cb[pa][:, n2 * 512:(n2 + 1) * 512],
                     ew2T[ec][:, i * 128:(i + 1) * 128],
                     so16[ec][:, n2 * 512: n2 * 512 + 512],
                     start=(ec == 0), stop=(ec == 1))
            for n2 in range(2) for ec in range(2)][-1])
        cb_ev[pa] = p.step("vector", [cb_mm[pa], tok_ew2], lambda v, i=i, pa=pa: [
            v.scalar_tensor_tensor(x[:, i * D + n2 * 512: i * D + n2 * 512 + 512],
                                   p_cb[pa][:, n2 * 512:(n2 + 1) * 512],
                                   rsum2[:, i:i + 1],
                                   x[:, i * D + n2 * 512: i * D + n2 * 512 + 512],
                                   ALU.mult, ALU.add)
            for n2 in range(2)][-1])
    tok_ffn0 = p.step("vector", [t for t in cb_ev if t], lambda v: v.engine_nop())
    p.free("p_cb0", "p_cb1", "ew2T0", "ew2T1", "so16_0", "so16_1", "slotT",
           "ew2", "rsum2")
    p.free("xn")

    if upto < 5:
        return _finish(tok_ffn0, x)

    bar0 = p.barrier()
    p.set_guard("pT", bar0)

    rcq = p.sb("rcq", [128, TLOC], BF16); rsq = p.sb("rsq", [128, TLOC], BF16)
    rck = p.sb("rck", [128, TLOC], BF16); rsk = p.sb("rsk", [128, TLOC], BF16)
    maskT_s = p.sb("maskT_s", [128, RH * RC])
    cdec_s = p.sb("cdec_s", [128, RH * RC])
    kdec_s = p.sb("kdec_s", [128, RH])
    flag_s = p.sb("flag_s", [128, 1])
    gnw_s = p.sb("gnw_s", [128, RDV])
    tstage = p.sb("tstage", [128, TLOC])
    tok_tb = bar0
    for tin, tout in ((rope_cos_q, rcq), (rope_sin_q, rsq), (rope_cos, rck), (rope_sin, rsk)):
        tok_ld = p.dma("sync", [tok_tb, bar0], [(tstage[:], tin)])
        tok_tb = p.step("vector", [tok_ld], lambda v, tout=tout: v.tensor_copy(tout[:], tstage[:]))
    tok_rc = p.dma("sync", [bar0], [
        (gnw_s[:], gnw_b), (flag_s[:], ret_flag)] + [
        (maskT_s[:, h * RC:(h + 1) * RC], ret_maskT[h]) for h in range(RH)] + [
        (cdec_s[:, h * RC:(h + 1) * RC], ret_cdec[h]) for h in range(RH)] + [
        (kdec_s[:, h:h + 1], ret_kdec[h]) for h in range(RH)])
    tok_rc = p.step("vector", [tok_rc, tok_tb], lambda v: v.engine_nop())
    p.free("tstage")

    NSC = 4
    SCH = TLOC // NSC
    CPS = SCH // RC
    tok_l0 = tok_ffn0

    p_pj = p.ps("p_pj", [128, 1024])
    p_vps = p_pj
    rtmp = p.sb("rtmp", [128, 4 * SCH], BF16)
    rt2 = p.sb("rt2", [128, 4 * SCH], BF16)
    qd16 = p.sb("qd16", [128, 2 * 256], BF16)
    smid16 = p.sb("smid16", [128, 2 * 512], BF16)
    q16 = p.sb("q16", [128, 2 * TLOC], BF16)

    def proj_rope1(dst16, wslc, tcol0, cosT, sinT, deps, slot=0, dcol=0):
        deps = list(deps)
        evg = p.guard("proj_ev")
        rg = p.guard(f"projS{slot}")
        off = slot * 2 * SCH
        ev = []
        for oc in range(2):
            mm = p.step("tensor", deps + [evg], lambda t, oc=oc, wslc=wslc,
                        tcol0=tcol0: [
                t.matmul(p_pj[:, oc * 512: oc * 512 + 512],
                         wslc[:, dc * 256 + oc * 128: dc * 256 + oc * 128 + 128],
                         xnT[:, dc * TLOC + tcol0: dc * TLOC + tcol0 + SCH],
                         start=(dc == 0), stop=(dc == 7)) for dc in range(8)][-1])
            ev.append(p.step("vector", [mm, rg] + deps, lambda v, oc=oc, off=off:
                v.tensor_copy(rtmp[:, off + oc * SCH: off + (oc + 1) * SCH],
                              p_pj[:, oc * 512: oc * 512 + 512])))
        tok_raw = p.step("vector", ev, lambda v: v.engine_nop())
        p.set_guard("proj_ev", tok_raw)
        cs = cosT[:, tcol0:tcol0 + SCH]
        sn = sinT[:, tcol0:tcol0 + SCH]
        a0 = rtmp[:, off:off + SCH]; a1 = rtmp[:, off + SCH:off + 2 * SCH]
        b0 = rt2[:, off:off + SCH]; b1 = rt2[:, off + SCH:off + 2 * SCH]
        tA = p.step("vector", [tok_raw], lambda v, a0=a0, b0=b0, cs=cs:
                    v.tensor_mul(b0, a0, cs))
        tB = p.step("gpsimd", [tok_raw], lambda g_, a1=a1, b1=b1, sn=sn:
                    g_.tensor_mul(b1, a1, sn))
        r0 = p.step("vector", [tA, tB] + deps, lambda v, b0=b0, b1=b1,
                    dst16=dst16, dcol=dcol: v.tensor_sub(
                        dst16[:, dcol:dcol + SCH], b0, b1))
        tC = p.step("vector", [r0], lambda v, a1=a1, b0=b0, cs=cs:
                    v.tensor_mul(b0, a1, cs))
        tD = p.step("gpsimd", [r0], lambda g_, a0=a0, b1=b1, sn=sn:
                    g_.tensor_mul(b1, a0, sn))
        r1 = p.step("vector", [tC, tD] + deps, lambda v, b0=b0, b1=b1,
                    dst16=dst16, dcol=dcol: v.tensor_add(
                        dst16[:, dcol + SCH:dcol + 2 * SCH], b0, b1))
        p.set_guard(f"projS{slot}", r1)
        return r1

    st_bounce = [st_snd, st_snd1]
    st_rbounce = [st_rcv, st_rcv1]
    wq_handles = {}
    o_buf = {}
    tok_h_done = {}
    for h in range(RH):
        pair_i = h % 2
        if True:
            wqb = p.sb(f"wqb{h}", [128, 8 * 256], BF16)
            wq_handles[h] = wqb
            o_h = p.sb(f"o_h{h}", [128, RNC * 512])
            o_buf[h] = o_h
            wkb = p.sb(f"wkb{h}", [128, 8 * 256], BF16)
            wvb = p.sb(f"wvb{h}", [128, 8 * 512], BF16)
            k16 = p.sb(f"k16_{h}", [128, 2 * SCH], BF16)
            v16 = p.sb(f"v16_{h}", [128, 4 * 512], BF16)
            ktm = p.sb(f"ktm_{h}", [128, 2 * 256], BF16)
            S_f = p.sb(f"S_f{h}", [128, 2 * 512])
            S16 = p.sb(f"S16_{h}", [128, 2 * 512], BF16)
            at16 = p.sb(f"at16_{h}", [128, 2 * 128], BF16)
            p_at = p.ps(f"p_at{h}", [128, 128])
            p_o = p.ps(f"p_o{h}", [128, 512])
            p_ds = p.ps(f"p_ds{h}", [128, 1024])
            tok_wq = p.dma("gpsimd", [bar0, p.guard("wq")], [
                (wqb[:, dc * 256:(dc + 1) * 256],
                 WqT[dc * 128:(dc + 1) * 128, h * 256: h * 256 + 256]) for dc in range(8)])
            tok_wk = p.dma("gpsimd", [bar0, p.guard("wq")], [
                (wkb[:, dc * 256:(dc + 1) * 256],
                 WkT[dc * 128:(dc + 1) * 128, h * 256: h * 256 + 256]) for dc in range(8)])
            tok_wv = p.dma("gpsimd", [bar0, p.guard("wq")], [
                (wvb[:, dc * 512:(dc + 1) * 512],
                 WvT[dc * 128:(dc + 1) * 128, h * 512: h * 512 + 512]) for dc in range(8)])
            tok_z = p.step("vector", [tok_l0, tok_rc, bar0, p.guard("xadd")],
                           lambda v, S_f=S_f: v.memset(S_f[:], 0.0))
            tok_z16 = p.step("scalar", [tok_z], lambda s, S16=S16: s.memzero(S16[:]))

            s16_last = [tok_z16]
            supd_last = [tok_z]
            oev_last = [tok_z]
            atev_last = [tok_z]
            ds_last = [tok_z16]
            qdw = [None, None]
            atw = [None, None]
            scT = [tok_z16]; scV = [tok_z]
            oev_toks = []
            for sc in range(NSC):
                tcol0 = sc * SCH
                pdeps = [tok_rc, tok_xnT, scT[0], scV[0], p.guard("xadd")]
                tq = proj_rope1(q16, wqb, tcol0, rcq, rsq,
                                pdeps + [tok_wq, p.guard("q16")], slot=0,
                                dcol=sc * 2 * SCH)
                tk = proj_rope1(k16, wkb, tcol0, rck, rsk,
                                pdeps + [tok_wk], slot=1)
                vev = [None, None]
                for tt_ in range(4):
                    pa_ = tt_ % 2
                    vdeps = pdeps + [tok_wv, p.guard("proj_ev")] + \
                            ([vev[pa_]] if vev[pa_] else [])
                    vmm = p.step("tensor", vdeps, lambda t, tt_=tt_, pa_=pa_,
                                 wvb=wvb, tcol0=tcol0: [
                        t.matmul(p_vps[:, pa_ * 512: pa_ * 512 + 512],
                                 xnT[:, dc * TLOC + tcol0 + tt_ * 128:
                                     dc * TLOC + tcol0 + tt_ * 128 + 128],
                                 wvb[:, dc * 512:(dc + 1) * 512],
                                 start=(dc == 0), stop=(dc == 7)) for dc in range(8)][-1])
                    vev[pa_] = p.step("vector", [vmm] + pdeps, lambda v, tt_=tt_, pa_=pa_,
                                      v16=v16: v.tensor_copy(
                        v16[:, tt_ * 512:(tt_ + 1) * 512],
                        p_vps[:, pa_ * 512: pa_ * 512 + 512]))
                tv = p.step("vector", [e for e in vev if e], lambda v: v.engine_nop())
                p.set_guard("proj_ev", tv)
                for cc in range(CPS):
                    c = sc * CPS + cc
                    pa = c % 2
                    kblks = [(k16[:, blk * SCH + cc * 128: blk * SCH + cc * 128 + 128],
                              ktm[:, pa * 256 + blk * 128: pa * 256 + blk * 128 + 128])
                             for blk in range(2)]
                    tok_ktm = transposes(kblks, [tk, ds_last[0]],
                                         scale_col=kdec_s[:, h:h + 1])
                    qdeps = [tq, tok_rc] + ([qdw[pa]] if qdw[pa] else [])
                    tok_qd = p.step("vector", qdeps, lambda v, pa=pa, cc=cc, h=h,
                                    qo=sc * 2 * SCH: [
                        v.tensor_mul(qd16[:, pa * 256 + blk * 128: pa * 256 + blk * 128 + 128],
                                     q16[:, qo + blk * SCH + cc * 128:
                                         qo + blk * SCH + cc * 128 + 128],
                                     cdec_s[:, h * RC:(h + 1) * RC]) for blk in range(2)][-1])
                    amm = p.step("tensor", [tk, tq, atev_last[0]],
                                 lambda t, cc=cc, p_at=p_at, k16=k16,
                                 qo=sc * 2 * SCH: [
                        t.matmul(p_at[:],
                                 k16[:, blk * SCH + cc * 128: blk * SCH + cc * 128 + 128],
                                 q16[:, qo + blk * SCH + cc * 128:
                                     qo + blk * SCH + cc * 128 + 128],
                                 start=(blk == 0), stop=(blk == 1)) for blk in range(2)][-1])
                    aev = p.step("vector", [amm, tok_rc] + ([atw[pa]] if atw[pa] else []),
                                 lambda v, pa=pa, h=h, p_at=p_at, at16=at16: v.tensor_mul(
                        at16[:, pa * 128:(pa + 1) * 128], p_at[:],
                        maskT_s[:, h * RC:(h + 1) * RC]))
                    atev_last[0] = aev
                    omm = p.step("tensor", [aev, tok_qd, tv, oev_last[0]] + s16_last,
                                 lambda t, pa=pa, cc=cc, p_o=p_o, at16=at16, S16=S16, v16=v16: ([
                        t.matmul(p_o[:], at16[:, pa * 128:(pa + 1) * 128],
                                 v16[:, cc * 512:(cc + 1) * 512], start=True, stop=False)] + [
                        t.matmul(p_o[:], qd16[:, pa * 256 + blk * 128: pa * 256 + blk * 128 + 128],
                                 S16[:, blk * 512:(blk + 1) * 512],
                                 start=False, stop=(blk == 1)) for blk in range(2)])[-1])
                    qdw[pa] = omm
                    atw[pa] = omm
                    oev = p.step("vector", [omm], lambda v, c=c, p_o=p_o, o_h=o_h:
                        v.tensor_copy(o_h[:, c * 512:(c + 1) * 512], p_o[:]))
                    oev_last[0] = oev
                    oev_toks.append(oev)
                    dmm = p.step("tensor", [tok_ktm, tv] + supd_last,
                                 lambda t, pa=pa, cc=cc, p_ds=p_ds, ktm=ktm, v16=v16: [
                        t.matmul(p_ds[:, blk * 512:(blk + 1) * 512],
                                 ktm[:, pa * 256 + blk * 128: pa * 256 + blk * 128 + 128],
                                 v16[:, cc * 512:(cc + 1) * 512],
                                 start=True, stop=True) for blk in range(2)][-1])
                    ds_last[0] = dmm
                    gC = GAMMAS[h] ** RC
                    supd0 = p.step("vector", [dmm] + s16_last,
                                   lambda v, gC=gC, S_f=S_f, p_ds=p_ds:
                        v.scalar_tensor_tensor(S_f[:, 0:512], S_f[:, 0:512], gC,
                                               p_ds[:, 0:512], ALU.mult, ALU.add))
                    supd1 = p.step("vector", [dmm] + s16_last,
                                   lambda v, gC=gC, S_f=S_f, p_ds=p_ds:
                        v.scalar_tensor_tensor(S_f[:, 512:1024], S_f[:, 512:1024],
                                               gC, p_ds[:, 512:1024],
                                               ALU.mult, ALU.add))
                    supd_last = [supd0, supd1]
                    s16a = p.step("scalar", [supd0, omm],
                                  lambda s, S16=S16, S_f=S_f:
                        s.copy(S16[:, 0:512], S_f[:, 0:512]))
                    s16b = p.step("gpsimd", [supd1, omm],
                                  lambda g_, S16=S16, S_f=S_f:
                        g_.tensor_copy(S16[:, 512:1024], S_f[:, 512:1024]))
                    s16_last = [s16a, s16b]
                scT[0] = p.merge("tensor", [])
                scV[0] = p.step("vector", [], lambda v: v.engine_nop())
            tok_scan = p.step("vector", oev_toks + s16_last + supd_last,
                              lambda v: v.engine_nop())
            p.set_guard("q16", tok_scan)
            tok_h_done[h] = tok_scan
            sb_ = st_bounce[pair_i]
            tok_sp = p.dma("gpsimd", [tok_scan, p.guard(f"stb{pair_i}")], [
                (sb_.ap()[blk * 128: blk * 128 + 128, :],
                 S_f[:, blk * 512:(blk + 1) * 512]) for blk in range(2)])
            tok_h_done[h] = tok_sp
            p.free(f"wkb{h}", f"wvb{h}", f"k16_{h}", f"v16_{h}", f"ktm_{h}",
                   f"S_f{h}", f"S16_{h}", f"at16_{h}",
                   f"p_at{h}", f"p_o{h}", f"p_ds{h}")
            p.set_guard("wq", tok_scan)

        rb_ = st_rbounce[pair_i]
        tok_scoll = p.coll([tok_h_done[h]], "AllGather", ALU.bypass,
                           PAIRS, [st_bounce[pair_i].ap().opt()], [rb_.ap().opt()])
        tok_sm = p.dma("gpsimd", [tok_scoll, p.guard("smid")], [
            (smid16[:, sl * 512:(sl + 1) * 512], rb_.ap()[sl * 128:(sl + 1) * 128, :])
            for sl in range(2)])
        tok_sm16 = p.step("vector", [tok_sm, tok_rc], lambda v: [
            v.tensor_scalar(smid16[:, sl * 512:(sl + 1) * 512],
                            smid16[:, sl * 512:(sl + 1) * 512],
                            flag_s[:, 0:1], None, ALU.mult) for sl in range(2)][-1])
        p.set_guard(f"stb{pair_i}", tok_scoll)

        if True:
            o_h = o_buf[h]
            wgb = p.sb(f"wgb{h}", [128, 8 * 512], BF16)
            wob = p.sb(f"wob{h}", [128, 4 * 1024], BF16)
            tok_wg = p.dma("gpsimd", [bar0, p.guard("wgb"), tok_scan], [
                (wgb[:, dc * 512:(dc + 1) * 512],
                 WgT[dc * 128:(dc + 1) * 128, h * 512: h * 512 + 512]) for dc in range(8)])
            tok_wo = p.dma("gpsimd", [bar0, p.guard("wob"), tok_scan], [
                (wob[:, ic * 1024:(ic + 1) * 1024],
                 WoT[h * 512 + ic * 128: h * 512 + ic * 128 + 128, :]) for ic in range(4)])
            sg16 = p.sb(f"sg16_{h}", [128, 2 * 512], BF16)
            og16 = p.sb(f"og16_{h}", [128, 2 * 512], BF16)
            ogT = p.sb(f"ogT_{h}", [128, 2 * 512], BF16)
            t_on = p.sb(f"t_on_{h}", [128, 2 * 512])
            vst = p.sb(f"vst_{h}", [128, RNC])
            p_c = p.ps(f"p_c_{h}", [128, 512])
            p_g = p.ps(f"p_g_{h}", [128, 512])
            p_w = p.ps(f"p_w_{h}", [128, 1024])
            cadd_last = [tok_sm16]
            tsil_last = [tok_sm16]
            xadd_last = [tok_sm16, p.guard("xadd")]
            qdw2 = [None, None]
            sgw = [None, None]
            ogw = [None, None]
            ogTw = [None, None]
            tonw = [None, None]
            sqw = [None, None]
            sq_toks = []
            smoff = 0
            for c in range(RNC):
                pa = c % 2
                gCj = GAMMAS[h] ** (RC * c)
                qo = (c // CPS) * 2 * SCH
                cc = c % CPS
                qdeps = [tok_scan, tok_rc] + ([qdw2[pa]] if qdw2[pa] else [])
                tok_qdg = p.step("vector", qdeps,
                                 lambda v, pa=pa, cc=cc, h=h, gCj=gCj, qo=qo: [
                    v.scalar_tensor_tensor(
                        qd16[:, pa * 256 + blk * 128: pa * 256 + blk * 128 + 128],
                        q16[:, qo + blk * SCH + cc * 128:
                            qo + blk * SCH + cc * 128 + 128],
                        gCj, cdec_s[:, h * RC:(h + 1) * RC],
                        ALU.mult, ALU.mult) for blk in range(2)][-1])
                cmm = p.step("tensor", [tok_qdg, tok_sm16, cadd_last[0]],
                             lambda t, pa=pa, p_c=p_c, smoff=smoff: [
                    t.matmul(p_c[:],
                             qd16[:, pa * 256 + blk * 128: pa * 256 + blk * 128 + 128],
                             smid16[:, smoff + blk * 512: smoff + blk * 512 + 512],
                             start=(blk == 0), stop=(blk == 1)) for blk in range(2)][-1])
                qdw2[pa] = cmm
                cadd = p.step("vector", [cmm], lambda v, c=c, p_c=p_c, o_h=o_h:
                    v.tensor_add(o_h[:, c * 512:(c + 1) * 512],
                                 o_h[:, c * 512:(c + 1) * 512], p_c[:]))
                cadd_last[0] = cadd
                svdeps = [cadd] + ([sqw[pa]] if sqw[pa] else [])
                sqw[pa] = p.step("scalar", svdeps,
                                 lambda s, c=c, pa=pa, o_h=o_h, vst=vst:
                    s.activation(sq2[:, (pa % 2) * 512:(pa % 2) * 512 + 512],
                                 o_h[:, c * 512:(c + 1) * 512],
                                 AF.Square, accum_out=vst[:, c:c + 1]))
                sq_toks.append(sqw[pa])
            tvb = p.step("vector", sq_toks, lambda v, vst=vst: v.tensor_scalar(
                vst[:], vst[:], 1.0 / RDV, LN_EPS, ALU.mult, ALU.add))
            tvb = p.step("scalar", [tvb], lambda s, vst=vst: s.activation(
                vst[:], vst[:], AF.Sqrt))
            tvb = p.step("vector", [tvb], lambda v, vst=vst: v.reciprocal(
                vst[:], vst[:]))
            for c in range(RNC):
                pa = c % 2
                gmm = p.step("tensor", [tok_wg, tok_xnT, tsil_last[0]],
                             lambda t, c=c, p_g=p_g, wgb=wgb: [
                    t.matmul(p_g[:],
                             xnT[:, dc * TLOC + c * 128: dc * TLOC + c * 128 + 128],
                             wgb[:, dc * 512:(dc + 1) * 512],
                             start=(dc == 0), stop=(dc == 7)) for dc in range(8)][-1])
                tsil = p.step("scalar", [gmm] + ([sgw[pa]] if sgw[pa] else []),
                              lambda s, pa=pa, p_g=p_g, sg16=sg16: s.activation(
                    sg16[:, pa * 512:(pa + 1) * 512], p_g[:], AF.Silu))
                tsil_last[0] = tsil
                tdeps = [tvb, tok_rc] + ([tonw[pa]] if tonw[pa] else [])
                ton = p.step("vector", tdeps, lambda v, c=c, pa=pa, o_h=o_h,
                             vst=vst, t_on=t_on: v.scalar_tensor_tensor(
                    t_on[:, pa * 512:(pa + 1) * 512],
                    o_h[:, c * 512:(c + 1) * 512],
                    vst[:, c:c + 1], gnw_s[:], ALU.mult, ALU.mult))
                ogdeps = [ton, tsil] + ([ogw[pa]] if ogw[pa] else [])
                tog = p.step("gpsimd", ogdeps, lambda g_, pa=pa, og16=og16,
                             t_on=t_on, sg16=sg16: g_.tensor_mul(
                    og16[:, pa * 512:(pa + 1) * 512],
                    t_on[:, pa * 512:(pa + 1) * 512],
                    sg16[:, pa * 512:(pa + 1) * 512]))
                sgw[pa] = tog
                tonw[pa] = tog
                oblks = [(og16[:, pa * 512 + ic * 128: pa * 512 + ic * 128 + 128],
                          ogT[:, pa * 512 + ic * 128: pa * 512 + ic * 128 + 128])
                         for ic in range(4)]
                tok_ogT = transposes(oblks, [tog] + ([ogTw[pa]] if ogTw[pa] else []))
                ogw[pa] = tok_ogT
                wmm = p.step("tensor", [tok_ogT, tok_wo] + xadd_last,
                             lambda t, pa=pa, p_w=p_w, ogT=ogT, wob=wob: [
                    t.matmul(p_w[:, n2 * 512:(n2 + 1) * 512],
                             ogT[:, pa * 512 + ic * 128: pa * 512 + ic * 128 + 128],
                             wob[:, ic * 1024 + n2 * 512: ic * 1024 + n2 * 512 + 512],
                             start=(ic == 0), stop=(ic == 3))
                    for n2 in range(2) for ic in range(4)][-1])
                ogTw[pa] = wmm
                xadd = p.step("vector", [wmm], lambda v, c=c, p_w=p_w: [
                    v.tensor_add(x[:, c * D + n2 * 512: c * D + n2 * 512 + 512],
                                 x[:, c * D + n2 * 512: c * D + n2 * 512 + 512],
                                 p_w[:, n2 * 512:(n2 + 1) * 512]) for n2 in range(2)][-1])
                xadd_last = [xadd]
            tok_epi_h = p.step("vector", [xadd_last[0], cadd_last[0]],
                               lambda v: v.engine_nop())
            p.set_guard("xadd", tok_epi_h)
            p.set_guard("smid", tok_epi_h)
            p.set_guard("wgb", tok_epi_h); p.set_guard("wob", tok_epi_h)
            p.set_guard("q16", tok_epi_h)
            p.set_guard("wq", tok_epi_h)
            p.free(f"wqb{h}", f"wgb{h}", f"wob{h}", f"sg16_{h}", f"og16_{h}",
                   f"ogT_{h}", f"t_on_{h}", f"vst_{h}",
                   f"p_c_{h}", f"p_g_{h}", f"p_w_{h}", f"o_h{h}")

    tok_l0_done = p.step("vector", [p.guard("xadd")], lambda v: v.engine_nop())
    p.free("rcq", "rsq", "rck", "rsk", "maskT_s", "cdec_s", "kdec_s", "flag_s",
           "gnw_s", "q16", "qd16", "smid16",
           "p_pj", "rtmp", "rt2")
    bar1 = p.barrier()

    if upto < 6:
        return _finish(tok_l0_done, x)

    xnT3 = xnT
    ones32s = p.sb("ones32s", [128, 128])
    logm = p.sb("logm", [128, NT * ME])
    oneh = p.sb("oneh", [128, NT * ME])
    mscale = p.sb("mscale", [128, NT])
    ag = p.sb("ag", [128, D]); ab = p.sb("ab", [128, D])
    mrank_s = p.sb("mrank_s", [128, ME])
    mpfx_s = p.sb("mpfx_s", [NCORES, 1])
    nmx = p.sb("nmx", [128, NT])
    gate = p.sb("gate", [128, NT])
    rts = p.sb("rts", [128, 2 * ME])
    ohT = p.sb("ohT", [8, TLOC], BF16)
    scn = p.sb("scn", [8, TLOC // 2])
    scnc = p.sb("scnc", [8, 2])
    scnT = p.sb("scnT", [128, NT * ME])
    cnts = p.sb("cnts", [NCORES, ME])
    pfxT = p.sb("pfxT", [1, 8])
    pfxb = p.sb("pfxb", [128, ME])
    xn2 = p.sb("xn2", [128, NT * D])
    tok_l1c = p.dma("sync", [bar1], [(ag[:], a_gamma_b), (ab[:], a_beta_b),
                                     (ones32s[:], ones_f32_in),
                                     (mrank_s[:], m_rank_in), (mpfx_s[:], m_prefix_in)])
    tok_xn2 = ln_to(xn2, x, [tok_l0_done, bar1])
    ln3_stats = ln_to(None, xn2, [tok_xn2], stats_only=True)

    x3t = p.sb("x3t", [128, D])
    x3t16 = p.sb("x3t16", [128, D], BF16)
    x3_w = [None, None]
    x3T_w = [None, None]
    for i in range(NT):
        pa = i % 2
        d_3 = [ln3_stats, tok_l1c] + ([x3T_w[0]] if x3T_w[0] else [])
        t3 = p.step("vector", d_3, lambda v, i=i: v.tensor_scalar(
            x3t[:, 0:D], xn2[:, i * D:(i + 1) * D],
            stats[:, 2 * i:2 * i + 1], stats[:, 32 + i:32 + i + 1],
            ALU.subtract, ALU.mult))
        t3 = p.step("vector", [t3], lambda v: v.tensor_mul(x3t[:, 0:D], x3t[:, 0:D], ag[:]))
        t3 = p.step("vector", [t3], lambda v: v.tensor_add(x3t[:, 0:D], x3t[:, 0:D], ab[:]))
        x3_w[pa] = p.step("vector", [t3] + ([x3T_w[0]] if x3T_w[0] else []),
                          lambda v: v.tensor_copy(x3t16[:, 0:D], x3t[:, 0:D]))
        blks3 = [(x3t16[:, dc * 128: dc * 128 + 128],
                  xnT3[:, dc * TLOC + i * 128: dc * TLOC + i * 128 + 128])
                 for dc in range(8)]
        x3T_w[0] = transposes(blks3, [x3_w[pa]])
    tok_xnT3 = p.merge("vector", [x3T_w[0]])

    wgr = p.sb("wgr", [128, 2 * D])
    ttd = p.sb("ttd", [128, 2 * D])
    ttr_last = {}
    wge_tok = [None, None]
    tr_w = [None, None]
    for e in range(ME):
        sl = e % 2
        wdeps = [tok_l1c] + ([ttr_last.get(e - 2)] if ttr_last.get(e - 2) else [])
        wge_tok[sl] = p.dma("sync", wdeps,
                            [(wgr[:, sl * D:(sl + 1) * D],
                              wg_bc_in[e * 128:(e + 1) * 128, :])])
        t_ = None
        for i in range(NT):
            sl2 = i % 2
            tdeps = ([wge_tok[sl], tok_xn2] if i == 0 else [t_]) + \
                    ([tr_w[sl2]] if tr_w[sl2] else [])
            t_ = p.step("vector", tdeps, lambda v, i=i, sl=sl, sl2=sl2:
                v.tensor_mul(ttd[:, sl2 * D:(sl2 + 1) * D],
                             xn2[:, i * D:(i + 1) * D],
                             wgr[:, sl * D:(sl + 1) * D]))
            tr_w[sl2] = p.step("scalar", [t_], lambda s, i=i, e=e, sl2=sl2:
                s.activation(ttd[:, sl2 * D:(sl2 + 1) * D],
                             ttd[:, sl2 * D:(sl2 + 1) * D], AF.Copy,
                             accum_out=logm[:, i * ME + e: i * ME + e + 1]))
        ttr_last[e] = p.merge("scalar", [tr_w[0], tr_w[1]])
    tok_logm = ttr_last[ME - 1]
    p.free("ttd", "wgr", "x3t16", "x3t")
    bar_rt = p.barrier()
    pTf_cell[0] = p.ps("pTf", [128, 128])

    tok_gate = [bar_rt]
    oh_toks = []
    for i in range(NT):
        t1 = p.step("vector", [tok_logm] + ([tok_gate[0]] if tok_gate[0] else []),
                    lambda v, i=i: v.reduce_max(nmx[:, i:i + 1],
                                                logm[:, i * ME:(i + 1) * ME],
                                                axis=AX.X, negate=True))
        t2 = p.step("scalar", [t1], lambda s, i=i: s.activation(
            rts[:, 0:ME], logm[:, i * ME:(i + 1) * ME], AF.Exp,
            bias=nmx[:, i:i + 1], accum_out=gate[:, i:i + 1]))
        t3 = p.step("vector", [t1], lambda v, i=i: v.tensor_scalar(
            oneh[:, i * ME:(i + 1) * ME], logm[:, i * ME:(i + 1) * ME],
            nmx[:, i:i + 1], None, ALU.add))
        t4 = p.step("vector", [t3], lambda v, i=i: v.tensor_scalar(
            oneh[:, i * ME:(i + 1) * ME], oneh[:, i * ME:(i + 1) * ME],
            0.0, None, ALU.is_ge))
        t5 = p.step("vector", [t4, tok_l1c, t2], lambda v, i=i: v.tensor_mul(
            rts[:, ME:2 * ME], oneh[:, i * ME:(i + 1) * ME], mrank_s[:]))
        t6 = p.step("vector", [t5], lambda v, i=i: v.reduce_max(
            nmx[:, i:i + 1], rts[:, ME:2 * ME], axis=AX.X, negate=True))
        t7 = p.step("vector", [t6], lambda v, i=i: v.tensor_scalar(
            rts[:, ME:2 * ME], rts[:, ME:2 * ME], nmx[:, i:i + 1], None, ALU.add))
        t8 = p.step("vector", [t7], lambda v, i=i: v.tensor_scalar(
            oneh[:, i * ME:(i + 1) * ME], rts[:, ME:2 * ME], 0.0, None, ALU.is_ge))
        oh_toks.append(t8)
        tok_gate[0] = t8
    tok_oneh = p.merge("vector", oh_toks[-2:])
    tok_grec = p.step("vector", [tok_oneh], lambda v: v.reciprocal(gate[:], gate[:]))

    tok_oh16 = p.step("vector", [tok_grec, bar_rt], lambda v: v.engine_nop())
    blksO = [(oneh[:, i * ME:(i + 1) * ME], ohT[0:8, i * 128:(i + 1) * 128])
             for i in range(NT)]
    tok_ohT = transposes(blksO, tok_oh16)
    HT = TLOC // 2
    tok_scnA = p.step("vector", [tok_ohT], lambda v: v.tensor_tensor_scan(
        scn[:], ohT[0:8, 0:HT], ohT[0:8, 0:HT], 0.0, ALU.add, ALU.bypass))
    tok_carry = p.step("vector", [tok_scnA], lambda v: v.tensor_copy(
        scnc[:, 0:1], scn[:, HT - 1:HT]))
    blksS1 = [(scn[0:8, i * 128:(i + 1) * 128], scnT[:, i * ME:(i + 1) * ME])
              for i in range(NT // 2)]
    tok_scnT1 = transposes(blksS1, [tok_carry])
    tok_scnB = p.step("vector", [tok_scnT1], lambda v: v.tensor_tensor_scan(
        scn[:], ohT[0:8, HT:TLOC], ohT[0:8, HT:TLOC], scnc[:, 0:1],
        ALU.add, ALU.bypass))
    blksS2 = [(scn[0:8, i * 128:(i + 1) * 128], scnT[:, (NT // 2 + i) * ME:(NT // 2 + i + 1) * ME])
              for i in range(NT // 2)]
    tok_scnT2 = transposes(blksS2, [tok_scnB])
    tok_scn = tok_scnB
    tok_cs = p.dma("gpsimd", [tok_scn], [(cnt_snd.ap()[0:1, 0:ME], scn[:, HT - 1:HT])])
    tok_cc8 = p.coll([tok_cs], "AllGather", ALU.bypass, ALL8,
                     [cnt_snd.ap().opt()], [cnt_rcv.ap().opt()])
    tok_cr = p.dma("gpsimd", [tok_cc8], [(cnts[:], cnt_rcv.ap()[:, :])])
    p_pfx = p.ps("p_pfx", [1, 8])
    tok_pfx = p.step("tensor", [tok_cr, tok_l1c], lambda t: t.matmul(
        p_pfx[:], mpfx_s[0:NCORES, :], cnts[:, 0:ME], start=True, stop=True))
    tok_pfxe = p.step("vector", [tok_pfx], lambda v: v.tensor_copy(
        pfxT[0:1, 0:8], p_pfx[:]))
    p_pfb = p.ps("p_pfb", [128, 8])
    tok_pfb = p.step("tensor", [tok_pfxe], lambda t: t.matmul(
        p_pfb[:], ones32s[0:1, :], pfxT[0:1, 0:8], start=True, stop=True))
    tok_pfbe = p.step("vector", [tok_pfb], lambda v: v.tensor_copy(pfxb[:], p_pfb[:]))
    tok_scnT = p.merge("vector", [tok_scnT2])
    sel_ev = [None]
    for i in range(NT):
        t1 = p.step("vector", [tok_scnT, tok_oneh] + ([sel_ev[0]] if sel_ev[0] else []),
                    lambda v, i=i: v.tensor_mul(rts[:, 0:ME], oneh[:, i * ME:(i + 1) * ME],
                                                scnT[:, i * ME:(i + 1) * ME]))
        t2 = p.step("vector", [t1], lambda v, i=i: v.reduce_sum(
            mscale[:, i:i + 1], rts[:, 0:ME], axis=AX.X))
        t3 = p.step("vector", [t2, tok_pfbe], lambda v, i=i: v.tensor_mul(
            rts[:, 0:ME], oneh[:, i * ME:(i + 1) * ME], pfxb[:]))
        t3 = p.step("vector", [t3], lambda v, i=i: v.reduce_sum(
            rts[:, ME:ME + 1], rts[:, 0:ME], axis=AX.X))
        t4 = p.step("vector", [t3], lambda v, i=i: v.scalar_tensor_tensor(
            mscale[:, i:i + 1], mscale[:, i:i + 1], -1.0, rts[:, ME:ME + 1],
            ALU.add, ALU.add))
        t5 = p.step("vector", [t4], lambda v, i=i: v.tensor_scalar(
            mscale[:, i:i + 1], mscale[:, i:i + 1], CAP, None, ALU.is_lt))
        t6 = p.step("vector", [t5, tok_grec], lambda v, i=i: v.tensor_mul(
            mscale[:, i:i + 1], mscale[:, i:i + 1], gate[:, i:i + 1]))
        sel_ev[0] = t6
    tok_mscale = p.merge("vector", [sel_ev[0]])

    if upto < 7:
        return _finish(tok_mscale, xn2)

    web = p.sb("web", [128, 8 * D], BF16)
    be16 = p.sb("be16", [1, D], BF16)
    tok_we = p.dma("gpsimd", [bar1, bar_rt], [
        (web[:, dc * D:(dc + 1) * D], mWe_in[dc * 128:(dc + 1) * 128, :])
        for dc in range(8)] + [(be16[:], m_be_in)])
    p.free("p_pfx", "p_pfb", "pTf")
    uT3 = p.sb("uT3", [128, D], BF16)
    x2m = p.sb("x2m", [128, D], BF16)
    p_y = [p.ps("p_y0", [128, 1024]), p.ps("p_y1", [128, 1024])]
    y_mm = [None, None]; y_ev = [None, None]
    cast_w = [None, None]; tr_w = [None, None]
    for i in range(NT):
        pa = i % 2
        cdeps = [tok_xn2, bar_rt] + ([tr_w[pa]] if tr_w[pa] else [])
        cast_w[pa] = p.step("vector", cdeps + [t_ for t_ in (tr_w[0], tr_w[1]) if t_],
                            lambda v, i=i: v.tensor_copy(
            x2m[:, 0:D], xn2[:, i * D:(i + 1) * D]))
        blksM = [(x2m[:, dc * 128: dc * 128 + 128],
                  uT3[:, dc * 128: dc * 128 + 128])
                 for dc in range(8)]
        tr_w[pa] = transposes(blksM, [cast_w[pa]] + [t_ for t_ in (y_mm[0], y_mm[1]) if t_])
        deps = [tok_we, tr_w[pa], tok_mscale] + ([y_ev[pa]] if y_ev[pa] else [])
        y_mm[pa] = p.step("tensor", deps, lambda t, i=i, pa=pa: ([
            t.matmul(p_y[pa][:, n2 * 512:(n2 + 1) * 512],
                     uT3[:, dc * 128: dc * 128 + 128],
                     web[:, dc * D + n2 * 512: dc * D + n2 * 512 + 512],
                     start=(dc == 0), stop=False)
            for n2 in range(2) for dc in range(8)] + [
            t.matmul(p_y[pa][:, n2 * 512:(n2 + 1) * 512],
                     ones16[0:1, 0:128], be16[0:1, n2 * 512: n2 * 512 + 512],
                     start=False, stop=True)
            for n2 in range(2)])[-1])
        y_ev[pa] = p.step("vector", [y_mm[pa], tok_mscale], lambda v, i=i, pa=pa: [
            v.scalar_tensor_tensor(x[:, i * D + n2 * 512: i * D + n2 * 512 + 512],
                                   p_y[pa][:, n2 * 512:(n2 + 1) * 512],
                                   mscale[:, i:i + 1],
                                   x[:, i * D + n2 * 512: i * D + n2 * 512 + 512],
                                   ALU.mult, ALU.add) for n2 in range(2)][-1])
    tok_moe = p.merge("vector", [y_ev[0], y_ev[1]])
    p.free("web", "be16", "p_y0", "p_y1", "uT3", "x2m", "xn2")

    if upto < 8:
        return _finish(tok_moe, x)

    p.free("logm", "oneh", "mscale", "ag", "ab",
           "mrank_s", "mpfx_s",
           "nmx", "gate", "rts", "ohT", "scn", "scnc", "scnT", "cnts", "pfxT", "pfxb")
    p.free("pTa", "pTb")
    bar_a = p.barrier()

    ps_big = p.ps("ps_big", [128, 2048])
    ps_qn = p.ps("ps_qn", [65, 512])
    ps_av = p.ps("ps_av", [65, 1024])
    ps_rdb = p.ps("ps_rdb", [64, 512])


    ISC = 1.0 / (ADH ** 0.5)
    NKB = T // 128
    xadd_last2 = [tok_moe]

    kbs_b = [p.sb("kbs_0", [1, 8]), p.sb("kbs_1", [1, 8])]
    coll = {}
    proj_evs = []
    k_dma = [None, None, None, None]
    sq_w = [None, None, None, None]
    v_dma = [None, None, None, None]
    for hh in range(2):
        nrow = p.sb(f"nrow_{hh}", [1, 640])
        wk2 = p.sb(f"wk2_{hh}", [128, 8 * 512], BF16)
        wv2 = p.sb(f"wv2_{hh}", [128, 8 * 512], BF16)
        kst = p.sb(f"kst_{hh}", [128, 4 * 512], BF16)
        vst2 = p.sb(f"vst2_{hh}", [128, 4 * 8 * VW], BF16)
        sqs = p.sb(f"sqs_{hh}", [128, 4 * 512], BF16)
        p_pr = ps_big
        p_n = ps_qn
        tok_w2l = p.dma("gpsimd", [bar_a, p.guard("projbufs")], [
            (wk2[:, dc * 512:(dc + 1) * 512],
             WqkvT[dc * 128:(dc + 1) * 128, D + hh * 512: D + hh * 512 + 512])
            for dc in range(8)] + [
            (wv2[:, dc * 512:(dc + 1) * 512],
             WqkvT[dc * 128:(dc + 1) * 128, 2 * D + hh * 512: 2 * D + hh * 512 + 512])
            for dc in range(8)])

        nred = [None]
        for pp in range(4):
            for tg in range(4):
                pa = (pp * 4 + tg) % 4
                deps = [tok_w2l, tok_xnT3] + ([k_dma[pa]] if k_dma[pa] else [])
                kmm = p.step("tensor", deps, lambda t, pp=pp, tg=tg, pa=pa, wk2=wk2: [
                    t.matmul(p_pr[:, pa * 512: pa * 512 + 512],
                             wk2[:, dc * 512 + pp * 128: dc * 512 + pp * 128 + 128],
                             xnT3[:, dc * TLOC + tg * 512: dc * TLOC + tg * 512 + 512],
                             start=(dc == 0), stop=(dc == 7)) for dc in range(8)][-1])
                kev = p.step("vector", [kmm], lambda v, pa=pa, kst=kst: v.tensor_copy(
                    kst[:, pa * 512: pa * 512 + 512], p_pr[:, pa * 512: pa * 512 + 512]))
                sdeps = [kev] + ([sq_w[pa]] if sq_w[pa] else [])
                tsq = p.step("scalar", sdeps, lambda s, pa=pa, kst=kst, sqs=sqs:
                    s.activation(sqs[:, pa * 512: pa * 512 + 512],
                                 kst[:, pa * 512: pa * 512 + 512], AF.Square))
                last_nm = None
                for hd in range(2):
                    idx = 512 + (pp * 2 + hd) * 4 + tg
                    nmdeps = [tsq] + ([nred[0]] if nred[0] else [])
                    last_nm = p.step("tensor", nmdeps,
                                     lambda t, pa=pa, hd=hd, sqs=sqs, p_n=p_n: t.matmul(
                        p_n[64:65, 0:512],
                        ones16[64 * hd:64 * hd + 64, 0:1],
                        sqs[64 * hd:64 * hd + 64, pa * 512: pa * 512 + 512],
                        start=True, stop=True))
                    ncp = p.step("vector", [last_nm], lambda v, nrow=nrow, p_n=p_n:
                        v.tensor_copy(nrow[0:1, 0:512], p_n[64:65, 0:512]))
                    nred[0] = p.step("vector", [ncp], lambda v, idx=idx, nrow=nrow:
                        v.reduce_max(nrow[0:1, idx:idx + 1],
                                     nrow[0:1, 0:512], axis=AX.X))
                sq_w[pa] = last_nm
                k_dma[pa] = p.dma("gpsimd", [kev], [
                    (k_snd[hh].ap()[pp * 128:(pp + 1) * 128, tg * 512:(tg + 1) * 512],
                     kst[:, pa * 512: pa * 512 + 512])])
        kb_tok = [nred[0]]
        for h8 in range(8):
            kb_tok[0] = p.step("vector", [kb_tok[0]],
                               lambda v, h8=h8, nrow=nrow: v.reduce_max(
                nrow[0:1, 576 + h8:577 + h8],
                nrow[0:1, 512 + h8 * 4: 512 + h8 * 4 + 4], axis=AX.X))

        for ti in range(NT):
            pa = ti % 4
            deps = [tok_w2l, tok_xnT3] + ([v_dma[pa]] if v_dma[pa] else [])
            vmm = p.step("tensor", deps, lambda t, ti=ti, pa=pa, wv2=wv2: [
                t.matmul(p_pr[:, pa * 512: pa * 512 + 512],
                         xnT3[:, dc * TLOC + ti * 128: dc * TLOC + ti * 128 + 128],
                         wv2[:, dc * 512:(dc + 1) * 512],
                         start=(dc == 0), stop=(dc == 7)) for dc in range(8)][-1])
            vz = p.step("vector", [vmm] + ([v_dma[pa]] if v_dma[pa] else []),
                        lambda v, pa=pa, vst2=vst2: v.memset(
                vst2[:, pa * 8 * VW:(pa + 1) * 8 * VW], 0.0))
            vev = p.step("vector", [vz], lambda v, pa=pa, vst2=vst2: [
                v.tensor_copy(
                    vst2[:, pa * 8 * VW + h8 * VW: pa * 8 * VW + h8 * VW + 64],
                    p_pr[:, pa * 512 + h8 * 64: pa * 512 + h8 * 64 + 64])
                for h8 in range(8)][-1])
            vone = p.step("vector", [vev], lambda v, pa=pa, vst2=vst2: [
                v.memset(vst2[:, pa * 8 * VW + h8 * VW + 64:
                              pa * 8 * VW + h8 * VW + 65], 1.0)
                for h8 in range(8)][-1])
            v_dma[pa] = p.dma("gpsimd", [vone], [
                (v_snd[hh].ap()[ti * 128:(ti + 1) * 128, :],
                 vst2[:, pa * 8 * VW:(pa + 1) * 8 * VW])])

        tok_kns = p.dma("gpsimd", [kb_tok[0]], [
            (kn_snd[hh].ap()[0:1, 0:8], nrow[0:1, 576:584])])
        tok_knc = p.coll([tok_kns] + ([coll[0][2]] if hh == 1 else []),
                         "AllGather", ALU.bypass, PAIRS,
                         [kn_snd[hh].ap().opt()], [kn_rcv[hh].ap().opt()])
        tok_kc = p.coll([d for d in k_dma if d] + [tok_knc],
                        "AllGather", ALU.bypass, PAIRS,
                        [k_snd[hh].ap().opt()], [k_rcv[hh].ap().opt()])
        tok_vc = p.coll([d for d in v_dma if d] + [tok_kc],
                        "AllGather", ALU.bypass, PAIRS,
                        [v_snd[hh].ap().opt()], [v_rcv[hh].ap().opt()])
        coll[hh] = (tok_knc, tok_kc, tok_vc)

        pdone = p.merge("vector", [d for d in k_dma if d] +
                        [d for d in v_dma if d] +
                        [d for d in sq_w if d] + [nred[0], kb_tok[0]])
        proj_evs.append(pdone)
        p.set_guard("projbufs", pdone)
        p.free(f"wk2_{hh}", f"wv2_{hh}", f"kst_{hh}", f"vst2_{hh}", f"sqs_{hh}",
               f"nrow_{hh}")
    psum_ok = p.merge("vector", proj_evs)
    p.set_guard("wq2", psum_ok)

    for hh in range(2):
        kbs = kbs_b[hh]
        wq2 = p.sb(f"wq2_{hh}", [128, 8 * 512], BF16)
        tok_w2l = p.dma("gpsimd", [p.guard("wq2")], [
            (wq2[:, dc * 512:(dc + 1) * 512],
             WqkvT[dc * 128:(dc + 1) * 128, hh * 512: hh * 512 + 512])
            for dc in range(8)])
        bar_kv = p.guard("wq2")
        tok_kc = coll[hh][1]

        knr = p.sb(f"knr_{hh}", [1, 16])
        tok_knl = p.dma("gpsimd", [coll[hh][0], bar_kv], [
            (knr[0:1, 0:8], kn_rcv[hh].ap()[0:1, :]),
            (knr[0:1, 8:16], kn_rcv[hh].ap()[1:2, :])])
        tkb = p.step("vector", [tok_knl], lambda v, knr=knr, kbs=kbs: v.tensor_max(
            kbs[0:1, 0:8], knr[0:1, 0:8], knr[0:1, 8:16]))
        tkb = p.step("scalar", [tkb], lambda s, kbs=kbs: s.activation(
            kbs[0:1, 0:8], kbs[0:1, 0:8], AF.Sqrt))

        vf = p.sb(f"vf_{hh}", [128, NKB * 8 * VW], BF16)
        tok_vf = p.dma("sync", [coll[hh][2], bar_kv], [
            (vf[:, kb * 8 * VW:(kb + 1) * 8 * VW],
             v_rcv[hh].ap()[kb * 128:(kb + 1) * 128, :])
            for kb in range(NKB)])
        o_half = p.sb(f"o_half_{hh}", [128, 4 * TLOC], BF16)
        qt = [p.sb(f"qt0_{hh}", [65, TLOC], BF16), p.sb(f"qt1_{hh}", [65, TLOC], BF16)]
        kfh = [p.sb(f"kfh0_{hh}", [65, T], BF16), p.sb(f"kfh1_{hh}", [65, T], BF16)]
        nq = p.sb(f"nq_{hh}", [1, TLOC])
        nq16 = p.sb(f"nq16_{hh}", [1, TLOC], BF16)
        ex16 = p.sb(f"ex16_{hh}", [128, 2048], BF16)
        qsq16 = p.sb(f"qsq16_{hh}", [64, 512], BF16)
        rdr = p.sb(f"rdr_{hh}", [1, 512])
        rdb_s = p.sb(f"rdb_s_{hh}", [64, 512])
        kb2 = p.sb(f"kb2_{hh}", [1, 8])
        p_q = ps_qn
        p_qn = ps_qn
        p_s = ps_big
        p_av = ps_av
        p_rdb = ps_rdb
        tkb2 = p.step("vector", [tkb], lambda v, kbs=kbs, kb2=kb2: v.tensor_mul(
            kb2[0:1, :], kbs[0:1, :], kbs[0:1, :]))
        tkb2 = p.step("vector", [tkb2], lambda v, kb2=kb2: v.tensor_scalar_mul(
            kb2[0:1, :], kb2[0:1, :], 1.0201))
        seed = p.guard("wq2")
        qev_g = [None]
        exp_w = [seed, seed]
        av_w = [None, None]
        av_prev = [None]
        on_prev = [None]
        on_prev_sl = [None, None]
        avq = [0]
        rd_w = [None]
        nq_w = [seed]
        qn_mm = [None]
        qrow_tok = {}
        qevl = {}
        kfo_tok = {}
        tail_prev = [None]

        def emit_kf(h8, dep_done):
            hpa = h8 % 2
            pp, hd = h8 // 2, h8 % 2
            kdeps = [tok_kc, bar_kv] + ([dep_done] if dep_done else [])
            tok_kfh = p.dma("sync", kdeps, [
                (kfh[hpa][0:64, r * TLOC:(r + 1) * TLOC],
                 k_rcv[hh].ap()[r * 512 + pp * 128 + hd * 64:
                                r * 512 + pp * 128 + hd * 64 + 64, :])
                for r in range(2)])
            kfo_tok[hpa] = p.step("vector", [tok_kfh], lambda v, hpa=hpa, kfh=kfh:
                                  v.memset(kfh[hpa][64:65, :], 1.0))

        def make_build(h8, dep_done):
            hpa = h8 % 2
            thunks = []
            qev_cell = {}

            def mk_qmm(tg):
                def th():
                    qdeps = [tok_w2l, tok_xnT3] + ([dep_done] if dep_done else []) \
                            + ([qev_g[0]] if qev_g[0] else [])
                    qmm = p.step("tensor", qdeps, lambda t, h8=h8, tg=tg, wq2=wq2,
                                 p_q=p_q, xnT3=xnT3: [
                        t.matmul(p_q[0:64, :],
                                 wq2[:, dc * 512 + h8 * 64: dc * 512 + h8 * 64 + 64],
                                 xnT3[:, dc * TLOC + tg * 512:
                                      dc * TLOC + tg * 512 + 512],
                                 start=(dc == 0), stop=(dc == 7))
                        for dc in range(8)][-1])
                    qev = p.step("vector", [qmm], lambda v, hpa=hpa, tg=tg, qt=qt,
                                 p_q=p_q: v.tensor_copy(
                        qt[hpa][0:64, tg * 512:(tg + 1) * 512], p_q[0:64, :]))
                    qev_g[0] = qev
                    qev_cell[tg] = qev
                    qev_cell[('sq', tg)] = p.step("vector",
                        [qev] + ([qn_mm[0]] if qn_mm[0] else []),
                        lambda v, hpa=hpa, tg=tg, qt=qt, qsq16=qsq16: v.tensor_mul(
                            qsq16[:], qt[hpa][0:64, tg * 512:(tg + 1) * 512],
                            qt[hpa][0:64, tg * 512:(tg + 1) * 512]))
                return th

            def mk_qn(tg):
                def th():
                    deps = [qev_cell[('sq', tg)]] + ([nq_w[0]] if nq_w[0] else [])
                    qn_mm[0] = p.step("tensor", deps, lambda t, qsq16=qsq16,
                                      p_qn=p_qn: t.matmul(
                        p_qn[64:65, :], ones16[0:64, 0:1], qsq16[:],
                        start=True, stop=True))
                    nwd = [qn_mm[0]] + ([tail_prev[0]] if tail_prev[0] else [])
                    nq_w[0] = p.step("vector", nwd, lambda v, tg=tg, nq=nq,
                                     p_qn=p_qn: v.tensor_copy(
                        nq[0:1, tg * 512:(tg + 1) * 512], p_qn[64:65, :]))
                return th

            for tg in range(4):
                thunks.append(mk_qmm(tg))
                thunks.append(mk_qn(tg))

            def tail():
                t0 = p.step("gpsimd", [nq_w[0], tkb2], lambda g, h8=h8, nq=nq,
                            kb2=kb2: g.tensor_scalar(
                    nq[0:1, 0:TLOC], nq[0:1, 0:TLOC], kb2[0:1, h8:h8 + 1], None,
                    ALU.mult))
                t1s = p.step("scalar", [t0], lambda s, nq=nq: s.activation(
                    nq[0:1, 0:TLOC], nq[0:1, 0:TLOC], AF.Sqrt))
                t2s = p.step("gpsimd", [t1s], lambda g, nq=nq, nq16=nq16:
                    g.tensor_scalar(nq16[0:1, :], nq[0:1, 0:TLOC], -1.0, -0.08,
                                    ALU.mult, ALU.add))
                qrow_tok[hpa] = p.dma("gpsimd", [t2s] +
                                      ([dep_done] if dep_done else []),
                                      [(qt[hpa][64:65, :], nq16[0:1, :])])
                qevl[hpa] = qev_cell[3]
                tail_prev[0] = qrow_tok[hpa]
            return thunks, tail

        emit_kf(0, None)
        th0, tl0 = make_build(0, None)
        for t in th0:
            t()
        tl0()
        flash_done = [None]
        for h8 in range(8):
            hpa = h8 % 2
            pp, hd = h8 // 2, h8 % 2
            prev_done = flash_done[0]
            nb = make_build(h8 + 1, prev_done) if h8 < 7 else None
            pend = list(nb[0]) if nb else []

            def emit_avs(pbx, etok, asl, h8=h8):
                basex = (pbx % 2) * 1024
                tokav = None
                for i in range(2):
                    kb = 2 * pbx + i
                    avdeps = [etok] + ([tok_vf] if kb == 0 else []) + \
                             ([on_prev_sl[asl]] if kb == 0 and on_prev_sl[asl]
                              else [])
                    tokav = p.step("tensor", avdeps, lambda t, kb=kb,
                                   col=basex + i * 512, h8=h8, vf=vf, ex16=ex16,
                                   p_av=p_av, asl=asl: t.matmul(
                        p_av[:, asl * 512:(asl + 1) * 512],
                        vf[:, kb * 8 * VW + h8 * VW: kb * 8 * VW + h8 * VW + 65],
                        ex16[:, col:col + 512],
                        start=(kb == 0), stop=(kb == NKB - 1)))
                av_w[pbx % 2] = tokav
                av_prev[0] = tokav

            for qq in range(4):
                asl = avq[0]
                avq[0] ^= 1
                prev = None
                for pb in range(NKB // 2):
                    ps_ = pb % 2
                    base = ps_ * 1024
                    kb0 = 2 * pb
                    sdeps = [kfo_tok[hpa], qrow_tok[hpa], qevl[hpa]] + \
                            ([exp_w[ps_]] if exp_w[ps_] else [])
                    smm = p.step("tensor", sdeps, lambda t, qq=qq, kb0=kb0,
                                 base=base, hpa=hpa, kfh=kfh, qt=qt, p_s=p_s: [
                        t.matmul(p_s[:, base + i * 512: base + i * 512 + 512],
                                 kfh[hpa][0:65, (kb0 + i) * 128:(kb0 + i + 1) * 128],
                                 qt[hpa][0:65, qq * 512:(qq + 1) * 512],
                                 start=True, stop=True) for i in range(2)][-1])
                    edeps = [smm] + ([av_w[ps_]] if av_w[ps_] else [])
                    etok = p.step("scalar", edeps, lambda s, base=base, p_s=p_s,
                                  ex16=ex16: s.activation(
                        ex16[:, base:base + 1024], p_s[:, base:base + 1024],
                        AF.Exp, scale=ISC))
                    exp_w[ps_] = etok
                    if prev is not None:
                        emit_avs(prev[0], prev[1], asl)
                    prev = (pb, etok)
                    if pend and qq == 1 and pb % 2 == 0:
                        pend.pop(0)()
                emit_avs(prev[0], prev[1], asl)
                if nb and qq == 0:
                    emit_kf(h8 + 1, prev_done)
                if nb and qq == 2:
                    nb[1]()
                t1 = p.step("vector", [av_prev[0]] + ([rd_w[0]] if rd_w[0] else []),
                            lambda v, p_av=p_av, rdr=rdr, asl=asl: v.reciprocal(
                    rdr[0:1, :], p_av[64:65, asl * 512:(asl + 1) * 512]))
                t2 = p.step("tensor", [t1], lambda t, rdr=rdr, p_rdb=p_rdb: t.matmul(
                    p_rdb[:], ones32s[0:1, 0:64], rdr[0:1, :], start=True, stop=True))
                t3 = p.step("vector", [t2], lambda v, p_rdb=p_rdb, rdb_s=rdb_s:
                    v.tensor_copy(rdb_s[:], p_rdb[:]))
                on_prev[0] = p.step("vector", [t3], lambda v, pp=pp, hd=hd, qq=qq,
                                    p_av=p_av, rdb_s=rdb_s, o_half=o_half, asl=asl:
                    v.tensor_mul(
                        o_half[64 * hd:64 * hd + 64,
                               pp * TLOC + qq * 512: pp * TLOC + qq * 512 + 512],
                        p_av[0:64, asl * 512:(asl + 1) * 512], rdb_s[:]))
                rd_w[0] = on_prev[0]
                on_prev_sl[asl] = on_prev[0]
            flash_done[0] = on_prev[0]
        tok_oh2 = p.merge("vector", [on_prev[0]])

        p.free(f"qt0_{hh}", f"qt1_{hh}", f"kfh0_{hh}", f"kfh1_{hh}",
               f"nq_{hh}", f"nq16_{hh}", f"ex16_{hh}", f"qsq16_{hh}",
               f"rdr_{hh}", f"rdb_s_{hh}", f"kb2_{hh}")
        bar_wo = p.barrier()
        wo2 = p.sb(f"wo2_{hh}", [128, 4 * D], BF16)
        bo16 = p.sb(f"bo16_{hh}", [1, D], BF16)
        tok_wo2 = p.dma("gpsimd", [bar_wo], [
            (wo2[:, pp * D:(pp + 1) * D],
             WoutT[hh * 512 + pp * 128: hh * 512 + pp * 128 + 128, :])
            for pp in range(4)] + [(bo16[:], b_out_in)])
        wo_mm = [None, None]; wo_ev = [None, None]
        for i in range(NT):
            pa = i % 2
            deps = [tok_oh2, tok_wo2, exp_w[0], exp_w[1]] + \
                   ([wo_ev[pa]] if wo_ev[pa] else [])
            wo_mm[pa] = p.step("tensor", deps, lambda t, i=i, pa=pa, hh=hh,
                               o_half=o_half, wo2=wo2, bo16=bo16: ([
                t.matmul(ps_big[:, pa * 1024 + n2 * 512: pa * 1024 + n2 * 512 + 512],
                         o_half[:, pp * TLOC + i * 128: pp * TLOC + i * 128 + 128],
                         wo2[:, pp * D + n2 * 512: pp * D + n2 * 512 + 512],
                         start=(pp == 0), stop=(pp == 3 and hh == 1))
                for n2 in range(2) for pp in range(4)] + ([
                t.matmul(ps_big[:, pa * 1024 + n2 * 512: pa * 1024 + n2 * 512 + 512],
                         ones16[0:1, 0:128], bo16[0:1, n2 * 512: n2 * 512 + 512],
                         start=False, stop=True)
                for n2 in range(2)] if hh == 0 else []))[-1])
            wo_ev[pa] = p.step("vector", [wo_mm[pa], xadd_last2[0]],
                               lambda v, i=i, pa=pa: [
                v.tensor_add(x[:, i * D + n2 * 512: i * D + n2 * 512 + 512],
                             x[:, i * D + n2 * 512: i * D + n2 * 512 + 512],
                             ps_big[:, pa * 1024 + n2 * 512: pa * 1024 + n2 * 512 + 512])
                for n2 in range(2)][-1])
            if hh == 1:
                p.out_dma([wo_ev[pa]],
                          [(x_out[i * 128:(i + 1) * 128, :], x[:, i * D:(i + 1) * D])],
                          wait=False)
        xadd_last2[0] = p.merge("vector", [wo_ev[0], wo_ev[1]])
        p.set_guard("wq2", xadd_last2[0])
        p.free(f"wo2_{hh}", f"bo16_{hh}",
               f"o_half_{hh}", f"vf_{hh}", f"knr_{hh}", f"wq2_{hh}")
    p.free("kbs_1", "kbs_0")

    return _finish(None, x)


def kernel(_upto=99, **inputs):
    g, shards = _prep_inputs(inputs)
    nc = build_nc(_upto)
    in_maps = []
    for c in range(NCORES):
        m = dict(g)
        m["x_in"] = shards[c]
        m.update(_host_consts(c))
        in_maps.append(m)
    res = run_bass_kernel_spmd(nc, in_maps, core_ids=list(range(NCORES)))
    out = np.concatenate([res.results[c]["x_out"] for c in range(NCORES)], axis=0)
    return out.reshape(B, T, D)


# version 15 (final)
# speedup vs baseline: 1.6095x; 1.0266x over previous
import numpy as np
import concourse.bass as bass
import concourse.mybir as mybir
from concourse.bass_utils import run_bass_kernel_spmd

F32 = mybir.dt.float32
BF16 = mybir.dt.bfloat16
AF = mybir.ActivationFunctionType
ALU = mybir.AluOpType
AX = mybir.AxisListType

B, T, D = 4, 4096, 1024
NCORES = 8
TLOC = 2048
NT = TLOC // 128
LN_EPS = 1e-5

RH = 4
RDK, RDV = 256, 512
RC = 128
RNC = TLOC // RC
GAMMAS = [1.0 - 2.0 ** (-5.0 - h) for h in range(RH)]

NE, NS = 16, 16
ES = NE * NS
FH = 256
NORM_EPS = 1e-12

AH, ADH = 16, 64
ME = 8
CAP = float((B * T) // ME)

_BUILD_UPTO = 99


def _host_consts(core):
    import ml_dtypes
    half = core % 2
    c = {}
    inv_freq = 1.0 / (10000.0 ** (np.arange(0, RDK, 2, dtype=np.float64) / RDK))
    tpos = half * TLOC + np.arange(TLOC, dtype=np.float64)
    fr = inv_freq[:, None] * tpos[None, :]
    c["rope_cos"] = np.cos(fr).astype(np.float32)
    c["rope_sin"] = np.sin(fr).astype(np.float32)
    c["rope_cos_q"] = (np.cos(fr) * (RDK ** -0.5)).astype(np.float32)
    c["rope_sin_q"] = (np.sin(fr) * (RDK ** -0.5)).astype(np.float32)
    maskT = np.zeros((RH, RC, RC), dtype=np.float32)
    cdec = np.zeros((RH, 128, RC), dtype=np.float32)
    kdec = np.zeros((RH, RC, 1), dtype=np.float32)
    for h, g in enumerate(GAMMAS):
        i = np.arange(RC, dtype=np.float64)
        dif = i[None, :] - i[:, None]
        maskT[h] = np.where(dif >= 0, g ** dif, 0.0).astype(np.float32)
        cdec[h] = np.broadcast_to((g ** (i + 1.0)).astype(np.float32), (128, RC))
        kdec[h] = (g ** (RC - 1.0 - i)).astype(np.float32)[:, None]
    c["ret_maskT"] = maskT
    c["ret_cdec"] = cdec
    c["ret_kdec"] = kdec
    c["ret_flag"] = np.full((128, 1), float(half), dtype=np.float32)
    c["ident"] = np.eye(128, dtype=np.float32).astype(ml_dtypes.bfloat16)
    c["ident_f32"] = np.eye(128, dtype=np.float32)
    c["ones_f32"] = np.ones((128, 128), dtype=np.float32)
    c["ones_bf16"] = np.ones((128, 128), dtype=np.float32).astype(ml_dtypes.bfloat16)
    c["m_rank"] = np.broadcast_to(np.arange(ME, 0, -1, dtype=np.float32), (128, ME)).copy()
    c["m_prefix"] = (np.arange(NCORES) < core).astype(np.float32)[:, None]
    return c


def _prep_inputs(inp):
    g = {}
    g["WqT"] = np.ascontiguousarray(np.asarray(inp["L0_attn_Wq"]).T)
    g["WkT"] = np.ascontiguousarray(np.asarray(inp["L0_attn_Wk"]).T)
    g["WvT"] = np.ascontiguousarray(np.asarray(inp["L0_attn_Wv"]).T)
    g["WgT"] = np.ascontiguousarray(np.asarray(inp["L0_attn_Wg"]).T)
    g["WoT"] = np.ascontiguousarray(np.asarray(inp["L0_attn_Wo"]).T)
    g["gnw_b"] = np.broadcast_to(np.asarray(inp["L0_attn_gnorm_w"]), (128, RDV)).copy()
    g["f_gamma_b"] = np.broadcast_to(np.asarray(inp["L0_ffn_ln_gamma"]), (128, D)).copy()
    g["f_beta_b"] = np.broadcast_to(np.asarray(inp["L0_ffn_ln_beta"]), (128, D)).copy()
    g["mu2d"] = np.ascontiguousarray(np.asarray(inp["L0_ffn_mu"]).reshape(D, ES))
    g["scale_b"] = np.broadcast_to(np.asarray(inp["L0_ffn_scale"]).reshape(1, 1), (128, 1)).copy()
    g["w1f"] = np.ascontiguousarray(np.asarray(inp["L0_ffn_w1"]).reshape(NE * D, FH))
    g["b1T"] = np.ascontiguousarray(np.asarray(inp["L0_ffn_b1"]))
    g["w2f"] = np.ascontiguousarray(np.asarray(inp["L0_ffn_w2"]).reshape(NE * FH, D))
    g["b2"] = np.ascontiguousarray(np.asarray(inp["L0_ffn_b2"]))
    g["a_gamma_b"] = np.broadcast_to(np.asarray(inp["L1_attn_ln_gamma"]), (128, D)).copy()
    g["a_beta_b"] = np.broadcast_to(np.asarray(inp["L1_attn_ln_beta"]), (128, D)).copy()
    g["WqkvT"] = np.ascontiguousarray(np.asarray(inp["L1_attn_w_qkv"]).T)
    g["WoutT"] = np.ascontiguousarray(np.asarray(inp["L1_attn_w_out"]).T)
    g["b_out"] = np.asarray(inp["L1_attn_b_out"]).reshape(1, D)
    g["mWg"] = np.ascontiguousarray(np.asarray(inp["L1_ffn_Wg"]))
    _wg = np.asarray(inp["L1_ffn_Wg"], dtype=np.float32)
    g["wg_bc"] = np.ascontiguousarray(
        np.broadcast_to(_wg.T[:, None, :], (ME, 128, D))).reshape(ME * 128, D)
    g["mWe"] = np.ascontiguousarray(np.asarray(inp["L1_ffn_We"]))
    g["m_be"] = np.asarray(inp["L1_ffn_be"]).reshape(1, D)
    x = np.asarray(inp["L0_attn_x"], dtype=np.float32).reshape(B * T, D)
    shards = [np.ascontiguousarray(x[c * TLOC:(c + 1) * TLOC]) for c in range(NCORES)]
    return g, shards


ENGS = ("tensor", "vector", "scalar", "gpsimd", "sync")


class Builder:
    def __init__(self, nc):
        self.nc = nc
        self._allocs = {}
        self.q = {e: [] for e in ENGS}
        self.sem_cm = {}
        self.sem = {}
        self.cnt = {e: 0 for e in ENGS}
        self.wm = {e: {} for e in ENGS}
        for e in ENGS:
            cm = nc.semaphore("sem_" + e)
            self.sem_cm[e] = cm
            self.sem[e] = cm.__enter__()
        self.NDMA = 4
        for d in ["dma_out"] + [f"dma_{e}{i}" for e in ("sync", "gpsimd")
                                for i in range(4)]:
            cm = nc.semaphore(d)
            self.sem_cm[d] = cm
            self.sem[d] = cm.__enter__()
        self.dmacnt = {f"dma_{e}{i}": 0 for e in ("sync", "gpsimd") for i in range(4)}
        self.dmarr = {"sync": 0, "gpsimd": 0}
        self.outcnt = 0
        cm = nc.semaphore("sem_cc")
        self.sem_cm["cc"] = cm
        self.sem["cc"] = cm.__enter__()
        self.cccnt = 0
        self.guards = {}

    def guard(self, name):
        return self.guards.get(name, ("vector", 0))

    def set_guard(self, name, tok):
        self.guards[name] = tok

    def _merge_emit(self, eng):
        if eng in ("vector", "gpsimd"):
            return lambda en: en.engine_nop()
        if eng in ("tensor", "scalar"):
            return lambda en: en.drain()
        return lambda en: en.nop()

    def merge(self, eng, deps):
        return self.step(eng, deps, self._merge_emit(eng))

    def barrier(self):
        toks = [self.merge(e, []) for e in ENGS]
        return self.merge("vector", toks)

    def sb(self, name, shape, dt=F32, side="right"):
        cm = self.nc.sbuf_tensor(name, shape, dt, side=side)
        h = cm.__enter__()
        self._alloc_idx = getattr(self, "_alloc_idx", 0) + 1
        self._allocs[name] = (cm, self._alloc_idx)
        return h

    def ps(self, name, shape, dt=F32, side="right"):
        cm = self.nc.psum_tensor(name, shape, dt, side=side)
        h = cm.__enter__()
        self._alloc_idx = getattr(self, "_alloc_idx", 0) + 1
        self._allocs[name] = (cm, self._alloc_idx)
        return h

    def free(self, *names):
        items = sorted([(self._allocs[n][1], n) for n in names], reverse=True)
        for _, n in items:
            self._allocs.pop(n)[0].__exit__(None, None, None)

    def _waits(self, eng, deps):
        wm = self.wm[eng]
        todo = []
        for key, val in deps:
            if val <= 0:
                continue
            if wm.get(key, 0) >= val:
                continue
            wm[key] = val
            todo.append((self.sem[key], val))
        return todo

    def step(self, eng, deps, emit, name=""):
        todo = self._waits(eng, deps)
        self.cnt[eng] += 1
        tok = (eng, self.cnt[eng])

        def run(e, todo=todo, emit=emit, tok=tok, eng=eng):
            for h, v in todo:
                e.wait_ge(h, v)
            insts = emit(e)
            last = insts[-1] if isinstance(insts, (list, tuple)) else insts
            last.then_inc(self.sem[eng], 1)
        self.q[eng].append(run)
        return tok

    def dma(self, eng, deps, pairs, name=""):
        i = self.dmarr[eng]
        self.dmarr[eng] = (i + 1) % 4
        key = f"dma_{eng}{i}"
        todo = self._waits(eng, deps + [(key, self.dmacnt[key])])
        n = len(pairs)
        self.dmacnt[key] += 16 * n
        tok = (key, self.dmacnt[key])

        def run(e, todo=todo, pairs=pairs, key=key):
            for h, v in todo:
                e.wait_ge(h, v)
            for dst, src in pairs:
                e.dma_start(out=dst, in_=src).then_inc(self.sem[key], 16)
        self.q[eng].append(run)
        return tok

    def coll(self, deps, kind, op, groups, ins, outs):
        todo = self._waits("gpsimd", deps + [("cc", self.cccnt)])
        self.cccnt += 1
        tok = ("cc", self.cccnt)

        def run(e, todo=todo):
            for h, v in todo:
                e.wait_ge(h, v)
            e.collective_compute(kind, op, replica_groups=groups, ins=ins,
                                 outs=outs).then_inc(self.sem["cc"], 1)
        self.q["gpsimd"].append(run)
        return tok

    def out_dma(self, deps, pairs, wait=True):
        todo = self._waits("sync", deps + [(f"dma_sync{i}", self.dmacnt[f"dma_sync{i}"])
                                           for i in range(4)])
        n = len(pairs)
        self.outcnt += 16 * n
        tok = ("dma_out", self.outcnt)
        final = self.outcnt

        def run(e, todo=todo, pairs=pairs, final=final, wait=wait):
            for h, v in todo:
                e.wait_ge(h, v)
            for dst, src in pairs:
                e.dma_start(out=dst, in_=src).then_inc(self.sem["dma_out"], 16)
            if wait:
                e.wait_ge(self.sem["dma_out"], final)
        self.q["sync"].append(run)
        return tok

    def out_wait(self):
        final = self.outcnt

        def run(e, final=final):
            e.wait_ge(self.sem["dma_out"], final)
        self.q["sync"].append(run)

    def finalize(self, block):
        for e in ENGS:
            fns = self.q[e]
            if not fns:
                continue

            def run_all(eng, fns=fns):
                for f in fns:
                    f(eng)
            getattr(block, e)(run_all)


def build_nc(upto=99):
    nc = bass.Bass("TRN2", target_bir_lowering=False, dynamic_dma_scratch_size=16384)
    bld = Builder(nc)
    p = bld

    def din(name, shape, dt=F32):
        return nc.dram_tensor(name, shape, dt, kind="ExternalInput").ap()

    x_in = din("x_in", [TLOC, D])
    WqT, WkT = din("WqT", [D, D]), din("WkT", [D, D])
    WvT, WgT = din("WvT", [D, 2 * D]), din("WgT", [D, 2 * D])
    WoT = din("WoT", [2 * D, D])
    gnw_b = din("gnw_b", [128, RDV])
    f_gamma_b, f_beta_b = din("f_gamma_b", [128, D]), din("f_beta_b", [128, D])
    mu2d = din("mu2d", [D, ES])
    scale_b = din("scale_b", [128, 1])
    w1f = din("w1f", [NE * D, FH])
    b1_in = din("b1T", [NE, FH])
    w2f = din("w2f", [NE * FH, D])
    b2_in = din("b2", [NE, D])
    a_gamma_b, a_beta_b = din("a_gamma_b", [128, D]), din("a_beta_b", [128, D])
    WqkvT = din("WqkvT", [D, 3 * D])
    WoutT = din("WoutT", [D, D])
    b_out_in = din("b_out", [1, D])
    mWg_in = din("mWg", [D, ME])
    wg_bc_in = din("wg_bc", [ME * 128, D])
    mWe_in = din("mWe", [D, D])
    m_be_in = din("m_be", [1, D])
    rope_cos, rope_sin = din("rope_cos", [128, TLOC]), din("rope_sin", [128, TLOC])
    rope_cos_q, rope_sin_q = din("rope_cos_q", [128, TLOC]), din("rope_sin_q", [128, TLOC])
    ret_maskT = din("ret_maskT", [RH, RC, RC])
    ret_cdec = din("ret_cdec", [RH, 128, RC])
    ret_kdec = din("ret_kdec", [RH, RC, 1])
    ret_flag = din("ret_flag", [128, 1])
    ident_in = din("ident", [128, 128], BF16)
    identf_in = din("ident_f32", [128, 128])
    ones_f32_in = din("ones_f32", [128, 128])
    ones_bf16_in = din("ones_bf16", [128, 128], BF16)
    m_rank_in = din("m_rank", [128, ME])
    m_prefix_in = din("m_prefix", [NCORES, 1])

    x_out = nc.dram_tensor("x_out", [TLOC, D], F32, kind="ExternalOutput").ap()

    ffn_snd = nc.dram_tensor("ffn_snd", [ES, D + 8], F32)
    ffn_rcv = nc.dram_tensor("ffn_rcv", [2 * ES, D + 8], F32)
    st_snd = nc.dram_tensor("st_snd", [RDK, RDV], F32)
    st_rcv = nc.dram_tensor("st_rcv", [2 * RDK, RDV], F32)
    st_snd1 = nc.dram_tensor("st_snd1", [RDK, RDV], F32)
    st_rcv1 = nc.dram_tensor("st_rcv1", [2 * RDK, RDV], F32)
    cnt_snd = nc.dram_tensor("cnt_snd", [1, ME], F32)
    cnt_rcv = nc.dram_tensor("cnt_rcv", [NCORES, ME], F32)
    VW = 66
    k_snd = [nc.dram_tensor(f"k_snd{i}", [512, TLOC], BF16) for i in range(2)]
    k_rcv = [nc.dram_tensor(f"k_rcv{i}", [1024, TLOC], BF16) for i in range(2)]
    v_snd = [nc.dram_tensor(f"v_snd{i}", [TLOC, 8 * VW], BF16) for i in range(2)]
    v_rcv = [nc.dram_tensor(f"v_rcv{i}", [2 * TLOC, 8 * VW], BF16) for i in range(2)]
    kn_snd = [nc.dram_tensor(f"kn_snd{i}", [1, 8], F32) for i in range(2)]
    kn_rcv = [nc.dram_tensor(f"kn_rcv{i}", [2, 8], F32) for i in range(2)]

    PAIRS = [[0, 1], [2, 3], [4, 5], [6, 7]]
    ALL8 = [list(range(NCORES))]

    x = p.sb("x", [128, NT * D], side="left")
    ident = p.sb("ident_s", [128, 128], BF16, side="left")
    identf = p.sb("identf_s", [128, 128], side="left")
    ones16 = p.sb("ones16_s", [128, 128], BF16, side="left")
    stats = p.sb("stats", [128, 96], side="left")
    bnb = p.sb("bnb", [128, 24], side="left")
    sq2 = p.sb("sq2", [128, D], side="left")
    xnT = p.sb("xnT", [128, 8 * TLOC], BF16, side="left")
    xn = p.sb("xn", [128, NT * D], side="left")

    tok_x = p.dma("sync", [], [(x[:, i * D:(i + 1) * D], x_in[i * 128:(i + 1) * 128, :])
                               for i in range(NT)])
    tok_const = p.dma("sync", [], [(ident[:], ident_in), (identf[:], identf_in),
                                   (ones16[:], ones_bf16_in)])

    def ln_to(dst, src, dep, gamma=None, beta=None, dst16=None, name="ln",
              stats_only=False):
        dep = list(dep) if isinstance(dep, list) else [dep]
        stok = []
        for i in range(NT):
            a = src[:, i * D:(i + 1) * D]
            deps0 = dep + [p.guard("stats")]
            if i >= 2:
                deps0.append(stok[i - 2])
            t0 = p.step("vector", deps0, lambda v, a=a, i=i: [
                v.bn_stats(bnb[:, 12 * (i % 2):12 * (i % 2) + 6], a[:, 0:512]),
                v.bn_stats(bnb[:, 12 * (i % 2) + 6:12 * (i % 2) + 12], a[:, 512:1024]),
            ][-1])
            t1 = p.step("vector", [t0], lambda v, i=i: v.bn_aggr(
                stats[:, 2 * i:2 * i + 2], bnb[:, 12 * (i % 2):12 * (i % 2) + 12]))
            stok.append(t1)
        t2 = p.step("vector", stok, lambda v: [
            v.tensor_scalar_add(stats[:, 32 + i:32 + i + 1],
                                stats[:, 2 * i + 1:2 * i + 2], LN_EPS)
            for i in range(NT)][-1])
        t3 = p.step("scalar", [t2], lambda s: s.activation(
            stats[:, 32:32 + NT], stats[:, 32:32 + NT], AF.Sqrt))
        t4 = p.step("vector", [t3], lambda v: v.reciprocal(
            stats[:, 32:32 + NT], stats[:, 32:32 + NT]))
        if stats_only:
            p.set_guard("stats", t4)
            return t4
        wa = [p.step("vector", [t4] + dep, lambda v, i=i: v.tensor_scalar(
            dst[:, i * D:(i + 1) * D], src[:, i * D:(i + 1) * D],
            stats[:, 2 * i:2 * i + 1], stats[:, 32 + i:32 + i + 1],
            ALU.subtract, ALU.mult)) for i in range(NT)]
        if gamma is not None:
            wa = [p.step("vector", [wa[i]], lambda v, i=i: v.tensor_mul(
                dst[:, i * D:(i + 1) * D], dst[:, i * D:(i + 1) * D], gamma[:]))
                for i in range(NT)]
            wa = [p.step("vector", [wa[i]], lambda v, i=i: v.tensor_add(
                dst[:, i * D:(i + 1) * D], dst[:, i * D:(i + 1) * D], beta[:]))
                for i in range(NT)]
        if dst16 is not None:
            wa = [p.step("vector", [wa[i]], lambda v, i=i: v.tensor_copy(
                dst16[:, i * D:(i + 1) * D], dst[:, i * D:(i + 1) * D]))
                for i in range(NT)]
        fin = p.step("vector", wa, lambda v: v.engine_nop())
        p.set_guard("stats", fin)
        return fin

    pTa = p.ps("pTa", [128, 128], BF16)
    pTb = p.ps("pTb", [128, 128], BF16)
    pTf_cell = [None]

    def transposes(blocks, dep, extra_dep=None, scale_col=None):
        ev = [p.guard("pT"), p.guard("pT")]
        last = None
        deps0 = (list(dep) if isinstance(dep, list) else [dep])
        if extra_dep is not None:
            deps0.append(extra_dep)
        for b, (srcap, dstap) in enumerate(blocks):
            fp = (srcap.dtype == F32)
            pts = (pTf_cell[0], pTf_cell[0]) if fp else (pTa, pTb)
            pr, pc = srcap.shape[0], srcap.shape[1]
            pt = pts[b % 2][0:pc, 0:pr]
            tdeps = deps0 + [ev[b % 2]] + ([ev[1 - b % 2]] if fp else [])
            idm = identf if fp else ident
            tt = p.step("tensor", tdeps, lambda t, s=srcap, pt=pt, pr=pr, idm=idm:
                        t.transpose(pt, s, idm[0:pr, 0:pr]))

            if scale_col is None and b % 2 == 1:
                ev[b % 2] = p.step("scalar", [tt], lambda s, pt=pt, d=dstap:
                                   s.activation(d, pt, AF.Copy))
            else:
                def em(v, pt=pt, d=dstap, b=b):
                    if scale_col is None:
                        return v.tensor_copy(d, pt)
                    return v.tensor_scalar(d, pt, scale_col, None, ALU.mult)
                ev[b % 2] = p.step("vector", [tt], em)
            last = ev[b % 2]
        fin = p.step("vector", [e for e in ev if e], lambda v: v.engine_nop()) \
            if len(blocks) > 1 else last
        p.set_guard("pT", fin)
        return fin

    def _finish(tok, src):
        if tok is not None:
            p.out_dma([tok], [(x_out[i * 128:(i + 1) * 128, :],
                               src[:, i * D:(i + 1) * D]) for i in range(NT)])
        else:
            p.out_wait()
        blk_cm = nc.Block()
        block = blk_cm.__enter__()
        bld.finalize(block)
        blk_cm.__exit__(None, None, None)
        return nc

    tok_xn = ln_to(xn, x, tok_x)

    ew2 = p.sb("ew2", [128, NT * ES], BF16)
    rsum2 = p.sb("rsum2", [128, NT])
    slp = [p.sb("slp0", [128, D + 8]), p.sb("slp1", [128, D + 8])]
    fg = p.sb("fg", [128, D]); fb = p.sb("fb", [128, D]); sclt = p.sb("sclt", [128, 8])
    tok_fgb = p.dma("sync", [], [(fg[:], f_gamma_b), (fb[:], f_beta_b),
                                 (sclt[:, 0:1], scale_b)])
    tok_sabs = p.step("scalar", [tok_fgb], lambda s: s.activation(
        sclt[:, 0:1], sclt[:, 0:1], AF.Abs))
    nscl = p.sb("nscl", [128, 1])
    tok_nscl = p.step("vector", [tok_sabs], lambda v: v.tensor_scalar(
        nscl[:], sclt[:, 0:1], -1.02, -1e-3, ALU.mult, ALU.add))

    mu_u = p.sb("mu_u", [128, 8 * ES], BF16)
    mu_s = p.sb("mu_s", [128, 4 * ES])
    musq16 = p.sb("musq16", [128, 4 * ES], BF16)
    p_mr = p.ps("p_mr", [1, ES])
    tok_half = []
    tok_mu_parts = []
    for half in range(2):
        tok_mu = p.dma("sync", [tok_half[-1]] if tok_half else [], [
            (mu_s[:, j * ES:(j + 1) * ES], mu2d[(half * 4 + j) * 128:(half * 4 + j) * 128 + 128, :])
            for j in range(4)])
        tok_msq = p.step("scalar", [tok_mu] + ([tok_half[-1]] if tok_half else []),
                         lambda s: [
            s.activation(musq16[:, j * ES:(j + 1) * ES], mu_s[:, j * ES:(j + 1) * ES],
                         AF.Square) for j in range(4)][-1])
        tok_mr = p.step("tensor", [tok_msq, tok_const], lambda t, half=half: [
            t.matmul(p_mr[:], ones16[:, 0:1], musq16[:, j * ES:(j + 1) * ES],
                     start=(half == 0 and j == 0), stop=(half == 1 and j == 3))
            for j in range(4)][-1])
        tok_half.append(p.step("vector", [tok_mr], lambda v: v.engine_nop()))
        tok_mu_parts.append((tok_mu, tok_mr))
    mu_rn = p.sb("mu_rn", [1, ES])
    mu_rn16 = p.sb("mu_rn16", [1, ES], BF16)
    mu_rb = p.sb("mu_rb", [128, ES])
    p_mb = p.ps("p_mb", [128, ES])
    tok_mrn = p.step("vector", [tok_half[-1]], lambda v: v.tensor_copy(mu_rn[:], p_mr[:]))
    tok_mrn = p.step("scalar", [tok_mrn], lambda s: s.activation(mu_rn[:], mu_rn[:], AF.Sqrt))
    tok_mrn = p.step("vector", [tok_mrn], lambda v: v.tensor_scalar_max(mu_rn[:], mu_rn[:], NORM_EPS))
    tok_mrn = p.step("vector", [tok_mrn], lambda v: v.reciprocal(mu_rn[:], mu_rn[:]))
    tok_mrn16 = p.step("vector", [tok_mrn], lambda v: v.tensor_copy(mu_rn16[:], mu_rn[:]))
    tok_mb = p.step("tensor", [tok_mrn16], lambda t: t.matmul(
        p_mb[:], ones16[0:1, :], mu_rn16[:], start=True, stop=True))
    tok_mrb = p.step("vector", [tok_mb], lambda v: v.tensor_copy(mu_rb[:], p_mb[:]))
    tok_muh = tok_mrb
    for half in range(2):
        tok_mu2 = p.dma("sync", [tok_muh], [
            (mu_s[:, j * ES:(j + 1) * ES], mu2d[(half * 4 + j) * 128:(half * 4 + j) * 128 + 128, :])
            for j in range(4)])
        tok_muh = p.step("vector", [tok_mu2, tok_mrb], lambda v, half=half: [
            v.tensor_mul(mu_u[:, (half * 4 + j) * ES:(half * 4 + j) * ES + ES],
                         mu_s[:, j * ES:(j + 1) * ES], mu_rb[:]) for j in range(4)][-1])
    tok_muu = tok_muh
    p.free("mu_s", "musq16", "p_mr", "mu_rn", "mu_rn16", "mu_rb", "p_mb")

    xn16t = p.sb("xn16t", [128, D], BF16)
    xnf_t = p.sb("xnf_t", [128, D])
    u16 = p.sb("u16", [128, D], BF16)
    uT = p.sb("uT", [128, D], BF16)
    xnf16 = p.sb("xnf16", [128, D], BF16)
    ewt = p.sb("ewt", [128, 2 * ES], BF16)
    mxc = p.sb("mxc", [128, 4])
    p_sl = [p.ps("p_sl0", [128, 1024]), p.ps("p_sl1", [128, 1024])]
    p_gs = p.ps("p_gs", [128, 2])
    p_lg0 = p.ps("p_lg0", [128, ES])
    p_lg = [p_lg0, p_lg0]

    ln2_stats = ln_to(None, xn, [tok_xn], stats_only=True)

    xn16w = [None, None]; xnTw = [None, None]
    xfw = [None, None]; u16w = [None, None]; uTw = [None, None]
    lgw = [None, None]; eww = [None, None]; xf16w = [None, None]
    slmm_last = [None]
    ew2_toks = []
    for i in range(NT):
        pa = i % 2
        d_a = [tok_xn] + [t_ for t_ in (xnTw[0], xnTw[1]) if t_]
        xn16w[pa] = p.step("vector", d_a, lambda v, i=i: v.tensor_copy(
            xn16t[:, 0:D], xn[:, i * D:(i + 1) * D]))
        blksT = [(xn16t[:, dc * 128: dc * 128 + 128],
                  xnT[:, dc * TLOC + i * 128: dc * TLOC + i * 128 + 128])
                 for dc in range(8)]
        xnTw[pa] = transposes(blksT, [xn16w[pa], tok_const])
        d_b = [ln2_stats, tok_fgb] + ([xf16w[1 - pa]] if xf16w[1 - pa] else []) + \
              ([u16w[1 - pa]] if u16w[1 - pa] else [])
        t_f = p.step("vector", d_b, lambda v, i=i: v.tensor_scalar(
            xnf_t[:, 0:D], xn[:, i * D:(i + 1) * D],
            stats[:, 2 * i:2 * i + 1], stats[:, 32 + i:32 + i + 1],
            ALU.subtract, ALU.mult))
        t_f = p.step("vector", [t_f], lambda v: v.tensor_mul(
            xnf_t[:, 0:D], xnf_t[:, 0:D], fg[:]))
        t_f = p.step("vector", [t_f], lambda v: v.tensor_add(
            xnf_t[:, 0:D], xnf_t[:, 0:D], fb[:]))
        t_n = p.step("scalar", [t_f], lambda s, i=i: s.activation(
            sq2[:, 0:D], xnf_t[:, 0:D], AF.Square,
            accum_out=stats[:, 48 + i:48 + i + 1]))
        t_n = p.step("scalar", [t_n], lambda s, i=i: s.activation(
            stats[:, 48 + i:48 + i + 1], stats[:, 48 + i:48 + i + 1], AF.Ln))
        t_n = p.step("scalar", [t_n], lambda s, i=i: s.activation(
            stats[:, 48 + i:48 + i + 1], stats[:, 48 + i:48 + i + 1], AF.Exp,
            scale=-0.5))
        t_n = p.step("vector", [t_n], lambda v, i=i: v.tensor_scalar_min(
            stats[:, 48 + i:48 + i + 1], stats[:, 48 + i:48 + i + 1],
            1.0 / NORM_EPS))
        t_n = p.step("vector", [t_n, tok_sabs], lambda v, i=i: v.tensor_scalar(
            stats[:, 48 + i:48 + i + 1], stats[:, 48 + i:48 + i + 1],
            sclt[:, 0:1], None, ALU.mult))
        d_u = [t_n] + [t_ for t_ in (uTw[0], uTw[1]) if t_]
        u16w[pa] = p.step("vector", d_u, lambda v, i=i: v.tensor_scalar(
            u16[:, 0:D], xnf_t[:, 0:D],
            stats[:, 48 + i:48 + i + 1], None, ALU.mult))
        blksU = [(u16[:, dc * 128: dc * 128 + 128],
                  uT[:, dc * 128: dc * 128 + 128])
                 for dc in range(8)]
        d_e = [u16w[pa]] + [t_ for t_ in (lgw[0], lgw[1]) if t_]
        uTw[pa] = transposes(blksU, d_e)
        d_fm = [uTw[pa], tok_muu] + [t for t in eww if t]
        lgw[pa] = p.step("tensor", d_fm, lambda t, pa=pa: [
            t.matmul(p_lg[pa][:], uT[:, dc * 128: dc * 128 + 128],
                     mu_u[:, dc * ES:(dc + 1) * ES], start=(dc == 0), stop=(dc == 7))
            for dc in range(8)][-1])
        d_g = [lgw[pa], tok_nscl] + ([slmm_last[0]] if slmm_last[0] and i >= 2 else [])
        eww[pa] = p.step("scalar", d_g, lambda s, i=i, pa=pa: s.activation(
            ewt[:, pa * ES:(pa + 1) * ES], p_lg[pa][:], AF.Exp, bias=nscl[:, 0:1],
            accum_out=stats[:, 64 + i:64 + i + 1]))
        t_m = p.step("vector", [eww[pa]], lambda v, i=i, pa=pa: v.reduce_max(
            mxc[:, pa:pa + 1], ewt[:, pa * ES:(pa + 1) * ES], axis=AX.X))
        t_m2 = p.step("vector", [t_m], lambda v, i=i, pa=pa: v.reciprocal(
            mxc[:, 2 + pa: 3 + pa], mxc[:, pa:pa + 1]))
        tok_e2 = p.step("vector", [t_m2], lambda v, i=i, pa=pa: v.tensor_scalar(
            ew2[:, i * ES:(i + 1) * ES], ewt[:, pa * ES:(pa + 1) * ES],
            mxc[:, 2 + pa:3 + pa], None, ALU.mult))
        ew2_toks.append(tok_e2)
        t_rs = p.step("vector", [eww[pa]], lambda v, i=i: v.reciprocal(
            stats[:, 80 + i:80 + i + 1], stats[:, 64 + i:64 + i + 1]))
        t_rs = p.step("vector", [t_rs, t_m], lambda v, i=i, pa=pa: v.tensor_scalar(
            rsum2[:, i:i + 1], mxc[:, pa:pa + 1], stats[:, 80 + i:80 + i + 1],
            None, ALU.mult))
        ew2_toks.append(t_rs)
        d_j = [t_f] + ([slmm_last[0]] if slmm_last[0] and i >= 2 else [])
        xf16w[pa] = p.step("vector", d_j + [t_ for t_ in (slmm_last[0],) if t_],
                           lambda v: v.tensor_copy(xnf16[:, 0:D], xnf_t[:, 0:D]))
        slmm_last[0] = p.step("tensor", [xf16w[pa], eww[pa], tok_const],
                              lambda t, i=i, pa=pa: ([
            t.matmul(p_sl[ec][:, nn * 512:(nn + 1) * 512],
                     ewt[:, pa * ES + ec * 128: pa * ES + ec * 128 + 128],
                     xnf16[:, nn * 512: nn * 512 + 512],
                     start=(i == 0), stop=(i == NT - 1))
            for ec in range(2) for nn in range(2)] + [
            t.matmul(p_gs[:, ec:ec + 1],
                     ewt[:, pa * ES + ec * 128: pa * ES + ec * 128 + 128],
                     ones16[:, 0:1], start=(i == 0 and ec == 0),
                     stop=(i == NT - 1 and ec == 1))
            for ec in range(2)])[-1])
    tok_ew2 = p.step("vector", ew2_toks[-4:], lambda v: v.engine_nop())
    tok_slpz = p.step("vector", [], lambda v: [
        v.memset(slp[ec][:, D + 1:D + 8], 0.0) for ec in range(2)][-1])
    sl_ev = [p.step("vector", [slmm_last[0], tok_slpz], lambda v, ec=ec: [
        v.tensor_copy(slp[ec][:, 0:512], p_sl[ec][:, 0:512]),
        v.tensor_copy(slp[ec][:, 512:1024], p_sl[ec][:, 512:1024]),
        v.tensor_copy(slp[ec][:, D:D + 1], p_gs[:, ec:ec + 1]),
    ][-1]) for ec in range(2)]
    tok_xnT = p.step("vector", [xnTw[0], xnTw[1]], lambda v: v.engine_nop())

    tok_ffs = p.dma("gpsimd", sl_ev, [(ffn_snd.ap()[0:128, :], slp[0][:]),
                                      (ffn_snd.ap()[128:256, :], slp[1][:])])
    tok_ffc = p.coll([tok_ffs], "AllGather", ALU.bypass, PAIRS,
                     [ffn_snd.ap().opt()], [ffn_rcv.ap().opt()])
    p.free("xn16t", "xnf_t", "u16", "uT", "xnf16", "ewt", "mxc",
           "p_lg0", "mu_u", "fg", "fb", "nscl", "sclt")
    p.free("slp0", "slp1")
    p.free("p_sl0", "p_sl1", "p_gs")

    if upto < 4:
        return _finish(tok_ffc, xn)

    bar_p4 = p.barrier()
    slotT = p.sb("slotT", [128, 8 * ES], BF16)
    sli16 = [p.sb("sli16_0", [128, D], BF16), p.sb("sli16_1", [128, D], BF16)]
    sli = [p.sb("sli0", [128, D + 8]), p.sb("sli1", [128, D + 8])]
    slr = [p.sb("slr00", [128, D + 8]), p.sb("slr01", [128, D + 8]),
           p.sb("slr10", [128, D + 8]), p.sb("slr11", [128, D + 8])]
    tok_ffr = p.dma("gpsimd", [tok_ffc, tok_ew2, tok_xnT, bar_p4], [
        (slr[0][:], ffn_rcv.ap()[0:128, :]), (slr[1][:], ffn_rcv.ap()[128:256, :]),
        (slr[2][:], ffn_rcv.ap()[256:384, :]), (slr[3][:], ffn_rcv.ap()[384:512, :])])
    tok_sli = [p.step("vector", [tok_ffr], lambda v, ec=ec: v.tensor_add(
        sli[ec][:], slr[ec][:], slr[2 + ec][:])) for ec in range(2)]
    tok_gsr = [p.step("vector", [tok_sli[ec], p.guard("stats")], lambda v, ec=ec:
        v.reciprocal(stats[:, 90 + ec:91 + ec], sli[ec][:, D:D + 1])) for ec in range(2)]
    tok_sli16 = [p.step("vector", [tok_gsr[ec]], lambda v, ec=ec: v.tensor_scalar(
        sli16[ec][:], sli[ec][:, 0:D], stats[:, 90 + ec:91 + ec], None, ALU.mult))
        for ec in range(2)]
    p.free("slr00", "slr01", "slr10", "slr11", "sli0", "sli1")
    blks_s = []
    for ec in range(2):
        for dc in range(8):
            blks_s.append((sli16[ec][:, dc * 128:(dc + 1) * 128],
                           slotT[:, dc * ES + ec * 128: dc * ES + ec * 128 + 128]))
    tok_slotT = transposes(blks_s, tok_sli16[0], extra_dep=tok_sli16[1])
    p.free("sli16_0", "sli16_1")
    bar_slr = p.barrier()

    so16 = [p.sb("so16_0", [128, D], BF16), p.sb("so16_1", [128, D], BF16)]
    so_stage = [p.sb("so_st0", [16, D], BF16), p.sb("so_st1", [16, D], BF16)]
    b1s = p.sb("b1s", [128, 2 * NE])
    tok_b1 = p.dma("sync", [bar_slr], [(b1s[:, e * 2 + hc: e * 2 + hc + 1],
                                 b1_in[e:e + 1, hc * 128:(hc + 1) * 128])
                                for e in range(NE) for hc in range(2)])
    b2s16 = p.sb("b2s16", [33, D], BF16)
    w1b = [p.sb("w1b0", [128, 8 * FH], BF16), p.sb("w1b1", [128, 8 * FH], BF16)]
    w2b = [p.sb("w2b0", [128, 2 * D], BF16), p.sb("w2b1", [128, 2 * D], BF16)]
    hT = [p.sb("hT0", [128, 32], BF16), p.sb("hT1", [128, 32], BF16)]
    p_h = [p.ps("p_h0", [128, 32]), p.ps("p_h1", [128, 32])]
    p_so = [p.ps("p_so0", [16, 1024]), p.ps("p_so1", [16, 1024])]
    w1_tok = [None, None]; w2_tok = [None, None]
    hmm_tok = [None, None]; hact_tok = [None, None]
    somm_tok = [None, None]; soev_tok = [None, None]
    for e in range(NE):
        pa = e % 2
        d1 = [bar_slr] + ([hmm_tok[pa]] if hmm_tok[pa] else [])
        w1_tok[pa] = p.dma("gpsimd", d1, [
            (w1b[pa][:, dc * FH:(dc + 1) * FH],
             w1f[e * D + dc * 128: e * D + dc * 128 + 128, :]) for dc in range(8)])
        d2 = [bar_slr] + ([somm_tok[pa]] if somm_tok[pa] else [])
        w2_tok[pa] = p.dma("gpsimd", d2, [
            (w2b[pa][:, hc * D:(hc + 1) * D],
             w2f[e * FH + hc * 128: e * FH + hc * 128 + 128, :]) for hc in range(2)] + [
            (b2s16[32 * pa:32 * pa + 1, :], b2_in[e:e + 1, :])])
        hdeps = [w1_tok[pa], tok_slotT] + ([hact_tok[pa]] if hact_tok[pa] else [])
        hmm_tok[pa] = p.step("tensor", hdeps, lambda t, e=e, pa=pa: [
            t.matmul(p_h[pa][:, hc * 16:(hc + 1) * 16],
                     w1b[pa][:, dc * FH + hc * 128: dc * FH + hc * 128 + 128],
                     slotT[:, dc * ES + e * 16: dc * ES + e * 16 + 16],
                     start=(dc == 0), stop=(dc == 7))
            for hc in range(2) for dc in range(8)][-1])
        hact_deps = [hmm_tok[pa], tok_b1] + ([somm_tok[pa]] if somm_tok[pa] else [])
        hact_tok[pa] = p.step("scalar", hact_deps, lambda s, e=e, pa=pa: [
            s.activation(hT[pa][:, hc * 16:(hc + 1) * 16], p_h[pa][:, hc * 16:(hc + 1) * 16],
                         AF.Gelu, bias=b1s[:, e * 2 + hc: e * 2 + hc + 1])
            for hc in range(2)][-1])
        sdeps = [hact_tok[pa], w2_tok[pa]] + ([soev_tok[pa]] if soev_tok[pa] else [])
        somm_tok[pa] = p.step("tensor", sdeps, lambda t, e=e, pa=pa: ([
            t.matmul(p_so[pa][:, n2 * 512:(n2 + 1) * 512],
                     hT[pa][:, hc * 16:(hc + 1) * 16],
                     w2b[pa][:, hc * D + n2 * 512: hc * D + n2 * 512 + 512],
                     start=(hc == 0), stop=False)
            for n2 in range(2) for hc in range(2)] + [
            t.matmul(p_so[pa][:, n2 * 512:(n2 + 1) * 512],
                     ones16[32 * pa:32 * pa + 1, 0:16],
                     b2s16[32 * pa:32 * pa + 1, n2 * 512: n2 * 512 + 512],
                     start=False, stop=True)
            for n2 in range(2)])[-1])
        sodeps = [somm_tok[pa]] + ([soev_tok[pa]] if soev_tok[pa] else [])
        so_cp = p.step("vector", sodeps, lambda v, pa=pa:
            v.tensor_copy(so_stage[pa][:], p_so[pa][:, 0:1024]))
        soev_tok[pa] = p.dma("gpsimd", [so_cp], [
            (so16[e // 8][(e % 8) * 16:(e % 8) * 16 + 16, :], so_stage[pa][:])])
    tok_so = p.step("vector", [t for t in soev_tok if t], lambda v: v.engine_nop())
    p.free("w1b0", "w1b1", "w2b0", "w2b1", "hT0", "hT1", "p_h0", "p_h1",
           "p_so0", "p_so1", "b2s16", "b1s", "so_st0", "so_st1")

    ew2T = [p.sb("ew2T0", [128, TLOC], BF16), p.sb("ew2T1", [128, TLOC], BF16)]
    p_cb = [p.ps("p_cb0", [128, 1024]), p.ps("p_cb1", [128, 1024])]
    blks_e = []
    for ec in range(2):
        for i in range(NT):
            blks_e.append((ew2[:, i * ES + ec * 128: i * ES + ec * 128 + 128],
                           ew2T[ec][:, i * 128:(i + 1) * 128]))
    tok_ew2T = transposes(blks_e, [tok_ew2, bar_p4, tok_so])
    cb_mm = [None, None]; cb_ev = [None, None]
    for i in range(NT):
        pa = i % 2
        deps = [tok_ew2T, tok_so] + ([cb_ev[pa]] if cb_ev[pa] else [])
        cb_mm[pa] = p.step("tensor", deps, lambda t, i=i, pa=pa: [
            t.matmul(p_cb[pa][:, n2 * 512:(n2 + 1) * 512],
                     ew2T[ec][:, i * 128:(i + 1) * 128],
                     so16[ec][:, n2 * 512: n2 * 512 + 512],
                     start=(ec == 0), stop=(ec == 1))
            for n2 in range(2) for ec in range(2)][-1])
        cb_ev[pa] = p.step("vector", [cb_mm[pa], tok_ew2], lambda v, i=i, pa=pa: [
            v.scalar_tensor_tensor(x[:, i * D + n2 * 512: i * D + n2 * 512 + 512],
                                   p_cb[pa][:, n2 * 512:(n2 + 1) * 512],
                                   rsum2[:, i:i + 1],
                                   x[:, i * D + n2 * 512: i * D + n2 * 512 + 512],
                                   ALU.mult, ALU.add)
            for n2 in range(2)][-1])
    tok_ffn0 = p.step("vector", [t for t in cb_ev if t], lambda v: v.engine_nop())
    p.free("p_cb0", "p_cb1", "ew2T0", "ew2T1", "so16_0", "so16_1", "slotT",
           "ew2", "rsum2")
    p.free("xn")

    if upto < 5:
        return _finish(tok_ffn0, x)

    bar0 = p.barrier()
    p.set_guard("pT", bar0)

    rcq = p.sb("rcq", [128, TLOC], BF16); rsq = p.sb("rsq", [128, TLOC], BF16)
    rck = p.sb("rck", [128, TLOC], BF16); rsk = p.sb("rsk", [128, TLOC], BF16)
    maskT_s = p.sb("maskT_s", [128, RH * RC])
    cdec_s = p.sb("cdec_s", [128, RH * RC])
    kdec_s = p.sb("kdec_s", [128, RH])
    flag_s = p.sb("flag_s", [128, 1])
    gnw_s = p.sb("gnw_s", [128, RDV])
    tstage = p.sb("tstage", [128, TLOC])
    tok_tb = bar0
    for tin, tout in ((rope_cos_q, rcq), (rope_sin_q, rsq), (rope_cos, rck), (rope_sin, rsk)):
        tok_ld = p.dma("sync", [tok_tb, bar0], [(tstage[:], tin)])
        tok_tb = p.step("vector", [tok_ld], lambda v, tout=tout: v.tensor_copy(tout[:], tstage[:]))
    tok_rc = p.dma("sync", [bar0], [
        (gnw_s[:], gnw_b), (flag_s[:], ret_flag)] + [
        (maskT_s[:, h * RC:(h + 1) * RC], ret_maskT[h]) for h in range(RH)] + [
        (cdec_s[:, h * RC:(h + 1) * RC], ret_cdec[h]) for h in range(RH)] + [
        (kdec_s[:, h:h + 1], ret_kdec[h]) for h in range(RH)])
    tok_rc = p.step("vector", [tok_rc, tok_tb], lambda v: v.engine_nop())
    p.free("tstage")

    NSC = 4
    SCH = TLOC // NSC
    CPS = SCH // RC
    tok_l0 = tok_ffn0

    p_pj = p.ps("p_pj", [128, 1024])
    p_vps = p_pj
    rtmp = p.sb("rtmp", [128, 4 * SCH], BF16)
    rt2 = p.sb("rt2", [128, 4 * SCH], BF16)
    qd16 = p.sb("qd16", [128, 2 * 256], BF16)
    smid16 = p.sb("smid16", [128, 2 * 512], BF16)
    q16 = p.sb("q16", [128, 2 * TLOC], BF16)

    def proj_rope1(dst16, wslc, tcol0, cosT, sinT, deps, slot=0, dcol=0):
        deps = list(deps)
        evg = p.guard("proj_ev")
        rg = p.guard(f"projS{slot}")
        off = slot * 2 * SCH
        ev = []
        for oc in range(2):
            mm = p.step("tensor", deps + [evg], lambda t, oc=oc, wslc=wslc,
                        tcol0=tcol0: [
                t.matmul(p_pj[:, oc * 512: oc * 512 + 512],
                         wslc[:, dc * 256 + oc * 128: dc * 256 + oc * 128 + 128],
                         xnT[:, dc * TLOC + tcol0: dc * TLOC + tcol0 + SCH],
                         start=(dc == 0), stop=(dc == 7)) for dc in range(8)][-1])
            ev.append(p.step("vector", [mm, rg] + deps, lambda v, oc=oc, off=off:
                v.tensor_copy(rtmp[:, off + oc * SCH: off + (oc + 1) * SCH],
                              p_pj[:, oc * 512: oc * 512 + 512])))
        tok_raw = p.step("vector", ev, lambda v: v.engine_nop())
        p.set_guard("proj_ev", tok_raw)
        cs = cosT[:, tcol0:tcol0 + SCH]
        sn = sinT[:, tcol0:tcol0 + SCH]
        a0 = rtmp[:, off:off + SCH]; a1 = rtmp[:, off + SCH:off + 2 * SCH]
        b0 = rt2[:, off:off + SCH]; b1 = rt2[:, off + SCH:off + 2 * SCH]
        tA = p.step("vector", [tok_raw], lambda v, a0=a0, b0=b0, cs=cs:
                    v.tensor_mul(b0, a0, cs))
        tB = p.step("gpsimd", [tok_raw], lambda g_, a1=a1, b1=b1, sn=sn:
                    g_.tensor_mul(b1, a1, sn))
        r0 = p.step("vector", [tA, tB] + deps, lambda v, b0=b0, b1=b1,
                    dst16=dst16, dcol=dcol: v.tensor_sub(
                        dst16[:, dcol:dcol + SCH], b0, b1))
        tC = p.step("vector", [r0], lambda v, a1=a1, b0=b0, cs=cs:
                    v.tensor_mul(b0, a1, cs))
        tD = p.step("gpsimd", [r0], lambda g_, a0=a0, b1=b1, sn=sn:
                    g_.tensor_mul(b1, a0, sn))
        r1 = p.step("vector", [tC, tD] + deps, lambda v, b0=b0, b1=b1,
                    dst16=dst16, dcol=dcol: v.tensor_add(
                        dst16[:, dcol + SCH:dcol + 2 * SCH], b0, b1))
        p.set_guard(f"projS{slot}", r1)
        return r1

    st_bounce = [st_snd, st_snd1]
    st_rbounce = [st_rcv, st_rcv1]
    wq_handles = {}
    o_buf = {}
    tok_h_done = {}
    for h in range(RH):
        pair_i = h % 2
        if True:
            wqb = p.sb(f"wqb{h}", [128, 8 * 256], BF16)
            wq_handles[h] = wqb
            o_h = p.sb(f"o_h{h}", [128, RNC * 512])
            o_buf[h] = o_h
            wkb = p.sb(f"wkb{h}", [128, 8 * 256], BF16)
            wvb = p.sb(f"wvb{h}", [128, 8 * 512], BF16)
            k16 = p.sb(f"k16_{h}", [128, 2 * SCH], BF16)
            v16 = p.sb(f"v16_{h}", [128, 4 * 512], BF16)
            ktm = p.sb(f"ktm_{h}", [128, 2 * 256], BF16)
            S_f = p.sb(f"S_f{h}", [128, 2 * 512])
            S16 = p.sb(f"S16_{h}", [128, 2 * 512], BF16)
            at16 = p.sb(f"at16_{h}", [128, 2 * 128], BF16)
            p_at = p.ps(f"p_at{h}", [128, 128])
            p_o = p.ps(f"p_o{h}", [128, 512])
            p_ds = p.ps(f"p_ds{h}", [128, 1024])
            tok_wq = p.dma("gpsimd", [bar0, p.guard("wq")], [
                (wqb[:, dc * 256:(dc + 1) * 256],
                 WqT[dc * 128:(dc + 1) * 128, h * 256: h * 256 + 256]) for dc in range(8)])
            tok_wk = p.dma("gpsimd", [bar0, p.guard("wq")], [
                (wkb[:, dc * 256:(dc + 1) * 256],
                 WkT[dc * 128:(dc + 1) * 128, h * 256: h * 256 + 256]) for dc in range(8)])
            tok_wv = p.dma("gpsimd", [bar0, p.guard("wq")], [
                (wvb[:, dc * 512:(dc + 1) * 512],
                 WvT[dc * 128:(dc + 1) * 128, h * 512: h * 512 + 512]) for dc in range(8)])
            tok_z = p.step("vector", [tok_l0, tok_rc, bar0, p.guard("xadd")],
                           lambda v, S_f=S_f: v.memset(S_f[:], 0.0))
            tok_z16 = p.step("scalar", [tok_z], lambda s, S16=S16: s.memzero(S16[:]))

            s16_last = [tok_z16]
            supd_last = [tok_z]
            oev_last = [tok_z]
            atev_last = [tok_z]
            ds_last = [tok_z16]
            qdw = [None, None]
            atw = [None, None]
            scT = [tok_z16]; scV = [tok_z]
            oev_toks = []
            for sc in range(NSC):
                tcol0 = sc * SCH
                pdeps = [tok_rc, tok_xnT, scT[0], scV[0], p.guard("xadd")]
                tq = proj_rope1(q16, wqb, tcol0, rcq, rsq,
                                pdeps + [tok_wq, p.guard("q16")], slot=0,
                                dcol=sc * 2 * SCH)
                tk = proj_rope1(k16, wkb, tcol0, rck, rsk,
                                pdeps + [tok_wk], slot=1)
                vev = [None, None]
                for tt_ in range(4):
                    pa_ = tt_ % 2
                    vdeps = pdeps + [tok_wv, p.guard("proj_ev")] + \
                            ([vev[pa_]] if vev[pa_] else [])
                    vmm = p.step("tensor", vdeps, lambda t, tt_=tt_, pa_=pa_,
                                 wvb=wvb, tcol0=tcol0: [
                        t.matmul(p_vps[:, pa_ * 512: pa_ * 512 + 512],
                                 xnT[:, dc * TLOC + tcol0 + tt_ * 128:
                                     dc * TLOC + tcol0 + tt_ * 128 + 128],
                                 wvb[:, dc * 512:(dc + 1) * 512],
                                 start=(dc == 0), stop=(dc == 7)) for dc in range(8)][-1])
                    vev[pa_] = p.step("vector", [vmm] + pdeps, lambda v, tt_=tt_, pa_=pa_,
                                      v16=v16: v.tensor_copy(
                        v16[:, tt_ * 512:(tt_ + 1) * 512],
                        p_vps[:, pa_ * 512: pa_ * 512 + 512]))
                tv = p.step("vector", [e for e in vev if e], lambda v: v.engine_nop())
                p.set_guard("proj_ev", tv)
                for cc in range(CPS):
                    c = sc * CPS + cc
                    pa = c % 2
                    kblks = [(k16[:, blk * SCH + cc * 128: blk * SCH + cc * 128 + 128],
                              ktm[:, pa * 256 + blk * 128: pa * 256 + blk * 128 + 128])
                             for blk in range(2)]
                    tok_ktm = transposes(kblks, [tk, ds_last[0]],
                                         scale_col=kdec_s[:, h:h + 1])
                    qdeps = [tq, tok_rc] + ([qdw[pa]] if qdw[pa] else [])
                    tok_qd = p.step("vector", qdeps, lambda v, pa=pa, cc=cc, h=h,
                                    qo=sc * 2 * SCH: [
                        v.tensor_mul(qd16[:, pa * 256 + blk * 128: pa * 256 + blk * 128 + 128],
                                     q16[:, qo + blk * SCH + cc * 128:
                                         qo + blk * SCH + cc * 128 + 128],
                                     cdec_s[:, h * RC:(h + 1) * RC]) for blk in range(2)][-1])
                    amm = p.step("tensor", [tk, tq, atev_last[0]],
                                 lambda t, cc=cc, p_at=p_at, k16=k16,
                                 qo=sc * 2 * SCH: [
                        t.matmul(p_at[:],
                                 k16[:, blk * SCH + cc * 128: blk * SCH + cc * 128 + 128],
                                 q16[:, qo + blk * SCH + cc * 128:
                                     qo + blk * SCH + cc * 128 + 128],
                                 start=(blk == 0), stop=(blk == 1)) for blk in range(2)][-1])
                    aev = p.step("vector", [amm, tok_rc] + ([atw[pa]] if atw[pa] else []),
                                 lambda v, pa=pa, h=h, p_at=p_at, at16=at16: v.tensor_mul(
                        at16[:, pa * 128:(pa + 1) * 128], p_at[:],
                        maskT_s[:, h * RC:(h + 1) * RC]))
                    atev_last[0] = aev
                    omm = p.step("tensor", [aev, tok_qd, tv, oev_last[0]] + s16_last,
                                 lambda t, pa=pa, cc=cc, p_o=p_o, at16=at16, S16=S16, v16=v16: ([
                        t.matmul(p_o[:], at16[:, pa * 128:(pa + 1) * 128],
                                 v16[:, cc * 512:(cc + 1) * 512], start=True, stop=False)] + [
                        t.matmul(p_o[:], qd16[:, pa * 256 + blk * 128: pa * 256 + blk * 128 + 128],
                                 S16[:, blk * 512:(blk + 1) * 512],
                                 start=False, stop=(blk == 1)) for blk in range(2)])[-1])
                    qdw[pa] = omm
                    atw[pa] = omm
                    oev = p.step("vector", [omm], lambda v, c=c, p_o=p_o, o_h=o_h:
                        v.tensor_copy(o_h[:, c * 512:(c + 1) * 512], p_o[:]))
                    oev_last[0] = oev
                    oev_toks.append(oev)
                    dmm = p.step("tensor", [tok_ktm, tv] + supd_last,
                                 lambda t, pa=pa, cc=cc, p_ds=p_ds, ktm=ktm, v16=v16: [
                        t.matmul(p_ds[:, blk * 512:(blk + 1) * 512],
                                 ktm[:, pa * 256 + blk * 128: pa * 256 + blk * 128 + 128],
                                 v16[:, cc * 512:(cc + 1) * 512],
                                 start=True, stop=True) for blk in range(2)][-1])
                    ds_last[0] = dmm
                    gC = GAMMAS[h] ** RC
                    supd0 = p.step("vector", [dmm] + s16_last,
                                   lambda v, gC=gC, S_f=S_f, p_ds=p_ds:
                        v.scalar_tensor_tensor(S_f[:, 0:512], S_f[:, 0:512], gC,
                                               p_ds[:, 0:512], ALU.mult, ALU.add))
                    supd1 = p.step("vector", [dmm] + s16_last,
                                   lambda v, gC=gC, S_f=S_f, p_ds=p_ds:
                        v.scalar_tensor_tensor(S_f[:, 512:1024], S_f[:, 512:1024],
                                               gC, p_ds[:, 512:1024],
                                               ALU.mult, ALU.add))
                    supd_last = [supd0, supd1]
                    s16a = p.step("scalar", [supd0, omm],
                                  lambda s, S16=S16, S_f=S_f:
                        s.copy(S16[:, 0:512], S_f[:, 0:512]))
                    s16b = p.step("gpsimd", [supd1, omm],
                                  lambda g_, S16=S16, S_f=S_f:
                        g_.tensor_copy(S16[:, 512:1024], S_f[:, 512:1024]))
                    s16_last = [s16a, s16b]
                scT[0] = p.merge("tensor", [])
                scV[0] = p.step("vector", [], lambda v: v.engine_nop())
            tok_scan = p.step("vector", oev_toks + s16_last + supd_last,
                              lambda v: v.engine_nop())
            p.set_guard("q16", tok_scan)
            tok_h_done[h] = tok_scan
            sb_ = st_bounce[pair_i]
            tok_sp = p.dma("gpsimd", [tok_scan, p.guard(f"stb{pair_i}")], [
                (sb_.ap()[blk * 128: blk * 128 + 128, :],
                 S_f[:, blk * 512:(blk + 1) * 512]) for blk in range(2)])
            tok_h_done[h] = tok_sp
            p.free(f"wkb{h}", f"wvb{h}", f"k16_{h}", f"v16_{h}", f"ktm_{h}",
                   f"S_f{h}", f"S16_{h}", f"at16_{h}",
                   f"p_at{h}", f"p_o{h}", f"p_ds{h}")
            p.set_guard("wq", tok_scan)

        rb_ = st_rbounce[pair_i]
        tok_scoll = p.coll([tok_h_done[h]], "AllGather", ALU.bypass,
                           PAIRS, [st_bounce[pair_i].ap().opt()], [rb_.ap().opt()])
        tok_sm = p.dma("gpsimd", [tok_scoll, p.guard("smid")], [
            (smid16[:, sl * 512:(sl + 1) * 512], rb_.ap()[sl * 128:(sl + 1) * 128, :])
            for sl in range(2)])
        tok_sm16 = p.step("vector", [tok_sm, tok_rc], lambda v: [
            v.tensor_scalar(smid16[:, sl * 512:(sl + 1) * 512],
                            smid16[:, sl * 512:(sl + 1) * 512],
                            flag_s[:, 0:1], None, ALU.mult) for sl in range(2)][-1])
        p.set_guard(f"stb{pair_i}", tok_scoll)

        if True:
            o_h = o_buf[h]
            wgb = p.sb(f"wgb{h}", [128, 8 * 512], BF16)
            wob = p.sb(f"wob{h}", [128, 4 * 1024], BF16)
            tok_wg = p.dma("gpsimd", [bar0, p.guard("wgb"), tok_scan], [
                (wgb[:, dc * 512:(dc + 1) * 512],
                 WgT[dc * 128:(dc + 1) * 128, h * 512: h * 512 + 512]) for dc in range(8)])
            tok_wo = p.dma("gpsimd", [bar0, p.guard("wob"), tok_scan], [
                (wob[:, ic * 1024:(ic + 1) * 1024],
                 WoT[h * 512 + ic * 128: h * 512 + ic * 128 + 128, :]) for ic in range(4)])
            sg16 = p.sb(f"sg16_{h}", [128, 2 * 512], BF16)
            og16 = p.sb(f"og16_{h}", [128, 2 * 512], BF16)
            ogT = p.sb(f"ogT_{h}", [128, 2 * 512], BF16)
            t_on = p.sb(f"t_on_{h}", [128, 2 * 512])
            vst = p.sb(f"vst_{h}", [128, RNC])
            p_c = p.ps(f"p_c_{h}", [128, 512])
            p_g = p.ps(f"p_g_{h}", [128, 512])
            p_w = p.ps(f"p_w_{h}", [128, 1024])
            cadd_last = [tok_sm16]
            tsil_last = [tok_sm16]
            xadd_last = [tok_sm16, p.guard("xadd")]
            qdw2 = [None, None]
            sgw = [None, None]
            ogw = [None, None]
            ogTw = [None, None]
            tonw = [None, None]
            sqw = [None, None]
            sq_toks = []
            smoff = 0
            for c in range(RNC):
                pa = c % 2
                gCj = GAMMAS[h] ** (RC * c)
                qo = (c // CPS) * 2 * SCH
                cc = c % CPS
                qdeps = [tok_scan, tok_rc] + ([qdw2[pa]] if qdw2[pa] else [])
                tok_qdg = p.step("vector", qdeps,
                                 lambda v, pa=pa, cc=cc, h=h, gCj=gCj, qo=qo: [
                    v.scalar_tensor_tensor(
                        qd16[:, pa * 256 + blk * 128: pa * 256 + blk * 128 + 128],
                        q16[:, qo + blk * SCH + cc * 128:
                            qo + blk * SCH + cc * 128 + 128],
                        gCj, cdec_s[:, h * RC:(h + 1) * RC],
                        ALU.mult, ALU.mult) for blk in range(2)][-1])
                cmm = p.step("tensor", [tok_qdg, tok_sm16, cadd_last[0]],
                             lambda t, pa=pa, p_c=p_c, smoff=smoff: [
                    t.matmul(p_c[:],
                             qd16[:, pa * 256 + blk * 128: pa * 256 + blk * 128 + 128],
                             smid16[:, smoff + blk * 512: smoff + blk * 512 + 512],
                             start=(blk == 0), stop=(blk == 1)) for blk in range(2)][-1])
                qdw2[pa] = cmm
                cadd = p.step("vector", [cmm], lambda v, c=c, p_c=p_c, o_h=o_h:
                    v.tensor_add(o_h[:, c * 512:(c + 1) * 512],
                                 o_h[:, c * 512:(c + 1) * 512], p_c[:]))
                cadd_last[0] = cadd
                svdeps = [cadd] + ([sqw[pa]] if sqw[pa] else [])
                sqw[pa] = p.step("scalar", svdeps,
                                 lambda s, c=c, pa=pa, o_h=o_h, vst=vst:
                    s.activation(sq2[:, (pa % 2) * 512:(pa % 2) * 512 + 512],
                                 o_h[:, c * 512:(c + 1) * 512],
                                 AF.Square, accum_out=vst[:, c:c + 1]))
                sq_toks.append(sqw[pa])
            tvb = p.step("vector", sq_toks, lambda v, vst=vst: v.tensor_scalar(
                vst[:], vst[:], 1.0 / RDV, LN_EPS, ALU.mult, ALU.add))
            tvb = p.step("scalar", [tvb], lambda s, vst=vst: s.activation(
                vst[:], vst[:], AF.Sqrt))
            tvb = p.step("vector", [tvb], lambda v, vst=vst: v.reciprocal(
                vst[:], vst[:]))
            for c in range(RNC):
                pa = c % 2
                gmm = p.step("tensor", [tok_wg, tok_xnT, tsil_last[0]],
                             lambda t, c=c, p_g=p_g, wgb=wgb: [
                    t.matmul(p_g[:],
                             xnT[:, dc * TLOC + c * 128: dc * TLOC + c * 128 + 128],
                             wgb[:, dc * 512:(dc + 1) * 512],
                             start=(dc == 0), stop=(dc == 7)) for dc in range(8)][-1])
                tsil = p.step("scalar", [gmm] + ([sgw[pa]] if sgw[pa] else []),
                              lambda s, pa=pa, p_g=p_g, sg16=sg16: s.activation(
                    sg16[:, pa * 512:(pa + 1) * 512], p_g[:], AF.Silu))
                tsil_last[0] = tsil
                tdeps = [tvb, tok_rc] + ([tonw[pa]] if tonw[pa] else [])
                ton = p.step("vector", tdeps, lambda v, c=c, pa=pa, o_h=o_h,
                             vst=vst, t_on=t_on: v.scalar_tensor_tensor(
                    t_on[:, pa * 512:(pa + 1) * 512],
                    o_h[:, c * 512:(c + 1) * 512],
                    vst[:, c:c + 1], gnw_s[:], ALU.mult, ALU.mult))
                ogdeps = [ton, tsil] + ([ogw[pa]] if ogw[pa] else [])
                tog = p.step("gpsimd", ogdeps, lambda g_, pa=pa, og16=og16,
                             t_on=t_on, sg16=sg16: g_.tensor_mul(
                    og16[:, pa * 512:(pa + 1) * 512],
                    t_on[:, pa * 512:(pa + 1) * 512],
                    sg16[:, pa * 512:(pa + 1) * 512]))
                sgw[pa] = tog
                tonw[pa] = tog
                oblks = [(og16[:, pa * 512 + ic * 128: pa * 512 + ic * 128 + 128],
                          ogT[:, pa * 512 + ic * 128: pa * 512 + ic * 128 + 128])
                         for ic in range(4)]
                tok_ogT = transposes(oblks, [tog] + ([ogTw[pa]] if ogTw[pa] else []))
                ogw[pa] = tok_ogT
                wmm = p.step("tensor", [tok_ogT, tok_wo] + xadd_last,
                             lambda t, pa=pa, p_w=p_w, ogT=ogT, wob=wob: [
                    t.matmul(p_w[:, n2 * 512:(n2 + 1) * 512],
                             ogT[:, pa * 512 + ic * 128: pa * 512 + ic * 128 + 128],
                             wob[:, ic * 1024 + n2 * 512: ic * 1024 + n2 * 512 + 512],
                             start=(ic == 0), stop=(ic == 3))
                    for n2 in range(2) for ic in range(4)][-1])
                ogTw[pa] = wmm
                xadd = p.step("vector", [wmm], lambda v, c=c, p_w=p_w: [
                    v.tensor_add(x[:, c * D + n2 * 512: c * D + n2 * 512 + 512],
                                 x[:, c * D + n2 * 512: c * D + n2 * 512 + 512],
                                 p_w[:, n2 * 512:(n2 + 1) * 512]) for n2 in range(2)][-1])
                xadd_last = [xadd]
            tok_epi_h = p.step("vector", [xadd_last[0], cadd_last[0]],
                               lambda v: v.engine_nop())
            p.set_guard("xadd", tok_epi_h)
            p.set_guard("smid", tok_epi_h)
            p.set_guard("wgb", tok_epi_h); p.set_guard("wob", tok_epi_h)
            p.set_guard("q16", tok_epi_h)
            p.set_guard("wq", tok_epi_h)
            p.free(f"wqb{h}", f"wgb{h}", f"wob{h}", f"sg16_{h}", f"og16_{h}",
                   f"ogT_{h}", f"t_on_{h}", f"vst_{h}",
                   f"p_c_{h}", f"p_g_{h}", f"p_w_{h}", f"o_h{h}")

    tok_l0_done = p.step("vector", [p.guard("xadd")], lambda v: v.engine_nop())
    p.free("rcq", "rsq", "rck", "rsk", "maskT_s", "cdec_s", "kdec_s", "flag_s",
           "gnw_s", "q16", "qd16", "smid16",
           "p_pj", "rtmp", "rt2")
    bar1 = p.barrier()

    if upto < 6:
        return _finish(tok_l0_done, x)

    xnT3 = xnT
    ones32s = p.sb("ones32s", [128, 128])
    logm = p.sb("logm", [128, NT * ME])
    oneh = p.sb("oneh", [128, NT * ME])
    mscale = p.sb("mscale", [128, NT])
    ag = p.sb("ag", [128, D]); ab = p.sb("ab", [128, D])
    mrank_s = p.sb("mrank_s", [128, ME])
    mpfx_s = p.sb("mpfx_s", [NCORES, 1])
    nmx = p.sb("nmx", [128, NT])
    gate = p.sb("gate", [128, NT])
    rts = p.sb("rts", [128, 2 * ME])
    ohT = p.sb("ohT", [8, TLOC], BF16)
    scn = p.sb("scn", [8, TLOC // 2])
    scnc = p.sb("scnc", [8, 2])
    scnT = p.sb("scnT", [128, NT * ME])
    cnts = p.sb("cnts", [NCORES, ME])
    pfxT = p.sb("pfxT", [1, 8])
    pfxb = p.sb("pfxb", [128, ME])
    xn2 = p.sb("xn2", [128, NT * D])
    tok_l1c = p.dma("sync", [bar1], [(ag[:], a_gamma_b), (ab[:], a_beta_b),
                                     (ones32s[:], ones_f32_in),
                                     (mrank_s[:], m_rank_in), (mpfx_s[:], m_prefix_in)])
    tok_xn2 = ln_to(xn2, x, [tok_l0_done, bar1])
    ln3_stats = ln_to(None, xn2, [tok_xn2], stats_only=True)

    x3t = p.sb("x3t", [128, D])
    x3t16 = p.sb("x3t16", [128, D], BF16)
    x3_w = [None, None]
    x3T_w = [None, None]
    for i in range(NT):
        pa = i % 2
        d_3 = [ln3_stats, tok_l1c] + ([x3T_w[0]] if x3T_w[0] else [])
        t3 = p.step("vector", d_3, lambda v, i=i: v.tensor_scalar(
            x3t[:, 0:D], xn2[:, i * D:(i + 1) * D],
            stats[:, 2 * i:2 * i + 1], stats[:, 32 + i:32 + i + 1],
            ALU.subtract, ALU.mult))
        t3 = p.step("vector", [t3], lambda v: v.tensor_mul(x3t[:, 0:D], x3t[:, 0:D], ag[:]))
        t3 = p.step("vector", [t3], lambda v: v.tensor_add(x3t[:, 0:D], x3t[:, 0:D], ab[:]))
        x3_w[pa] = p.step("vector", [t3] + ([x3T_w[0]] if x3T_w[0] else []),
                          lambda v: v.tensor_copy(x3t16[:, 0:D], x3t[:, 0:D]))
        blks3 = [(x3t16[:, dc * 128: dc * 128 + 128],
                  xnT3[:, dc * TLOC + i * 128: dc * TLOC + i * 128 + 128])
                 for dc in range(8)]
        x3T_w[0] = transposes(blks3, [x3_w[pa]])
    tok_xnT3 = p.merge("vector", [x3T_w[0]])

    wgr = p.sb("wgr", [128, 2 * D])
    ttd = p.sb("ttd", [128, 2 * D])
    ttr_last = {}
    wge_tok = [None, None]
    tr_w = [None, None]
    for e in range(ME):
        sl = e % 2
        wdeps = [tok_l1c] + ([ttr_last.get(e - 2)] if ttr_last.get(e - 2) else [])
        wge_tok[sl] = p.dma("sync", wdeps,
                            [(wgr[:, sl * D:(sl + 1) * D],
                              wg_bc_in[e * 128:(e + 1) * 128, :])])
        t_ = None
        for i in range(NT):
            sl2 = i % 2
            tdeps = ([wge_tok[sl], tok_xn2] if i == 0 else [t_]) + \
                    ([tr_w[sl2]] if tr_w[sl2] else [])
            t_ = p.step("vector", tdeps, lambda v, i=i, sl=sl, sl2=sl2:
                v.tensor_mul(ttd[:, sl2 * D:(sl2 + 1) * D],
                             xn2[:, i * D:(i + 1) * D],
                             wgr[:, sl * D:(sl + 1) * D]))
            tr_w[sl2] = p.step("scalar", [t_], lambda s, i=i, e=e, sl2=sl2:
                s.activation(ttd[:, sl2 * D:(sl2 + 1) * D],
                             ttd[:, sl2 * D:(sl2 + 1) * D], AF.Copy,
                             accum_out=logm[:, i * ME + e: i * ME + e + 1]))
        ttr_last[e] = p.merge("scalar", [tr_w[0], tr_w[1]])
    tok_logm = ttr_last[ME - 1]
    p.free("ttd", "wgr", "x3t16", "x3t")
    bar_rt = p.barrier()
    pTf_cell[0] = p.ps("pTf", [128, 128])

    tok_gate = [bar_rt]
    oh_toks = []
    for i in range(NT):
        t1 = p.step("vector", [tok_logm] + ([tok_gate[0]] if tok_gate[0] else []),
                    lambda v, i=i: v.reduce_max(nmx[:, i:i + 1],
                                                logm[:, i * ME:(i + 1) * ME],
                                                axis=AX.X, negate=True))
        t2 = p.step("scalar", [t1], lambda s, i=i: s.activation(
            rts[:, 0:ME], logm[:, i * ME:(i + 1) * ME], AF.Exp,
            bias=nmx[:, i:i + 1], accum_out=gate[:, i:i + 1]))
        t3 = p.step("vector", [t1], lambda v, i=i: v.tensor_scalar(
            oneh[:, i * ME:(i + 1) * ME], logm[:, i * ME:(i + 1) * ME],
            nmx[:, i:i + 1], None, ALU.add))
        t4 = p.step("vector", [t3], lambda v, i=i: v.tensor_scalar(
            oneh[:, i * ME:(i + 1) * ME], oneh[:, i * ME:(i + 1) * ME],
            0.0, None, ALU.is_ge))
        t5 = p.step("vector", [t4, tok_l1c, t2], lambda v, i=i: v.tensor_mul(
            rts[:, ME:2 * ME], oneh[:, i * ME:(i + 1) * ME], mrank_s[:]))
        t6 = p.step("vector", [t5], lambda v, i=i: v.reduce_max(
            nmx[:, i:i + 1], rts[:, ME:2 * ME], axis=AX.X, negate=True))
        t7 = p.step("vector", [t6], lambda v, i=i: v.tensor_scalar(
            rts[:, ME:2 * ME], rts[:, ME:2 * ME], nmx[:, i:i + 1], None, ALU.add))
        t8 = p.step("vector", [t7], lambda v, i=i: v.tensor_scalar(
            oneh[:, i * ME:(i + 1) * ME], rts[:, ME:2 * ME], 0.0, None, ALU.is_ge))
        oh_toks.append(t8)
        tok_gate[0] = t8
    tok_oneh = p.merge("vector", oh_toks[-2:])
    tok_grec = p.step("vector", [tok_oneh], lambda v: v.reciprocal(gate[:], gate[:]))

    tok_oh16 = p.step("vector", [tok_grec, bar_rt], lambda v: v.engine_nop())
    blksO = [(oneh[:, i * ME:(i + 1) * ME], ohT[0:8, i * 128:(i + 1) * 128])
             for i in range(NT)]
    tok_ohT = transposes(blksO, tok_oh16)
    HT = TLOC // 2
    tok_scnA = p.step("vector", [tok_ohT], lambda v: v.tensor_tensor_scan(
        scn[:], ohT[0:8, 0:HT], ohT[0:8, 0:HT], 0.0, ALU.add, ALU.bypass))
    tok_carry = p.step("vector", [tok_scnA], lambda v: v.tensor_copy(
        scnc[:, 0:1], scn[:, HT - 1:HT]))
    blksS1 = [(scn[0:8, i * 128:(i + 1) * 128], scnT[:, i * ME:(i + 1) * ME])
              for i in range(NT // 2)]
    tok_scnT1 = transposes(blksS1, [tok_carry])
    tok_scnB = p.step("vector", [tok_scnT1], lambda v: v.tensor_tensor_scan(
        scn[:], ohT[0:8, HT:TLOC], ohT[0:8, HT:TLOC], scnc[:, 0:1],
        ALU.add, ALU.bypass))
    blksS2 = [(scn[0:8, i * 128:(i + 1) * 128], scnT[:, (NT // 2 + i) * ME:(NT // 2 + i + 1) * ME])
              for i in range(NT // 2)]
    tok_scnT2 = transposes(blksS2, [tok_scnB])
    tok_scn = tok_scnB
    tok_cs = p.dma("gpsimd", [tok_scn], [(cnt_snd.ap()[0:1, 0:ME], scn[:, HT - 1:HT])])
    tok_cc8 = p.coll([tok_cs], "AllGather", ALU.bypass, ALL8,
                     [cnt_snd.ap().opt()], [cnt_rcv.ap().opt()])
    tok_cr = p.dma("gpsimd", [tok_cc8], [(cnts[:], cnt_rcv.ap()[:, :])])
    p_pfx = p.ps("p_pfx", [1, 8])
    tok_pfx = p.step("tensor", [tok_cr, tok_l1c], lambda t: t.matmul(
        p_pfx[:], mpfx_s[0:NCORES, :], cnts[:, 0:ME], start=True, stop=True))
    tok_pfxe = p.step("vector", [tok_pfx], lambda v: v.tensor_copy(
        pfxT[0:1, 0:8], p_pfx[:]))
    p_pfb = p.ps("p_pfb", [128, 8])
    tok_pfb = p.step("tensor", [tok_pfxe], lambda t: t.matmul(
        p_pfb[:], ones32s[0:1, :], pfxT[0:1, 0:8], start=True, stop=True))
    tok_pfbe = p.step("vector", [tok_pfb], lambda v: v.tensor_copy(pfxb[:], p_pfb[:]))
    tok_scnT = p.merge("vector", [tok_scnT2])
    sel_ev = [None]
    for i in range(NT):
        t1 = p.step("vector", [tok_scnT, tok_oneh] + ([sel_ev[0]] if sel_ev[0] else []),
                    lambda v, i=i: v.tensor_mul(rts[:, 0:ME], oneh[:, i * ME:(i + 1) * ME],
                                                scnT[:, i * ME:(i + 1) * ME]))
        t2 = p.step("vector", [t1], lambda v, i=i: v.reduce_sum(
            mscale[:, i:i + 1], rts[:, 0:ME], axis=AX.X))
        t3 = p.step("vector", [t2, tok_pfbe], lambda v, i=i: v.tensor_mul(
            rts[:, 0:ME], oneh[:, i * ME:(i + 1) * ME], pfxb[:]))
        t3 = p.step("vector", [t3], lambda v, i=i: v.reduce_sum(
            rts[:, ME:ME + 1], rts[:, 0:ME], axis=AX.X))
        t4 = p.step("vector", [t3], lambda v, i=i: v.scalar_tensor_tensor(
            mscale[:, i:i + 1], mscale[:, i:i + 1], -1.0, rts[:, ME:ME + 1],
            ALU.add, ALU.add))
        t5 = p.step("vector", [t4], lambda v, i=i: v.tensor_scalar(
            mscale[:, i:i + 1], mscale[:, i:i + 1], CAP, None, ALU.is_lt))
        t6 = p.step("vector", [t5, tok_grec], lambda v, i=i: v.tensor_mul(
            mscale[:, i:i + 1], mscale[:, i:i + 1], gate[:, i:i + 1]))
        sel_ev[0] = t6
    tok_mscale = p.merge("vector", [sel_ev[0]])

    if upto < 7:
        return _finish(tok_mscale, xn2)

    web = p.sb("web", [128, 8 * D], BF16)
    be16 = p.sb("be16", [1, D], BF16)
    tok_we = p.dma("gpsimd", [bar1, bar_rt], [
        (web[:, dc * D:(dc + 1) * D], mWe_in[dc * 128:(dc + 1) * 128, :])
        for dc in range(8)] + [(be16[:], m_be_in)])
    p.free("p_pfx", "p_pfb", "pTf")
    uT3 = p.sb("uT3", [128, D], BF16)
    x2m = p.sb("x2m", [128, D], BF16)
    p_y = [p.ps("p_y0", [128, 1024]), p.ps("p_y1", [128, 1024])]
    y_mm = [None, None]; y_ev = [None, None]
    cast_w = [None, None]; tr_w = [None, None]
    for i in range(NT):
        pa = i % 2
        cdeps = [tok_xn2, bar_rt] + ([tr_w[pa]] if tr_w[pa] else [])
        cast_w[pa] = p.step("vector", cdeps + [t_ for t_ in (tr_w[0], tr_w[1]) if t_],
                            lambda v, i=i: v.tensor_copy(
            x2m[:, 0:D], xn2[:, i * D:(i + 1) * D]))
        blksM = [(x2m[:, dc * 128: dc * 128 + 128],
                  uT3[:, dc * 128: dc * 128 + 128])
                 for dc in range(8)]
        tr_w[pa] = transposes(blksM, [cast_w[pa]] + [t_ for t_ in (y_mm[0], y_mm[1]) if t_])
        deps = [tok_we, tr_w[pa], tok_mscale] + ([y_ev[pa]] if y_ev[pa] else [])
        y_mm[pa] = p.step("tensor", deps, lambda t, i=i, pa=pa: ([
            t.matmul(p_y[pa][:, n2 * 512:(n2 + 1) * 512],
                     uT3[:, dc * 128: dc * 128 + 128],
                     web[:, dc * D + n2 * 512: dc * D + n2 * 512 + 512],
                     start=(dc == 0), stop=False)
            for n2 in range(2) for dc in range(8)] + [
            t.matmul(p_y[pa][:, n2 * 512:(n2 + 1) * 512],
                     ones16[0:1, 0:128], be16[0:1, n2 * 512: n2 * 512 + 512],
                     start=False, stop=True)
            for n2 in range(2)])[-1])
        y_ev[pa] = p.step("vector", [y_mm[pa], tok_mscale], lambda v, i=i, pa=pa: [
            v.scalar_tensor_tensor(x[:, i * D + n2 * 512: i * D + n2 * 512 + 512],
                                   p_y[pa][:, n2 * 512:(n2 + 1) * 512],
                                   mscale[:, i:i + 1],
                                   x[:, i * D + n2 * 512: i * D + n2 * 512 + 512],
                                   ALU.mult, ALU.add) for n2 in range(2)][-1])
    tok_moe = p.merge("vector", [y_ev[0], y_ev[1]])
    p.free("web", "be16", "p_y0", "p_y1", "uT3", "x2m", "xn2")

    if upto < 8:
        return _finish(tok_moe, x)

    p.free("logm", "oneh", "mscale", "ag", "ab",
           "mrank_s", "mpfx_s",
           "nmx", "gate", "rts", "ohT", "scn", "scnc", "scnT", "cnts", "pfxT", "pfxb")
    p.free("pTa", "pTb")
    bar_a = p.barrier()

    ps_big = p.ps("ps_big", [128, 2048])
    ps_qn = p.ps("ps_qn", [65, 512])
    ps_av = p.ps("ps_av", [65, 1024])
    ps_rdb = p.ps("ps_rdb", [64, 512])


    ISC = 1.0 / (ADH ** 0.5)
    NKB = T // 128
    xadd_last2 = [tok_moe]

    kbs_b = [p.sb("kbs_0", [1, 8]), p.sb("kbs_1", [1, 8])]
    coll = {}
    proj_evs = []
    k_dma = [None, None, None, None]
    sq_w = [None, None, None, None]
    v_dma = [None, None, None, None]
    for hh in range(2):
        nrow = p.sb(f"nrow_{hh}", [1, 640])
        wk2 = p.sb(f"wk2_{hh}", [128, 8 * 512], BF16)
        wv2 = p.sb(f"wv2_{hh}", [128, 8 * 512], BF16)
        kst = p.sb(f"kst_{hh}", [128, 4 * 512], BF16)
        vst2 = p.sb(f"vst2_{hh}", [128, 4 * 8 * VW], BF16)
        sqs = p.sb(f"sqs_{hh}", [128, 4 * 512], BF16)
        p_pr = ps_big
        p_n = ps_qn
        tok_w2l = p.dma("gpsimd", [bar_a, p.guard("projbufs")], [
            (wk2[:, dc * 512:(dc + 1) * 512],
             WqkvT[dc * 128:(dc + 1) * 128, D + hh * 512: D + hh * 512 + 512])
            for dc in range(8)] + [
            (wv2[:, dc * 512:(dc + 1) * 512],
             WqkvT[dc * 128:(dc + 1) * 128, 2 * D + hh * 512: 2 * D + hh * 512 + 512])
            for dc in range(8)])

        nred = [None]
        for pp in range(4):
            for tg in range(4):
                pa = (pp * 4 + tg) % 4
                deps = [tok_w2l, tok_xnT3] + ([k_dma[pa]] if k_dma[pa] else [])
                kmm = p.step("tensor", deps, lambda t, pp=pp, tg=tg, pa=pa, wk2=wk2: [
                    t.matmul(p_pr[:, pa * 512: pa * 512 + 512],
                             wk2[:, dc * 512 + pp * 128: dc * 512 + pp * 128 + 128],
                             xnT3[:, dc * TLOC + tg * 512: dc * TLOC + tg * 512 + 512],
                             start=(dc == 0), stop=(dc == 7)) for dc in range(8)][-1])
                kev = p.step("vector", [kmm], lambda v, pa=pa, kst=kst: v.tensor_copy(
                    kst[:, pa * 512: pa * 512 + 512], p_pr[:, pa * 512: pa * 512 + 512]))
                sdeps = [kev] + ([sq_w[pa]] if sq_w[pa] else [])
                tsq = p.step("scalar", sdeps, lambda s, pa=pa, kst=kst, sqs=sqs:
                    s.activation(sqs[:, pa * 512: pa * 512 + 512],
                                 kst[:, pa * 512: pa * 512 + 512], AF.Square))
                last_nm = None
                for hd in range(2):
                    idx = 512 + (pp * 2 + hd) * 4 + tg
                    nmdeps = [tsq] + ([nred[0]] if nred[0] else [])
                    last_nm = p.step("tensor", nmdeps,
                                     lambda t, pa=pa, hd=hd, sqs=sqs, p_n=p_n: t.matmul(
                        p_n[64:65, 0:512],
                        ones16[64 * hd:64 * hd + 64, 0:1],
                        sqs[64 * hd:64 * hd + 64, pa * 512: pa * 512 + 512],
                        start=True, stop=True))
                    ncp = p.step("vector", [last_nm], lambda v, nrow=nrow, p_n=p_n:
                        v.tensor_copy(nrow[0:1, 0:512], p_n[64:65, 0:512]))
                    nred[0] = p.step("vector", [ncp], lambda v, idx=idx, nrow=nrow:
                        v.reduce_max(nrow[0:1, idx:idx + 1],
                                     nrow[0:1, 0:512], axis=AX.X))
                sq_w[pa] = last_nm
                k_dma[pa] = p.dma("gpsimd", [kev], [
                    (k_snd[hh].ap()[pp * 128:(pp + 1) * 128, tg * 512:(tg + 1) * 512],
                     kst[:, pa * 512: pa * 512 + 512])])
        kb_tok = [nred[0]]
        for h8 in range(8):
            kb_tok[0] = p.step("vector", [kb_tok[0]],
                               lambda v, h8=h8, nrow=nrow: v.reduce_max(
                nrow[0:1, 576 + h8:577 + h8],
                nrow[0:1, 512 + h8 * 4: 512 + h8 * 4 + 4], axis=AX.X))

        for ti in range(NT):
            pa = ti % 4
            deps = [tok_w2l, tok_xnT3] + ([v_dma[pa]] if v_dma[pa] else [])
            vmm = p.step("tensor", deps, lambda t, ti=ti, pa=pa, wv2=wv2: [
                t.matmul(p_pr[:, pa * 512: pa * 512 + 512],
                         xnT3[:, dc * TLOC + ti * 128: dc * TLOC + ti * 128 + 128],
                         wv2[:, dc * 512:(dc + 1) * 512],
                         start=(dc == 0), stop=(dc == 7)) for dc in range(8)][-1])
            vz = p.step("vector", [vmm] + ([v_dma[pa]] if v_dma[pa] else []),
                        lambda v, pa=pa, vst2=vst2: v.memset(
                vst2[:, pa * 8 * VW:(pa + 1) * 8 * VW], 0.0))
            vev = p.step("vector", [vz], lambda v, pa=pa, vst2=vst2: [
                v.tensor_copy(
                    vst2[:, pa * 8 * VW + h8 * VW: pa * 8 * VW + h8 * VW + 64],
                    p_pr[:, pa * 512 + h8 * 64: pa * 512 + h8 * 64 + 64])
                for h8 in range(8)][-1])
            vone = p.step("vector", [vev], lambda v, pa=pa, vst2=vst2: [
                v.memset(vst2[:, pa * 8 * VW + h8 * VW + 64:
                              pa * 8 * VW + h8 * VW + 65], 1.0)
                for h8 in range(8)][-1])
            v_dma[pa] = p.dma("gpsimd", [vone], [
                (v_snd[hh].ap()[ti * 128:(ti + 1) * 128, :],
                 vst2[:, pa * 8 * VW:(pa + 1) * 8 * VW])])

        tok_kns = p.dma("gpsimd", [kb_tok[0]], [
            (kn_snd[hh].ap()[0:1, 0:8], nrow[0:1, 576:584])])
        tok_knc = p.coll([tok_kns] + ([coll[0][2]] if hh == 1 else []),
                         "AllGather", ALU.bypass, PAIRS,
                         [kn_snd[hh].ap().opt()], [kn_rcv[hh].ap().opt()])
        tok_kc = p.coll([d for d in k_dma if d] + [tok_knc],
                        "AllGather", ALU.bypass, PAIRS,
                        [k_snd[hh].ap().opt()], [k_rcv[hh].ap().opt()])
        tok_vc = p.coll([d for d in v_dma if d] + [tok_kc],
                        "AllGather", ALU.bypass, PAIRS,
                        [v_snd[hh].ap().opt()], [v_rcv[hh].ap().opt()])
        coll[hh] = (tok_knc, tok_kc, tok_vc)

        pdone = p.merge("vector", [d for d in k_dma if d] +
                        [d for d in v_dma if d] +
                        [d for d in sq_w if d] + [nred[0], kb_tok[0]])
        proj_evs.append(pdone)
        p.set_guard("projbufs", pdone)
        p.free(f"wk2_{hh}", f"wv2_{hh}", f"kst_{hh}", f"vst2_{hh}", f"sqs_{hh}",
               f"nrow_{hh}")
    psum_ok = p.merge("vector", proj_evs)
    p.set_guard("wq2", psum_ok)

    for hh in range(2):
        kbs = kbs_b[hh]
        wq2 = p.sb(f"wq2_{hh}", [128, 8 * 512], BF16)
        tok_w2l = p.dma("gpsimd", [p.guard("wq2")], [
            (wq2[:, dc * 512:(dc + 1) * 512],
             WqkvT[dc * 128:(dc + 1) * 128, hh * 512: hh * 512 + 512])
            for dc in range(8)])
        bar_kv = p.guard("wq2")
        tok_kc = coll[hh][1]

        knr = p.sb(f"knr_{hh}", [1, 16])
        tok_knl = p.dma("gpsimd", [coll[hh][0], bar_kv], [
            (knr[0:1, 0:8], kn_rcv[hh].ap()[0:1, :]),
            (knr[0:1, 8:16], kn_rcv[hh].ap()[1:2, :])])
        tkb = p.step("vector", [tok_knl], lambda v, knr=knr, kbs=kbs: v.tensor_max(
            kbs[0:1, 0:8], knr[0:1, 0:8], knr[0:1, 8:16]))

        vf = p.sb(f"vf_{hh}", [128, NKB * 8 * VW], BF16)
        tok_vf = p.dma("sync", [coll[hh][2], bar_kv], [
            (vf[:, kb * 8 * VW:(kb + 1) * 8 * VW],
             v_rcv[hh].ap()[kb * 128:(kb + 1) * 128, :])
            for kb in range(NKB)])
        o_half = p.sb(f"o_half_{hh}", [128, 4 * TLOC], BF16)
        qt = [p.sb(f"qt0_{hh}", [65, TLOC], BF16), p.sb(f"qt1_{hh}", [65, TLOC], BF16)]
        kfh = [p.sb(f"kfh0_{hh}", [65, T], BF16), p.sb(f"kfh1_{hh}", [65, T], BF16)]
        nq = p.sb(f"nq_{hh}", [1, TLOC])
        nq16 = p.sb(f"nq16_{hh}", [1, TLOC], BF16)
        ex16 = p.sb(f"ex16_{hh}", [128, 2048], BF16)
        qsq16 = p.sb(f"qsq16_{hh}", [64, 512], BF16)
        rdr = p.sb(f"rdr_{hh}", [1, 512])
        rdb_s = p.sb(f"rdb_s_{hh}", [64, 512])
        kb2 = p.sb(f"kb2_{hh}", [1, 8])
        p_q = ps_qn
        p_qn = ps_qn
        p_s = ps_big
        p_av = ps_av
        p_rdb = ps_rdb
        tkb2 = p.step("vector", [tkb], lambda v, kbs=kbs, kb2=kb2:
            v.tensor_scalar_mul(kb2[0:1, :], kbs[0:1, :], 1.0201))
        seed = p.guard("wq2")
        qev_g = [None]
        exp_w = [seed, seed]
        av_w = [None, None]
        av_prev = [None]
        on_prev = [None]
        on_prev_sl = [None, None]
        avq = [0]
        rd_w = [None]
        nq_w = [seed]
        qn_mm = [None]
        qrow_tok = {}
        qevl = {}
        kfo_tok = {}
        tail_prev = [None]

        def emit_kf(h8, dep_done):
            hpa = h8 % 2
            pp, hd = h8 // 2, h8 % 2
            kdeps = [tok_kc, bar_kv] + ([dep_done] if dep_done else [])
            tok_kfh = p.dma("sync", kdeps, [
                (kfh[hpa][0:64, r * TLOC:(r + 1) * TLOC],
                 k_rcv[hh].ap()[r * 512 + pp * 128 + hd * 64:
                                r * 512 + pp * 128 + hd * 64 + 64, :])
                for r in range(2)])
            kfo_tok[hpa] = p.step("vector", [tok_kfh], lambda v, hpa=hpa, kfh=kfh:
                                  v.memset(kfh[hpa][64:65, :], 1.0))

        def make_build(h8, dep_done):
            hpa = h8 % 2
            thunks = []
            qev_cell = {}

            def mk_qmm(tg):
                def th():
                    qdeps = [tok_w2l, tok_xnT3] + ([dep_done] if dep_done else []) \
                            + ([qev_g[0]] if qev_g[0] else [])
                    qmm = p.step("tensor", qdeps, lambda t, h8=h8, tg=tg, wq2=wq2,
                                 p_q=p_q, xnT3=xnT3: [
                        t.matmul(p_q[0:64, :],
                                 wq2[:, dc * 512 + h8 * 64: dc * 512 + h8 * 64 + 64],
                                 xnT3[:, dc * TLOC + tg * 512:
                                      dc * TLOC + tg * 512 + 512],
                                 start=(dc == 0), stop=(dc == 7))
                        for dc in range(8)][-1])
                    qev = p.step("vector", [qmm], lambda v, hpa=hpa, tg=tg, qt=qt,
                                 p_q=p_q: v.tensor_copy(
                        qt[hpa][0:64, tg * 512:(tg + 1) * 512], p_q[0:64, :]))
                    qev_g[0] = qev
                    qev_cell[tg] = qev
                    qev_cell[('sq', tg)] = p.step("vector",
                        [qev] + ([qn_mm[0]] if qn_mm[0] else []),
                        lambda v, hpa=hpa, tg=tg, qt=qt, qsq16=qsq16: v.tensor_mul(
                            qsq16[:], qt[hpa][0:64, tg * 512:(tg + 1) * 512],
                            qt[hpa][0:64, tg * 512:(tg + 1) * 512]))
                return th

            def mk_qn(tg):
                def th():
                    deps = [qev_cell[('sq', tg)]] + ([nq_w[0]] if nq_w[0] else [])
                    qn_mm[0] = p.step("tensor", deps, lambda t, qsq16=qsq16,
                                      p_qn=p_qn: t.matmul(
                        p_qn[64:65, :], ones16[0:64, 0:1], qsq16[:],
                        start=True, stop=True))
                    nwd = [qn_mm[0]] + ([tail_prev[0]] if tail_prev[0] else [])
                    nq_w[0] = p.step("vector", nwd, lambda v, tg=tg, nq=nq,
                                     p_qn=p_qn: v.tensor_copy(
                        nq[0:1, tg * 512:(tg + 1) * 512], p_qn[64:65, :]))
                return th

            for tg in range(4):
                thunks.append(mk_qmm(tg))
                thunks.append(mk_qn(tg))

            def tail():
                t0 = p.step("gpsimd", [nq_w[0], tkb2], lambda g, h8=h8, nq=nq,
                            kb2=kb2: g.tensor_scalar(
                    nq[0:1, 0:TLOC], nq[0:1, 0:TLOC], kb2[0:1, h8:h8 + 1], None,
                    ALU.mult))
                t1s = p.step("scalar", [t0], lambda s, nq=nq: s.activation(
                    nq[0:1, 0:TLOC], nq[0:1, 0:TLOC], AF.Ln))
                t1s = p.step("scalar", [t1s], lambda s, nq=nq: s.activation(
                    nq[0:1, 0:TLOC], nq[0:1, 0:TLOC], AF.Exp, scale=0.5))
                t2s = p.step("gpsimd", [t1s], lambda g, nq=nq, nq16=nq16:
                    g.tensor_scalar(nq16[0:1, :], nq[0:1, 0:TLOC], -1.0, -0.08,
                                    ALU.mult, ALU.add))
                qrow_tok[hpa] = p.dma("gpsimd", [t2s] +
                                      ([dep_done] if dep_done else []),
                                      [(qt[hpa][64:65, :], nq16[0:1, :])])
                qevl[hpa] = qev_cell[3]
                tail_prev[0] = qrow_tok[hpa]
            return thunks, tail

        emit_kf(0, None)
        th0, tl0 = make_build(0, None)
        for t in th0:
            t()
        tl0()
        flash_done = [None]
        for h8 in range(8):
            hpa = h8 % 2
            pp, hd = h8 // 2, h8 % 2
            prev_done = flash_done[0]
            nb = make_build(h8 + 1, prev_done) if h8 < 7 else None
            pend = list(nb[0]) if nb else []

            def emit_avs(pbx, etok, asl, h8=h8):
                basex = (pbx % 2) * 1024
                tokav = None
                for i in range(2):
                    kb = 2 * pbx + i
                    avdeps = [etok] + ([tok_vf] if kb == 0 else []) + \
                             ([on_prev_sl[asl]] if kb == 0 and on_prev_sl[asl]
                              else [])
                    tokav = p.step("tensor", avdeps, lambda t, kb=kb,
                                   col=basex + i * 512, h8=h8, vf=vf, ex16=ex16,
                                   p_av=p_av, asl=asl: t.matmul(
                        p_av[:, asl * 512:(asl + 1) * 512],
                        vf[:, kb * 8 * VW + h8 * VW: kb * 8 * VW + h8 * VW + 65],
                        ex16[:, col:col + 512],
                        start=(kb == 0), stop=(kb == NKB - 1)))
                av_w[pbx % 2] = tokav
                av_prev[0] = tokav

            for qq in range(4):
                asl = avq[0]
                avq[0] ^= 1
                prev = None
                for pb in range(NKB // 2):
                    ps_ = pb % 2
                    base = ps_ * 1024
                    kb0 = 2 * pb
                    sdeps = [kfo_tok[hpa], qrow_tok[hpa], qevl[hpa]] + \
                            ([exp_w[ps_]] if exp_w[ps_] else [])
                    smm = p.step("tensor", sdeps, lambda t, qq=qq, kb0=kb0,
                                 base=base, hpa=hpa, kfh=kfh, qt=qt, p_s=p_s: [
                        t.matmul(p_s[:, base + i * 512: base + i * 512 + 512],
                                 kfh[hpa][0:65, (kb0 + i) * 128:(kb0 + i + 1) * 128],
                                 qt[hpa][0:65, qq * 512:(qq + 1) * 512],
                                 start=True, stop=True) for i in range(2)][-1])
                    edeps = [smm] + ([av_w[ps_]] if av_w[ps_] else [])
                    etok = p.step("scalar", edeps, lambda s, base=base, p_s=p_s,
                                  ex16=ex16: s.activation(
                        ex16[:, base:base + 1024], p_s[:, base:base + 1024],
                        AF.Exp, scale=ISC))
                    exp_w[ps_] = etok
                    if prev is not None:
                        emit_avs(prev[0], prev[1], asl)
                    prev = (pb, etok)
                    if pend and qq == 1 and pb % 2 == 0:
                        pend.pop(0)()
                emit_avs(prev[0], prev[1], asl)
                if nb and qq == 0:
                    emit_kf(h8 + 1, prev_done)
                if nb and qq == 2:
                    nb[1]()
                t1 = p.step("vector", [av_prev[0]] + ([rd_w[0]] if rd_w[0] else []),
                            lambda v, p_av=p_av, rdr=rdr, asl=asl: v.reciprocal(
                    rdr[0:1, :], p_av[64:65, asl * 512:(asl + 1) * 512]))
                t2 = p.step("tensor", [t1], lambda t, rdr=rdr, p_rdb=p_rdb: t.matmul(
                    p_rdb[:], ones32s[0:1, 0:64], rdr[0:1, :], start=True, stop=True))
                t3 = p.step("vector", [t2], lambda v, p_rdb=p_rdb, rdb_s=rdb_s:
                    v.tensor_copy(rdb_s[:], p_rdb[:]))
                on_prev[0] = p.step("vector", [t3], lambda v, pp=pp, hd=hd, qq=qq,
                                    p_av=p_av, rdb_s=rdb_s, o_half=o_half, asl=asl:
                    v.tensor_mul(
                        o_half[64 * hd:64 * hd + 64,
                               pp * TLOC + qq * 512: pp * TLOC + qq * 512 + 512],
                        p_av[0:64, asl * 512:(asl + 1) * 512], rdb_s[:]))
                rd_w[0] = on_prev[0]
                on_prev_sl[asl] = on_prev[0]
            flash_done[0] = on_prev[0]
        tok_oh2 = p.merge("vector", [on_prev[0]])

        p.free(f"qt0_{hh}", f"qt1_{hh}", f"kfh0_{hh}", f"kfh1_{hh}",
               f"nq_{hh}", f"nq16_{hh}", f"ex16_{hh}", f"qsq16_{hh}",
               f"rdr_{hh}", f"rdb_s_{hh}", f"kb2_{hh}")
        bar_wo = p.barrier()
        wo2 = p.sb(f"wo2_{hh}", [128, 4 * D], BF16)
        bo16 = p.sb(f"bo16_{hh}", [1, D], BF16)
        tok_wo2 = p.dma("gpsimd", [bar_wo], [
            (wo2[:, pp * D:(pp + 1) * D],
             WoutT[hh * 512 + pp * 128: hh * 512 + pp * 128 + 128, :])
            for pp in range(4)] + [(bo16[:], b_out_in)])
        wo_mm = [None, None]; wo_ev = [None, None]
        for i in range(NT):
            pa = i % 2
            deps = [tok_oh2, tok_wo2, exp_w[0], exp_w[1]] + \
                   ([wo_ev[pa]] if wo_ev[pa] else [])
            wo_mm[pa] = p.step("tensor", deps, lambda t, i=i, pa=pa, hh=hh,
                               o_half=o_half, wo2=wo2, bo16=bo16: ([
                t.matmul(ps_big[:, pa * 1024 + n2 * 512: pa * 1024 + n2 * 512 + 512],
                         o_half[:, pp * TLOC + i * 128: pp * TLOC + i * 128 + 128],
                         wo2[:, pp * D + n2 * 512: pp * D + n2 * 512 + 512],
                         start=(pp == 0), stop=(pp == 3 and hh == 1))
                for n2 in range(2) for pp in range(4)] + ([
                t.matmul(ps_big[:, pa * 1024 + n2 * 512: pa * 1024 + n2 * 512 + 512],
                         ones16[0:1, 0:128], bo16[0:1, n2 * 512: n2 * 512 + 512],
                         start=False, stop=True)
                for n2 in range(2)] if hh == 0 else []))[-1])
            wo_ev[pa] = p.step("vector", [wo_mm[pa], xadd_last2[0]],
                               lambda v, i=i, pa=pa: [
                v.tensor_add(x[:, i * D + n2 * 512: i * D + n2 * 512 + 512],
                             x[:, i * D + n2 * 512: i * D + n2 * 512 + 512],
                             ps_big[:, pa * 1024 + n2 * 512: pa * 1024 + n2 * 512 + 512])
                for n2 in range(2)][-1])
            if hh == 1:
                p.out_dma([wo_ev[pa]],
                          [(x_out[i * 128:(i + 1) * 128, :], x[:, i * D:(i + 1) * D])],
                          wait=False)
        xadd_last2[0] = p.merge("vector", [wo_ev[0], wo_ev[1]])
        p.set_guard("wq2", xadd_last2[0])
        p.free(f"wo2_{hh}", f"bo16_{hh}",
               f"o_half_{hh}", f"vf_{hh}", f"knr_{hh}", f"wq2_{hh}")
    p.free("kbs_1", "kbs_0")

    return _finish(None, x)


def kernel(_upto=99, **inputs):
    g, shards = _prep_inputs(inputs)
    nc = build_nc(_upto)
    in_maps = []
    for c in range(NCORES):
        m = dict(g)
        m["x_in"] = shards[c]
        m.update(_host_consts(c))
        in_maps.append(m)
    res = run_bass_kernel_spmd(nc, in_maps, core_ids=list(range(NCORES)))
    out = np.concatenate([res.results[c]["x_out"] for c in range(NCORES)], axis=0)
    return out.reshape(B, T, D)
